# version 04
# baseline (speedup 1.0000x reference)
import numpy as np
import concourse.bass as bass
import concourse.mybir as mybir
from concourse.bass_utils import run_bass_kernel_spmd

F32 = mybir.dt.float32
BF16 = mybir.dt.bfloat16
I32 = mybir.dt.int32
AF = mybir.ActivationFunctionType
ALU = mybir.AluOpType
AX = mybir.AxisListType

T = 1024
D = 2048
TT = T // 128
DC = D // 128
NEG = -1.0e4
EPS = 1e-5

ENGINES = ("sync", "tensor", "vector", "scalar", "gpsimd")
L1F_CUT = 4
ALL_STAGES = ("l0a", "l0f", "l1a", "l1f", "l2a", "l2f", "l3a", "l3f")


class Ctr:
    def __init__(self, nc, name):
        self.h = nc.alloc_semaphore(name)
        self.n = 0


class Prog:

    def __init__(self, nc):
        self.nc = nc
        self.q = {e: [] for e in ENGINES}
        self.ctrs = {}
        self.pools = {}
        self.pool_i = {}
        self.waited = {e: {} for e in ENGINES}
        self.selfc = {}

    def ctr(self, name):
        if name not in self.ctrs:
            self.ctrs[name] = Ctr(self.nc, name)
        return self.ctrs[name]

    def _dedup(self, e, wait):
        w = []
        for c, v in wait:
            if v <= 0:
                continue
            key = id(c)
            if v > self.waited[e].get(key, 0):
                self.waited[e][key] = v
                w.append((c.h, v))
        return w

    def op(self, e, fn, wait=(), inc=None, k=1):
        selfc = self.selfc.get(e)
        wait = list(wait)
        if selfc is not None:
            wait.append((selfc, selfc.n))
            if inc is None:
                inc = selfc
        w = self._dedup(e, wait)
        if inc is not None:
            inc.n += k

        def g(eng, w=w, fn=fn, inc=inc, k=k):
            for h, v in w:
                eng.wait_ge(h, v)
            bi = fn(eng)
            if inc is not None:
                bi.then_inc(inc.h, k)

        self.q[e].append(g)
        return inc.n if inc is not None else None

    def dma(self, e, fn, wait=(), pool=None, nctr=8):
        pool = pool or e
        key = (e, pool)
        if key not in self.pools:
            self.pools[key] = [self.ctr(f"dma_{e}_{pool}_{i}") for i in range(nctr)]
            self.pool_i[key] = -1
        self.pool_i[key] = (self.pool_i[key] + 1) % len(self.pools[key])
        c = self.pools[key][self.pool_i[key]]
        self.op(e, fn, wait=list(wait) + [(c, c.n)], inc=c, k=16)
        return (c, c.n)

    def wait(self, e, waits):
        w = self._dedup(e, waits)
        if not w:
            return

        def g(eng, w=w):
            for h, v in w:
                eng.wait_ge(h, v)

        self.q[e].append(g)

    def finalize(self):
        with self.nc.Block() as block:
            def runner(stream):
                def body(eng):
                    for g in stream:
                        g(eng)
                return body
            block.sync(runner(self.q["sync"]))
            block.tensor(runner(self.q["tensor"]))
            block.vector(runner(self.q["vector"]))
            block.scalar(runner(self.q["scalar"]))
            block.gpsimd(runner(self.q["gpsimd"]))


class Arena:
    def __init__(self, nc, name, cols):
        self.t = nc.alloc_sbuf_tensor(name, [128, cols], F32)
        self.cols = cols
        self.off = 0
        self.marks = []
        self.hi = 0

    def alloc(self, cols):
        assert self.off + cols <= self.cols, \
            f"arena overflow {self.off}+{cols}>{self.cols}"
        ap = self.t.ap()[:, self.off:self.off + cols]
        self.off += cols
        self.hi = max(self.hi, self.off)
        return ap

    def mark(self):
        self.marks.append(self.off)

    def release(self):
        self.off = self.marks.pop()



def build_program(stages=ALL_STAGES):
    nc = bass.Bass("TRN2", target_bir_lowering=False, debug=False)
    p = Prog(nc)

    def din(name, shape, dtype=F32):
        return nc.dram_tensor(name, list(shape), dtype, kind="ExternalInput").ap()

    x0 = din("x0", (T, D))
    ident_in = din("ident", (128, 128))
    gates_in = din("gates", (128, 8))
    rotm_in = din("rotm", (128, 128)) if "l0a" in stages else None

    W = {}

    def wdecl(name, shape):
        W[name] = din(name, shape)

    if "l0f" in stages:
        wdecl("l0f_wT", (DC, 128, D))
        wdecl("l0f_acat", (128, 2048))
        wdecl("l0f_bcat", (128, 2048))
        wdecl("l0f_gwt", (128, 128))
        wdecl("l0f_brow", (1, D))
        wdecl("l0f_gbrow", (1, 8))
    if "l3f" in stages:
        wdecl("l3f_w1", (DC, 128, D))
        wdecl("l3f_vcat", (DC, 128, 512))
        wdecl("l3f_ucat", (4, 128, D))
        wdecl("l3f_w2", (128, 128))
        wdecl("l3f_b1f", (128, 16))
        wdecl("l3f_b2row", (1, 8))
    if "l1a" in stages:
        wdecl("l1a_wq", (DC, 128, D))
        wdecl("l1a_wk", (DC, 128, D))
        wdecl("l1a_wv", (DC, 128, D))
        wdecl("l1a_woT", (DC, 128, D))
        wdecl("l1a_bq", (128, 16))
        wdecl("l1a_bk", (128, 16))
        wdecl("l1a_bv", (128, 16))
        wdecl("l1a_bo", (1, D))
    if "l2a" in stages:
        wdecl("l2a_win", (32, 128, D))
        wdecl("l2a_cw", (DC, 128, 8192))
        wdecl("l2a_convb", (128, 16))
        wdecl("l2a_wx", (128, DC * 160))
        wdecl("l2a_wdt", (128, D))
        wdecl("l2a_bdt", (128, 16))
        wdecl("l2a_alog", (DC, 128, 16))
        wdecl("l2a_dD", (128, 16))
        wdecl("l2a_wout", (DC, 128, D))
    if "l2f" in stages:
        wdecl("l2f_gw", (DC, 128, 8192))
        wdecl("l2f_vw", (DC, 128, 8192))
        wdecl("l2f_ow", (64, 128, D))
        wdecl("l2f_gb", (1, 8192))
        wdecl("l2f_vb", (1, 8192))
        wdecl("l2f_ob", (1, D))
    if "l1f" in stages:
        wdecl("l1f_gw", (44, 128, D))
        wdecl("l1f_uw", (44, 128, D))
        wdecl("l1f_dw", (44, 128, D))
        wdecl("l1f_gs", (128, 44))
        wdecl("l1f_us", (128, 44))
        wdecl("l1f_ds", (1, D))

    out = nc.dram_tensor("out", [T, D], F32, kind="ExternalOutput").ap()

    dram = {}
    for pre, kvc in (("l3a", 16), ("l0a", 4)):
        if pre not in stages:
            continue
        KVp = kvc * 128
        wdecl(pre + "_wq", (DC, 128, D))
        wdecl(pre + "_wk", (kvc, 128, D))
        wdecl(pre + "_wv", (DC, 128, KVp))
        wdecl(pre + "_woT", (DC, 128, D))
        wdecl(pre + "_bqf", (128, 16))
        wdecl(pre + "_bkf", (128, max(kvc, 1)))
        wdecl(pre + "_bv", (1, KVp))
        wdecl(pre + "_bo", (1, D))
        wdecl(pre + "_trisel", (128, 4096))
        wdecl(pre + "_kmaskq", (128, 32))
        if pre == "l0a":
            wdecl(pre + "_cos", (128, T))
            wdecl(pre + "_sinm", (128, T))
        dram[pre + "_qf"] = nc.dram_tensor(pre + "_qf", [D, T], F32)
        dram[pre + "_kin"] = nc.dram_tensor(pre + "_kin", [KVp, T], F32)
        dram[pre + "_kout"] = nc.dram_tensor(pre + "_kout", [2 * KVp, T], F32)
        dram[pre + "_vin"] = nc.dram_tensor(pre + "_vin", [T, KVp], F32)
        dram[pre + "_vout"] = nc.dram_tensor(pre + "_vout", [2 * T, KVp], F32)
        dram[pre + "_mkin"] = nc.dram_tensor(pre + "_mkin", [1, 2], F32)
        dram[pre + "_mkout"] = nc.dram_tensor(pre + "_mkout", [2, 2], F32)
    if "l2a" in stages:
        for nm in ("l2a_xc", "l2a_r", "l2a_xs", "l2a_u", "l2a_delta",
                   "l2a_y", "l2a_z"):
            dram[nm] = nc.dram_tensor(nm, [D, T], F32)
        dram["l2a_bc"] = nc.dram_tensor("l2a_bc", [32, T], F32)
        dram["l2a_hloin"] = nc.dram_tensor("l2a_hloin", [128, 48], F32)
        dram["l2a_hloout"] = nc.dram_tensor("l2a_hloout", [256, 48], F32)
        dram["l2a_hin"] = nc.dram_tensor("l2a_hin", [128, 256], F32)
        dram["l2a_hout"] = nc.dram_tensor("l2a_hout", [256, 256], F32)
    if "l2f" in stages:
        dram["l2f_h"] = nc.dram_tensor("l2f_h", [T, 8192], F32)
        if "l2a" in stages:
            dram["l2f_tg"] = nc.dram_tensor("l2f_tg", [DC, 128, 8192], BF16)
            dram["l2f_tv"] = nc.dram_tensor("l2f_tv", [DC, 128, 8192], BF16)
            dram["l2f_to"] = nc.dram_tensor("l2f_to", [64, 128, D], BF16)
        dram["l2f_kh"] = nc.dram_tensor("l2f_kh", [64, 128, T], BF16)
    if "l1a" in stages:
        dram["l1a_q"] = nc.dram_tensor("l1a_q", [D, T], F32)
        dram["l1a_kv"] = nc.dram_tensor("l1a_kv", [D, T], F32)
        dram["l1a_cin"] = nc.dram_tensor("l1a_cin", [128, 16], F32)
        dram["l1a_cout"] = nc.dram_tensor("l1a_cout", [256, 16], F32)

    xs_t = nc.alloc_sbuf_tensor("x_stream", [128, TT * D], F32)

    def xtile(tt):
        return xs_t.ap()[:, tt * D:(tt + 1) * D]

    sident = nc.alloc_sbuf_tensor("sident", [128, 128], F32)
    rotm_s = nc.alloc_sbuf_tensor("rotm_sb", [128, 128], F32)
    ones_r = nc.alloc_sbuf_tensor("ones_r", [1, 128], F32)
    ones_c = nc.alloc_sbuf_tensor("ones_c", [128, 1], F32)
    lnbuf = nc.alloc_sbuf_tensor("lnbuf", [128, 16], F32)
    gatesb = nc.alloc_sbuf_tensor("gatesb", [128, 8], F32)
    arena = Arena(nc, "arena", 36350)
    gpre = None
    if "l2f" in stages and "l2a" in stages:
        gpre = arena.alloc(8)

    ps = {n: nc.alloc_psum_tensor(n, [128, 512], F32)
          for n in ("psA", "psB", "psC", "psD", "psE", "psF", "psG", "psH")}

    guard = {}

    def gget(key):
        return guard.get(key, None)

    def gset(key, ctr):
        guard[key] = (ctr, ctr.n)

    def gwait(key):
        g = guard.get(key)
        return [g] if g else []

    cL = p.ctr("c_load")
    cS = p.ctr("c_store")
    cT = p.ctr("c_tensor")
    cV = p.ctr("c_vector")
    cSc = p.ctr("c_scalar")
    cG = p.ctr("c_gpsimd")
    p.selfc = {"vector": cV, "scalar": cSc, "gpsimd": cG}

    stream_ready = []
    for tt in range(TT):
        stream_ready.append(p.dma("sync", lambda e, tt=tt: e.dma_start(
            out=xtile(tt), in_=x0[tt * 128:(tt + 1) * 128, :])))
    stream_ready.append(p.dma("sync", lambda e: e.dma_start(
        out=sident.ap()[:, :], in_=ident_in)))
    stream_ready.append(p.dma("sync", lambda e: e.dma_start(
        out=gatesb.ap()[:, :], in_=gates_in)))
    if "l0a" in stages:
        rot_ready = p.dma("sync", lambda e: e.dma_start(
            out=rotm_s.ap()[:, :], in_=rotm_in))
    else:
        rot_ready = stream_ready[0]
    p.op("vector", lambda e: e.memset(ones_r.ap()[:, :], 1.0), inc=cV)
    p.op("vector", lambda e: e.memset(ones_c.ap()[:, :], 1.0), inc=cV)
    p.op("vector", lambda e: e.memset(lnbuf.ap()[:, 8:9], float(EPS)), inc=cV)
    ones_ready = cV.n


    TPB = ("psA", "psB", "psC", "psD")

    def transpose_tiles(pairs, wait=()):
        first = list(wait)
        for j, (src, dst) in enumerate(pairs):
            bank = TPB[j % 4]
            w = gwait(bank) + (first if j < 4 else [])
            p.op("tensor", lambda e, src=src, bank=bank: e.transpose(
                ps[bank].ap()[:, 0:128], src,
                sident.ap()[:, :]), wait=w, inc=cT)
            td = cT.n
            p.op("vector", lambda e, dst=dst, bank=bank: e.tensor_copy(
                dst, ps[bank].ap()[:, 0:128]),
                wait=[(cT, td)], inc=cV)
            gset(bank, cV)
        return cV.n

    def ln_tile_inplace(tile_ap, sq_ap, wait=()):
        b = lnbuf.ap()
        s_sum, s_sq, s_mean, s_var = b[:, 0:1], b[:, 1:2], b[:, 2:3], b[:, 3:4]
        s_std, s_rstd, s_bias, s_msq = b[:, 4:5], b[:, 5:6], b[:, 6:7], b[:, 7:8]
        p.op("vector", lambda e: e.reduce_sum(s_sum, tile_ap, AX.X),
             wait=list(wait) + gwait("lnbuf"), inc=cV)
        vsum = cV.n
        p.op("scalar", lambda e: e.activation(sq_ap, tile_ap, AF.Square,
                                              accum_out=s_sq),
             wait=list(wait) + gwait("lnbuf"), inc=cSc)
        ssq = cSc.n
        p.op("vector", lambda e: e.tensor_scalar(s_mean, s_sum, 1.0 / D, None,
                                                 ALU.mult),
             wait=[(cV, vsum)], inc=cV)
        p.op("vector", lambda e: e.tensor_mul(s_msq, s_mean, s_mean), inc=cV)
        p.op("vector", lambda e: e.tensor_scalar(s_var, s_sq, 1.0 / D, None,
                                                 ALU.mult),
             wait=[(cSc, ssq)], inc=cV)
        p.op("vector", lambda e: e.tensor_sub(s_var, s_var, s_msq), inc=cV)
        vvar = cV.n
        p.op("scalar", lambda e: e.activation(s_std, s_var, AF.Sqrt,
                                              bias=lnbuf.ap()[:, 8:9]),
             wait=[(cV, vvar), (cV, ones_ready)], inc=cSc)
        sstd = cSc.n
        p.op("vector", lambda e: e.reciprocal(s_rstd, s_std),
             wait=[(cSc, sstd)], inc=cV)
        p.op("vector", lambda e: e.tensor_mul(s_bias, s_mean, s_rstd), inc=cV)
        p.op("vector", lambda e: e.tensor_scalar(s_bias, s_bias, -1.0, None,
                                                 ALU.mult), inc=cV)
        vaff = cV.n
        p.op("scalar", lambda e: e.activation(tile_ap, tile_ap, AF.Identity,
                                              bias=s_bias, scale=s_rstd),
             wait=[(cV, vaff)], inc=cSc)
        gset("lnbuf", cSc)
        return cSc.n

    def bcast_row(row_ap, dst_ap, ncols, wait=()):
        done = 0
        for j in range(0, ncols, 512):
            w = min(512, ncols - j)
            p.op("tensor", lambda e, j=j, w=w: e.matmul(
                ps["psF"].ap()[:, 0:w], ones_r.ap()[:, :], row_ap[:, j:j + w],
                start=True, stop=True),
                wait=list(wait) + [(cV, ones_ready)] + gwait("psF"), inc=cT)
            td = cT.n
            p.op("vector", lambda e, j=j, w=w: e.tensor_copy(
                dst_ap[:, j:j + w], ps["psF"].ap()[:, 0:w]),
                wait=[(cT, td)], inc=cV)
            gset("psF", cV)
            done = cV.n
        return done

    def stage_l1f(ready):
        IC = 44
        arena.mark()
        xT_h = arena.alloc(DC * 512)
        hbuf = arena.alloc(11 * 512)
        wbuf = arena.alloc(2 * 2 * D)
        dwb = arena.alloc(2 * 512)
        dsb = arena.alloc(D)
        dsrow_a = arena.alloc(D)
        acc = arena.alloc(4 * D)
        scl = arena.alloc(96)
        sq = xT_h[:, 0:D]

        gs_s, us_s = scl[:, 0:44], scl[:, 44:88]
        t1 = p.dma("sync", lambda e: e.dma_start(out=gs_s, in_=W["l1f_gs"]))
        t2 = p.dma("sync", lambda e: e.dma_start(out=us_s, in_=W["l1f_us"]))
        scl_ready = [t1, t2]
        dsrow = dsrow_a[0:1, :]
        t3 = p.dma("sync", lambda e: e.dma_start(out=dsrow, in_=W["l1f_ds"]))
        dsb_done = bcast_row(dsrow, dsb, D, wait=[t3])

        ln_waits = []
        for th in range(2):
            tok0 = th * 4
            pairs = []
            for tt4 in range(4):
                for dc in range(DC):
                    pairs.append((xtile(tok0 + tt4)[:, dc * 128:(dc + 1) * 128],
                                  xT_h[:, dc * 512 + tt4 * 128:dc * 512 + tt4 * 128 + 128]))
            xt_done = transpose_tiles(pairs, wait=list(ready))

            def xT_col(dc):
                return xT_h[:, dc * 512:(dc + 1) * 512]

            if L1F_CUT < 2:
                continue

            for grp in range(4):
                for icg in range(11):
                    ic = grp * 11 + icg
                    slot = ic % 2
                    gw_s = wbuf[:, slot * 2 * D:slot * 2 * D + D]
                    uw_s = wbuf[:, slot * 2 * D + D:slot * 2 * D + 2 * D]
                    tg = p.dma("sync", lambda e, ic=ic, gw_s=gw_s: e.dma_start(
                        out=gw_s, in_=W["l1f_gw"][ic]),
                        wait=gwait(("l1f_w", slot)))
                    tu = p.dma("sync", lambda e, ic=ic, uw_s=uw_s: e.dma_start(
                        out=uw_s, in_=W["l1f_uw"][ic]))
                    for dc in range(DC):
                        p.op("tensor", lambda e, dc=dc, gw_s=gw_s: e.matmul(
                            ps["psA"].ap()[:, :], gw_s[:, dc * 128:(dc + 1) * 128],
                            xT_col(dc), start=(dc == 0), stop=(dc == DC - 1)),
                            wait=([tg, tu, (cV, xt_done)] + gwait("psA"))
                            if dc == 0 else (), inc=None)
                    for dc in range(DC):
                        p.op("tensor", lambda e, dc=dc, uw_s=uw_s: e.matmul(
                            ps["psB"].ap()[:, :], uw_s[:, dc * 128:(dc + 1) * 128],
                            xT_col(dc), start=(dc == 0), stop=(dc == DC - 1)),
                            wait=gwait("psB") if dc == 0 else (),
                            inc=(cT if dc == DC - 1 else None))
                    gset(("l1f_w", slot), cT)
                    mm = cT.n
                    hsl = hbuf[:, icg * 512:(icg + 1) * 512]
                    p.op("scalar", lambda e, ic=ic, hsl=hsl: e.activation(
                        hsl, ps["psA"].ap()[:, :], AF.Silu,
                        scale=gs_s[:, ic:ic + 1]),
                        wait=[(cT, mm)] + scl_ready + gwait("l1f_hfree"),
                        inc=cSc)
                    sg = cSc.n
                    gset("psA", cSc)
                    p.op("vector", lambda e, ic=ic, hsl=hsl: e.scalar_tensor_tensor(
                        hsl, ps["psB"].ap()[:, :], us_s[:, ic:ic + 1], hsl,
                        ALU.mult, ALU.mult),
                        wait=[(cSc, sg)], inc=cV)
                    gset("psB", cV)
                    guard[("hready", icg)] = (cV, cV.n)

                if L1F_CUT < 3:
                    continue
                pnames = ("psC", "psD", "psE", "psF")
                for og in range(4):
                    for icg in range(11):
                        ic = grp * 11 + icg
                        dslot = icg % 2
                        dw_s = dwb[:, dslot * 512:(dslot + 1) * 512]
                        tdw = p.dma("sync", lambda e, ic=ic, og=og, dw_s=dw_s: e.dma_start(
                            out=dw_s, in_=W["l1f_dw"][ic, :, og * 512:(og + 1) * 512]),
                            wait=gwait(("l1f_dw", dslot)))
                        for t4 in range(4):
                            p.op("tensor", lambda e, icg=icg, t4=t4, dw_s=dw_s: e.matmul(
                                ps[pnames[t4]].ap()[:, :],
                                hbuf[:, icg * 512 + t4 * 128:icg * 512 + (t4 + 1) * 128],
                                dw_s, start=(icg == 0), stop=(icg == 10)),
                                wait=[tdw, guard[("hready", icg)]]
                                + (gwait(pnames[t4]) if icg == 0 else []),
                                inc=(cT if (t4 == 3) else None))
                        gset(("l1f_dw", dslot), cT)
                    mmd = cT.n
                    for t4 in range(4):
                        dst = acc[:, t4 * D + og * 512:t4 * D + (og + 1) * 512]
                        if grp == 0:
                            p.op("vector", lambda e, t4=t4, dst=dst: e.tensor_copy(
                                dst, ps[pnames[t4]].ap()[:, :]),
                                wait=[(cT, mmd)], inc=cV)
                        else:
                            p.op("vector", lambda e, t4=t4, dst=dst: e.tensor_add(
                                dst, dst, ps[pnames[t4]].ap()[:, :]),
                                wait=[(cT, mmd)], inc=cV)
                        gset(pnames[t4], cV)
                gset("l1f_hfree", cT)
            if L1F_CUT < 4:
                continue
            for tt4 in range(4):
                tt = tok0 + tt4
                for og in range(4):
                    a_sl = acc[:, tt4 * D + og * 512:tt4 * D + (og + 1) * 512]
                    x_sl = xtile(tt)[:, og * 512:(og + 1) * 512]
                    p.op("vector", lambda e, a_sl=a_sl, og=og: e.tensor_mul(
                        a_sl, a_sl, dsb[:, og * 512:(og + 1) * 512]),
                        wait=[(cV, dsb_done)], inc=cV)
                    vm = cV.n
                    p.op("vector", lambda e, x_sl=x_sl, a_sl=a_sl: e.tensor_add(
                        x_sl, x_sl, a_sl), wait=[(cV, vm)], inc=cV)
                vres = cV.n
                ln_waits.append(ln_tile_inplace(xtile(tt), sq, wait=[(cV, vres)]))
        arena.release()
        if L1F_CUT < 4:
            return [(cT, cT.n), (cV, cV.n), (cSc, cSc.n)]
        return [(cSc, max(ln_waits))]

    def stage_l0f(ready):
        arena.mark()
        xT = arena.alloc(DC * T)
        acat = arena.alloc(DC * 128)
        bcat = arena.alloc(128 * 16)
        gwt = arena.alloc(DC * 8)
        wrow_a = arena.alloc(D)
        gbrow_a = arena.alloc(8)
        wtile = arena.alloc(2 * 512)
        lgb = arena.alloc(TT * 8)
        wgb = arena.alloc(TT * 8)
        sgb = arena.alloc(40)
        xaw_t = arena.alloc(128)
        xaw_f = arena.alloc(TT * 128)
        sq = acat[:, 0:D]

        tw = [p.dma("sync", lambda e: e.dma_start(out=acat, in_=W["l0f_acat"])),
              p.dma("sync", lambda e: e.dma_start(out=bcat, in_=W["l0f_bcat"])),
              p.dma("sync", lambda e: e.dma_start(out=gwt, in_=W["l0f_gwt"])),
              p.dma("sync", lambda e: e.dma_start(out=wrow_a[0:1, :], in_=W["l0f_brow"])),
              p.dma("sync", lambda e: e.dma_start(out=gbrow_a[0:1, :], in_=W["l0f_gbrow"]))]

        pairs = []
        for tt in range(TT):
            for dc in range(DC):
                pairs.append((xtile(tt)[:, dc * 128:(dc + 1) * 128],
                              xT[:, dc * T + tt * 128:dc * T + (tt + 1) * 128]))
        xt_done = transpose_tiles(pairs, wait=list(ready))

        def xTt(dc, tt):
            return xT[:, dc * T + tt * 128:dc * T + (tt + 1) * 128]

        for tt in range(TT):
            for dc in range(DC):
                p.op("tensor", lambda e, dc=dc, tt=tt: e.matmul(
                    ps["psE"].ap()[:, 0:8], xTt(dc, tt), gwt[:, dc * 8:(dc + 1) * 8],
                    start=(dc == 0), stop=False),
                    wait=([(cV, xt_done)] + tw + gwait("psE")) if dc == 0 else (),
                    inc=None)
            p.op("tensor", lambda e: e.matmul(
                ps["psE"].ap()[:, 0:8], ones_r.ap()[:, :], gbrow_a[0:1, 0:8],
                start=False, stop=True), wait=[(cV, ones_ready)], inc=cT)
            mm = cT.n
            lg = lgb[:, tt * 8:(tt + 1) * 8]
            wg = wgb[:, tt * 8:(tt + 1) * 8]
            m1, m2 = sgb[:, 32:33], sgb[:, 33:34]
            den, rcp = sgb[:, 34:35], sgb[:, 35:36]
            tb, msk, et = sgb[:, 0:8], sgb[:, 8:16], sgb[:, 16:24]
            p.op("vector", lambda e, lg=lg: e.tensor_copy(lg, ps["psE"].ap()[:, 0:8]),
                 wait=[(cT, mm)], inc=cV)
            gset("psE", cV)
            p.op("vector", lambda e, lg=lg: e.reduce_max(m1, lg, AX.X))
            p.op("vector", lambda e, lg=lg: e.tensor_scalar(tb, lg, m1, None, ALU.subtract))
            p.op("vector", lambda e: e.tensor_scalar(msk, tb, 0.0, None, ALU.is_ge))
            p.op("vector", lambda e: e.scalar_tensor_tensor(msk, msk, -1e9, tb,
                                                            ALU.mult, ALU.add))
            p.op("vector", lambda e: e.reduce_max(m2, msk, AX.X))
            p.op("vector", lambda e: e.tensor_scalar(msk, tb, m2, None, ALU.is_ge),
                 inc=cV)
            vsel = cV.n
            p.op("scalar", lambda e: e.activation(et, tb, AF.Exp),
                 wait=[(cV, vsel)], inc=cSc)
            se = cSc.n
            p.op("vector", lambda e: e.tensor_mul(et, et, msk), wait=[(cSc, se)])
            p.op("vector", lambda e: e.reduce_sum(den, et, AX.X))
            p.op("vector", lambda e: e.reciprocal(rcp, den))
            p.op("vector", lambda e, wg=wg: e.tensor_scalar(wg, et, rcp, None,
                                                            ALU.mult), inc=cV)
            guard[("l0f_wg", tt)] = (cV, cV.n)

        for tt in range(TT):
            for dc in range(DC):
                p.op("tensor", lambda e, dc=dc, tt=tt: e.matmul(
                    ps["psF"].ap()[:, 0:128], xTt(dc, tt), acat[:, dc * 128:(dc + 1) * 128],
                    start=(dc == 0), stop=(dc == DC - 1)),
                    wait=(gwait("psF")) if dc == 0 else (),
                    inc=(cT if dc == DC - 1 else None))
            mm = cT.n
            wg = wgb[:, tt * 8:(tt + 1) * 8]
            p.op("vector", lambda e, tt=tt, wg=wg: e.tensor_tensor(
                xaw_t.rearrange("p (e r) -> p e r", r=16),
                ps["psF"].ap()[:, 0:128].rearrange("p (e r) -> p e r", r=16),
                wg.broadcast_to([128, 8, 16]), ALU.mult),
                wait=[(cT, mm), guard[("l0f_wg", tt)]], inc=cV)
            gset("psF", cV)
            vx = cV.n
            transpose_tiles([(xaw_t, xaw_f[:, tt * 128:(tt + 1) * 128])],
                            wait=[(cV, vx)])
        xawf_done = cV.n

        pnames8 = ("psA", "psB", "psC", "psD", "psE", "psF", "psG", "psH")
        res_done = {}
        for oc in range(4):
            for dc in range(DC):
                slot = dc % 2
                wt = wtile[:, slot * 512:(slot + 1) * 512]
                twt = p.dma("sync", lambda e, dc=dc, oc=oc, wt=wt: e.dma_start(
                    out=wt, in_=W["l0f_wT"][dc, :, oc * 512:(oc + 1) * 512]),
                    wait=gwait(("l0f_wt", slot)))
                for tt in range(TT):
                    p.op("tensor", lambda e, dc=dc, tt=tt, wt=wt: e.matmul(
                        ps[pnames8[tt]].ap()[:, :], xTt(dc, tt), wt,
                        start=(dc == 0), stop=False),
                        wait=[twt] + (gwait(pnames8[tt]) + [(cV, xawf_done)]
                                      if dc == 0 else []),
                        inc=(cT if tt == TT - 1 else None))
                gset(("l0f_wt", slot), cT)
            for tt in range(TT):
                p.op("tensor", lambda e, tt=tt, oc=oc: e.matmul(
                    ps[pnames8[tt]].ap()[:, :], xaw_f[:, tt * 128:(tt + 1) * 128],
                    bcat[:, oc * 512:(oc + 1) * 512], start=False, stop=False),
                    inc=None)
                p.op("tensor", lambda e, tt=tt, oc=oc: e.matmul(
                    ps[pnames8[tt]].ap()[:, :], ones_r.ap()[:, :],
                    wrow_a[0:1, oc * 512:(oc + 1) * 512], start=False, stop=True),
                    inc=cT)
                mm = cT.n
                p.op("vector", lambda e, tt=tt, oc=oc: e.tensor_add(
                    xtile(tt)[:, oc * 512:(oc + 1) * 512],
                    xtile(tt)[:, oc * 512:(oc + 1) * 512],
                    ps[pnames8[tt]].ap()[:, :]), wait=[(cT, mm)], inc=cV)
                gset(pnames8[tt], cV)
                res_done[(tt, oc)] = (cV, cV.n)
        ln_waits = []
        for tt in range(TT):
            ln_waits.append(ln_tile_inplace(
                xtile(tt), sq, wait=[res_done[(tt, oc)] for oc in range(4)]))
        arena.release()
        return [(cSc, max(ln_waits))]

    def stage_l3f(ready):
        arena.mark()
        xT = arena.alloc(DC * T)
        vxb = arena.alloc(TT * 512)
        vxsf = arena.alloc(TT * 512)
        w1b = arena.alloc(2 * D)
        vct = arena.alloc(2 * 512)
        uct = arena.alloc(2 * 512)
        hec = arena.alloc(T)
        w2s = arena.alloc(DC * 8)
        alps = arena.alloc(TT * 8)
        alph = arena.alloc(TT * 8)
        b1f = arena.alloc(16)
        onesrow = arena.alloc(512)
        b2r = arena.alloc(8)
        alphaT = arena.alloc(T)
        sgb3 = arena.alloc(16)
        sq = xT[:, 0:D]

        tw = [p.dma("sync", lambda e: e.dma_start(out=w2s, in_=W["l3f_w2"])),
              p.dma("sync", lambda e: e.dma_start(out=b1f, in_=W["l3f_b1f"])),
              p.dma("sync", lambda e: e.dma_start(out=b2r[0:1, :], in_=W["l3f_b2row"]))]
        p.op("vector", lambda e: e.memset(onesrow[0:1, :], 1.0), inc=cV)
        ones512 = cV.n

        pairs = []
        for tt in range(TT):
            for dc in range(DC):
                pairs.append((xtile(tt)[:, dc * 128:(dc + 1) * 128],
                              xT[:, dc * T + tt * 128:dc * T + (tt + 1) * 128]))
        xt_done = transpose_tiles(pairs, wait=list(ready))

        def xTt(dc, tt):
            return xT[:, dc * T + tt * 128:dc * T + (tt + 1) * 128]

        pnames8 = ("psA", "psB", "psC", "psD", "psE", "psF", "psG", "psH")

        for dc in range(DC):
            slot = dc % 2
            vt = vct[:, slot * 512:(slot + 1) * 512]
            tv = p.dma("sync", lambda e, dc=dc, vt=vt: e.dma_start(
                out=vt, in_=W["l3f_vcat"][dc]), wait=gwait(("l3f_vct", slot)))
            for tt in range(TT):
                p.op("tensor", lambda e, dc=dc, tt=tt, vt=vt: e.matmul(
                    ps[pnames8[tt]].ap()[:, :], xTt(dc, tt), vt,
                    start=(dc == 0), stop=(dc == DC - 1)),
                    wait=[tv] + ([(cV, xt_done)] + gwait(pnames8[tt])
                                 if dc == 0 else []),
                    inc=(cT if tt == TT - 1 else None))
            gset(("l3f_vct", slot), cT)
        mmv = cT.n
        for tt in range(TT):
            p.op("vector", lambda e, tt=tt: e.tensor_copy(
                vxb[:, tt * 512:(tt + 1) * 512], ps[pnames8[tt]].ap()[:, :]),
                wait=[(cT, mmv)], inc=cV)
            gset(pnames8[tt], cV)

        for ec in range(DC):
            slot = ec % 2
            w1s = w1b[:, slot * D:(slot + 1) * D]
            tw1 = p.dma("sync", lambda e, ec=ec, w1s=w1s: e.dma_start(
                out=w1s, in_=W["l3f_w1"][ec]), wait=gwait(("l3f_w1", slot)))
            for tc in range(2):
                bank = "psA" if tc == 0 else "psB"
                for dc in range(DC):
                    p.op("tensor", lambda e, dc=dc, tc=tc, w1s=w1s, bank=bank: e.matmul(
                        ps[bank].ap()[:, :], w1s[:, dc * 128:(dc + 1) * 128],
                        xT[:, dc * T + tc * 512:dc * T + (tc + 1) * 512],
                        start=(dc == 0), stop=(dc == DC - 1)),
                        wait=[tw1] + (gwait(bank) if dc == 0 else []),
                        inc=(cT if dc == DC - 1 else None))
                mmh = cT.n
                p.op("scalar", lambda e, ec=ec, tc=tc, bank=bank: e.activation(
                    hec[:, tc * 512:(tc + 1) * 512], ps[bank].ap()[:, :], AF.Relu,
                    bias=b1f[:, ec:ec + 1]),
                    wait=[(cT, mmh)] + tw + gwait(("l3f_hec", tc)), inc=cSc)
                gset(bank, cSc)
                sh_done = cSc.n
                abank = "psC" if tc == 0 else "psD"
                p.op("tensor", lambda e, ec=ec, tc=tc, abank=abank: e.matmul(
                    ps[abank].ap()[0:8, :], w2s[:, ec * 8:(ec + 1) * 8],
                    hec[:, tc * 512:(tc + 1) * 512],
                    start=(ec == 0), stop=False),
                    wait=[(cSc, sh_done)] + (gwait(abank) if ec == 0 else []),
                    inc=cT)
                guard[("l3f_hec", tc)] = (cT, cT.n)
            gset(("l3f_w1", slot), cT)
        for tc in range(2):
            abank = "psC" if tc == 0 else "psD"
            p.op("tensor", lambda e, abank=abank: e.matmul(
                ps[abank].ap()[0:8, :], b2r[0:1, 0:8], onesrow[0:1, :],
                start=False, stop=True), wait=[(cV, ones512)], inc=cT)
            mmb = cT.n
            p.op("vector", lambda e, tc=tc, abank=abank: e.tensor_copy(
                alphaT[0:8, tc * 512:(tc + 1) * 512], ps[abank].ap()[0:8, :]),
                wait=[(cT, mmb)], inc=cV)
            gset(abank, cV)
        at_done = cV.n

        for tt in range(TT):
            bank = TPB[tt % 4]
            p.op("tensor", lambda e, tt=tt, bank=bank: e.transpose(
                ps[bank].ap()[:, 0:8], alphaT[0:8, tt * 128:(tt + 1) * 128],
                sident.ap()[0:8, 0:8]),
                wait=[(cV, at_done)] + gwait(bank), inc=cT)
            td = cT.n
            p.op("vector", lambda e, tt=tt, bank=bank: e.tensor_copy(
                alps[:, tt * 8:(tt + 1) * 8], ps[bank].ap()[:, 0:8]),
                wait=[(cT, td)], inc=cV)
            gset(bank, cV)
            ap_ = alps[:, tt * 8:(tt + 1) * 8]
            aw = alph[:, tt * 8:(tt + 1) * 8]
            m1, den, rcp = sgb3[:, 0:1], sgb3[:, 1:2], sgb3[:, 2:3]
            tb = sgb3[:, 8:16]
            p.op("vector", lambda e, ap_=ap_: e.reduce_max(m1, ap_, AX.X))
            p.op("vector", lambda e, ap_=ap_: e.tensor_scalar(tb, ap_, m1, None,
                                                              ALU.subtract), inc=cV)
            vt_ = cV.n
            p.op("scalar", lambda e: e.activation(tb, tb, AF.Exp),
                 wait=[(cV, vt_)], inc=cSc)
            se = cSc.n
            p.op("vector", lambda e: e.reduce_sum(den, tb, AX.X), wait=[(cSc, se)])
            p.op("vector", lambda e: e.reciprocal(rcp, den))
            p.op("vector", lambda e, aw=aw: e.tensor_scalar(aw, tb, rcp, None,
                                                            ALU.mult), inc=cV)
            guard[("l3f_aw", tt)] = (cV, cV.n)

        for tt in range(TT):
            p.op("vector", lambda e, tt=tt: e.tensor_tensor(
                hec[:, 0:512].rearrange("p (k r) -> p k r", r=64),
                vxb[:, tt * 512:(tt + 1) * 512].rearrange("p (k r) -> p k r", r=64),
                alph[:, tt * 8:(tt + 1) * 8].broadcast_to([128, 8, 64]), ALU.mult),
                wait=[guard[("l3f_aw", tt)]] + gwait(("l3f_vxs",))
                + gwait(("l3f_hec", 0)) + gwait(("l3f_hec", 1)),
                inc=cV)
            vsc = cV.n
            prs = [(hec[:, j * 128:(j + 1) * 128],
                    vxsf[:, tt * 512 + j * 128:tt * 512 + (j + 1) * 128])
                   for j in range(4)]
            transpose_tiles(prs, wait=[(cV, vsc)])
            gset(("l3f_vxs",), cV)
        vxsf_done = cV.n

        res_done = {}
        for oc in range(4):
            for krc in range(4):
                slot = krc % 2
                ut = uct[:, slot * 512:(slot + 1) * 512]
                tu = p.dma("sync", lambda e, krc=krc, oc=oc, ut=ut: e.dma_start(
                    out=ut, in_=W["l3f_ucat"][krc, :, oc * 512:(oc + 1) * 512]),
                    wait=gwait(("l3f_uct", slot)))
                for tt in range(TT):
                    p.op("tensor", lambda e, krc=krc, tt=tt, ut=ut: e.matmul(
                        ps[pnames8[tt]].ap()[:, :],
                        vxsf[:, tt * 512 + krc * 128:tt * 512 + (krc + 1) * 128], ut,
                        start=(krc == 0), stop=(krc == 3)),
                        wait=[tu, (cV, vxsf_done)] + (gwait(pnames8[tt])
                                                      if krc == 0 else []),
                        inc=(cT if tt == TT - 1 else None))
                gset(("l3f_uct", slot), cT)
            mmo = cT.n
            for tt in range(TT):
                p.op("vector", lambda e, tt=tt, oc=oc: e.tensor_add(
                    xtile(tt)[:, oc * 512:(oc + 1) * 512],
                    xtile(tt)[:, oc * 512:(oc + 1) * 512],
                    ps[pnames8[tt]].ap()[:, :]), wait=[(cT, mmo)], inc=cV)
                gset(pnames8[tt], cV)
                res_done[(tt, oc)] = (cV, cV.n)
        ln_waits = []
        for tt in range(TT):
            ln_waits.append(ln_tile_inplace(
                xtile(tt), sq, wait=[res_done[(tt, oc)] for oc in range(4)]))
        arena.release()
        return [(cSc, max(ln_waits))]

    def stage_l1a(ready):
        SCALE = float(128.0 ** -0.5)
        arena.mark()
        xT = arena.alloc(DC * T)
        wsl = arena.alloc(2 * D)
        kbuf = arena.alloc(T)
        ubuf = arena.alloc(T)
        phib = arena.alloc(T)
        bqf = arena.alloc(16)
        bkf = arena.alloc(16)
        bvf = arena.alloc(16)
        borow = arena.alloc(D)
        bob = arena.alloc(D)
        carryb = arena.alloc(16)
        c0b = arena.alloc(16)
        c1b = arena.alloc(16)
        ceff = arena.alloc(16)
        wot = arena.alloc(2 * 512)
        sq = arena.alloc(D)
        ctx = xT

        tw = [p.dma("sync", lambda e: e.dma_start(out=bqf, in_=W["l1a_bq"])),
              p.dma("sync", lambda e: e.dma_start(out=bkf, in_=W["l1a_bk"])),
              p.dma("sync", lambda e: e.dma_start(out=bvf, in_=W["l1a_bv"])),
              p.dma("sync", lambda e: e.dma_start(out=borow[0:1, :], in_=W["l1a_bo"]))]
        bob_done = bcast_row(borow[0:1, :], bob, D, wait=[tw[3]])

        pairs = []
        for tt in range(TT):
            for dc in range(DC):
                pairs.append((xtile(tt)[:, dc * 128:(dc + 1) * 128],
                              xT[:, dc * T + tt * 128:dc * T + (tt + 1) * 128]))
        xt_done = transpose_tiles(pairs, wait=list(ready))

        def proj_ec(wname, ec, slot, bias_col, extra_wait=()):
            wsl_s = wsl[:, slot * D:(slot + 1) * D]
            twl = p.dma("sync", lambda e, ec=ec, wsl_s=wsl_s, wname=wname: e.dma_start(
                out=wsl_s, in_=W[wname][ec]), wait=gwait(("l1a_w", slot)))
            mms = []
            for tc in range(2):
                bank = "psE" if tc == 0 else "psF"
                for dc in range(DC):
                    p.op("tensor", lambda e, dc=dc, tc=tc, wsl_s=wsl_s, bank=bank: e.matmul(
                        ps[bank].ap()[:, :], wsl_s[:, dc * 128:(dc + 1) * 128],
                        xT[:, dc * T + tc * 512:dc * T + (tc + 1) * 512],
                        start=(dc == 0), stop=(dc == DC - 1)),
                        wait=[twl, (cV, xt_done)] + (gwait(bank) + list(extra_wait)
                                                     if dc == 0 else []),
                        inc=(cT if dc == DC - 1 else None))
                mms.append(cT.n)
            gset(("l1a_w", slot), cT)
            return mms

        q_tok, kv_tok = {}, {}
        for ec in range(DC):
            mms = proj_ec("l1a_wq", ec, 0, None)
            for tc in range(2):
                bank = "psE" if tc == 0 else "psF"
                p.op("scalar", lambda e, tc=tc, ec=ec, bank=bank: e.activation(
                    kbuf[:, tc * 512:(tc + 1) * 512], ps[bank].ap()[:, :],
                    AF.Identity, bias=bqf[:, ec:ec + 1]),
                    wait=[(cT, mms[tc])] + tw + gwait("l1a_kbuf"), inc=cSc)
                gset(bank, cSc)
            qd = cSc.n
            tq = p.dma("gpsimd", lambda e, ec=ec: e.dma_start(
                out=dram["l1a_q"].ap()[ec * 128:(ec + 1) * 128, :], in_=kbuf),
                wait=[(cSc, qd)])
            q_tok[ec] = tq
            guard["l1a_kbuf"] = tq
            mms = proj_ec("l1a_wk", ec, 1, None)
            for tc in range(2):
                bank = "psE" if tc == 0 else "psF"
                p.op("scalar", lambda e, tc=tc, ec=ec, bank=bank: e.activation(
                    kbuf[:, tc * 512:(tc + 1) * 512], ps[bank].ap()[:, :],
                    AF.Identity, bias=bkf[:, ec:ec + 1]),
                    wait=[(cT, mms[tc]), tq], inc=cSc)
                gset(bank, cSc)
            kd = cSc.n
            p.op("vector", lambda e: e.tensor_scalar_min(phib, kbuf, 0.0),
                 wait=[(cSc, kd)], inc=cV)
            vmin = cV.n
            p.op("scalar", lambda e: e.activation(phib, phib, AF.Exp),
                 wait=[(cV, vmin)], inc=cSc)
            se = cSc.n
            p.op("vector", lambda e: e.scalar_tensor_tensor(
                phib, kbuf, 0.0, phib, ALU.max, ALU.add),
                wait=[(cSc, se)], inc=cV)
            phid = cV.n
            mms = proj_ec("l1a_wv", ec, 0, ())
            for tc in range(2):
                bank = "psE" if tc == 0 else "psF"
                p.op("scalar", lambda e, tc=tc, ec=ec, bank=bank: e.activation(
                    ubuf[:, tc * 512:(tc + 1) * 512], ps[bank].ap()[:, :],
                    AF.Identity, bias=bvf[:, ec:ec + 1]),
                    wait=[(cT, mms[tc])] + gwait("l1a_ubuf"), inc=cSc)
                gset(bank, cSc)
            vd = cSc.n
            p.op("vector", lambda e: e.tensor_mul(ubuf, ubuf, phib),
                 wait=[(cSc, vd), (cV, phid)], inc=cV)
            p.op("vector", lambda e: e.tensor_tensor_scan(
                kbuf, ones_c.ap()[:, :].broadcast_to([128, T]), ubuf, 0.0,
                ALU.mult, ALU.add), inc=cV)
            p.op("vector", lambda e, ec=ec: e.tensor_copy(
                carryb[:, ec:ec + 1], kbuf[:, T - 1:T]), inc=cV)
            kvd = cV.n
            tkv = p.dma("gpsimd", lambda e, ec=ec: e.dma_start(
                out=dram["l1a_kv"].ap()[ec * 128:(ec + 1) * 128, :], in_=kbuf),
                wait=[(cV, kvd)])
            kv_tok[ec] = tkv
            guard["l1a_kbuf"] = tkv
            guard["l1a_ubuf"] = (cV, kvd)
            guard["l1a_ubuf2"] = (cV, kvd)
            guard["l1a_kbuf2"] = tkv

        tci = p.dma("gpsimd", lambda e: e.dma_start(
            out=dram["l1a_cin"].ap(), in_=carryb), wait=[(cV, cV.n)])
        p.op("gpsimd", lambda e: e.collective_compute(
            "AllGather", ALU.bypass,
            replica_groups=[[0, 1], [2, 3], [4, 5], [6, 7]],
            ins=[dram["l1a_cin"].ap()], outs=[dram["l1a_cout"].ap()]),
            wait=[tci], inc=cG)
        ccd = cG.n
        t0 = p.dma("gpsimd", lambda e: e.dma_start(
            out=c0b, in_=dram["l1a_cout"].ap()[0:128, :]), wait=[(cG, ccd)])
        t1 = p.dma("gpsimd", lambda e: e.dma_start(
            out=c1b, in_=dram["l1a_cout"].ap()[128:256, :]), wait=[(cG, ccd)])
        p.op("vector", lambda e: e.tensor_scalar(
            ceff, c0b, gatesb.ap()[:, 0:1], None, ALU.mult), wait=[t0, t1], inc=cV)
        p.op("vector", lambda e: e.scalar_tensor_tensor(
            ceff, c1b, gatesb.ap()[:, 1:2], ceff, ALU.mult, ALU.add), inc=cV)
        ceff_done = cV.n

        for ec in range(DC):
            tq = p.dma("sync", lambda e, ec=ec: e.dma_start(
                out=ubuf, in_=dram["l1a_q"].ap()[ec * 128:(ec + 1) * 128, :]),
                wait=gwait("l1a_ubuf2") + [q_tok[ec]])
            tk = p.dma("sync", lambda e, ec=ec: e.dma_start(
                out=kbuf, in_=dram["l1a_kv"].ap()[ec * 128:(ec + 1) * 128, :]),
                wait=gwait("l1a_kbuf2") + [kv_tok[ec]])
            p.op("vector", lambda e: e.tensor_scalar_min(phib, ubuf, 0.0),
                 wait=[tq], inc=cV)
            vmin = cV.n
            p.op("scalar", lambda e: e.activation(phib, phib, AF.Exp),
                 wait=[(cV, vmin)], inc=cSc)
            se = cSc.n
            p.op("vector", lambda e: e.scalar_tensor_tensor(
                phib, ubuf, 0.0, phib, ALU.max, ALU.add),
                wait=[(cSc, se)], inc=cV)
            guard["l1a_ubuf2"] = (cV, cV.n)
            p.op("vector", lambda e, ec=ec: e.scalar_tensor_tensor(
                ctx[:, ec * T:(ec + 1) * T], kbuf, ceff[:, ec:ec + 1], phib,
                ALU.add, ALU.mult),
                wait=[tk, (cV, ceff_done)], inc=cV)
            guard["l1a_kbuf2"] = (cV, cV.n)
        ctx_done = cV.n

        pnames8 = ("psA", "psB", "psC", "psD", "psE", "psF", "psG", "psH")
        res_done = {}
        for oc in range(4):
            for ec in range(DC):
                slot = ec % 2
                wt = wot[:, slot * 512:(slot + 1) * 512]
                twt = p.dma("sync", lambda e, ec=ec, oc=oc, wt=wt: e.dma_start(
                    out=wt, in_=W["l1a_woT"][ec, :, oc * 512:(oc + 1) * 512]),
                    wait=gwait(("l1a_wo", slot)))
                for tt in range(TT):
                    p.op("tensor", lambda e, ec=ec, tt=tt, wt=wt: e.matmul(
                        ps[pnames8[tt]].ap()[:, :],
                        ctx[:, ec * T + tt * 128:ec * T + (tt + 1) * 128], wt,
                        start=(ec == 0), stop=(ec == DC - 1)),
                        wait=[twt, (cV, ctx_done)] + (gwait(pnames8[tt])
                                                      if ec == 0 else []),
                        inc=(cT if tt == TT - 1 else None))
                gset(("l1a_wo", slot), cT)
            mmo = cT.n
            for tt in range(TT):
                p.op("vector", lambda e, tt=tt, oc=oc: e.scalar_tensor_tensor(
                    phib[:, 0:512], ps[pnames8[tt]].ap()[:, :], SCALE,
                    bob[:, oc * 512:(oc + 1) * 512], ALU.mult, ALU.add),
                    wait=[(cT, mmo), (cV, bob_done)], inc=cV)
                p.op("vector", lambda e, tt=tt, oc=oc: e.tensor_add(
                    xtile(tt)[:, oc * 512:(oc + 1) * 512],
                    xtile(tt)[:, oc * 512:(oc + 1) * 512], phib[:, 0:512]), inc=cV)
                gset(pnames8[tt], cV)
                res_done[(tt, oc)] = (cV, cV.n)
        ln_waits = []
        for tt in range(TT):
            ln_waits.append(ln_tile_inplace(
                xtile(tt), sq, wait=[res_done[(tt, oc)] for oc in range(4)]))
        arena.release()
        return [(cSc, max(ln_waits))]

    def stage_attn(ready, pre, nheads_kv, rope):
        ISC = float(128.0 ** -0.5)
        KV = nheads_kv * 128
        KVC = nheads_kv
        arena.mark()
        ofm = arena.alloc(DC * T)
        wsl = arena.alloc(2 * D)
        khb = arena.alloc(2048)
        vhb = arena.alloc(2048)
        qbuf = arena.alloc(512)
        pb = arena.alloc(512)
        trisel_s = arena.alloc(4096)
        kmq_s = arena.alloc(32)
        smb = arena.alloc(32)
        rows = arena.alloc(KV + D)
        kfb = arena.alloc(T)
        sqs = arena.alloc(T)
        bqf_s = arena.alloc(16)
        bkf_s = arena.alloc(16)
        bvrow, borow = rows[0:1, 0:KV], rows[0:1, KV:KV + D]

        tw = [p.dma("sync", lambda e: e.dma_start(out=trisel_s, in_=W[pre + "_trisel"])),
              p.dma("sync", lambda e: e.dma_start(out=kmq_s, in_=W[pre + "_kmaskq"])),
              p.dma("sync", lambda e: e.dma_start(out=bvrow, in_=W[pre + "_bv"])),
              p.dma("sync", lambda e: e.dma_start(out=borow, in_=W[pre + "_bo"])),
              p.dma("sync", lambda e: e.dma_start(out=bqf_s, in_=W[pre + "_bqf"])),
              p.dma("sync", lambda e: e.dma_start(out=bkf_s[:, 0:KVC],
                                                   in_=W[pre + "_bkf"]))]
        if rope:
            rcos = khb[:, 0:T]
            rsin = khb[:, T:2 * T]
            tw.append(p.dma("sync", lambda e: e.dma_start(out=rcos, in_=W[pre + "_cos"])))
            tw.append(p.dma("sync", lambda e: e.dma_start(out=rsin, in_=W[pre + "_sinm"])))

        pairs = []
        xT = ofm
        for tt in range(TT):
            for dc in range(DC):
                pairs.append((xtile(tt)[:, dc * 128:(dc + 1) * 128],
                              xT[:, dc * T + tt * 128:dc * T + (tt + 1) * 128]))
        xt_done = transpose_tiles(pairs, wait=list(ready))

        m11 = smb[0:1, 0:1]
        rmcol = smb[:, 1:2]
        mxcol = smb[:, 2:3]
        mcol = smb[:, 3:4]
        biasq = smb[:, 16:32]

        def rope_apply(buf, done_in):
            p.op("vector", lambda e: e.tensor_mul(sqs, buf, rcos),
                 wait=[done_in] + tw, inc=cV)
            t1d = cV.n
            for tc in range(2):
                bank = "psG" if tc == 0 else "psH"
                p.op("tensor", lambda e, tc=tc, bank=bank: e.matmul(
                    ps[bank].ap()[:, :], rotm_s.ap()[:, :],
                    buf[:, tc * 512:(tc + 1) * 512], start=True, stop=True),
                    wait=[done_in] + gwait(bank) + [rot_ready], inc=cT)
                rqd = cT.n
                p.op("vector", lambda e, tc=tc, bank=bank: e.tensor_tensor(
                    qbuf, ps[bank].ap()[:, :], rsin[:, tc * 512:(tc + 1) * 512],
                    ALU.mult), wait=[(cT, rqd), (cV, t1d)], inc=cV)
                gset(bank, cV)
                p.op("vector", lambda e, tc=tc: e.tensor_add(
                    buf[:, tc * 512:(tc + 1) * 512], qbuf,
                    sqs[:, tc * 512:(tc + 1) * 512]), inc=cV)
            return (cV, cV.n)

        def proj_fm(wname, ec, slot, extra=()):
            wsl_s = wsl[:, slot * D:(slot + 1) * D]
            twl = p.dma("sync", lambda e, ec=ec, wsl_s=wsl_s, wname=wname: e.dma_start(
                out=wsl_s, in_=W[wname][ec]), wait=gwait(("at_w", slot)))
            mms = []
            for tc in range(2):
                bank = "psE" if tc == 0 else "psF"
                for dc in range(DC):
                    p.op("tensor", lambda e, dc=dc, tc=tc, wsl_s=wsl_s, bank=bank: e.matmul(
                        ps[bank].ap()[:, :], wsl_s[:, dc * 128:(dc + 1) * 128],
                        xT[:, dc * T + tc * 512:dc * T + (tc + 1) * 512],
                        start=(dc == 0), stop=(dc == DC - 1)),
                        wait=[twl, (cV, xt_done)] + (gwait(bank) + list(extra)
                                                     if dc == 0 else []),
                        inc=(cT if dc == DC - 1 else None))
                mms.append(cT.n)
            gset(("at_w", slot), cT)
            return mms

        qtoks, ktoks, vtoks = {}, [], []
        for ec in range(DC):
            mms = proj_fm(pre + "_wq", ec, ec % 2)
            for tc in range(2):
                bank = "psE" if tc == 0 else "psF"
                p.op("scalar", lambda e, tc=tc, ec=ec, bank=bank: e.activation(
                    kfb[:, tc * 512:(tc + 1) * 512], ps[bank].ap()[:, :],
                    AF.Identity, bias=bqf_s[:, ec:ec + 1]),
                    wait=[(cT, mms[tc])] + tw + gwait("at_kfb"), inc=cSc)
                gset(bank, cSc)
            qd = (cSc, cSc.n)
            if rope:
                qd = rope_apply(kfb, qd)
            tqs = p.dma("gpsimd", lambda e, ec=ec: e.dma_start(
                out=dram[pre + "_qf"].ap()[ec * 128:(ec + 1) * 128, :], in_=kfb),
                wait=[qd])
            qtoks[ec] = tqs
            guard["at_kfb"] = tqs
        for ec in range(KVC):
            mms = proj_fm(pre + "_wk", ec, ec % 2)
            for tc in range(2):
                bank = "psE" if tc == 0 else "psF"
                p.op("scalar", lambda e, tc=tc, ec=ec, bank=bank: e.activation(
                    kfb[:, tc * 512:(tc + 1) * 512], ps[bank].ap()[:, :],
                    AF.Identity, bias=bkf_s[:, ec:ec + 1]),
                    wait=[(cT, mms[tc])] + gwait("at_kfb"), inc=cSc)
                gset(bank, cSc)
            kd = (cSc, cSc.n)
            if rope:
                kd = rope_apply(kfb, kd)
            tks = p.dma("gpsimd", lambda e, ec=ec: e.dma_start(
                out=dram[pre + "_kin"].ap()[ec * 128:(ec + 1) * 128, :], in_=kfb),
                wait=[kd])
            guard["at_kfb"] = tks
            ktoks.append(tks)
        pnames8 = ("psA", "psB", "psC", "psD", "psE", "psF", "psG", "psH")
        v_tok = None
        for oc in range(KV // 512):
            for dc in range(DC):
                slot = dc % 2
                wt = wsl[:, slot * D:slot * D + 512]
                twt = p.dma("sync", lambda e, dc=dc, oc=oc, wt=wt: e.dma_start(
                    out=wt, in_=W[pre + "_wv"][dc, :, oc * 512:(oc + 1) * 512]),
                    wait=gwait(("at_w", slot)))
                for tt in range(TT):
                    p.op("tensor", lambda e, dc=dc, tt=tt, wt=wt: e.matmul(
                        ps[pnames8[tt]].ap()[:, :],
                        xT[:, dc * T + tt * 128:dc * T + (tt + 1) * 128], wt,
                        start=(dc == 0), stop=False),
                        wait=[twt] + (gwait(pnames8[tt]) if dc == 0 else []),
                        inc=(cT if tt == TT - 1 else None))
                gset(("at_w", slot), cT)
            for tt in range(TT):
                p.op("tensor", lambda e, tt=tt, oc=oc: e.matmul(
                    ps[pnames8[tt]].ap()[:, :], ones_r.ap()[:, :],
                    bvrow[:, oc * 512:(oc + 1) * 512], start=False, stop=True),
                    wait=[(cV, ones_ready)] + tw, inc=cT)
                mmv = cT.n
                p.op("vector", lambda e, tt=tt: e.tensor_copy(
                    qbuf, ps[pnames8[tt]].ap()[:, :]),
                    wait=[(cT, mmv)] + gwait("at_qbuf"), inc=cV)
                gset(pnames8[tt], cV)
                vd = cV.n
                v_tok = p.dma("gpsimd", lambda e, tt=tt, oc=oc: e.dma_start(
                    out=dram[pre + "_vin"].ap()[tt * 128:(tt + 1) * 128,
                                                oc * 512:(oc + 1) * 512],
                    in_=qbuf), wait=[(cV, vd)])
                vtoks.append(v_tok)
                guard["at_qbuf"] = v_tok

        RG = [[0, 1], [2, 3], [4, 5], [6, 7]]
        p.op("gpsimd", lambda e: e.collective_compute(
            "AllGather", ALU.bypass, replica_groups=RG,
            ins=[dram[pre + "_kin"].ap()], outs=[dram[pre + "_kout"].ap()]),
            wait=ktoks, inc=cG)
        p.op("gpsimd", lambda e: e.collective_compute(
            "AllGather", ALU.bypass, replica_groups=RG,
            ins=[dram[pre + "_vin"].ap()], outs=[dram[pre + "_vout"].ap()]),
            wait=vtoks, inc=cG)
        cc_done = cG.n
        bias_done = (cV, 0)

        for h in range(DC):
            hkv = h // (DC // KVC)
            tkh = [p.dma("sync", lambda e, hkv=hkv: e.dma_start(
                out=khb[:, 0:T], in_=dram[pre + "_kout"].ap()[
                    hkv * 128:(hkv + 1) * 128, :]),
                wait=gwait("at_khb") + [(cG, cc_done)]),
                p.dma("sync", lambda e, hkv=hkv: e.dma_start(
                    out=khb[:, T:2 * T], in_=dram[pre + "_kout"].ap()[
                        KV + hkv * 128:KV + (hkv + 1) * 128, :]))]
            tvh = []
            for gq in range(4):
                tvh.append(p.dma("sync", lambda e, hkv=hkv, gq=gq: e.dma_start(
                    out=vhb[:, gq * 512:(gq + 1) * 512].rearrange(
                        "p (g c) -> p g c", c=128),
                    in_=dram[pre + "_vout"].ap()[gq * 512:(gq + 1) * 512,
                                                 hkv * 128:(hkv + 1) * 128]
                    .rearrange("(g p) c -> p g c", p=128)),
                    wait=gwait("at_vhb") if gq == 0 else ()))
            for qb in range(2):
                tqh = p.dma("sync", lambda e, h=h, qb=qb: e.dma_start(
                    out=qbuf, in_=dram[pre + "_qf"].ap()[
                        h * 128:(h + 1) * 128, qb * 512:(qb + 1) * 512]),
                    wait=gwait("at_qbuf") + [qtoks[h]])
                glist = list(range(0, 8)) + list(range(8, 12 + 4 * qb))
                nl = len(glist)

                def s_tile(g, sb, extra_w=()):
                    slot = -1
                    if 4 * qb <= g < 4 * qb + 4:
                        slot = g - 4 * qb
                    elif 8 + 4 * qb <= g < 12 + 4 * qb:
                        slot = 4 + (g - 8 - 4 * qb)
                    w0 = gwait(sb) + list(extra_w)
                    if slot >= 0:
                        p.op("tensor", lambda e, slot=slot, sb=sb: e.matmul(
                            ps[sb].ap()[:, :], sident.ap()[:, :],
                            trisel_s[:, slot * 512:(slot + 1) * 512],
                            start=True, stop=False), wait=w0, inc=None)
                    p.op("tensor", lambda e, g=g, sb=sb, slot=slot: e.matmul(
                        ps[sb].ap()[:, :], khb[:, g * 128:(g + 1) * 128], qbuf,
                        start=(slot < 0), stop=True),
                        wait=w0 if slot < 0 else (), inc=cT)
                    return cT.n

                p.op("vector", lambda e: e.memset(mxcol, -1e30), inc=cV)
                for idx, g in enumerate(glist):
                    sb = "psE" if idx % 2 == 0 else "psF"
                    smm = s_tile(g, sb, ([tqh] + tkh + tvh) if idx < 2 else ())
                    p.op("vector", lambda e, sb=sb: e.reduce_max(
                        rmcol, ps[sb].ap()[:, :], AX.X), wait=[(cT, smm)], inc=cV)
                    p.op("vector", lambda e, qb=qb, g=g: e.scalar_tensor_tensor(
                        mxcol, rmcol, kmq_s[:, qb * 16 + g:qb * 16 + g + 1],
                        mxcol, ALU.add, ALU.max), inc=cV)
                    gset(sb, cV)
                mx_done = cV.n
                p.op("tensor", lambda e: e.transpose(
                    ps["psG"].ap()[0:1, 0:128], mxcol, sident.ap()[:, :]),
                    wait=[(cV, mx_done)] + gwait("psG"), inc=cT)
                tpd = cT.n
                p.op("vector", lambda e: e.reduce_max(
                    m11, ps["psG"].ap()[0:1, 0:128], AX.X),
                    wait=[(cT, tpd)], inc=cV)
                gset("psG", cV)
                p.op("vector", lambda e: e.tensor_scalar(
                    m11, m11, -ISC, None, ALU.mult), inc=cV)
                nmd = cV.n
                p.op("tensor", lambda e: e.matmul(
                    ps["psG"].ap()[:, 0:1], ones_r.ap()[:, :], m11,
                    start=True, stop=True),
                    wait=[(cV, nmd), (cV, ones_ready)] + gwait("psG"), inc=cT)
                bmd = cT.n
                p.op("vector", lambda e: e.tensor_copy(mcol, ps["psG"].ap()[:, 0:1]),
                     wait=[(cT, bmd)], inc=cV)
                gset("psG", cV)
                p.op("vector", lambda e, qb=qb: e.tensor_tensor(
                    biasq, kmq_s[:, qb * 16:(qb + 1) * 16],
                    mcol.broadcast_to([128, 16]), ALU.add), inc=cV)
                bq_done = cV.n

                for idx, g in enumerate(glist):
                    sb = "psE" if idx % 2 == 0 else "psF"
                    smm = s_tile(g, sb)
                    p.op("scalar", lambda e, g=g, sb=sb: e.activation(
                        pb, ps[sb].ap()[:, :], AF.Exp, scale=ISC,
                        bias=biasq[:, g:g + 1]),
                        wait=[(cT, smm), (cV, bq_done)], inc=cSc)
                    gset(sb, cSc)
                    pd = cSc.n
                    p.op("tensor", lambda e, idx=idx, nl=nl: e.matmul(
                        ps["psH"].ap()[0:1, :], ones_c.ap()[:, :], pb,
                        start=(idx == 0), stop=(idx == nl - 1)),
                        wait=[(cSc, pd)] + (gwait("psH") if idx == 0 else []),
                        inc=None)
                    p.op("tensor", lambda e, g=g, idx=idx, nl=nl: e.matmul(
                        ps["psG"].ap()[:, :], vhb[:, g * 128:(g + 1) * 128], pb,
                        start=(idx == 0), stop=(idx == nl - 1)),
                        wait=(gwait("psG") if idx == 0 else []),
                        inc=(cT if idx == nl - 1 else None))
                mmo = cT.n
                p.op("vector", lambda e: e.tensor_copy(
                    pb[0:1, :], ps["psH"].ap()[0:1, :]), wait=[(cT, mmo)], inc=cV)
                p.op("vector", lambda e: e.reciprocal(pb[0:1, :], pb[0:1, :]),
                     inc=cV)
                rd = cV.n
                p.op("tensor", lambda e: e.matmul(
                    ps["psH"].ap()[:, :], ones_r.ap()[:, :], pb[0:1, :],
                    start=True, stop=True), wait=[(cV, rd)], inc=cT)
                rbd = cT.n
                gset("psH", cT)
                p.op("vector", lambda e: e.tensor_copy(
                    sqs[:, 0:512], ps["psH"].ap()[:, :]), wait=[(cT, rbd)], inc=cV)
                gset("psH", cV)
                p.op("vector", lambda e, h=h, qb=qb: e.tensor_tensor(
                    ofm[:, h * T + qb * 512:h * T + (qb + 1) * 512],
                    ps["psG"].ap()[:, :], sqs[:, 0:512], ALU.mult), inc=cV)
                gset("psG", cV)
                guard["at_qbuf"] = (cV, cV.n)
            gset("at_khb", cV)
            gset("at_vhb", cV)

        res_done = {}
        for oc in range(4):
            for ec in range(DC):
                slot = ec % 2
                wt = wsl[:, slot * D:slot * D + 512]
                twt = p.dma("sync", lambda e, ec=ec, oc=oc, wt=wt: e.dma_start(
                    out=wt, in_=W[pre + "_woT"][ec, :, oc * 512:(oc + 1) * 512]),
                    wait=gwait(("at_w", slot)))
                for tt in range(TT):
                    qb, j = tt // 4, tt % 4
                    p.op("tensor", lambda e, ec=ec, tt=tt, qb=qb, j=j, wt=wt: e.matmul(
                        ps[pnames8[tt]].ap()[:, :],
                        ofm[:, ec * T + qb * 512 + j * 128:
                            ec * T + qb * 512 + (j + 1) * 128], wt,
                        start=(ec == 0), stop=False),
                        wait=[twt] + (gwait(pnames8[tt]) if ec == 0 else []),
                        inc=(cT if tt == TT - 1 else None))
                gset(("at_w", slot), cT)
            for tt in range(TT):
                p.op("tensor", lambda e, tt=tt, oc=oc: e.matmul(
                    ps[pnames8[tt]].ap()[:, :], ones_r.ap()[:, :],
                    borow[:, oc * 512:(oc + 1) * 512], start=False, stop=True),
                    wait=[(cV, ones_ready)], inc=cT)
                mmw = cT.n
                p.op("vector", lambda e, tt=tt, oc=oc: e.tensor_add(
                    xtile(tt)[:, oc * 512:(oc + 1) * 512],
                    xtile(tt)[:, oc * 512:(oc + 1) * 512],
                    ps[pnames8[tt]].ap()[:, :]), wait=[(cT, mmw)], inc=cV)
                gset(pnames8[tt], cV)
                res_done[(tt, oc)] = (cV, cV.n)
        ln_waits = []
        for tt in range(TT):
            ln_waits.append(ln_tile_inplace(
                xtile(tt), ofm[:, 0:D], wait=[res_done[(tt, oc)] for oc in range(4)]))
        arena.release()
        return [(cSc, max(ln_waits))]

    def quant_round(dst_ap, src_ap, scol, qtmp, width):
        xs = qtmp[:, 0:width]
        tf = qtmp[:, width:2 * width]
        ti = qtmp[:, 2 * width:3 * width].bitcast(I32)
        p.op("vector", lambda e: e.tensor_scalar(xs, src_ap, scol, None, ALU.mult),
             inc=cV)
        p.op("vector", lambda e: e.tensor_copy(ti, xs), inc=cV)
        p.op("vector", lambda e: e.tensor_copy(tf, ti), inc=cV)
        p.op("vector", lambda e: e.tensor_sub(xs, xs, tf), inc=cV)
        p.op("vector", lambda e: e.scalar_tensor_tensor(
            tf, xs, 0.5, tf, ALU.is_ge, ALU.add), inc=cV)
        p.op("vector", lambda e: e.scalar_tensor_tensor(
            tf, xs, -0.5, tf, ALU.is_ge, ALU.add), inc=cV)
        p.op("vector", lambda e: e.tensor_scalar(dst_ap, tf, -1.0, None, ALU.add),
             inc=cV)
        return cV.n

    def stage_l2f(ready):
        IC = 64
        IP = 16
        arena.mark()
        kxf = arena.alloc(DC * T // 2).bitcast(BF16)
        terng = arena.alloc(DC * 256).bitcast(BF16)
        ternv = arena.alloc(DC * 256).bitcast(BF16)
        wfp = arena.alloc(512)
        wabs = arena.alloc(512)
        wsg = arena.alloc(512)
        qtmp = arena.alloc(3 * D)
        kxtm = arena.alloc(D)
        hpiece = arena.alloc(512)
        gpiece = arena.alloc(512)
        gbb = arena.alloc(512)
        vbb = arena.alloc(512)
        obb = arena.alloc(D)
        rows = arena.alloc(512 + 512 + D)
        khstage = arena.alloc(T).bitcast(BF16)
        khic = arena.alloc(T // 2).bitcast(BF16)
        owt = arena.alloc(256).bitcast(BF16)
        smq = arena.alloc(80)
        sq = qtmp[:, 0:D]
        gbrow = rows[0:1, 0:512]
        vbrow = rows[0:1, 512:1024]
        obrow = rows[0:1, 1024:1024 + D]

        gxc = smq[:, 0:8]
        sxc = smq[:, 8:16]
        hmx = smq[:, 16:24]
        shc = smq[:, 24:32]
        sgg = smq[:, 32:40]
        sgv = smq[:, 40:48]
        sgo = smq[:, 48:56]
        acc1 = smq[:, 58:59]
        tmp1 = smq[:, 59:60]
        m11b = smq[0:1, 60:61]
        gamcol = {"g": smq[:, 61:62], "v": smq[:, 62:63], "o": smq[:, 63:64]}
        g05col = {"g": smq[:, 64:65], "v": smq[:, 65:66], "o": smq[:, 66:67]}

        tob = p.dma("sync", lambda e: e.dma_start(out=obrow, in_=W["l2f_ob"]))
        obb_done = bcast_row(obrow, obb, D, wait=[tob])
        p.op("vector", lambda e: e.memset(hmx, 0.0), inc=cV)

        PRE = gpre is not None
        if PRE:
            gammas = {k: prep_state["done"][k] for k in "gvo"}
            for ci, key in enumerate("gvo"):
                p.op("vector", lambda e, ci=ci, key=key: e.tensor_copy(
                    gamcol[key], gpre[:, ci:ci + 1]),
                    wait=[prep_state["done"][key]], inc=cV)
                gammas[key] = (cV, cV.n)
        if not PRE:
            gammas = {}
        for wname, nslab, key in (() if PRE else
                                  (("l2f_gw", 16, "g"), ("l2f_vw", 16, "v"),
                                   ("l2f_ow", 64, "o"))):
            p.op("vector", lambda e: e.memset(acc1, 0.0), inc=cV)
            npc = 4 if nslab == 16 else 1
            for sl in range(nslab):
                for pc in range(npc):
                    tl = p.dma("sync", lambda e, wname=wname, sl=sl, pc=pc: e.dma_start(
                        out=qtmp[:, 0:2048],
                        in_=W[wname][sl, :, pc * 2048:(pc + 1) * 2048]),
                        wait=gwait("l2f_qtmp"))
                    p.op("scalar", lambda e: e.activation(
                        qtmp[:, 2048:4096], qtmp[:, 0:2048], AF.Abs,
                        accum_out=tmp1), wait=[tl] + gwait("l2f_tmp1"), inc=cSc)
                    sd = cSc.n
                    guard["l2f_qtmp"] = (cSc, sd)
                    p.op("vector", lambda e: e.tensor_add(acc1, acc1, tmp1),
                         wait=[(cSc, sd)], inc=cV)
                    gset("l2f_tmp1", cV)
            ad = cV.n
            p.op("tensor", lambda e: e.matmul(
                ps["psH"].ap()[0:1, 0:1], ones_c.ap()[:, :], acc1,
                start=True, stop=True),
                wait=[(cV, ad), (cV, ones_ready)] + gwait("psH"), inc=cT)
            td = cT.n
            p.op("vector", lambda e: e.tensor_scalar(
                m11b, ps["psH"].ap()[0:1, 0:1], 1.0 / (8192.0 * 2048.0), None,
                ALU.mult), wait=[(cT, td)], inc=cV)
            gset("psH", cV)
            p.op("vector", lambda e: e.tensor_scalar_max(m11b, m11b, 1e-5), inc=cV)
            gd = cV.n
            p.op("tensor", lambda e: e.matmul(
                ps["psH"].ap()[:, 0:1], ones_r.ap()[:, :], m11b,
                start=True, stop=True), wait=[(cV, gd)] + gwait("psH"), inc=cT)
            td = cT.n
            p.op("vector", lambda e, key=key: e.tensor_copy(
                gamcol[key], ps["psH"].ap()[:, 0:1]), wait=[(cT, td)], inc=cV)
            gset("psH", cV)
            p.op("vector", lambda e, key=key: e.tensor_scalar(
                g05col[key], gamcol[key], 0.5, None, ALU.mult), inc=cV)
            gammas[key] = (cV, cV.n)

        for tt in range(TT):
            p.op("vector", lambda e, tt=tt: e.reduce_max(
                gxc[:, tt:tt + 1], xtile(tt), AX.X, apply_absolute_value=True),
                wait=list(ready), inc=cV)
            p.op("vector", lambda e, tt=tt: e.tensor_scalar_max(
                gxc[:, tt:tt + 1], gxc[:, tt:tt + 1], 1e-5), inc=cV)
            p.op("vector", lambda e, tt=tt: e.reciprocal(
                sxc[:, tt:tt + 1], gxc[:, tt:tt + 1]), inc=cV)
            p.op("vector", lambda e, tt=tt: e.tensor_scalar(
                sxc[:, tt:tt + 1], sxc[:, tt:tt + 1], 127.0, None, ALU.mult), inc=cV)
            p.wait("vector", gwait("l2f_kxtm2"))
            quant_round(kxtm, xtile(tt), sxc[:, tt:tt + 1], qtmp, D)
            kd = cV.n
            for dc in range(DC):
                bank = TPB[dc % 4]
                p.op("tensor", lambda e, dc=dc, bank=bank: e.transpose(
                    ps[bank].ap()[:, 0:128], kxtm[:, dc * 128:(dc + 1) * 128],
                    sident.ap()[:, :]), wait=[(cV, kd)] + gwait(bank), inc=cT)
                td2 = cT.n
                p.op("vector", lambda e, dc=dc, tt=tt, bank=bank: e.tensor_copy(
                    kxf[:, (dc * TT + tt) * 128:(dc * TT + tt + 1) * 128],
                    ps[bank].ap()[:, 0:128]), wait=[(cT, td2)], inc=cV)
                gset(bank, cV)
            guard["l2f_kxtm2"] = (cT, cT.n)
        kx_done = cV.n

        def kxt(dc, tt):
            return kxf[:, (dc * TT + tt) * 128:(dc * TT + tt + 1) * 128]

        for key, dst in (("g", sgg), ("v", sgv)):
            p.op("vector", lambda e, key=key, dst=dst: e.tensor_tensor(
                dst, gxc, gamcol[key].broadcast_to([128, 8]), ALU.mult),
                wait=[gammas[key], (cV, kx_done)], inc=cV)
            p.op("vector", lambda e, dst=dst: e.tensor_scalar(
                dst, dst, 1.0 / 127.0, None, ALU.mult), inc=cV)
        sg_done = cV.n

        TDN = {"l2f_gw": "l2f_tg", "l2f_vw": "l2f_tv", "l2f_ow": "l2f_to"}

        def ternize(wname, sl, cols, dst_bf, key, extra=()):
            if PRE:
                tk = prep_state["ttoks"][(TDN[wname], sl, cols)]
                return p.dma("sync", lambda e, wname=wname, sl=sl, cols=cols,
                             dst_bf=dst_bf: e.dma_start(
                    out=dst_bf,
                    in_=dram[TDN[wname]].ap()[sl, :, cols * 512:(cols + 1) * 512]),
                    wait=list(extra) + [tk])
            tl = p.dma("sync", lambda e, wname=wname, sl=sl, cols=cols: e.dma_start(
                out=wfp, in_=W[wname][sl, :, cols * 512:(cols + 1) * 512]),
                wait=gwait("l2f_wfp") + list(extra))
            p.op("scalar", lambda e: e.activation(wabs, wfp, AF.Abs),
                 wait=[tl] + gwait("l2f_wsg"), inc=cSc)
            p.op("scalar", lambda e: e.sign(wsg, wfp), inc=cSc)
            sd = cSc.n
            guard["l2f_wfp"] = (cSc, sd)
            p.op("vector", lambda e, key=key: e.tensor_scalar(
                wabs, wabs, g05col[key], None, ALU.is_ge),
                wait=[(cSc, sd), gammas[key]], inc=cV)
            p.op("vector", lambda e, dst_bf=dst_bf: e.tensor_tensor(
                dst_bf, wabs, wsg, ALU.mult), inc=cV)
            guard["l2f_wsg"] = (cV, cV.n)
            return (cV, cV.n)

        h_toks = {}
        for ip in range(IP):
            tgb = p.dma("sync", lambda e, ip=ip: e.dma_start(
                out=gbrow, in_=W["l2f_gb"][0:1, ip * 512:(ip + 1) * 512]),
                wait=gwait("l2f_gbrow"))
            gb_done = bcast_row(gbrow, gbb, 512, wait=[tgb])
            guard["l2f_gbrow"] = (cV, gb_done)
            tvb = p.dma("sync", lambda e, ip=ip: e.dma_start(
                out=vbrow, in_=W["l2f_vb"][0:1, ip * 512:(ip + 1) * 512]),
                wait=gwait("l2f_vbrow"))
            vb_done = bcast_row(vbrow, vbb, 512, wait=[tvb])
            guard["l2f_vbrow"] = (cV, vb_done)
            tern_toks = []
            for dc in range(DC):
                tern_toks.append(ternize(
                    "l2f_gw", dc, ip, terng[:, dc * 512:(dc + 1) * 512], "g",
                    extra=gwait(("l2f_tern", 0))))
                tern_toks.append(ternize(
                    "l2f_vw", dc, ip, ternv[:, dc * 512:(dc + 1) * 512], "v",
                    extra=gwait(("l2f_tern", 0))))
            mmg = mmv = 0
            for tt in range(TT):
                for bank, tern in (("psA", terng), ("psB", ternv)):
                    for dc in range(DC):
                        p.op("tensor", lambda e, dc=dc, tt=tt, bank=bank, tern=tern: e.matmul(
                            ps[bank].ap()[:, :], kxt(dc, tt),
                            tern[:, dc * 512:(dc + 1) * 512],
                            start=(dc == 0), stop=(dc == DC - 1)),
                            wait=(tern_toks + [(cV, kx_done)] + gwait(bank))
                            if dc == 0 else (),
                            inc=(cT if dc == DC - 1 else None))
                mmg, mmv = cT.n - 1, cT.n
                p.op("vector", lambda e, tt=tt: e.tensor_scalar(
                    gpiece, ps["psA"].ap()[:, :], sgg[:, tt:tt + 1], None, ALU.mult),
                    wait=[(cT, mmg), (cV, sg_done)] + gwait("l2f_gp"), inc=cV)
                gset("psA", cV)
                p.op("vector", lambda e: e.tensor_add(gpiece, gpiece, gbb),
                     wait=[(cV, gb_done)], inc=cV)
                vg = cV.n
                p.op("scalar", lambda e: e.activation(gpiece, gpiece, AF.Silu),
                     wait=[(cV, vg)], inc=cSc)
                sg_ = cSc.n
                p.op("vector", lambda e, tt=tt: e.tensor_scalar(
                    hpiece, ps["psB"].ap()[:, :], sgv[:, tt:tt + 1], None, ALU.mult),
                    wait=[(cT, mmv)] + gwait("l2f_hp"), inc=cV)
                gset("psB", cV)
                p.op("vector", lambda e: e.tensor_add(hpiece, hpiece, vbb),
                     wait=[(cV, vb_done)], inc=cV)
                p.op("vector", lambda e: e.tensor_mul(hpiece, hpiece, gpiece),
                     wait=[(cSc, sg_)], inc=cV)
                guard["l2f_gp"] = (cV, cV.n)
                p.op("vector", lambda e: e.reduce_max(
                    tmp1, hpiece, AX.X, apply_absolute_value=True), inc=cV)
                p.op("vector", lambda e, tt=tt: e.tensor_max(
                    hmx[:, tt:tt + 1], hmx[:, tt:tt + 1], tmp1), inc=cV)
                hd = cV.n
                h_toks[(tt, ip)] = p.dma("gpsimd", lambda e, tt=tt, ip=ip: e.dma_start(
                    out=dram["l2f_h"].ap()[tt * 128:(tt + 1) * 128,
                                           ip * 512:(ip + 1) * 512],
                    in_=hpiece), wait=[(cV, hd)])
                guard["l2f_hp"] = h_toks[(tt, ip)]
            guard[("l2f_tern", 0)] = (cT, mmv)

        p.op("vector", lambda e: e.tensor_scalar_max(hmx, hmx, 1e-5), inc=cV)
        p.op("vector", lambda e: e.reciprocal(shc, hmx), inc=cV)
        p.op("vector", lambda e: e.tensor_scalar(shc, shc, 127.0, None, ALU.mult),
             inc=cV)
        p.op("vector", lambda e: e.tensor_tensor(
            sgo, hmx, gamcol["o"].broadcast_to([128, 8]), ALU.mult),
            wait=[gammas["o"]], inc=cV)
        p.op("vector", lambda e: e.tensor_scalar(
            sgo, sgo, 1.0 / 127.0, None, ALU.mult), inc=cV)
        sh_done = cV.n
        kh_toks = {}
        for tt in range(TT):
            for pcq in range(4):
                tl = p.dma("sync", lambda e, tt=tt, pcq=pcq: e.dma_start(
                    out=kxtm,
                    in_=dram["l2f_h"].ap()[tt * 128:(tt + 1) * 128,
                                           pcq * 2048:(pcq + 1) * 2048]),
                    wait=gwait("l2f_kxtm2") + [h_toks[(tt, pcq * 4 + i)]
                                               for i in range(4)])
                p.op("vector", lambda e: e.engine_nop(),
                     wait=[tl, (cV, sh_done)], inc=cV)
                quant_round(kxtm, kxtm, shc[:, tt:tt + 1], qtmp, D)
                qd = cV.n
                for dc in range(DC):
                    bank = TPB[dc % 4]
                    p.op("tensor", lambda e, dc=dc, bank=bank: e.transpose(
                        ps[bank].ap()[:, 0:128], kxtm[:, dc * 128:(dc + 1) * 128],
                        sident.ap()[:, :]), wait=[(cV, qd)] + gwait(bank), inc=cT)
                    td2 = cT.n
                    p.op("vector", lambda e, dc=dc, bank=bank: e.tensor_copy(
                        khstage[:, dc * 128:(dc + 1) * 128],
                        ps[bank].ap()[:, 0:128]),
                        wait=[(cT, td2)] + gwait("l2f_khs"), inc=cV)
                    gset(bank, cV)
                guard["l2f_kxtm2"] = (cT, cT.n)
                kd = cV.n
                kh_toks[(tt, pcq)] = p.dma("gpsimd", lambda e, tt=tt, pcq=pcq: e.dma_start(
                    out=dram["l2f_kh"].ap()[pcq * 16:(pcq + 1) * 16, :,
                                            tt * 128:(tt + 1) * 128]
                    .transpose([1, 0, 2]),
                    in_=khstage.rearrange("p (c k) -> p c k", k=128)),
                    wait=[(cV, kd)] + gwait("l2f_khs"))
                guard["l2f_khs"] = kh_toks[(tt, pcq)]

        pnames8 = ("psA", "psB", "psC", "psD", "psE", "psF", "psG", "psH")
        res_done = {}
        for oc in range(4):
            for ic in range(IC):
                tko = p.dma("sync", lambda e, ic=ic: e.dma_start(
                    out=khic, in_=dram["l2f_kh"].ap()[ic // 4 if False else ic]
                    if False else dram["l2f_kh"].ap()[ic]),
                    wait=gwait("l2f_khic") + [kh_toks[(tt2, ic // 16)]
                                              for tt2 in range(TT)])
                od = ternize("l2f_ow", ic, oc, owt, "o",
                             extra=gwait("l2f_wsg"))
                for tt in range(TT):
                    p.op("tensor", lambda e, ic=ic, tt=tt: e.matmul(
                        ps[pnames8[tt]].ap()[:, :],
                        khic[:, tt * 128:(tt + 1) * 128], owt,
                        start=(ic == 0), stop=(ic == IC - 1)),
                        wait=[tko, od] + (gwait(pnames8[tt]) if ic == 0 else []),
                        inc=(cT if tt == TT - 1 else None))
                guard["l2f_khic"] = (cT, cT.n)
                guard["l2f_wsg"] = (cT, cT.n)
            mmo = cT.n
            for tt in range(TT):
                p.op("vector", lambda e, tt=tt, oc=oc: e.scalar_tensor_tensor(
                    gpiece, ps[pnames8[tt]].ap()[:, :], sgo[:, tt:tt + 1],
                    obb[:, oc * 512:(oc + 1) * 512], ALU.mult, ALU.add),
                    wait=[(cT, mmo), (cV, obb_done)] + gwait("l2f_gp"), inc=cV)
                p.op("vector", lambda e, tt=tt, oc=oc: e.tensor_add(
                    xtile(tt)[:, oc * 512:(oc + 1) * 512],
                    xtile(tt)[:, oc * 512:(oc + 1) * 512], gpiece), inc=cV)
                guard["l2f_gp"] = (cV, cV.n)
                gset(pnames8[tt], cV)
                res_done[(tt, oc)] = (cV, cV.n)
        ln_waits = []
        for tt in range(TT):
            ln_waits.append(ln_tile_inplace(
                xtile(tt), sq, wait=[res_done[(tt, oc)] for oc in range(4)]))
        arena.release()
        return [(cSc, max(ln_waits))]

    prep_state = {"steps": [], "done": {}}

    def build_l2f_prepass(buf):
        qa = buf[:, 0:512]
        qb_ = buf[:, 512:1024]
        qs = buf[:, 1024:1536]
        acc1 = buf[:, 1536:1537]
        tmp1 = buf[:, 1537:1538]
        m11 = buf[0:1, 1538:1539]
        tern_bf = buf[:, 1539:1539 + 256].bitcast(BF16)
        steps = prep_state["steps"]
        wspec = (("l2f_gw", "l2f_tg", 16, "g", 0), ("l2f_vw", "l2f_tv", 16, "v", 1),
                 ("l2f_ow", "l2f_to", 64, "o", 2))

        def emit_gamma_piece(wname, sl, pc, first, last, key, ci):
            def go():
                tl = p.dma("scalar", lambda e: e.dma_start(
                    out=qa, in_=W[wname][sl, :, pc * 512:(pc + 1) * 512]),
                    wait=gwait("pp_qa"))
                if first:
                    p.op("vector", lambda e: e.memset(acc1, 0.0),
                         wait=gwait("pp_acc"), inc=cV)
                p.op("scalar", lambda e: e.activation(
                    qb_, qa, AF.Abs, accum_out=tmp1),
                    wait=[tl] + gwait("pp_tmp1"), inc=cSc)
                sd = cSc.n
                guard["pp_qa"] = (cSc, sd)
                p.op("vector", lambda e: e.tensor_add(acc1, acc1, tmp1),
                     wait=[(cSc, sd)], inc=cV)
                gset("pp_tmp1", cV)
                if last:
                    ad = cV.n
                    p.op("tensor", lambda e: e.matmul(
                        ps["psH"].ap()[0:1, 0:1], ones_c.ap()[:, :], acc1,
                        start=True, stop=True),
                        wait=[(cV, ad), (cV, ones_ready)] + gwait("psH"), inc=cT)
                    td = cT.n
                    p.op("vector", lambda e: e.tensor_scalar(
                        m11, ps["psH"].ap()[0:1, 0:1], 1.0 / (8192.0 * 2048.0),
                        None, ALU.mult), wait=[(cT, td)], inc=cV)
                    gset("psH", cV)
                    p.op("vector", lambda e: e.tensor_scalar_max(m11, m11, 1e-5),
                         inc=cV)
                    gd = cV.n
                    p.op("tensor", lambda e: e.matmul(
                        ps["psH"].ap()[:, 0:1], ones_r.ap()[:, :], m11,
                        start=True, stop=True),
                        wait=[(cV, gd)] + gwait("psH"), inc=cT)
                    td2 = cT.n
                    p.op("vector", lambda e, ci=ci: e.tensor_copy(
                        gpre[:, ci:ci + 1], ps["psH"].ap()[:, 0:1]),
                        wait=[(cT, td2)], inc=cV)
                    gset("psH", cV)
                    p.op("vector", lambda e, ci=ci: e.tensor_scalar(
                        gpre[:, 3 + ci:4 + ci], gpre[:, ci:ci + 1], 0.5, None,
                        ALU.mult), inc=cV)
                    prep_state["done"][key] = (cV, cV.n)
                    gset("pp_acc", cV)
            return go

        def emit_tern_piece(wname, dname, sl, pc, key, ci):
            def go():
                tl = p.dma("scalar", lambda e: e.dma_start(
                    out=qa, in_=W[wname][sl, :, pc * 512:(pc + 1) * 512]),
                    wait=gwait("pp_qa"))
                p.op("scalar", lambda e: e.activation(qb_, qa, AF.Abs),
                     wait=[tl] + gwait("pp_qb"), inc=cSc)
                p.op("scalar", lambda e: e.sign(qs, qa), inc=cSc)
                sd = cSc.n
                guard["pp_qa"] = (cSc, sd)
                p.op("vector", lambda e, ci=ci: e.tensor_scalar(
                    qb_, qb_, gpre[:, 3 + ci:4 + ci], None, ALU.is_ge),
                    wait=[(cSc, sd), prep_state["done"][key]], inc=cV)
                p.op("vector", lambda e: e.tensor_tensor(
                    tern_bf, qb_, qs, ALU.mult),
                    wait=gwait("pp_tern"), inc=cV)
                gset("pp_qb", cV)
                vd = cV.n
                tk = p.dma("gpsimd", lambda e: e.dma_start(
                    out=dram[dname].ap()[sl, :, pc * 512:(pc + 1) * 512],
                    in_=tern_bf), wait=[(cV, vd)])
                guard["pp_tern"] = tk
                prep_state.setdefault("ttoks", {})[(dname, sl, pc)] = tk
            return go

        for wname, dname, nslab, key, ci in wspec:
            npcs = (8192 if nslab == 16 else 2048) // 512
            tot = nslab * npcs
            i = 0
            for sl in range(nslab):
                for pc in range(npcs):
                    steps.append(emit_gamma_piece(wname, sl, pc, i == 0,
                                                  i == tot - 1, key, ci))
                    i += 1
        for wname, dname, nslab, key, ci in wspec:
            npcs = (8192 if nslab == 16 else 2048) // 512
            for sl in range(nslab):
                for pc in range(npcs):
                    steps.append(emit_tern_piece(wname, dname, sl, pc, key, ci))

    def emit_prepass_chunk(k):
        steps = prep_state["steps"]
        for _ in range(min(k, len(steps))):
            steps.pop(0)()

    def stage_l2a(ready):
        NS = 16
        TCN = 4
        TC = T // TCN
        RG = [[0, 1], [2, 3], [4, 5], [6, 7]]

        def fence():
            w = [(cT, cT.n), (cV, cV.n), (cSc, cSc.n)]
            for key, ctrs in p.pools.items():
                for c in ctrs:
                    w.append((c, c.n))
            p.wait("sync", w)
            p.wait("gpsimd", w)

        arena.mark()
        halob = arena.alloc(48)
        h0 = arena.alloc(48)
        h1 = arena.alloc(48)
        haloe = arena.alloc(48)
        if gpre is not None:
            build_l2f_prepass(arena.alloc(1800))
        arena.mark()
        xT = arena.alloc(DC * T)
        wsl = arena.alloc(2 * D)
        evb = arena.alloc(T)

        pairs = []
        for tt in range(TT):
            for dc in range(DC):
                pairs.append((xtile(tt)[:, dc * 128:(dc + 1) * 128],
                              xT[:, dc * T + tt * 128:dc * T + (tt + 1) * 128]))
        xt_done = transpose_tiles(pairs, wait=list(ready))

        xc_toks, r_toks = {}, {}
        for ec in range(32):
            slot = ec % 2
            wsl_s = wsl[:, slot * D:(slot + 1) * D]
            twl = p.dma("sync", lambda e, ec=ec, wsl_s=wsl_s: e.dma_start(
                out=wsl_s, in_=W["l2a_win"][ec]), wait=gwait(("l2a_w", slot)))
            for tc2 in range(2):
                bank = "psE" if tc2 == 0 else "psF"
                for dc in range(DC):
                    p.op("tensor", lambda e, dc=dc, tc2=tc2, wsl_s=wsl_s, bank=bank: e.matmul(
                        ps[bank].ap()[:, :], wsl_s[:, dc * 128:(dc + 1) * 128],
                        xT[:, dc * T + tc2 * 512:dc * T + (tc2 + 1) * 512],
                        start=(dc == 0), stop=(dc == DC - 1)),
                        wait=[twl, (cV, xt_done)] + (gwait(bank) if dc == 0 else []),
                        inc=(cT if dc == DC - 1 else None))
                mm = cT.n
                p.op("vector", lambda e, tc2=tc2, bank=bank: e.tensor_copy(
                    evb[:, tc2 * 512:(tc2 + 1) * 512], ps[bank].ap()[:, :]),
                    wait=[(cT, mm)] + gwait("l2a_evb"), inc=cV)
                gset(bank, cV)
            gset(("l2a_w", slot), cT)
            ed = cV.n
            if ec < 16:
                p.op("vector", lambda e, ec=ec: e.tensor_copy(
                    halob[:, ec * 3:(ec + 1) * 3], evb[:, T - 3:T]), inc=cV)
                tsp = p.dma("gpsimd", lambda e, ec=ec: e.dma_start(
                    out=dram["l2a_xc"].ap()[ec * 128:(ec + 1) * 128, :], in_=evb),
                    wait=[(cV, cV.n)])
                xc_toks[ec] = tsp
            else:
                tsp = p.dma("gpsimd", lambda e, ec=ec: e.dma_start(
                    out=dram["l2a_r"].ap()[(ec - 16) * 128:(ec - 15) * 128, :],
                    in_=evb), wait=[(cV, ed)])
                r_toks[ec - 16] = tsp
            guard["l2a_evb"] = tsp
        hde = cV.n
        thi = p.dma("gpsimd", lambda e: e.dma_start(
            out=dram["l2a_hloin"].ap(), in_=halob), wait=[(cV, hde)])
        p.op("gpsimd", lambda e: e.collective_compute(
            "AllGather", ALU.bypass, replica_groups=RG,
            ins=[dram["l2a_hloin"].ap()], outs=[dram["l2a_hloout"].ap()]),
            wait=[thi], inc=cG)
        ccd = cG.n
        t0 = p.dma("gpsimd", lambda e: e.dma_start(
            out=h0, in_=dram["l2a_hloout"].ap()[0:128, :]), wait=[(cG, ccd)])
        t1 = p.dma("gpsimd", lambda e: e.dma_start(
            out=h1, in_=dram["l2a_hloout"].ap()[128:256, :]), wait=[(cG, ccd)])
        p.op("vector", lambda e: e.tensor_scalar(
            haloe, h0, gatesb.ap()[:, 0:1], None, ALU.mult), wait=[t0, t1], inc=cV)
        p.op("vector", lambda e: e.scalar_tensor_tensor(
            haloe, h1, gatesb.ap()[:, 1:2], haloe, ALU.mult, ALU.add), inc=cV)
        halo_done = cV.n
        arena.release()
        fence()

        arena.mark()
        XST = 1027
        xcbuf = arena.alloc(DC * XST)
        cwsl = arena.alloc(8192)
        wxsl = arena.alloc(DC * 160)
        cbf = arena.alloc(16)
        piece = arena.alloc(512)
        upiece = arena.alloc(512)

        twx = p.dma("sync", lambda e: e.dma_start(
            out=wxsl.rearrange("p (c k) -> p c k", k=160), in_=W["l2a_wx"]))
        tcb = p.dma("sync", lambda e: e.dma_start(out=cbf, in_=W["l2a_convb"]))
        for ic in range(DC):
            tld = p.dma("sync", lambda e, ic=ic: e.dma_start(
                out=xcbuf[:, ic * XST + 3:(ic + 1) * XST],
                in_=dram["l2a_xc"].ap()[ic * 128:(ic + 1) * 128, :]),
                wait=[xc_toks[ic]])
            p.op("vector", lambda e, ic=ic: e.tensor_copy(
                xcbuf[:, ic * XST:ic * XST + 3], haloe[:, ic * 3:(ic + 1) * 3]),
                wait=[tld, (cV, halo_done)], inc=cV)
        xcb_done = cV.n
        u_toks, xs_toks = {}, {}
        for oc in range(DC):
            tcw = p.dma("sync", lambda e, oc=oc: e.dma_start(
                out=cwsl, in_=W["l2a_cw"][oc]), wait=gwait("l2a_cwsl"))
            for tc2 in range(2):
                bank = "psE" if tc2 == 0 else "psF"
                first = True
                for tap in range(4):
                    for ic in range(DC):
                        p.op("tensor", lambda e, tap=tap, ic=ic, tc2=tc2, bank=bank,
                             first=first: e.matmul(
                            ps[bank].ap()[:, :],
                            cwsl[:, tap * 2048 + ic * 128:tap * 2048 + (ic + 1) * 128],
                            xcbuf[:, ic * XST + tap + tc2 * 512:
                                  ic * XST + tap + tc2 * 512 + 512],
                            start=first, stop=(tap == 3 and ic == DC - 1)),
                            wait=[tcw, (cV, xcb_done)] + (gwait(bank) if first else []),
                            inc=(cT if (tap == 3 and ic == DC - 1) else None))
                        first = False
                mm = cT.n
                p.op("vector", lambda e, oc=oc, bank=bank: e.tensor_scalar(
                    piece, ps[bank].ap()[:, :], cbf[:, oc:oc + 1], None, ALU.add),
                    wait=[(cT, mm), tcb] + gwait("l2a_piece"), inc=cV)
                gset(bank, cV)
                pd = cV.n
                xs_toks[(oc, tc2)] = p.dma("gpsimd", lambda e, oc=oc, tc2=tc2: e.dma_start(
                    out=dram["l2a_xs"].ap()[oc * 128:(oc + 1) * 128,
                                            tc2 * 512:(tc2 + 1) * 512],
                    in_=piece), wait=[(cV, pd)])
                guard["l2a_piece"] = xs_toks[(oc, tc2)]
                p.op("scalar", lambda e: e.activation(upiece, piece, AF.Silu),
                     wait=[(cV, pd)] + gwait("l2a_upiece"), inc=cSc)
                ud = cSc.n
                u_toks[(oc, tc2)] = p.dma("gpsimd", lambda e, oc=oc, tc2=tc2: e.dma_start(
                    out=dram["l2a_u"].ap()[oc * 128:(oc + 1) * 128,
                                           tc2 * 512:(tc2 + 1) * 512],
                    in_=upiece), wait=[(cSc, ud)])
                guard["l2a_upiece"] = u_toks[(oc, tc2)]
                dbank = "psA" if tc2 == 0 else "psB"
                dbank2 = "psC" if tc2 == 0 else "psD"
                p.op("tensor", lambda e, oc=oc, dbank=dbank: e.matmul(
                    ps[dbank].ap()[:, :], wxsl[:, oc * 160:oc * 160 + 128], upiece,
                    start=(oc == 0), stop=(oc == DC - 1)),
                    wait=[twx, (cSc, ud)] + (gwait(dbank) if oc == 0 else []),
                    inc=None)
                p.op("tensor", lambda e, oc=oc, dbank2=dbank2: e.matmul(
                    ps[dbank2].ap()[0:32, :], wxsl[:, oc * 160 + 128:(oc + 1) * 160],
                    upiece, start=(oc == 0), stop=(oc == DC - 1)),
                    wait=(gwait(dbank2) if oc == 0 else []),
                    inc=(cT if oc == DC - 1 else None))
            gset("l2a_cwsl", cT)
            if gpre is not None:
                emit_prepass_chunk(50)
        mmdbc = cT.n
        arena.release()
        fence()

        arena.mark()
        dpre = arena.alloc(T)
        bcb = arena.alloc(T)
        wdt = arena.alloc(D)
        bdtf = arena.alloc(16)
        dtile = arena.alloc(T)
        spz = arena.alloc(2 * T)
        for tc2 in range(2):
            p.op("vector", lambda e, tc2=tc2: e.tensor_copy(
                dpre[:, tc2 * 512:(tc2 + 1) * 512],
                ps["psA" if tc2 == 0 else "psB"].ap()[:, :]),
                wait=[(cT, mmdbc)], inc=cV)
            gset("psA" if tc2 == 0 else "psB", cV)
            p.op("vector", lambda e, tc2=tc2: e.tensor_copy(
                bcb[0:32, tc2 * 512:(tc2 + 1) * 512],
                ps["psC" if tc2 == 0 else "psD"].ap()[0:32, :]), inc=cV)
            gset("psC" if tc2 == 0 else "psD", cV)
        dpre_done = cV.n
        twd = p.dma("sync", lambda e: e.dma_start(out=wdt, in_=W["l2a_wdt"]))
        tbd = p.dma("sync", lambda e: e.dma_start(out=bdtf, in_=W["l2a_bdt"]))
        d_toks = {}
        for dc in range(DC):
            for tc2 in range(2):
                bank = "psE" if tc2 == 0 else "psF"
                p.op("tensor", lambda e, dc=dc, tc2=tc2, bank=bank: e.matmul(
                    ps[bank].ap()[:, :], wdt[:, dc * 128:(dc + 1) * 128],
                    dpre[:, tc2 * 512:(tc2 + 1) * 512], start=True, stop=True),
                    wait=[twd, (cV, dpre_done)] + gwait(bank), inc=cT)
                mm = cT.n
                zsl = spz[:, tc2 * 512:(tc2 + 1) * 512]
                asl = spz[:, 1024 + tc2 * 512:1024 + (tc2 + 1) * 512]
                p.op("vector", lambda e, dc=dc, bank=bank, zsl=zsl: e.tensor_scalar(
                    zsl, ps[bank].ap()[:, :], bdtf[:, dc:dc + 1], None, ALU.add),
                    wait=[(cT, mm), tbd] + gwait(("l2a_z", tc2)), inc=cV)
                gset(bank, cV)
                zd = cV.n
                p.op("scalar", lambda e, zsl=zsl, asl=asl: e.activation(
                    asl, zsl, AF.Abs), wait=[(cV, zd)], inc=cSc)
                p.op("scalar", lambda e, asl=asl: e.activation(
                    asl, asl, AF.Exp, scale=-1.0), inc=cSc)
                p.op("scalar", lambda e, asl=asl: e.activation(
                    asl, asl, AF.Ln, bias=1.0), inc=cSc)
                lnd = cSc.n
                p.op("vector", lambda e, dc=dc, tc2=tc2, zsl=zsl, asl=asl: e.scalar_tensor_tensor(
                    dtile[:, tc2 * 512:(tc2 + 1) * 512], zsl, 0.0, asl,
                    ALU.max, ALU.add),
                    wait=[(cSc, lnd)] + gwait("l2a_dtile"), inc=cV)
                guard[("l2a_z", tc2)] = (cV, cV.n)
            dd = cV.n
            d_toks[dc] = p.dma("gpsimd", lambda e, dc=dc: e.dma_start(
                out=dram["l2a_delta"].ap()[dc * 128:(dc + 1) * 128, :], in_=dtile),
                wait=[(cV, dd)] + gwait("l2a_dtile"))
            guard["l2a_dtile"] = d_toks[dc]
        bc_tok = p.dma("gpsimd", lambda e: e.dma_start(
            out=dram["l2a_bc"].ap(), in_=bcb[0:32, :]), wait=[(cV, dpre_done)])
        arena.release()
        fence()

        arena.mark()
        Bb = arena.alloc(NS * TC)
        Cb = arena.alloc(NS * TC)
        dAb2 = [arena.alloc(NS * TC), arena.alloc(NS * TC)]
        dBu2 = [arena.alloc(NS * TC), arena.alloc(NS * TC)]
        hb = arena.alloc(NS * TC)
        dsl2 = [arena.alloc(TC), arena.alloc(TC)]
        usl2 = [arena.alloc(TC), arena.alloc(TC)]
        ysl2 = [arena.alloc(TC), arena.alloc(TC)]
        flat = hb
        hprev = arena.alloc(256)
        hin = arena.alloc(256)
        hcA = arena.alloc(256)
        hcB = arena.alloc(256)
        Abuf = arena.alloc(256)
        zeroc = arena.alloc(1)

        p.op("vector", lambda e: e.memset(hprev, 0.0), inc=cV)
        p.op("vector", lambda e: e.memset(zeroc, 0.0), inc=cV)
        tal = p.dma("sync", lambda e: e.dma_start(
            out=Abuf.rearrange("p (c k) -> p c k", k=16), in_=W["l2a_alog"]))
        p.op("scalar", lambda e: e.activation(Abuf, Abuf, AF.Exp),
             wait=[tal], inc=cSc)
        sa = cSc.n
        p.op("vector", lambda e: e.tensor_scalar(Abuf, Abuf, -1.0, None, ALU.mult),
             wait=[(cSc, sa)], inc=cV)
        a_done = cV.n

        if gpre is not None:
            emit_prepass_chunk(10 ** 9)
        y_toks = {}
        for tcn in range(TCN):
            bcd = {}
            for which, dst, rowoff in (("B", Bb, 0), ("C", Cb, 16)):
                tfl = p.dma("gpsimd", lambda e, rowoff=rowoff, tcn=tcn: e.dma_start(
                    out=flat[0:1, 0:NS * TC],
                    in_=dram["l2a_bc"].ap()[rowoff:rowoff + 16,
                                            tcn * TC:(tcn + 1) * TC]),
                    wait=gwait("l2a_flat") + [bc_tok, (cV, cV.n)])
                dn = 0
                for j in range(0, NS * TC, 512):
                    p.op("tensor", lambda e, j=j: e.matmul(
                        ps["psG"].ap()[:, :], ones_r.ap()[:, :],
                        flat[0:1, j:j + 512], start=True, stop=True),
                        wait=[tfl, (cV, ones_ready)] + gwait("psG"), inc=cT)
                    td = cT.n
                    p.op("vector", lambda e, j=j, dst=dst: e.tensor_copy(
                        dst[:, j:j + 512], ps["psG"].ap()[:, :]),
                        wait=[(cT, td)], inc=cV)
                    gset("psG", cV)
                bcd[which] = cV.n
                guard["l2a_flat"] = (cV, cV.n)
            for dc in range(DC):
                sl = dc % 2
                dAb, dBu = dAb2[sl], dBu2[sl]
                dsl, usl, ysl = dsl2[sl], usl2[sl], ysl2[sl]
                tdl = p.dma("sync", lambda e, dc=dc, tcn=tcn, dsl=dsl: e.dma_start(
                    out=dsl, in_=dram["l2a_delta"].ap()[dc * 128:(dc + 1) * 128,
                                                        tcn * TC:(tcn + 1) * TC]),
                    wait=gwait(("l2a_dsl", sl)) + [d_toks[dc]])
                tul = p.dma("sync", lambda e, dc=dc, tcn=tcn, usl=usl: e.dma_start(
                    out=usl, in_=dram["l2a_u"].ap()[dc * 128:(dc + 1) * 128,
                                                    tcn * TC:(tcn + 1) * TC]),
                    wait=gwait(("l2a_usl", sl)) + [u_toks[(dc, 0)], u_toks[(dc, 1)]])
                for n in range(NS):
                    p.op("scalar", lambda e, dc=dc, n=n, dAb=dAb, dsl=dsl: e.activation(
                        dAb[:, n * TC:(n + 1) * TC], dsl, AF.Exp,
                        scale=Abuf[:, dc * 16 + n:dc * 16 + n + 1]),
                        wait=([tdl, (cV, a_done)] + gwait(("l2a_dAb", sl)))
                        if n == 0 else (), inc=cSc)
                da_done = cSc.n
                p.op("vector", lambda e, ysl=ysl, dsl=dsl, usl=usl: e.tensor_mul(
                    ysl, dsl, usl),
                    wait=[tul, tdl] + gwait(("l2a_ysl", sl)), inc=cV)
                p.op("vector", lambda e, dBu=dBu, ysl=ysl: e.tensor_tensor(
                    dBu.rearrange("p (n t) -> p n t", t=TC),
                    ysl.unsqueeze(1).broadcast_to([128, NS, TC]),
                    Bb.rearrange("p (n t) -> p n t", t=TC), ALU.mult),
                    wait=[(cV, bcd["B"])], inc=cV)
                dbu_done = cV.n
                for n in range(NS):
                    p.op("vector", lambda e, dc=dc, n=n, dAb=dAb, dBu=dBu: e.tensor_tensor_scan(
                        hb[:, n * TC:(n + 1) * TC], dAb[:, n * TC:(n + 1) * TC],
                        dBu[:, n * TC:(n + 1) * TC],
                        hprev[:, dc * 16 + n:dc * 16 + n + 1],
                        ALU.mult, ALU.add),
                        wait=[(cSc, da_done), (cV, dbu_done)], inc=cV)
                p.op("vector", lambda e, dc=dc: e.tensor_copy(
                    hprev[:, dc * 16:(dc + 1) * 16],
                    hb[:, TC - 1:NS * TC:TC]), inc=cV)
                guard[("l2a_usl", sl)] = (cV, cV.n)
                p.op("vector", lambda e, dAb=dAb: e.tensor_tensor(
                    dAb.rearrange("p (n t) -> p n t", t=TC),
                    hb.rearrange("p (n t) -> p n t", t=TC),
                    Cb.rearrange("p (n t) -> p n t", t=TC), ALU.mult),
                    wait=[(cV, bcd["C"])], inc=cV)
                guard[("l2a_dAb", sl)] = (cV, cV.n)
                p.op("vector", lambda e, ysl=ysl, dAb=dAb: e.reduce_sum(
                    ysl, dAb.rearrange("p (n t) -> p t n", t=TC), AX.X), inc=cV)
                guard[("l2a_dsl", sl)] = (cV, cV.n)
                yd = cV.n
                y_toks[(dc, tcn)] = p.dma("gpsimd", lambda e, dc=dc, tcn=tcn, ysl=ysl: e.dma_start(
                    out=dram["l2a_y"].ap()[dc * 128:(dc + 1) * 128,
                                           tcn * TC:(tcn + 1) * TC], in_=ysl),
                    wait=[(cV, yd)])
                guard[("l2a_ysl", sl)] = y_toks[(dc, tcn)]

        hd = cV.n
        thx = p.dma("gpsimd", lambda e: e.dma_start(
            out=dram["l2a_hin"].ap(), in_=hprev), wait=[(cV, hd)])
        p.op("gpsimd", lambda e: e.collective_compute(
            "AllGather", ALU.bypass, replica_groups=RG,
            ins=[dram["l2a_hin"].ap()], outs=[dram["l2a_hout"].ap()]),
            wait=[thx], inc=cG)
        ccd = cG.n
        t0 = p.dma("gpsimd", lambda e: e.dma_start(
            out=hcA, in_=dram["l2a_hout"].ap()[0:128, :]), wait=[(cG, ccd)])
        t1 = p.dma("gpsimd", lambda e: e.dma_start(
            out=hcB, in_=dram["l2a_hout"].ap()[128:256, :]), wait=[(cG, ccd)])
        p.op("vector", lambda e: e.tensor_scalar(
            hin, hcA, gatesb.ap()[:, 0:1], None, ALU.mult), wait=[t0, t1], inc=cV)
        p.op("vector", lambda e: e.scalar_tensor_tensor(
            hin, hcB, gatesb.ap()[:, 1:2], hin, ALU.mult, ALU.add), inc=cV)
        hin_done = cV.n

        tdd = p.dma("sync", lambda e: e.dma_start(out=hcA[:, 0:16],
                                                  in_=W["l2a_dD"]),
                    wait=[(cV, hin_done)])
        z_toks = {}
        hp2 = hcB
        p.op("vector", lambda e: e.tensor_copy(hp2, hin), wait=[(cV, hin_done)],
             inc=cV)
        for tcn in range(TCN):
            tfl = p.dma("gpsimd", lambda e, tcn=tcn: e.dma_start(
                out=flat[0:1, 0:NS * TC],
                in_=dram["l2a_bc"].ap()[16:32, tcn * TC:(tcn + 1) * TC]),
                wait=gwait("l2a_flat") + [bc_tok, (cV, cV.n)])
            for j in range(0, NS * TC, 512):
                p.op("tensor", lambda e, j=j: e.matmul(
                    ps["psG"].ap()[:, :], ones_r.ap()[:, :],
                    flat[0:1, j:j + 512], start=True, stop=True),
                    wait=[tfl, (cV, ones_ready)] + gwait("psG"), inc=cT)
                td = cT.n
                p.op("vector", lambda e, j=j: e.tensor_copy(
                    Cb[:, j:j + 512], ps["psG"].ap()[:, :]),
                    wait=[(cT, td)], inc=cV)
                gset("psG", cV)
            cb_done = cV.n
            guard["l2a_flat"] = (cV, cV.n)
            for dc in range(DC):
                sl = dc % 2
                dAb, dsl, usl, ysl = dAb2[sl], dsl2[sl], usl2[sl], ysl2[sl]
                rsl = dBu2[sl][:, 0:TC]
                tdl = p.dma("sync", lambda e, dc=dc, tcn=tcn, dsl=dsl: e.dma_start(
                    out=dsl, in_=dram["l2a_delta"].ap()[dc * 128:(dc + 1) * 128,
                                                        tcn * TC:(tcn + 1) * TC]),
                    wait=gwait(("l2a_dsl", sl)))
                for n in range(NS):
                    p.op("scalar", lambda e, dc=dc, n=n, dAb=dAb, dsl=dsl: e.activation(
                        dAb[:, n * TC:(n + 1) * TC], dsl, AF.Exp,
                        scale=Abuf[:, dc * 16 + n:dc * 16 + n + 1]),
                        wait=([tdl] + gwait(("l2a_dAb", sl))) if n == 0 else (),
                        inc=cSc)
                da_done = cSc.n
                guard[("l2a_dsl", sl)] = (cSc, da_done)
                for n in range(NS):
                    p.op("vector", lambda e, dc=dc, n=n, dAb=dAb: e.tensor_tensor_scan(
                        hb[:, n * TC:(n + 1) * TC], dAb[:, n * TC:(n + 1) * TC],
                        zeroc.broadcast_to([128, TC]),
                        hp2[:, dc * 16 + n:dc * 16 + n + 1],
                        ALU.mult, ALU.add),
                        wait=[(cSc, da_done)], inc=cV)
                p.op("vector", lambda e, dc=dc: e.tensor_copy(
                    hp2[:, dc * 16:(dc + 1) * 16],
                    hb[:, TC - 1:NS * TC:TC]), inc=cV)
                p.op("vector", lambda e, dAb=dAb: e.tensor_tensor(
                    dAb.rearrange("p (n t) -> p n t", t=TC),
                    hb.rearrange("p (n t) -> p n t", t=TC),
                    Cb.rearrange("p (n t) -> p n t", t=TC), ALU.mult),
                    wait=[(cV, cb_done)], inc=cV)
                guard[("l2a_dAb", sl)] = (cV, cV.n)
                p.op("vector", lambda e, usl=usl, dAb=dAb: e.reduce_sum(
                    usl, dAb.rearrange("p (n t) -> p t n", t=TC), AX.X),
                    wait=gwait(("l2a_usl", sl)), inc=cV)
                ycd = cV.n
                tyl = p.dma("sync", lambda e, dc=dc, tcn=tcn, ysl=ysl: e.dma_start(
                    out=ysl, in_=dram["l2a_y"].ap()[dc * 128:(dc + 1) * 128,
                                                    tcn * TC:(tcn + 1) * TC]),
                    wait=gwait(("l2a_ysl", sl)) + [y_toks[(dc, tcn)]])
                p.op("vector", lambda e, ysl=ysl, usl=usl: e.tensor_add(
                    ysl, ysl, usl), wait=[tyl], inc=cV)
                ad_ = cV.n
                tu2 = p.dma("sync", lambda e, dc=dc, tcn=tcn, usl=usl: e.dma_start(
                    out=usl, in_=dram["l2a_u"].ap()[dc * 128:(dc + 1) * 128,
                                                    tcn * TC:(tcn + 1) * TC]),
                    wait=[(cV, ad_)])
                tr2 = p.dma("sync", lambda e, dc=dc, tcn=tcn, rsl=rsl: e.dma_start(
                    out=rsl, in_=dram["l2a_r"].ap()[dc * 128:(dc + 1) * 128,
                                                    tcn * TC:(tcn + 1) * TC]),
                    wait=gwait(("l2a_rsl", sl)))
                p.op("scalar", lambda e, rsl=rsl: e.activation(rsl, rsl, AF.Silu),
                     wait=[tr2], inc=cSc)
                sr = cSc.n
                p.op("vector", lambda e, dc=dc, ysl=ysl, usl=usl: e.scalar_tensor_tensor(
                    ysl, usl, hcA[:, dc:dc + 1], ysl, ALU.mult, ALU.add),
                    wait=[tu2, tdd], inc=cV)
                guard[("l2a_usl", sl)] = (cV, cV.n)
                p.op("vector", lambda e, ysl=ysl, rsl=rsl: e.tensor_mul(
                    ysl, ysl, rsl), wait=[(cSc, sr)], inc=cV)
                zd = cV.n
                guard[("l2a_rsl", sl)] = (cV, zd)
                z_toks[(dc, tcn)] = p.dma("gpsimd", lambda e, dc=dc, tcn=tcn, ysl=ysl: e.dma_start(
                    out=dram["l2a_z"].ap()[dc * 128:(dc + 1) * 128,
                                           tcn * TC:(tcn + 1) * TC], in_=ysl),
                    wait=[(cV, zd)])
                guard[("l2a_ysl", sl)] = z_toks[(dc, tcn)]
        arena.release()
        fence()

        arena.mark()
        zbuf = arena.alloc(DC * T)
        wosl = arena.alloc(2 * D)
        xspc = arena.alloc(512)
        ofm_t = arena.alloc(512)
        sq = zbuf[:, 0:D]
        tz = {}
        for dc in range(DC):
            tz[dc] = p.dma("sync", lambda e, dc=dc: e.dma_start(
                out=zbuf[:, dc * T:(dc + 1) * T],
                in_=dram["l2a_z"].ap()[dc * 128:(dc + 1) * 128, :]),
                wait=[z_toks[(dc, tcn2)] for tcn2 in range(TCN)])
        res_done = {}
        for oc in range(DC):
            slot = oc % 2
            wo_s = wosl[:, slot * D:(slot + 1) * D]
            two = p.dma("sync", lambda e, oc=oc, wo_s=wo_s: e.dma_start(
                out=wo_s, in_=W["l2a_wout"][oc]), wait=gwait(("l2a_wo", slot)))
            for tc2 in range(2):
                bank = "psE" if tc2 == 0 else "psF"
                for dc in range(DC):
                    p.op("tensor", lambda e, dc=dc, tc2=tc2, wo_s=wo_s, bank=bank: e.matmul(
                        ps[bank].ap()[:, :], wo_s[:, dc * 128:(dc + 1) * 128],
                        zbuf[:, dc * T + tc2 * 512:dc * T + (tc2 + 1) * 512],
                        start=(dc == 0), stop=(dc == DC - 1)),
                        wait=[two, tz[dc]] + (gwait(bank) if dc == 0 else []),
                        inc=(cT if dc == DC - 1 else None))
                mm = cT.n
                txs = p.dma("sync", lambda e, oc=oc, tc2=tc2: e.dma_start(
                    out=xspc, in_=dram["l2a_xs"].ap()[oc * 128:(oc + 1) * 128,
                                                      tc2 * 512:(tc2 + 1) * 512]),
                    wait=gwait("l2a_xspc") + [xs_toks[(oc, tc2)]])
                p.op("vector", lambda e, bank=bank: e.tensor_add(
                    ofm_t, ps[bank].ap()[:, :], xspc),
                    wait=[(cT, mm), txs] + gwait("l2a_ofmt"), inc=cV)
                gset(bank, cV)
                guard["l2a_xspc"] = (cV, cV.n)
                od = cV.n
                for j in range(4):
                    tbank = TPB[j]
                    p.op("tensor", lambda e, j=j, tbank=tbank: e.transpose(
                        ps[tbank].ap()[:, 0:128], ofm_t[:, j * 128:(j + 1) * 128],
                        sident.ap()[:, :]), wait=[(cV, od)] + gwait(tbank), inc=cT)
                    td = cT.n
                    tt = tc2 * 4 + j
                    p.op("vector", lambda e, tt=tt, oc=oc, tbank=tbank: e.tensor_add(
                        xtile(tt)[:, oc * 128:(oc + 1) * 128],
                        xtile(tt)[:, oc * 128:(oc + 1) * 128],
                        ps[tbank].ap()[:, 0:128]), wait=[(cT, td)], inc=cV)
                    gset(tbank, cV)
                    res_done[(tt, oc)] = (cV, cV.n)
                guard["l2a_ofmt"] = (cV, cV.n)
            gset(("l2a_wo", slot), cT)
        all_mm = (cT, cT.n)
        ln_waits = []
        for tt in range(TT):
            ln_waits.append(ln_tile_inplace(
                xtile(tt), sq,
                wait=[res_done[(tt, oc)] for oc in range(DC)] + [all_mm]))
        arena.release()
        arena.release()
        return [(cSc, max(ln_waits))]

    def stage_l3a(ready):
        return stage_attn(ready, "l3a", 16, False)

    def stage_l0a(ready):
        return stage_attn(ready, "l0a", 4, True)

    stage_fns = {"l1f": stage_l1f, "l0f": stage_l0f, "l3f": stage_l3f,
                 "l1a": stage_l1a, "l3a": stage_l3a, "l0a": stage_l0a,
                 "l2f": stage_l2f, "l2a": stage_l2a}
    ready = stream_ready
    for s in stages:
        if s not in stage_fns:
            raise NotImplementedError(s)
        p.wait("sync", ready)
        p.wait("gpsimd", ready)
        ready = stage_fns[s](ready)

    toks = []
    for tt in range(TT):
        toks.append(p.dma("gpsimd", lambda e, tt=tt: e.dma_start(
            out=out[tt * 128:(tt + 1) * 128, :], in_=xtile(tt)),
            wait=list(ready)))
    p.wait("gpsimd", toks)

    p.finalize()
    return nc



def _ecslabs(wt):
    nec = wt.shape[1] // 128
    return np.ascontiguousarray(
        wt.reshape(16, 128, nec, 128).transpose(2, 1, 0, 3).reshape(nec, 128, 2048))


def _attn_prep(sh, pre, WqT, WkT, WvT, WoT, bq, bk, bv, bo, kvc):
    sh[pre + "_wq"] = _ecslabs(WqT)
    sh[pre + "_wk"] = _ecslabs(WkT)
    sh[pre + "_wv"] = np.ascontiguousarray(WvT.reshape(16, 128, kvc * 128))
    sh[pre + "_woT"] = np.ascontiguousarray(WoT.reshape(16, 128, 2048))
    sh[pre + "_bqf"] = np.ascontiguousarray(bq.reshape(16, 128).T)
    sh[pre + "_bkf"] = np.ascontiguousarray(bk.reshape(kvc, 128).T)
    sh[pre + "_bv"] = np.ascontiguousarray(bv[None, :])
    sh[pre + "_bo"] = np.ascontiguousarray(bo[None, :])


def _tri_patterns():
    pats = []
    for i in range(4):
        m = np.zeros((128, 512), np.float32)
        for qt in range(4):
            if qt < i:
                m[:, qt * 128:(qt + 1) * 128] = NEG
            elif qt == i:
                m[:, qt * 128:(qt + 1) * 128] = np.where(
                    np.arange(128)[:, None] <= np.arange(128)[None, :], 0.0, NEG
                ).astype(np.float32)
        pats.append(m)
    return np.concatenate(pats, axis=1)


def _rope_tables(t0):
    d = 128
    invf = 1.0 / (10000.0 ** (np.arange(0, d, 2) / d))
    tpos = np.arange(t0, t0 + T, dtype=np.float64)
    emb = np.concatenate([np.outer(invf, tpos)] * 2, axis=0)
    cos = np.cos(emb).astype(np.float32)
    sin = np.sin(emb).astype(np.float32)
    return np.ascontiguousarray(cos), np.ascontiguousarray(sin)


def _prep_weights(inputs, stages=ALL_STAGES):
    sh = {"ident": np.eye(128, dtype=np.float32)}
    f32 = lambda a: np.asarray(a, dtype=np.float32)
    if "l3a" in stages:
        WqkvT = f32(inputs["L3_attn_W_qkv"]).T
        bqkv = f32(inputs["L3_attn_b_qkv"])
        _attn_prep(sh, "l3a", WqkvT[:, 0:2048], WqkvT[:, 2048:4096],
                   WqkvT[:, 4096:6144], f32(inputs["L3_attn_W_out"]).T,
                   bqkv[0:2048], bqkv[2048:4096], bqkv[4096:6144],
                   f32(inputs["L3_attn_b_out"]), 16)
    if "l0a" in stages:
        z = np.zeros(2048, np.float32)
        zk = np.zeros(512, np.float32)
        _attn_prep(sh, "l0a", f32(inputs["L0_attn_Wq"]).T,
                   f32(inputs["L0_attn_Wk"]).T, f32(inputs["L0_attn_Wv"]).T,
                   f32(inputs["L0_attn_Wo"]).T, z, zk, zk, z, 4)
    if "l0f" in stages:
        Wb = f32(inputs["L0_ffn_W"])
        sh["l0f_wT"] = np.ascontiguousarray(Wb.T.reshape(16, 128, 2048))
        acat = f32(inputs["L0_ffn_lora_A"]).transpose(2, 0, 1).reshape(2048, 128)
        sh["l0f_acat"] = np.ascontiguousarray(
            acat.reshape(16, 128, 128).transpose(1, 0, 2).reshape(128, 2048))
        sh["l0f_bcat"] = np.ascontiguousarray(
            f32(inputs["L0_ffn_lora_B"]).transpose(0, 2, 1).reshape(128, 2048))
        gwt = f32(inputs["L0_ffn_gate_w"]).T
        sh["l0f_gwt"] = np.ascontiguousarray(
            gwt.reshape(16, 128, 8).transpose(1, 0, 2).reshape(128, 128))
        sh["l0f_brow"] = np.ascontiguousarray(f32(inputs["L0_ffn_b"])[None, :])
        sh["l0f_gbrow"] = np.ascontiguousarray(f32(inputs["L0_ffn_gate_b"])[None, :])
    if "l3f" in stages:
        W1 = f32(inputs["L3_ffn_W1"])
        sh["l3f_w1"] = np.ascontiguousarray(
            W1.reshape(16, 128, 16, 128).transpose(2, 1, 0, 3).reshape(16, 128, 2048))
        vcat = f32(inputs["L3_ffn_V"]).transpose(2, 0, 1).reshape(2048, 512)
        sh["l3f_vcat"] = np.ascontiguousarray(vcat.reshape(16, 128, 512))
        ucat = f32(inputs["L3_ffn_U"]).transpose(0, 2, 1).reshape(512, 2048)
        sh["l3f_ucat"] = np.ascontiguousarray(ucat.reshape(4, 128, 2048))
        w2 = f32(inputs["L3_ffn_W2"])
        sh["l3f_w2"] = np.ascontiguousarray(
            w2.reshape(16, 128, 8).transpose(1, 0, 2).reshape(128, 128))
        sh["l3f_b1f"] = np.ascontiguousarray(f32(inputs["L3_ffn_b1"]).reshape(16, 128).T)
        sh["l3f_b2row"] = np.ascontiguousarray(f32(inputs["L3_ffn_b2"])[None, :])
    if "l1a" in stages:
        for nm, key in (("l1a_wq", "L1_attn_Wq"), ("l1a_wk", "L1_attn_Wk"),
                        ("l1a_wv", "L1_attn_Wv")):
            wt_ = f32(inputs[key]).T
            sh[nm] = np.ascontiguousarray(
                wt_.reshape(16, 128, 16, 128).transpose(2, 1, 0, 3).reshape(16, 128, 2048))
        sh["l1a_woT"] = np.ascontiguousarray(f32(inputs["L1_attn_Wo"]).T.reshape(16, 128, 2048))
        for nm, key in (("l1a_bq", "L1_attn_bq"), ("l1a_bk", "L1_attn_bk"),
                        ("l1a_bv", "L1_attn_bv")):
            sh[nm] = np.ascontiguousarray(f32(inputs[key]).reshape(16, 128).T)
        sh["l1a_bo"] = np.ascontiguousarray(f32(inputs["L1_attn_bo"])[None, :])
    if "l2a" in stages:
        Win = f32(inputs["L2_attn_W_in"])
        sh["l2a_win"] = np.ascontiguousarray(
            Win.reshape(16, 128, 32, 128).transpose(2, 1, 0, 3).reshape(32, 128, 2048))
        cwT = f32(inputs["L2_attn_conv_w"]).transpose(2, 1, 0)
        sh["l2a_cw"] = np.ascontiguousarray(
            cwT.reshape(4, 16, 128, 16, 128).transpose(3, 2, 0, 1, 4)
            .reshape(16, 128, 8192))
        sh["l2a_convb"] = np.ascontiguousarray(
            f32(inputs["L2_attn_conv_b"]).reshape(16, 128).T)
        wx = f32(inputs["L2_attn_W_x"])
        sh["l2a_wx"] = np.ascontiguousarray(
            wx.reshape(16, 128, 160).transpose(1, 0, 2).reshape(128, 16 * 160))
        sh["l2a_wdt"] = np.ascontiguousarray(f32(inputs["L2_attn_W_dt"]))
        sh["l2a_bdt"] = np.ascontiguousarray(
            f32(inputs["L2_attn_b_dt"]).reshape(16, 128).T)
        sh["l2a_alog"] = np.ascontiguousarray(
            f32(inputs["L2_attn_A_log"]).reshape(16, 128, 16))
        sh["l2a_dD"] = np.ascontiguousarray(
            f32(inputs["L2_attn_D"]).reshape(16, 128).T)
        sh["l2a_wout"] = np.ascontiguousarray(
            f32(inputs["L2_attn_W_out"]).reshape(16, 128, 16, 128)
            .transpose(2, 1, 0, 3).reshape(16, 128, 2048))
    if "l2f" in stages:
        sh["l2f_gw"] = np.ascontiguousarray(
            f32(inputs["L2_ffn_gate_w"]).T.reshape(16, 128, 8192))
        sh["l2f_vw"] = np.ascontiguousarray(
            f32(inputs["L2_ffn_val_w"]).T.reshape(16, 128, 8192))
        sh["l2f_ow"] = np.ascontiguousarray(
            f32(inputs["L2_ffn_out_w"]).T.reshape(64, 128, 2048))
        sh["l2f_gb"] = np.ascontiguousarray(f32(inputs["L2_ffn_gate_b"])[None, :])
        sh["l2f_vb"] = np.ascontiguousarray(f32(inputs["L2_ffn_val_b"])[None, :])
        sh["l2f_ob"] = np.ascontiguousarray(f32(inputs["L2_ffn_out_b"])[None, :])
    if "l1f" in stages:
        gwT = f32(inputs["L1_ffn_gate_w"]).T
        uwT = f32(inputs["L1_ffn_up_w"]).T
        dwT = f32(inputs["L1_ffn_down_w"]).T
        sh["l1f_gw"] = np.ascontiguousarray(
            gwT.reshape(16, 128, 44, 128).transpose(2, 1, 0, 3).reshape(44, 128, 2048))
        sh["l1f_uw"] = np.ascontiguousarray(
            uwT.reshape(16, 128, 44, 128).transpose(2, 1, 0, 3).reshape(44, 128, 2048))
        sh["l1f_dw"] = np.ascontiguousarray(dwT.reshape(44, 128, 2048))
        sh["l1f_gs"] = np.ascontiguousarray(
            f32(inputs["L1_ffn_gate_s"]).reshape(44, 128).T)
        sh["l1f_us"] = np.ascontiguousarray(
            f32(inputs["L1_ffn_up_s"]).reshape(44, 128).T)
        sh["l1f_ds"] = np.ascontiguousarray(f32(inputs["L1_ffn_down_s"])[None, :])
    return sh


def _percore_maps(inputs, stages=ALL_STAGES):
    hs = np.asarray(inputs["L0_attn_hidden_states"], dtype=np.float32)
    sh = _prep_weights(inputs, stages)
    maps = []
    pats = _tri_patterns()
    zer = np.zeros_like(pats)
    triA = np.ascontiguousarray(np.concatenate([pats, zer], axis=1))
    triB = np.ascontiguousarray(np.concatenate([zer, pats], axis=1))
    kmA = np.zeros((128, 32), np.float32)
    for qb in range(2):
        for gg in range(8, 16):
            kmA[:, qb * 16 + gg] = NEG
        for gg in range(4 * qb + 4, 8):
            kmA[:, qb * 16 + gg] = NEG
    kmB = np.zeros((128, 32), np.float32)
    for core in range(8):
        seq, t0 = core // 2, (core % 2) * T
        m = dict(sh)
        m["x0"] = np.ascontiguousarray(hs[seq, t0:t0 + T, :])
        g = np.zeros((128, 8), dtype=np.float32)
        if core % 2 == 1:
            g[:, 0] = 1.0
            g[:, 2] = 1.0
        else:
            g[:, 3] = 1.0
        m["gates"] = g
        for pre in ("l3a", "l0a"):
            if pre + "_trisel" not in sh and pre + "_wq" not in sh:
                continue
            m[pre + "_trisel"] = triB if core % 2 else triA
            m[pre + "_kmaskq"] = kmB if core % 2 else kmA
        if "l0a_wq" in sh:
            cos, sinm = _rope_tables(t0)
            m["l0a_cos"] = cos
            m["l0a_sinm"] = sinm
            R = np.zeros((128, 128), np.float32)
            for dd in range(64):
                R[dd, dd + 64] = -1.0
                R[dd + 64, dd] = 1.0
            m["rotm"] = np.ascontiguousarray(R.T)
        maps.append(m)
    return maps


def run(inputs, stages=ALL_STAGES):
    maps = _percore_maps(inputs, stages)
    nc = build_program(stages)
    res = run_bass_kernel_spmd(nc, maps, core_ids=list(range(8)))
    outa = np.empty((4, 2048, 2048), dtype=np.float32)
    for core in range(8):
        seq, t0 = core // 2, (core % 2) * T
        outa[seq, t0:t0 + T, :] = res.results[core]["out"]
    return outa


def kernel(**inputs):
    return run(inputs, ALL_STAGES)


# version 09
# speedup vs baseline: 1.0212x; 1.0212x over previous
import numpy as np
import concourse.bass as bass
import concourse.mybir as mybir
from concourse.bass_utils import run_bass_kernel_spmd

F32 = mybir.dt.float32
BF16 = mybir.dt.bfloat16
I32 = mybir.dt.int32
AF = mybir.ActivationFunctionType
ALU = mybir.AluOpType
AX = mybir.AxisListType

T = 1024
D = 2048
TT = T // 128
DC = D // 128
NEG = -1.0e4
EPS = 1e-5

ENGINES = ("sync", "tensor", "vector", "scalar", "gpsimd")
L1F_CUT = 4
ALL_STAGES = ("l0a", "l0f", "l1a", "l1f", "l2a", "l2f", "l3a", "l3f")


class Ctr:
    def __init__(self, nc, name):
        self.h = nc.alloc_semaphore(name)
        self.n = 0


class Prog:

    def __init__(self, nc):
        self.nc = nc
        self.q = {e: [] for e in ENGINES}
        self.ctrs = {}
        self.pools = {}
        self.pool_i = {}
        self.waited = {e: {} for e in ENGINES}
        self.selfc = {}

    def ctr(self, name):
        if name not in self.ctrs:
            self.ctrs[name] = Ctr(self.nc, name)
        return self.ctrs[name]

    def _dedup(self, e, wait):
        w = []
        for c, v in wait:
            if v <= 0:
                continue
            key = id(c)
            if v > self.waited[e].get(key, 0):
                self.waited[e][key] = v
                w.append((c.h, v))
        return w

    def op(self, e, fn, wait=(), inc=None, k=1):
        selfc = self.selfc.get(e)
        wait = list(wait)
        if selfc is not None:
            wait.append((selfc, selfc.n))
            if inc is None:
                inc = selfc
        w = self._dedup(e, wait)
        if inc is not None:
            inc.n += k

        def g(eng, w=w, fn=fn, inc=inc, k=k):
            for h, v in w:
                eng.wait_ge(h, v)
            bi = fn(eng)
            if inc is not None:
                bi.then_inc(inc.h, k)

        self.q[e].append(g)
        return inc.n if inc is not None else None

    def dma(self, e, fn, wait=(), pool=None, nctr=8):
        pool = pool or e
        key = (e, pool)
        if key not in self.pools:
            self.pools[key] = [self.ctr(f"dma_{e}_{pool}_{i}") for i in range(nctr)]
            self.pool_i[key] = -1
        self.pool_i[key] = (self.pool_i[key] + 1) % len(self.pools[key])
        c = self.pools[key][self.pool_i[key]]
        self.op(e, fn, wait=list(wait) + [(c, c.n)], inc=c, k=16)
        return (c, c.n)

    def wait(self, e, waits):
        w = self._dedup(e, waits)
        if not w:
            return

        def g(eng, w=w):
            for h, v in w:
                eng.wait_ge(h, v)

        self.q[e].append(g)

    def finalize(self):
        with self.nc.Block() as block:
            def runner(stream):
                def body(eng):
                    for g in stream:
                        g(eng)
                return body
            block.sync(runner(self.q["sync"]))
            block.tensor(runner(self.q["tensor"]))
            block.vector(runner(self.q["vector"]))
            block.scalar(runner(self.q["scalar"]))
            block.gpsimd(runner(self.q["gpsimd"]))


class Arena:
    def __init__(self, nc, name, cols):
        self.t = nc.alloc_sbuf_tensor(name, [128, cols], F32)
        self.cols = cols
        self.off = 0
        self.marks = []
        self.hi = 0

    def alloc(self, cols):
        assert self.off + cols <= self.cols, \
            f"arena overflow {self.off}+{cols}>{self.cols}"
        ap = self.t.ap()[:, self.off:self.off + cols]
        self.off += cols
        self.hi = max(self.hi, self.off)
        return ap

    def mark(self):
        self.marks.append(self.off)

    def release(self):
        self.off = self.marks.pop()



def build_program(stages=ALL_STAGES):
    nc = bass.Bass("TRN2", target_bir_lowering=False, debug=False)
    p = Prog(nc)

    def din(name, shape, dtype=F32):
        return nc.dram_tensor(name, list(shape), dtype, kind="ExternalInput").ap()

    x0 = din("x0", (T, D))
    ident_in = din("ident", (128, 128))
    gates_in = din("gates", (128, 8))
    rotm_in = din("rotm", (128, 128)) if "l0a" in stages else None

    W = {}

    def wdecl(name, shape):
        W[name] = din(name, shape)

    if "l0f" in stages:
        wdecl("l0f_wT", (DC, 128, D))
        wdecl("l0f_acat", (128, 2048))
        wdecl("l0f_bcat", (128, 2048))
        wdecl("l0f_gwt", (128, 128))
        wdecl("l0f_brow", (1, D))
        wdecl("l0f_gbrow", (1, 8))
    if "l3f" in stages:
        wdecl("l3f_w1", (DC, 128, D))
        wdecl("l3f_vcat", (DC, 128, 512))
        wdecl("l3f_ucat", (4, 128, D))
        wdecl("l3f_w2", (128, 128))
        wdecl("l3f_b1f", (128, 16))
        wdecl("l3f_b2row", (1, 8))
    if "l1a" in stages:
        wdecl("l1a_wq", (DC, 128, D))
        wdecl("l1a_wk", (DC, 128, D))
        wdecl("l1a_wv", (DC, 128, D))
        wdecl("l1a_woT", (DC, 128, D))
        wdecl("l1a_bq", (128, 16))
        wdecl("l1a_bk", (128, 16))
        wdecl("l1a_bv", (128, 16))
        wdecl("l1a_bo", (1, D))
    if "l2a" in stages:
        wdecl("l2a_win", (32, 128, D))
        wdecl("l2a_cw", (DC, 128, 8192))
        wdecl("l2a_convb", (128, 16))
        wdecl("l2a_wx", (128, DC * 160))
        wdecl("l2a_wdt", (128, D))
        wdecl("l2a_bdt", (128, 16))
        wdecl("l2a_alog", (DC, 128, 16))
        wdecl("l2a_dD", (128, 16))
        wdecl("l2a_wout", (DC, 128, D))
    if "l2f" in stages:
        wdecl("l2f_gw", (DC, 128, 8192))
        wdecl("l2f_vw", (DC, 128, 8192))
        wdecl("l2f_ow", (64, 128, D))
        wdecl("l2f_gb", (1, 8192))
        wdecl("l2f_vb", (1, 8192))
        wdecl("l2f_ob", (1, D))
    if "l1f" in stages:
        wdecl("l1f_gw", (44, 128, D))
        wdecl("l1f_uw", (44, 128, D))
        wdecl("l1f_dw", (44, 128, D))
        wdecl("l1f_gs", (128, 44))
        wdecl("l1f_us", (128, 44))
        wdecl("l1f_ds", (1, D))

    out = nc.dram_tensor("out", [T, D], F32, kind="ExternalOutput").ap()

    dram = {}
    for pre, kvc in (("l3a", 16), ("l0a", 4)):
        if pre not in stages:
            continue
        KVp = kvc * 128
        wdecl(pre + "_wq", (DC, 128, D))
        wdecl(pre + "_wk", (kvc, 128, D))
        wdecl(pre + "_wv", (DC, 128, KVp))
        wdecl(pre + "_woT", (DC, 128, D))
        wdecl(pre + "_bqf", (128, 16))
        wdecl(pre + "_bkf", (128, max(kvc, 1)))
        wdecl(pre + "_bv", (1, KVp))
        wdecl(pre + "_bo", (1, D))
        wdecl(pre + "_trisel", (128, 4096))
        wdecl(pre + "_kmaskq", (128, 32))
        if pre == "l0a":
            wdecl(pre + "_cos", (128, T))
            wdecl(pre + "_sinm", (128, T))
        dram[pre + "_qf"] = nc.dram_tensor(pre + "_qf", [D, T], F32)
        dram[pre + "_kin"] = nc.dram_tensor(pre + "_kin", [KVp, T], F32)
        dram[pre + "_kout"] = nc.dram_tensor(pre + "_kout", [2 * KVp, T], F32)
        dram[pre + "_vin"] = nc.dram_tensor(pre + "_vin", [T, KVp], F32)
        dram[pre + "_vout"] = nc.dram_tensor(pre + "_vout", [2 * T, KVp], F32)
        dram[pre + "_mkin"] = nc.dram_tensor(pre + "_mkin", [1, 2], F32)
        dram[pre + "_mkout"] = nc.dram_tensor(pre + "_mkout", [2, 2], F32)
    if "l2a" in stages:
        for nm in ("l2a_xc", "l2a_r", "l2a_xs", "l2a_u", "l2a_delta",
                   "l2a_y", "l2a_z"):
            dram[nm] = nc.dram_tensor(nm, [D, T], F32)
        dram["l2a_bc"] = nc.dram_tensor("l2a_bc", [32, T], F32)
        dram["l2a_hloin"] = nc.dram_tensor("l2a_hloin", [128, 48], F32)
        dram["l2a_hloout"] = nc.dram_tensor("l2a_hloout", [256, 48], F32)
        dram["l2a_hin"] = nc.dram_tensor("l2a_hin", [128, 256], F32)
        dram["l2a_hout"] = nc.dram_tensor("l2a_hout", [256, 256], F32)
    if "l2f" in stages:
        dram["l2f_h"] = nc.dram_tensor("l2f_h", [T, 8192], F32)
        if "l2a" in stages:
            dram["l2f_tg"] = nc.dram_tensor("l2f_tg", [DC, 128, 8192], BF16)
            dram["l2f_tv"] = nc.dram_tensor("l2f_tv", [DC, 128, 8192], BF16)
            dram["l2f_to"] = nc.dram_tensor("l2f_to", [64, 128, D], BF16)
        dram["l2f_kh"] = nc.dram_tensor("l2f_kh", [64, 128, T], BF16)
    if "l1a" in stages:
        dram["l1a_q"] = nc.dram_tensor("l1a_q", [D, T], F32)
        dram["l1a_kv"] = nc.dram_tensor("l1a_kv", [D, T], F32)
        dram["l1a_cin"] = nc.dram_tensor("l1a_cin", [128, 16], F32)
        dram["l1a_cout"] = nc.dram_tensor("l1a_cout", [256, 16], F32)

    xs_t = nc.alloc_sbuf_tensor("x_stream", [128, TT * D], F32)

    def xtile(tt):
        return xs_t.ap()[:, tt * D:(tt + 1) * D]

    sident = nc.alloc_sbuf_tensor("sident", [128, 128], F32)
    rotm_s = nc.alloc_sbuf_tensor("rotm_sb", [128, 128], F32)
    ones_r = nc.alloc_sbuf_tensor("ones_r", [1, 128], F32)
    ones_c = nc.alloc_sbuf_tensor("ones_c", [128, 1], F32)
    lnbuf = nc.alloc_sbuf_tensor("lnbuf", [128, 16], F32)
    gatesb = nc.alloc_sbuf_tensor("gatesb", [128, 8], F32)
    arena = Arena(nc, "arena", 36350)
    gpre = None
    ppbuf = None
    if "l2f" in stages and "l2a" in stages:
        gpre = arena.alloc(8)
        ppbuf = arena.alloc(1800)

    ps = {n: nc.alloc_psum_tensor(n, [128, 512], F32)
          for n in ("psA", "psB", "psC", "psD", "psE", "psF", "psG", "psH")}

    guard = {}

    def gget(key):
        return guard.get(key, None)

    def gset(key, ctr):
        guard[key] = (ctr, ctr.n)

    def gwait(key):
        g = guard.get(key)
        return [g] if g else []

    cL = p.ctr("c_load")
    cS = p.ctr("c_store")
    cT = p.ctr("c_tensor")
    cV = p.ctr("c_vector")
    cSc = p.ctr("c_scalar")
    cG = p.ctr("c_gpsimd")
    p.selfc = {"vector": cV, "scalar": cSc, "gpsimd": cG}

    stream_ready = []
    for tt in range(TT):
        stream_ready.append(p.dma("sync", lambda e, tt=tt: e.dma_start(
            out=xtile(tt), in_=x0[tt * 128:(tt + 1) * 128, :])))
    stream_ready.append(p.dma("sync", lambda e: e.dma_start(
        out=sident.ap()[:, :], in_=ident_in)))
    stream_ready.append(p.dma("sync", lambda e: e.dma_start(
        out=gatesb.ap()[:, :], in_=gates_in)))
    if "l0a" in stages:
        rot_ready = p.dma("sync", lambda e: e.dma_start(
            out=rotm_s.ap()[:, :], in_=rotm_in))
    else:
        rot_ready = stream_ready[0]
    p.op("vector", lambda e: e.memset(ones_r.ap()[:, :], 1.0), inc=cV)
    p.op("vector", lambda e: e.memset(ones_c.ap()[:, :], 1.0), inc=cV)
    p.op("vector", lambda e: e.memset(lnbuf.ap()[:, 8:9], float(EPS)), inc=cV)
    ones_ready = cV.n


    TPB = ("psA", "psB", "psC", "psD")

    def transpose_tiles(pairs, wait=()):
        first = list(wait)
        for j, (src, dst) in enumerate(pairs):
            bank = TPB[j % 4]
            w = gwait(bank) + (first if j < 4 else [])
            p.op("tensor", lambda e, src=src, bank=bank: e.transpose(
                ps[bank].ap()[:, 0:128], src,
                sident.ap()[:, :]), wait=w, inc=cT)
            td = cT.n
            p.op("vector", lambda e, dst=dst, bank=bank: e.tensor_copy(
                dst, ps[bank].ap()[:, 0:128]),
                wait=[(cT, td)], inc=cV)
            gset(bank, cV)
        return cV.n

    def ln_tile_inplace(tile_ap, sq_ap, wait=()):
        b = lnbuf.ap()
        s_sum, s_sq, s_mean, s_var = b[:, 0:1], b[:, 1:2], b[:, 2:3], b[:, 3:4]
        s_std, s_rstd, s_bias, s_msq = b[:, 4:5], b[:, 5:6], b[:, 6:7], b[:, 7:8]
        p.op("vector", lambda e: e.reduce_sum(s_sum, tile_ap, AX.X),
             wait=list(wait) + gwait("lnbuf"), inc=cV)
        vsum = cV.n
        p.op("scalar", lambda e: e.activation(sq_ap, tile_ap, AF.Square,
                                              accum_out=s_sq),
             wait=list(wait) + gwait("lnbuf"), inc=cSc)
        ssq = cSc.n
        p.op("vector", lambda e: e.tensor_scalar(s_mean, s_sum, 1.0 / D, None,
                                                 ALU.mult),
             wait=[(cV, vsum)], inc=cV)
        p.op("vector", lambda e: e.tensor_mul(s_msq, s_mean, s_mean), inc=cV)
        p.op("vector", lambda e: e.tensor_scalar(s_var, s_sq, 1.0 / D, None,
                                                 ALU.mult),
             wait=[(cSc, ssq)], inc=cV)
        p.op("vector", lambda e: e.tensor_sub(s_var, s_var, s_msq), inc=cV)
        vvar = cV.n
        p.op("scalar", lambda e: e.activation(s_std, s_var, AF.Sqrt,
                                              bias=lnbuf.ap()[:, 8:9]),
             wait=[(cV, vvar), (cV, ones_ready)], inc=cSc)
        sstd = cSc.n
        p.op("vector", lambda e: e.reciprocal(s_rstd, s_std),
             wait=[(cSc, sstd)], inc=cV)
        p.op("vector", lambda e: e.tensor_mul(s_bias, s_mean, s_rstd), inc=cV)
        p.op("vector", lambda e: e.tensor_scalar(s_bias, s_bias, -1.0, None,
                                                 ALU.mult), inc=cV)
        vaff = cV.n
        p.op("scalar", lambda e: e.activation(tile_ap, tile_ap, AF.Identity,
                                              bias=s_bias, scale=s_rstd),
             wait=[(cV, vaff)], inc=cSc)
        gset("lnbuf", cSc)
        return cSc.n

    def bcast_row(row_ap, dst_ap, ncols, wait=()):
        done = 0
        for j in range(0, ncols, 512):
            w = min(512, ncols - j)
            p.op("tensor", lambda e, j=j, w=w: e.matmul(
                ps["psF"].ap()[:, 0:w], ones_r.ap()[:, :], row_ap[:, j:j + w],
                start=True, stop=True),
                wait=list(wait) + [(cV, ones_ready)] + gwait("psF"), inc=cT)
            td = cT.n
            p.op("vector", lambda e, j=j, w=w: e.tensor_copy(
                dst_ap[:, j:j + w], ps["psF"].ap()[:, 0:w]),
                wait=[(cT, td)], inc=cV)
            gset("psF", cV)
            done = cV.n
        return done

    def stage_l1f(ready):
        IC = 44
        arena.mark()
        xT_h = arena.alloc(DC * 512)
        hbuf = arena.alloc(9 * 512)
        wbuf = arena.alloc(2 * 2 * D)
        dwb = arena.alloc(2 * 512)
        dsb = arena.alloc(D)
        dsrow_a = arena.alloc(D)
        acc = arena.alloc(4 * D)
        scl = arena.alloc(96)
        sq = xT_h[:, 0:D]

        gs_s, us_s = scl[:, 0:44], scl[:, 44:88]
        t1 = p.dma("sync", lambda e: e.dma_start(out=gs_s, in_=W["l1f_gs"]))
        t2 = p.dma("sync", lambda e: e.dma_start(out=us_s, in_=W["l1f_us"]))
        scl_ready = [t1, t2]
        dsrow = dsrow_a[0:1, :]
        t3 = p.dma("sync", lambda e: e.dma_start(out=dsrow, in_=W["l1f_ds"]))
        dsb_done = bcast_row(dsrow, dsb, D, wait=[t3])

        ln_waits = []
        for th in range(2):
            tok0 = th * 4
            pairs = []
            for tt4 in range(4):
                for dc in range(DC):
                    pairs.append((xtile(tok0 + tt4)[:, dc * 128:(dc + 1) * 128],
                                  xT_h[:, dc * 512 + tt4 * 128:dc * 512 + tt4 * 128 + 128]))
            xt_done = transpose_tiles(pairs, wait=list(ready))

            def xT_col(dc):
                return xT_h[:, dc * 512:(dc + 1) * 512]

            if L1F_CUT < 2:
                continue

            GRPS = (9, 9, 9, 9, 8)
            for grp in range(5):
                gbase = sum(GRPS[:grp])
                gsz = GRPS[grp]
                for icg in range(gsz):
                    ic = gbase + icg
                    slot = ic % 2
                    gw_s = wbuf[:, slot * 2 * D:slot * 2 * D + D]
                    uw_s = wbuf[:, slot * 2 * D + D:slot * 2 * D + 2 * D]
                    tg = p.dma("sync", lambda e, ic=ic, gw_s=gw_s: e.dma_start(
                        out=gw_s, in_=W["l1f_gw"][ic]),
                        wait=gwait(("l1f_w", slot)))
                    tu = p.dma("sync", lambda e, ic=ic, uw_s=uw_s: e.dma_start(
                        out=uw_s, in_=W["l1f_uw"][ic]))
                    for dc in range(DC):
                        p.op("tensor", lambda e, dc=dc, gw_s=gw_s: e.matmul(
                            ps["psA"].ap()[:, :], gw_s[:, dc * 128:(dc + 1) * 128],
                            xT_col(dc), start=(dc == 0), stop=(dc == DC - 1)),
                            wait=([tg, tu, (cV, xt_done)] + gwait("psA"))
                            if dc == 0 else (), inc=None)
                    for dc in range(DC):
                        p.op("tensor", lambda e, dc=dc, uw_s=uw_s: e.matmul(
                            ps["psB"].ap()[:, :], uw_s[:, dc * 128:(dc + 1) * 128],
                            xT_col(dc), start=(dc == 0), stop=(dc == DC - 1)),
                            wait=gwait("psB") if dc == 0 else (),
                            inc=(cT if dc == DC - 1 else None))
                    gset(("l1f_w", slot), cT)
                    mm = cT.n
                    hsl = hbuf[:, icg * 512:(icg + 1) * 512]
                    p.op("scalar", lambda e, ic=ic, hsl=hsl: e.activation(
                        hsl, ps["psA"].ap()[:, :], AF.Silu,
                        scale=gs_s[:, ic:ic + 1]),
                        wait=[(cT, mm)] + scl_ready + gwait("l1f_hfree"),
                        inc=cSc)
                    sg = cSc.n
                    gset("psA", cSc)
                    p.op("vector", lambda e, ic=ic, hsl=hsl: e.scalar_tensor_tensor(
                        hsl, ps["psB"].ap()[:, :], us_s[:, ic:ic + 1], hsl,
                        ALU.mult, ALU.mult),
                        wait=[(cSc, sg)], inc=cV)
                    gset("psB", cV)
                    guard[("hready", icg)] = (cV, cV.n)

                if L1F_CUT < 3:
                    continue
                pnames = ("psC", "psD", "psE", "psF")
                for og in range(4):
                    for icg in range(gsz):
                        ic = gbase + icg
                        dslot = icg % 2
                        dw_s = dwb[:, dslot * 512:(dslot + 1) * 512]
                        tdw = p.dma("sync", lambda e, ic=ic, og=og, dw_s=dw_s: e.dma_start(
                            out=dw_s, in_=W["l1f_dw"][ic, :, og * 512:(og + 1) * 512]),
                            wait=gwait(("l1f_dw", dslot)))
                        for t4 in range(4):
                            p.op("tensor", lambda e, icg=icg, t4=t4, dw_s=dw_s,
                                 gsz=gsz: e.matmul(
                                ps[pnames[t4]].ap()[:, :],
                                hbuf[:, icg * 512 + t4 * 128:icg * 512 + (t4 + 1) * 128],
                                dw_s, start=(icg == 0), stop=(icg == gsz - 1)),
                                wait=[tdw, guard[("hready", icg)]]
                                + (gwait(pnames[t4]) if icg == 0 else []),
                                inc=(cT if (t4 == 3) else None))
                        gset(("l1f_dw", dslot), cT)
                    mmd = cT.n
                    for t4 in range(4):
                        dst = acc[:, t4 * D + og * 512:t4 * D + (og + 1) * 512]
                        if grp == 0:
                            p.op("vector", lambda e, t4=t4, dst=dst: e.tensor_copy(
                                dst, ps[pnames[t4]].ap()[:, :]),
                                wait=[(cT, mmd)], inc=cV)
                        else:
                            p.op("vector", lambda e, t4=t4, dst=dst: e.tensor_add(
                                dst, dst, ps[pnames[t4]].ap()[:, :]),
                                wait=[(cT, mmd)], inc=cV)
                        gset(pnames[t4], cV)
                gset("l1f_hfree", cT)
                emit_prepass_chunk(24)
            if L1F_CUT < 4:
                continue
            for tt4 in range(4):
                tt = tok0 + tt4
                for og in range(4):
                    a_sl = acc[:, tt4 * D + og * 512:tt4 * D + (og + 1) * 512]
                    x_sl = xtile(tt)[:, og * 512:(og + 1) * 512]
                    p.op("vector", lambda e, a_sl=a_sl, og=og: e.tensor_mul(
                        a_sl, a_sl, dsb[:, og * 512:(og + 1) * 512]),
                        wait=[(cV, dsb_done)], inc=cV)
                    vm = cV.n
                    p.op("vector", lambda e, x_sl=x_sl, a_sl=a_sl: e.tensor_add(
                        x_sl, x_sl, a_sl), wait=[(cV, vm)], inc=cV)
                vres = cV.n
                ln_waits.append(ln_tile_inplace(xtile(tt), sq, wait=[(cV, vres)]))
        arena.release()
        if L1F_CUT < 4:
            return [(cT, cT.n), (cV, cV.n), (cSc, cSc.n)]
        return [(cSc, max(ln_waits))]

    def stage_l0f(ready):
        arena.mark()
        xT = arena.alloc(DC * T)
        acat = arena.alloc(DC * 128)
        bcat = arena.alloc(128 * 16)
        gwt = arena.alloc(DC * 8)
        wrow_a = arena.alloc(D)
        gbrow_a = arena.alloc(8)
        wtile = arena.alloc(2 * 512)
        lgb = arena.alloc(TT * 8)
        wgb = arena.alloc(TT * 8)
        sgb = arena.alloc(40)
        xaw_t = arena.alloc(128)
        xaw_f = arena.alloc(TT * 128)
        sq = acat[:, 0:D]

        tw = [p.dma("sync", lambda e: e.dma_start(out=acat, in_=W["l0f_acat"])),
              p.dma("sync", lambda e: e.dma_start(out=bcat, in_=W["l0f_bcat"])),
              p.dma("sync", lambda e: e.dma_start(out=gwt, in_=W["l0f_gwt"])),
              p.dma("sync", lambda e: e.dma_start(out=wrow_a[0:1, :], in_=W["l0f_brow"])),
              p.dma("sync", lambda e: e.dma_start(out=gbrow_a[0:1, :], in_=W["l0f_gbrow"]))]

        pairs = []
        for tt in range(TT):
            for dc in range(DC):
                pairs.append((xtile(tt)[:, dc * 128:(dc + 1) * 128],
                              xT[:, dc * T + tt * 128:dc * T + (tt + 1) * 128]))
        xt_done = transpose_tiles(pairs, wait=list(ready))

        def xTt(dc, tt):
            return xT[:, dc * T + tt * 128:dc * T + (tt + 1) * 128]

        for tt in range(TT):
            for dc in range(DC):
                p.op("tensor", lambda e, dc=dc, tt=tt: e.matmul(
                    ps["psE"].ap()[:, 0:8], xTt(dc, tt), gwt[:, dc * 8:(dc + 1) * 8],
                    start=(dc == 0), stop=False),
                    wait=([(cV, xt_done)] + tw + gwait("psE")) if dc == 0 else (),
                    inc=None)
            p.op("tensor", lambda e: e.matmul(
                ps["psE"].ap()[:, 0:8], ones_r.ap()[:, :], gbrow_a[0:1, 0:8],
                start=False, stop=True), wait=[(cV, ones_ready)], inc=cT)
            mm = cT.n
            lg = lgb[:, tt * 8:(tt + 1) * 8]
            wg = wgb[:, tt * 8:(tt + 1) * 8]
            m1, m2 = sgb[:, 32:33], sgb[:, 33:34]
            den, rcp = sgb[:, 34:35], sgb[:, 35:36]
            tb, msk, et = sgb[:, 0:8], sgb[:, 8:16], sgb[:, 16:24]
            p.op("vector", lambda e, lg=lg: e.tensor_copy(lg, ps["psE"].ap()[:, 0:8]),
                 wait=[(cT, mm)], inc=cV)
            gset("psE", cV)
            p.op("vector", lambda e, lg=lg: e.reduce_max(m1, lg, AX.X))
            p.op("vector", lambda e, lg=lg: e.tensor_scalar(tb, lg, m1, None, ALU.subtract))
            p.op("vector", lambda e: e.tensor_scalar(msk, tb, 0.0, None, ALU.is_ge))
            p.op("vector", lambda e: e.scalar_tensor_tensor(msk, msk, -1e9, tb,
                                                            ALU.mult, ALU.add))
            p.op("vector", lambda e: e.reduce_max(m2, msk, AX.X))
            p.op("vector", lambda e: e.tensor_scalar(msk, tb, m2, None, ALU.is_ge),
                 inc=cV)
            vsel = cV.n
            p.op("scalar", lambda e: e.activation(et, tb, AF.Exp),
                 wait=[(cV, vsel)], inc=cSc)
            se = cSc.n
            p.op("vector", lambda e: e.tensor_mul(et, et, msk), wait=[(cSc, se)])
            p.op("vector", lambda e: e.reduce_sum(den, et, AX.X))
            p.op("vector", lambda e: e.reciprocal(rcp, den))
            p.op("vector", lambda e, wg=wg: e.tensor_scalar(wg, et, rcp, None,
                                                            ALU.mult), inc=cV)
            guard[("l0f_wg", tt)] = (cV, cV.n)
            emit_prepass_chunk(24)

        for tt in range(TT):
            for dc in range(DC):
                p.op("tensor", lambda e, dc=dc, tt=tt: e.matmul(
                    ps["psF"].ap()[:, 0:128], xTt(dc, tt), acat[:, dc * 128:(dc + 1) * 128],
                    start=(dc == 0), stop=(dc == DC - 1)),
                    wait=(gwait("psF")) if dc == 0 else (),
                    inc=(cT if dc == DC - 1 else None))
            mm = cT.n
            wg = wgb[:, tt * 8:(tt + 1) * 8]
            p.op("vector", lambda e, tt=tt, wg=wg: e.tensor_tensor(
                xaw_t.rearrange("p (e r) -> p e r", r=16),
                ps["psF"].ap()[:, 0:128].rearrange("p (e r) -> p e r", r=16),
                wg.broadcast_to([128, 8, 16]), ALU.mult),
                wait=[(cT, mm), guard[("l0f_wg", tt)]], inc=cV)
            gset("psF", cV)
            vx = cV.n
            transpose_tiles([(xaw_t, xaw_f[:, tt * 128:(tt + 1) * 128])],
                            wait=[(cV, vx)])
        xawf_done = cV.n

        pnames8 = ("psA", "psB", "psC", "psD", "psE", "psF", "psG", "psH")
        res_done = {}
        for oc in range(4):
            for dc in range(DC):
                slot = dc % 2
                wt = wtile[:, slot * 512:(slot + 1) * 512]
                twt = p.dma("sync", lambda e, dc=dc, oc=oc, wt=wt: e.dma_start(
                    out=wt, in_=W["l0f_wT"][dc, :, oc * 512:(oc + 1) * 512]),
                    wait=gwait(("l0f_wt", slot)))
                for tt in range(TT):
                    p.op("tensor", lambda e, dc=dc, tt=tt, wt=wt: e.matmul(
                        ps[pnames8[tt]].ap()[:, :], xTt(dc, tt), wt,
                        start=(dc == 0), stop=False),
                        wait=[twt] + (gwait(pnames8[tt]) + [(cV, xawf_done)]
                                      if dc == 0 else []),
                        inc=(cT if tt == TT - 1 else None))
                gset(("l0f_wt", slot), cT)
            for tt in range(TT):
                p.op("tensor", lambda e, tt=tt, oc=oc: e.matmul(
                    ps[pnames8[tt]].ap()[:, :], xaw_f[:, tt * 128:(tt + 1) * 128],
                    bcat[:, oc * 512:(oc + 1) * 512], start=False, stop=False),
                    inc=None)
                p.op("tensor", lambda e, tt=tt, oc=oc: e.matmul(
                    ps[pnames8[tt]].ap()[:, :], ones_r.ap()[:, :],
                    wrow_a[0:1, oc * 512:(oc + 1) * 512], start=False, stop=True),
                    inc=cT)
                mm = cT.n
                p.op("vector", lambda e, tt=tt, oc=oc: e.tensor_add(
                    xtile(tt)[:, oc * 512:(oc + 1) * 512],
                    xtile(tt)[:, oc * 512:(oc + 1) * 512],
                    ps[pnames8[tt]].ap()[:, :]), wait=[(cT, mm)], inc=cV)
                gset(pnames8[tt], cV)
                res_done[(tt, oc)] = (cV, cV.n)
        ln_waits = []
        for tt in range(TT):
            ln_waits.append(ln_tile_inplace(
                xtile(tt), sq, wait=[res_done[(tt, oc)] for oc in range(4)]))
        arena.release()
        return [(cSc, max(ln_waits))]

    def stage_l3f(ready):
        arena.mark()
        xT = arena.alloc(DC * T)
        vxb = arena.alloc(TT * 512)
        vxsf = arena.alloc(TT * 512)
        w1b = arena.alloc(2 * D)
        vct = arena.alloc(2 * 512)
        uct = arena.alloc(2 * 512)
        hec = arena.alloc(T)
        w2s = arena.alloc(DC * 8)
        alps = arena.alloc(TT * 8)
        alph = arena.alloc(TT * 8)
        b1f = arena.alloc(16)
        onesrow = arena.alloc(512)
        b2r = arena.alloc(8)
        alphaT = arena.alloc(T)
        sgb3 = arena.alloc(16)
        sq = xT[:, 0:D]

        tw = [p.dma("sync", lambda e: e.dma_start(out=w2s, in_=W["l3f_w2"])),
              p.dma("sync", lambda e: e.dma_start(out=b1f, in_=W["l3f_b1f"])),
              p.dma("sync", lambda e: e.dma_start(out=b2r[0:1, :], in_=W["l3f_b2row"]))]
        p.op("vector", lambda e: e.memset(onesrow[0:1, :], 1.0), inc=cV)
        ones512 = cV.n

        pairs = []
        for tt in range(TT):
            for dc in range(DC):
                pairs.append((xtile(tt)[:, dc * 128:(dc + 1) * 128],
                              xT[:, dc * T + tt * 128:dc * T + (tt + 1) * 128]))
        xt_done = transpose_tiles(pairs, wait=list(ready))

        def xTt(dc, tt):
            return xT[:, dc * T + tt * 128:dc * T + (tt + 1) * 128]

        pnames8 = ("psA", "psB", "psC", "psD", "psE", "psF", "psG", "psH")

        for dc in range(DC):
            slot = dc % 2
            vt = vct[:, slot * 512:(slot + 1) * 512]
            tv = p.dma("sync", lambda e, dc=dc, vt=vt: e.dma_start(
                out=vt, in_=W["l3f_vcat"][dc]), wait=gwait(("l3f_vct", slot)))
            for tt in range(TT):
                p.op("tensor", lambda e, dc=dc, tt=tt, vt=vt: e.matmul(
                    ps[pnames8[tt]].ap()[:, :], xTt(dc, tt), vt,
                    start=(dc == 0), stop=(dc == DC - 1)),
                    wait=[tv] + ([(cV, xt_done)] + gwait(pnames8[tt])
                                 if dc == 0 else []),
                    inc=(cT if tt == TT - 1 else None))
            gset(("l3f_vct", slot), cT)
        mmv = cT.n
        for tt in range(TT):
            p.op("vector", lambda e, tt=tt: e.tensor_copy(
                vxb[:, tt * 512:(tt + 1) * 512], ps[pnames8[tt]].ap()[:, :]),
                wait=[(cT, mmv)], inc=cV)
            gset(pnames8[tt], cV)

        for ec in range(DC):
            slot = ec % 2
            w1s = w1b[:, slot * D:(slot + 1) * D]
            tw1 = p.dma("sync", lambda e, ec=ec, w1s=w1s: e.dma_start(
                out=w1s, in_=W["l3f_w1"][ec]), wait=gwait(("l3f_w1", slot)))
            for tc in range(2):
                bank = "psA" if tc == 0 else "psB"
                for dc in range(DC):
                    p.op("tensor", lambda e, dc=dc, tc=tc, w1s=w1s, bank=bank: e.matmul(
                        ps[bank].ap()[:, :], w1s[:, dc * 128:(dc + 1) * 128],
                        xT[:, dc * T + tc * 512:dc * T + (tc + 1) * 512],
                        start=(dc == 0), stop=(dc == DC - 1)),
                        wait=[tw1] + (gwait(bank) if dc == 0 else []),
                        inc=(cT if dc == DC - 1 else None))
                mmh = cT.n
                p.op("scalar", lambda e, ec=ec, tc=tc, bank=bank: e.activation(
                    hec[:, tc * 512:(tc + 1) * 512], ps[bank].ap()[:, :], AF.Relu,
                    bias=b1f[:, ec:ec + 1]),
                    wait=[(cT, mmh)] + tw + gwait(("l3f_hec", tc)), inc=cSc)
                gset(bank, cSc)
                sh_done = cSc.n
                abank = "psC" if tc == 0 else "psD"
                p.op("tensor", lambda e, ec=ec, tc=tc, abank=abank: e.matmul(
                    ps[abank].ap()[0:8, :], w2s[:, ec * 8:(ec + 1) * 8],
                    hec[:, tc * 512:(tc + 1) * 512],
                    start=(ec == 0), stop=False),
                    wait=[(cSc, sh_done)] + (gwait(abank) if ec == 0 else []),
                    inc=cT)
                guard[("l3f_hec", tc)] = (cT, cT.n)
            gset(("l3f_w1", slot), cT)
        for tc in range(2):
            abank = "psC" if tc == 0 else "psD"
            p.op("tensor", lambda e, abank=abank: e.matmul(
                ps[abank].ap()[0:8, :], b2r[0:1, 0:8], onesrow[0:1, :],
                start=False, stop=True), wait=[(cV, ones512)], inc=cT)
            mmb = cT.n
            p.op("vector", lambda e, tc=tc, abank=abank: e.tensor_copy(
                alphaT[0:8, tc * 512:(tc + 1) * 512], ps[abank].ap()[0:8, :]),
                wait=[(cT, mmb)], inc=cV)
            gset(abank, cV)
        at_done = cV.n

        for tt in range(TT):
            bank = TPB[tt % 4]
            p.op("tensor", lambda e, tt=tt, bank=bank: e.transpose(
                ps[bank].ap()[:, 0:8], alphaT[0:8, tt * 128:(tt + 1) * 128],
                sident.ap()[0:8, 0:8]),
                wait=[(cV, at_done)] + gwait(bank), inc=cT)
            td = cT.n
            p.op("vector", lambda e, tt=tt, bank=bank: e.tensor_copy(
                alps[:, tt * 8:(tt + 1) * 8], ps[bank].ap()[:, 0:8]),
                wait=[(cT, td)], inc=cV)
            gset(bank, cV)
            ap_ = alps[:, tt * 8:(tt + 1) * 8]
            aw = alph[:, tt * 8:(tt + 1) * 8]
            m1, den, rcp = sgb3[:, 0:1], sgb3[:, 1:2], sgb3[:, 2:3]
            tb = sgb3[:, 8:16]
            p.op("vector", lambda e, ap_=ap_: e.reduce_max(m1, ap_, AX.X))
            p.op("vector", lambda e, ap_=ap_: e.tensor_scalar(tb, ap_, m1, None,
                                                              ALU.subtract), inc=cV)
            vt_ = cV.n
            p.op("scalar", lambda e: e.activation(tb, tb, AF.Exp),
                 wait=[(cV, vt_)], inc=cSc)
            se = cSc.n
            p.op("vector", lambda e: e.reduce_sum(den, tb, AX.X), wait=[(cSc, se)])
            p.op("vector", lambda e: e.reciprocal(rcp, den))
            p.op("vector", lambda e, aw=aw: e.tensor_scalar(aw, tb, rcp, None,
                                                            ALU.mult), inc=cV)
            guard[("l3f_aw", tt)] = (cV, cV.n)

        for tt in range(TT):
            p.op("vector", lambda e, tt=tt: e.tensor_tensor(
                hec[:, 0:512].rearrange("p (k r) -> p k r", r=64),
                vxb[:, tt * 512:(tt + 1) * 512].rearrange("p (k r) -> p k r", r=64),
                alph[:, tt * 8:(tt + 1) * 8].broadcast_to([128, 8, 64]), ALU.mult),
                wait=[guard[("l3f_aw", tt)]] + gwait(("l3f_vxs",))
                + gwait(("l3f_hec", 0)) + gwait(("l3f_hec", 1)),
                inc=cV)
            vsc = cV.n
            prs = [(hec[:, j * 128:(j + 1) * 128],
                    vxsf[:, tt * 512 + j * 128:tt * 512 + (j + 1) * 128])
                   for j in range(4)]
            transpose_tiles(prs, wait=[(cV, vsc)])
            gset(("l3f_vxs",), cV)
        vxsf_done = cV.n

        res_done = {}
        for oc in range(4):
            for krc in range(4):
                slot = krc % 2
                ut = uct[:, slot * 512:(slot + 1) * 512]
                tu = p.dma("sync", lambda e, krc=krc, oc=oc, ut=ut: e.dma_start(
                    out=ut, in_=W["l3f_ucat"][krc, :, oc * 512:(oc + 1) * 512]),
                    wait=gwait(("l3f_uct", slot)))
                for tt in range(TT):
                    p.op("tensor", lambda e, krc=krc, tt=tt, ut=ut: e.matmul(
                        ps[pnames8[tt]].ap()[:, :],
                        vxsf[:, tt * 512 + krc * 128:tt * 512 + (krc + 1) * 128], ut,
                        start=(krc == 0), stop=(krc == 3)),
                        wait=[tu, (cV, vxsf_done)] + (gwait(pnames8[tt])
                                                      if krc == 0 else []),
                        inc=(cT if tt == TT - 1 else None))
                gset(("l3f_uct", slot), cT)
            mmo = cT.n
            for tt in range(TT):
                p.op("vector", lambda e, tt=tt, oc=oc: e.tensor_add(
                    xtile(tt)[:, oc * 512:(oc + 1) * 512],
                    xtile(tt)[:, oc * 512:(oc + 1) * 512],
                    ps[pnames8[tt]].ap()[:, :]), wait=[(cT, mmo)], inc=cV)
                gset(pnames8[tt], cV)
                res_done[(tt, oc)] = (cV, cV.n)
        ln_waits = []
        for tt in range(TT):
            ln_waits.append(ln_tile_inplace(
                xtile(tt), sq, wait=[res_done[(tt, oc)] for oc in range(4)]))
        arena.release()
        return [(cSc, max(ln_waits))]

    def stage_l1a(ready):
        SCALE = float(128.0 ** -0.5)
        arena.mark()
        xT = arena.alloc(DC * T)
        wsl = arena.alloc(2 * D)
        kbuf = arena.alloc(T)
        ubuf = arena.alloc(T)
        phib = arena.alloc(T)
        bqf = arena.alloc(16)
        bkf = arena.alloc(16)
        bvf = arena.alloc(16)
        borow = arena.alloc(D)
        bob = arena.alloc(D)
        carryb = arena.alloc(16)
        c0b = arena.alloc(16)
        c1b = arena.alloc(16)
        ceff = arena.alloc(16)
        wot = arena.alloc(2 * 512)
        sq = arena.alloc(D)
        ctx = xT

        tw = [p.dma("sync", lambda e: e.dma_start(out=bqf, in_=W["l1a_bq"])),
              p.dma("sync", lambda e: e.dma_start(out=bkf, in_=W["l1a_bk"])),
              p.dma("sync", lambda e: e.dma_start(out=bvf, in_=W["l1a_bv"])),
              p.dma("sync", lambda e: e.dma_start(out=borow[0:1, :], in_=W["l1a_bo"]))]
        bob_done = bcast_row(borow[0:1, :], bob, D, wait=[tw[3]])

        pairs = []
        for tt in range(TT):
            for dc in range(DC):
                pairs.append((xtile(tt)[:, dc * 128:(dc + 1) * 128],
                              xT[:, dc * T + tt * 128:dc * T + (tt + 1) * 128]))
        xt_done = transpose_tiles(pairs, wait=list(ready))

        def proj_ec(wname, ec, slot, bias_col, extra_wait=()):
            wsl_s = wsl[:, slot * D:(slot + 1) * D]
            twl = p.dma("sync", lambda e, ec=ec, wsl_s=wsl_s, wname=wname: e.dma_start(
                out=wsl_s, in_=W[wname][ec]), wait=gwait(("l1a_w", slot)))
            mms = []
            for tc in range(2):
                bank = "psE" if tc == 0 else "psF"
                for dc in range(DC):
                    p.op("tensor", lambda e, dc=dc, tc=tc, wsl_s=wsl_s, bank=bank: e.matmul(
                        ps[bank].ap()[:, :], wsl_s[:, dc * 128:(dc + 1) * 128],
                        xT[:, dc * T + tc * 512:dc * T + (tc + 1) * 512],
                        start=(dc == 0), stop=(dc == DC - 1)),
                        wait=[twl, (cV, xt_done)] + (gwait(bank) + list(extra_wait)
                                                     if dc == 0 else []),
                        inc=(cT if dc == DC - 1 else None))
                mms.append(cT.n)
            gset(("l1a_w", slot), cT)
            return mms

        q_tok, kv_tok = {}, {}
        for ec in range(DC):
            mms = proj_ec("l1a_wq", ec, 0, None)
            for tc in range(2):
                bank = "psE" if tc == 0 else "psF"
                p.op("scalar", lambda e, tc=tc, ec=ec, bank=bank: e.activation(
                    kbuf[:, tc * 512:(tc + 1) * 512], ps[bank].ap()[:, :],
                    AF.Identity, bias=bqf[:, ec:ec + 1]),
                    wait=[(cT, mms[tc])] + tw + gwait("l1a_kbuf"), inc=cSc)
                gset(bank, cSc)
            qd = cSc.n
            tq = p.dma("gpsimd", lambda e, ec=ec: e.dma_start(
                out=dram["l1a_q"].ap()[ec * 128:(ec + 1) * 128, :], in_=kbuf),
                wait=[(cSc, qd)])
            q_tok[ec] = tq
            guard["l1a_kbuf"] = tq
            mms = proj_ec("l1a_wk", ec, 1, None)
            for tc in range(2):
                bank = "psE" if tc == 0 else "psF"
                p.op("scalar", lambda e, tc=tc, ec=ec, bank=bank: e.activation(
                    kbuf[:, tc * 512:(tc + 1) * 512], ps[bank].ap()[:, :],
                    AF.Identity, bias=bkf[:, ec:ec + 1]),
                    wait=[(cT, mms[tc]), tq], inc=cSc)
                gset(bank, cSc)
            kd = cSc.n
            p.op("vector", lambda e: e.tensor_scalar_min(phib, kbuf, 0.0),
                 wait=[(cSc, kd)], inc=cV)
            vmin = cV.n
            p.op("scalar", lambda e: e.activation(phib, phib, AF.Exp),
                 wait=[(cV, vmin)], inc=cSc)
            se = cSc.n
            p.op("vector", lambda e: e.scalar_tensor_tensor(
                phib, kbuf, 0.0, phib, ALU.max, ALU.add),
                wait=[(cSc, se)], inc=cV)
            phid = cV.n
            mms = proj_ec("l1a_wv", ec, 0, ())
            for tc in range(2):
                bank = "psE" if tc == 0 else "psF"
                p.op("scalar", lambda e, tc=tc, ec=ec, bank=bank: e.activation(
                    ubuf[:, tc * 512:(tc + 1) * 512], ps[bank].ap()[:, :],
                    AF.Identity, bias=bvf[:, ec:ec + 1]),
                    wait=[(cT, mms[tc])] + gwait("l1a_ubuf"), inc=cSc)
                gset(bank, cSc)
            vd = cSc.n
            p.op("vector", lambda e: e.tensor_mul(ubuf, ubuf, phib),
                 wait=[(cSc, vd), (cV, phid)], inc=cV)
            p.op("vector", lambda e: e.tensor_tensor_scan(
                kbuf, ones_c.ap()[:, :].broadcast_to([128, T]), ubuf, 0.0,
                ALU.mult, ALU.add), inc=cV)
            p.op("vector", lambda e, ec=ec: e.tensor_copy(
                carryb[:, ec:ec + 1], kbuf[:, T - 1:T]), inc=cV)
            kvd = cV.n
            tkv = p.dma("gpsimd", lambda e, ec=ec: e.dma_start(
                out=dram["l1a_kv"].ap()[ec * 128:(ec + 1) * 128, :], in_=kbuf),
                wait=[(cV, kvd)])
            kv_tok[ec] = tkv
            guard["l1a_kbuf"] = tkv
            guard["l1a_ubuf"] = (cV, kvd)
            guard["l1a_ubuf2"] = (cV, kvd)
            guard["l1a_kbuf2"] = tkv
            emit_prepass_chunk(22)

        tci = p.dma("gpsimd", lambda e: e.dma_start(
            out=dram["l1a_cin"].ap(), in_=carryb), wait=[(cV, cV.n)])
        p.op("gpsimd", lambda e: e.collective_compute(
            "AllGather", ALU.bypass,
            replica_groups=[[0, 1], [2, 3], [4, 5], [6, 7]],
            ins=[dram["l1a_cin"].ap()], outs=[dram["l1a_cout"].ap()]),
            wait=[tci], inc=cG)
        ccd = cG.n
        t0 = p.dma("gpsimd", lambda e: e.dma_start(
            out=c0b, in_=dram["l1a_cout"].ap()[0:128, :]), wait=[(cG, ccd)])
        t1 = p.dma("gpsimd", lambda e: e.dma_start(
            out=c1b, in_=dram["l1a_cout"].ap()[128:256, :]), wait=[(cG, ccd)])
        p.op("vector", lambda e: e.tensor_scalar(
            ceff, c0b, gatesb.ap()[:, 0:1], None, ALU.mult), wait=[t0, t1], inc=cV)
        p.op("vector", lambda e: e.scalar_tensor_tensor(
            ceff, c1b, gatesb.ap()[:, 1:2], ceff, ALU.mult, ALU.add), inc=cV)
        ceff_done = cV.n

        for ec in range(DC):
            tq = p.dma("sync", lambda e, ec=ec: e.dma_start(
                out=ubuf, in_=dram["l1a_q"].ap()[ec * 128:(ec + 1) * 128, :]),
                wait=gwait("l1a_ubuf2") + [q_tok[ec]])
            tk = p.dma("sync", lambda e, ec=ec: e.dma_start(
                out=kbuf, in_=dram["l1a_kv"].ap()[ec * 128:(ec + 1) * 128, :]),
                wait=gwait("l1a_kbuf2") + [kv_tok[ec]])
            p.op("vector", lambda e: e.tensor_scalar_min(phib, ubuf, 0.0),
                 wait=[tq], inc=cV)
            vmin = cV.n
            p.op("scalar", lambda e: e.activation(phib, phib, AF.Exp),
                 wait=[(cV, vmin)], inc=cSc)
            se = cSc.n
            p.op("vector", lambda e: e.scalar_tensor_tensor(
                phib, ubuf, 0.0, phib, ALU.max, ALU.add),
                wait=[(cSc, se)], inc=cV)
            guard["l1a_ubuf2"] = (cV, cV.n)
            p.op("vector", lambda e, ec=ec: e.scalar_tensor_tensor(
                ctx[:, ec * T:(ec + 1) * T], kbuf, ceff[:, ec:ec + 1], phib,
                ALU.add, ALU.mult),
                wait=[tk, (cV, ceff_done)], inc=cV)
            guard["l1a_kbuf2"] = (cV, cV.n)
        ctx_done = cV.n

        pnames8 = ("psA", "psB", "psC", "psD", "psE", "psF", "psG", "psH")
        res_done = {}
        for oc in range(4):
            for ec in range(DC):
                slot = ec % 2
                wt = wot[:, slot * 512:(slot + 1) * 512]
                twt = p.dma("sync", lambda e, ec=ec, oc=oc, wt=wt: e.dma_start(
                    out=wt, in_=W["l1a_woT"][ec, :, oc * 512:(oc + 1) * 512]),
                    wait=gwait(("l1a_wo", slot)))
                for tt in range(TT):
                    p.op("tensor", lambda e, ec=ec, tt=tt, wt=wt: e.matmul(
                        ps[pnames8[tt]].ap()[:, :],
                        ctx[:, ec * T + tt * 128:ec * T + (tt + 1) * 128], wt,
                        start=(ec == 0), stop=(ec == DC - 1)),
                        wait=[twt, (cV, ctx_done)] + (gwait(pnames8[tt])
                                                      if ec == 0 else []),
                        inc=(cT if tt == TT - 1 else None))
                gset(("l1a_wo", slot), cT)
            mmo = cT.n
            for tt in range(TT):
                p.op("vector", lambda e, tt=tt, oc=oc: e.scalar_tensor_tensor(
                    phib[:, 0:512], ps[pnames8[tt]].ap()[:, :], SCALE,
                    bob[:, oc * 512:(oc + 1) * 512], ALU.mult, ALU.add),
                    wait=[(cT, mmo), (cV, bob_done)], inc=cV)
                p.op("vector", lambda e, tt=tt, oc=oc: e.tensor_add(
                    xtile(tt)[:, oc * 512:(oc + 1) * 512],
                    xtile(tt)[:, oc * 512:(oc + 1) * 512], phib[:, 0:512]), inc=cV)
                gset(pnames8[tt], cV)
                res_done[(tt, oc)] = (cV, cV.n)
        ln_waits = []
        for tt in range(TT):
            ln_waits.append(ln_tile_inplace(
                xtile(tt), sq, wait=[res_done[(tt, oc)] for oc in range(4)]))
        arena.release()
        return [(cSc, max(ln_waits))]

    def stage_attn(ready, pre, nheads_kv, rope):
        ISC = float(128.0 ** -0.5)
        KV = nheads_kv * 128
        KVC = nheads_kv
        arena.mark()
        ofm = arena.alloc(DC * T)
        wsl = arena.alloc(2 * D)
        khb = arena.alloc(2048)
        vhb = arena.alloc(2048)
        qbuf = arena.alloc(512)
        pb = arena.alloc(512)
        trisel_s = arena.alloc(4096)
        kmq_s = arena.alloc(32)
        smb = arena.alloc(32)
        kfb = arena.alloc(T)
        sqs = arena.alloc(T if rope else 512)
        bqf_s = arena.alloc(16)
        bkf_s = arena.alloc(16)
        bvrow = vhb[0:1, 0:KV]
        borow = khb[0:1, 0:D]

        tw = [p.dma("sync", lambda e: e.dma_start(out=trisel_s, in_=W[pre + "_trisel"])),
              p.dma("sync", lambda e: e.dma_start(out=kmq_s, in_=W[pre + "_kmaskq"])),
              p.dma("sync", lambda e: e.dma_start(out=bvrow, in_=W[pre + "_bv"])),
              p.dma("sync", lambda e: e.dma_start(out=bqf_s, in_=W[pre + "_bqf"])),
              p.dma("sync", lambda e: e.dma_start(out=bkf_s[:, 0:KVC],
                                                   in_=W[pre + "_bkf"]))]
        if rope:
            rcos = khb[:, 0:T]
            rsin = khb[:, T:2 * T]
            tw.append(p.dma("sync", lambda e: e.dma_start(out=rcos, in_=W[pre + "_cos"])))
            tw.append(p.dma("sync", lambda e: e.dma_start(out=rsin, in_=W[pre + "_sinm"])))

        pairs = []
        xT = ofm
        for tt in range(TT):
            for dc in range(DC):
                pairs.append((xtile(tt)[:, dc * 128:(dc + 1) * 128],
                              xT[:, dc * T + tt * 128:dc * T + (tt + 1) * 128]))
        xt_done = transpose_tiles(pairs, wait=list(ready))

        m11 = smb[0:1, 0:1]
        rmcol = smb[:, 1:2]
        mxcol = smb[:, 2:3]
        mcol = smb[:, 3:4]
        biasq = smb[:, 16:32]

        def rope_apply(buf, done_in):
            p.op("vector", lambda e: e.tensor_mul(sqs, buf, rcos),
                 wait=[done_in] + tw, inc=cV)
            t1d = cV.n
            for tc in range(2):
                bank = "psG" if tc == 0 else "psH"
                p.op("tensor", lambda e, tc=tc, bank=bank: e.matmul(
                    ps[bank].ap()[:, :], rotm_s.ap()[:, :],
                    buf[:, tc * 512:(tc + 1) * 512], start=True, stop=True),
                    wait=[done_in] + gwait(bank) + [rot_ready], inc=cT)
                rqd = cT.n
                p.op("vector", lambda e, tc=tc, bank=bank: e.tensor_tensor(
                    qbuf, ps[bank].ap()[:, :], rsin[:, tc * 512:(tc + 1) * 512],
                    ALU.mult), wait=[(cT, rqd), (cV, t1d)], inc=cV)
                gset(bank, cV)
                p.op("vector", lambda e, tc=tc: e.tensor_add(
                    buf[:, tc * 512:(tc + 1) * 512], qbuf,
                    sqs[:, tc * 512:(tc + 1) * 512]), inc=cV)
            return (cV, cV.n)

        def proj_fm(wname, ec, slot, extra=()):
            wsl_s = wsl[:, slot * D:(slot + 1) * D]
            twl = p.dma("sync", lambda e, ec=ec, wsl_s=wsl_s, wname=wname: e.dma_start(
                out=wsl_s, in_=W[wname][ec]), wait=gwait(("at_w", slot)))
            mms = []
            for tc in range(2):
                bank = "psE" if tc == 0 else "psF"
                for dc in range(DC):
                    p.op("tensor", lambda e, dc=dc, tc=tc, wsl_s=wsl_s, bank=bank: e.matmul(
                        ps[bank].ap()[:, :], wsl_s[:, dc * 128:(dc + 1) * 128],
                        xT[:, dc * T + tc * 512:dc * T + (tc + 1) * 512],
                        start=(dc == 0), stop=(dc == DC - 1)),
                        wait=[twl, (cV, xt_done)] + (gwait(bank) + list(extra)
                                                     if dc == 0 else []),
                        inc=(cT if dc == DC - 1 else None))
                mms.append(cT.n)
            gset(("at_w", slot), cT)
            return mms

        qtoks, ktoks, vtoks = {}, [], []
        for ec in range(DC):
            mms = proj_fm(pre + "_wq", ec, ec % 2)
            for tc in range(2):
                bank = "psE" if tc == 0 else "psF"
                p.op("scalar", lambda e, tc=tc, ec=ec, bank=bank: e.activation(
                    kfb[:, tc * 512:(tc + 1) * 512], ps[bank].ap()[:, :],
                    AF.Identity, bias=bqf_s[:, ec:ec + 1]),
                    wait=[(cT, mms[tc])] + tw + gwait("at_kfb"), inc=cSc)
                gset(bank, cSc)
            qd = (cSc, cSc.n)
            if rope:
                qd = rope_apply(kfb, qd)
            tqs = p.dma("gpsimd", lambda e, ec=ec: e.dma_start(
                out=dram[pre + "_qf"].ap()[ec * 128:(ec + 1) * 128, :], in_=kfb),
                wait=[qd])
            qtoks[ec] = tqs
            guard["at_kfb"] = tqs
        for ec in range(KVC):
            mms = proj_fm(pre + "_wk", ec, ec % 2)
            for tc in range(2):
                bank = "psE" if tc == 0 else "psF"
                p.op("scalar", lambda e, tc=tc, ec=ec, bank=bank: e.activation(
                    kfb[:, tc * 512:(tc + 1) * 512], ps[bank].ap()[:, :],
                    AF.Identity, bias=bkf_s[:, ec:ec + 1]),
                    wait=[(cT, mms[tc])] + gwait("at_kfb"), inc=cSc)
                gset(bank, cSc)
            kd = (cSc, cSc.n)
            if rope:
                kd = rope_apply(kfb, kd)
            tks = p.dma("gpsimd", lambda e, ec=ec: e.dma_start(
                out=dram[pre + "_kin"].ap()[ec * 128:(ec + 1) * 128, :], in_=kfb),
                wait=[kd])
            guard["at_kfb"] = tks
            ktoks.append(tks)
        pnames8 = ("psA", "psB", "psC", "psD", "psE", "psF", "psG", "psH")
        v_tok = None
        for oc in range(KV // 512):
            for dc in range(DC):
                slot = dc % 2
                wt = wsl[:, slot * D:slot * D + 512]
                twt = p.dma("sync", lambda e, dc=dc, oc=oc, wt=wt: e.dma_start(
                    out=wt, in_=W[pre + "_wv"][dc, :, oc * 512:(oc + 1) * 512]),
                    wait=gwait(("at_w", slot)))
                for tt in range(TT):
                    p.op("tensor", lambda e, dc=dc, tt=tt, wt=wt: e.matmul(
                        ps[pnames8[tt]].ap()[:, :],
                        xT[:, dc * T + tt * 128:dc * T + (tt + 1) * 128], wt,
                        start=(dc == 0), stop=False),
                        wait=[twt] + (gwait(pnames8[tt]) if dc == 0 else []),
                        inc=(cT if tt == TT - 1 else None))
                gset(("at_w", slot), cT)
            for tt in range(TT):
                p.op("tensor", lambda e, tt=tt, oc=oc: e.matmul(
                    ps[pnames8[tt]].ap()[:, :], ones_r.ap()[:, :],
                    bvrow[:, oc * 512:(oc + 1) * 512], start=False, stop=True),
                    wait=[(cV, ones_ready)] + tw, inc=cT)
                mmv = cT.n
                p.op("vector", lambda e, tt=tt: e.tensor_copy(
                    qbuf, ps[pnames8[tt]].ap()[:, :]),
                    wait=[(cT, mmv)] + gwait("at_qbuf"), inc=cV)
                gset(pnames8[tt], cV)
                vd = cV.n
                v_tok = p.dma("gpsimd", lambda e, tt=tt, oc=oc: e.dma_start(
                    out=dram[pre + "_vin"].ap()[tt * 128:(tt + 1) * 128,
                                                oc * 512:(oc + 1) * 512],
                    in_=qbuf), wait=[(cV, vd)])
                vtoks.append(v_tok)
                guard["at_qbuf"] = v_tok

        RG = [[0, 1], [2, 3], [4, 5], [6, 7]]
        p.op("gpsimd", lambda e: e.collective_compute(
            "AllGather", ALU.bypass, replica_groups=RG,
            ins=[dram[pre + "_kin"].ap()], outs=[dram[pre + "_kout"].ap()]),
            wait=ktoks, inc=cG)
        p.op("gpsimd", lambda e: e.collective_compute(
            "AllGather", ALU.bypass, replica_groups=RG,
            ins=[dram[pre + "_vin"].ap()], outs=[dram[pre + "_vout"].ap()]),
            wait=vtoks, inc=cG)
        cc_done = cG.n
        bias_done = (cV, 0)

        for h in range(DC):
            hkv = h // (DC // KVC)
            tkh = [p.dma("sync", lambda e, hkv=hkv: e.dma_start(
                out=khb[:, 0:T], in_=dram[pre + "_kout"].ap()[
                    hkv * 128:(hkv + 1) * 128, :]),
                wait=gwait("at_khb") + [(cG, cc_done)]),
                p.dma("sync", lambda e, hkv=hkv: e.dma_start(
                    out=khb[:, T:2 * T], in_=dram[pre + "_kout"].ap()[
                        KV + hkv * 128:KV + (hkv + 1) * 128, :]))]
            tvh = []
            for gq in range(4):
                tvh.append(p.dma("sync", lambda e, hkv=hkv, gq=gq: e.dma_start(
                    out=vhb[:, gq * 512:(gq + 1) * 512].rearrange(
                        "p (g c) -> p g c", c=128),
                    in_=dram[pre + "_vout"].ap()[gq * 512:(gq + 1) * 512,
                                                 hkv * 128:(hkv + 1) * 128]
                    .rearrange("(g p) c -> p g c", p=128)),
                    wait=(gwait("at_vhb") + [(cG, cc_done)]) if gq == 0 else ()))
            for qb in range(2):
                tqh = p.dma("sync", lambda e, h=h, qb=qb: e.dma_start(
                    out=qbuf, in_=dram[pre + "_qf"].ap()[
                        h * 128:(h + 1) * 128, qb * 512:(qb + 1) * 512]),
                    wait=gwait("at_qbuf") + [qtoks[h]])
                glist = list(range(0, 8)) + list(range(8, 12 + 4 * qb))
                nl = len(glist)

                def s_tile(g, sb, extra_w=()):
                    slot = -1
                    if 4 * qb <= g < 4 * qb + 4:
                        slot = g - 4 * qb
                    elif 8 + 4 * qb <= g < 12 + 4 * qb:
                        slot = 4 + (g - 8 - 4 * qb)
                    w0 = gwait(sb) + list(extra_w)
                    if slot >= 0:
                        p.op("tensor", lambda e, slot=slot, sb=sb: e.matmul(
                            ps[sb].ap()[:, :], sident.ap()[:, :],
                            trisel_s[:, slot * 512:(slot + 1) * 512],
                            start=True, stop=False), wait=w0, inc=None)
                    p.op("tensor", lambda e, g=g, sb=sb, slot=slot: e.matmul(
                        ps[sb].ap()[:, :], khb[:, g * 128:(g + 1) * 128], qbuf,
                        start=(slot < 0), stop=True),
                        wait=w0 if slot < 0 else (), inc=cT)
                    return cT.n

                p.op("vector", lambda e: e.memset(mxcol, -1e30), inc=cV)
                for idx, g in enumerate(glist):
                    sb = "psE" if idx % 2 == 0 else "psF"
                    smm = s_tile(g, sb, ([tqh] + tkh + tvh) if idx < 2 else ())
                    p.op("vector", lambda e, sb=sb: e.reduce_max(
                        rmcol, ps[sb].ap()[:, :], AX.X), wait=[(cT, smm)], inc=cV)
                    p.op("vector", lambda e, qb=qb, g=g: e.scalar_tensor_tensor(
                        mxcol, rmcol, kmq_s[:, qb * 16 + g:qb * 16 + g + 1],
                        mxcol, ALU.add, ALU.max), inc=cV)
                    gset(sb, cV)
                mx_done = cV.n
                p.op("tensor", lambda e: e.transpose(
                    ps["psG"].ap()[0:1, 0:128], mxcol, sident.ap()[:, :]),
                    wait=[(cV, mx_done)] + gwait("psG"), inc=cT)
                tpd = cT.n
                p.op("vector", lambda e: e.reduce_max(
                    m11, ps["psG"].ap()[0:1, 0:128], AX.X),
                    wait=[(cT, tpd)], inc=cV)
                gset("psG", cV)
                p.op("vector", lambda e: e.tensor_scalar(
                    m11, m11, -ISC, None, ALU.mult), inc=cV)
                nmd = cV.n
                p.op("tensor", lambda e: e.matmul(
                    ps["psG"].ap()[:, 0:1], ones_r.ap()[:, :], m11,
                    start=True, stop=True),
                    wait=[(cV, nmd), (cV, ones_ready)] + gwait("psG"), inc=cT)
                bmd = cT.n
                p.op("vector", lambda e: e.tensor_copy(mcol, ps["psG"].ap()[:, 0:1]),
                     wait=[(cT, bmd)], inc=cV)
                gset("psG", cV)
                p.op("vector", lambda e, qb=qb: e.tensor_tensor(
                    biasq, kmq_s[:, qb * 16:(qb + 1) * 16],
                    mcol.broadcast_to([128, 16]), ALU.add), inc=cV)
                bq_done = cV.n

                for idx, g in enumerate(glist):
                    sb = "psE" if idx % 2 == 0 else "psF"
                    smm = s_tile(g, sb)
                    p.op("scalar", lambda e, g=g, sb=sb: e.activation(
                        pb, ps[sb].ap()[:, :], AF.Exp, scale=ISC,
                        bias=biasq[:, g:g + 1]),
                        wait=[(cT, smm), (cV, bq_done)], inc=cSc)
                    gset(sb, cSc)
                    pd = cSc.n
                    p.op("tensor", lambda e, idx=idx, nl=nl: e.matmul(
                        ps["psH"].ap()[0:1, :], ones_c.ap()[:, :], pb,
                        start=(idx == 0), stop=(idx == nl - 1)),
                        wait=[(cSc, pd)] + (gwait("psH") if idx == 0 else []),
                        inc=None)
                    p.op("tensor", lambda e, g=g, idx=idx, nl=nl: e.matmul(
                        ps["psG"].ap()[:, :], vhb[:, g * 128:(g + 1) * 128], pb,
                        start=(idx == 0), stop=(idx == nl - 1)),
                        wait=(gwait("psG") if idx == 0 else []),
                        inc=(cT if idx == nl - 1 else None))
                mmo = cT.n
                p.op("vector", lambda e: e.tensor_copy(
                    pb[0:1, :], ps["psH"].ap()[0:1, :]), wait=[(cT, mmo)], inc=cV)
                p.op("vector", lambda e: e.reciprocal(pb[0:1, :], pb[0:1, :]),
                     inc=cV)
                rd = cV.n
                p.op("tensor", lambda e: e.matmul(
                    ps["psH"].ap()[:, :], ones_r.ap()[:, :], pb[0:1, :],
                    start=True, stop=True), wait=[(cV, rd)], inc=cT)
                rbd = cT.n
                gset("psH", cT)
                p.op("vector", lambda e: e.tensor_copy(
                    sqs[:, 0:512], ps["psH"].ap()[:, :]), wait=[(cT, rbd)], inc=cV)
                gset("psH", cV)
                p.op("vector", lambda e, h=h, qb=qb: e.tensor_tensor(
                    ofm[:, h * T + qb * 512:h * T + (qb + 1) * 512],
                    ps["psG"].ap()[:, :], sqs[:, 0:512], ALU.mult), inc=cV)
                gset("psG", cV)
                guard["at_qbuf"] = (cV, cV.n)
            gset("at_khb", cV)
            gset("at_vhb", cV)
            emit_prepass_chunk(26)

        tbo = p.dma("sync", lambda e: e.dma_start(out=borow, in_=W[pre + "_bo"]),
                    wait=gwait("at_khb") + [(cT, cT.n)])
        res_done = {}
        for oc in range(4):
            for ec in range(DC):
                slot = ec % 2
                wt = wsl[:, slot * D:slot * D + 512]
                twt = p.dma("sync", lambda e, ec=ec, oc=oc, wt=wt: e.dma_start(
                    out=wt, in_=W[pre + "_woT"][ec, :, oc * 512:(oc + 1) * 512]),
                    wait=gwait(("at_w", slot)))
                for tt in range(TT):
                    qb, j = tt // 4, tt % 4
                    p.op("tensor", lambda e, ec=ec, tt=tt, qb=qb, j=j, wt=wt: e.matmul(
                        ps[pnames8[tt]].ap()[:, :],
                        ofm[:, ec * T + qb * 512 + j * 128:
                            ec * T + qb * 512 + (j + 1) * 128], wt,
                        start=(ec == 0), stop=False),
                        wait=[twt] + (gwait(pnames8[tt]) if ec == 0 else []),
                        inc=(cT if tt == TT - 1 else None))
                gset(("at_w", slot), cT)
            for tt in range(TT):
                p.op("tensor", lambda e, tt=tt, oc=oc: e.matmul(
                    ps[pnames8[tt]].ap()[:, :], ones_r.ap()[:, :],
                    borow[:, oc * 512:(oc + 1) * 512], start=False, stop=True),
                    wait=[(cV, ones_ready), tbo], inc=cT)
                mmw = cT.n
                p.op("vector", lambda e, tt=tt, oc=oc: e.tensor_add(
                    xtile(tt)[:, oc * 512:(oc + 1) * 512],
                    xtile(tt)[:, oc * 512:(oc + 1) * 512],
                    ps[pnames8[tt]].ap()[:, :]), wait=[(cT, mmw)], inc=cV)
                gset(pnames8[tt], cV)
                res_done[(tt, oc)] = (cV, cV.n)
        ln_waits = []
        for tt in range(TT):
            ln_waits.append(ln_tile_inplace(
                xtile(tt), ofm[:, 0:D], wait=[res_done[(tt, oc)] for oc in range(4)]))
        arena.release()
        return [(cSc, max(ln_waits))]

    def quant_round(dst_ap, src_ap, scol, qtmp, width):
        xs = qtmp[:, 0:width]
        tf = qtmp[:, width:2 * width]
        ti = qtmp[:, 2 * width:3 * width].bitcast(I32)
        p.op("vector", lambda e: e.tensor_scalar(xs, src_ap, scol, None, ALU.mult),
             inc=cV)
        p.op("vector", lambda e: e.tensor_copy(ti, xs), inc=cV)
        p.op("vector", lambda e: e.tensor_copy(tf, ti), inc=cV)
        p.op("vector", lambda e: e.tensor_sub(xs, xs, tf), inc=cV)
        p.op("vector", lambda e: e.scalar_tensor_tensor(
            tf, xs, 0.5, tf, ALU.is_ge, ALU.add), inc=cV)
        p.op("vector", lambda e: e.scalar_tensor_tensor(
            tf, xs, -0.5, tf, ALU.is_ge, ALU.add), inc=cV)
        p.op("vector", lambda e: e.tensor_scalar(dst_ap, tf, -1.0, None, ALU.add),
             inc=cV)
        return cV.n

    def stage_l2f(ready):
        IC = 64
        IP = 16
        arena.mark()
        kxf = arena.alloc(DC * T // 2).bitcast(BF16)
        terng = arena.alloc(DC * 256).bitcast(BF16)
        ternv = arena.alloc(DC * 256).bitcast(BF16)
        qtmp = arena.alloc(3 * D)
        if gpre is None:
            wfp = arena.alloc(512)
            wabs = arena.alloc(512)
            wsg = arena.alloc(512)
        else:
            wfp = wabs = wsg = None
        kxtm = arena.alloc(D)
        hpiece = arena.alloc(512)
        gpiece = arena.alloc(512)
        gbb = arena.alloc(512)
        vbb = arena.alloc(512)
        obb = arena.alloc(D)
        rows = arena.alloc(512 + 512 + D)
        khstage = arena.alloc(T).bitcast(BF16)
        khic = arena.alloc(T // 2).bitcast(BF16)
        owt = arena.alloc(256).bitcast(BF16)
        smq = arena.alloc(80)
        sq = qtmp[:, 0:D]
        gbrow = rows[0:1, 0:512]
        vbrow = rows[0:1, 512:1024]
        obrow = rows[0:1, 1024:1024 + D]

        gxc = smq[:, 0:8]
        sxc = smq[:, 8:16]
        hmx = smq[:, 16:24]
        shc = smq[:, 24:32]
        sgg = smq[:, 32:40]
        sgv = smq[:, 40:48]
        sgo = smq[:, 48:56]
        acc1 = smq[:, 58:59]
        tmp1 = smq[:, 59:60]
        m11b = smq[0:1, 60:61]
        gamcol = {"g": smq[:, 61:62], "v": smq[:, 62:63], "o": smq[:, 63:64]}
        g05col = {"g": smq[:, 64:65], "v": smq[:, 65:66], "o": smq[:, 66:67]}

        tob = p.dma("sync", lambda e: e.dma_start(out=obrow, in_=W["l2f_ob"]))
        obb_done = bcast_row(obrow, obb, D, wait=[tob])
        p.op("vector", lambda e: e.memset(hmx, 0.0), inc=cV)

        PRE = gpre is not None
        if PRE:
            gammas = {k: prep_state["done"][k] for k in "gvo"}
            for ci, key in enumerate("gvo"):
                p.op("vector", lambda e, ci=ci, key=key: e.tensor_copy(
                    gamcol[key], gpre[:, ci:ci + 1]),
                    wait=[prep_state["done"][key]], inc=cV)
                gammas[key] = (cV, cV.n)
        if not PRE:
            gammas = {}
        for wname, nslab, key in (() if PRE else
                                  (("l2f_gw", 16, "g"), ("l2f_vw", 16, "v"),
                                   ("l2f_ow", 64, "o"))):
            p.op("vector", lambda e: e.memset(acc1, 0.0), inc=cV)
            npc = 4 if nslab == 16 else 1
            for sl in range(nslab):
                for pc in range(npc):
                    tl = p.dma("sync", lambda e, wname=wname, sl=sl, pc=pc: e.dma_start(
                        out=qtmp[:, 0:2048],
                        in_=W[wname][sl, :, pc * 2048:(pc + 1) * 2048]),
                        wait=gwait("l2f_qtmp"))
                    p.op("scalar", lambda e: e.activation(
                        qtmp[:, 2048:4096], qtmp[:, 0:2048], AF.Abs,
                        accum_out=tmp1), wait=[tl] + gwait("l2f_tmp1"), inc=cSc)
                    sd = cSc.n
                    guard["l2f_qtmp"] = (cSc, sd)
                    p.op("vector", lambda e: e.tensor_add(acc1, acc1, tmp1),
                         wait=[(cSc, sd)], inc=cV)
                    gset("l2f_tmp1", cV)
            ad = cV.n
            p.op("tensor", lambda e: e.matmul(
                ps["psH"].ap()[0:1, 0:1], ones_c.ap()[:, :], acc1,
                start=True, stop=True),
                wait=[(cV, ad), (cV, ones_ready)] + gwait("psH"), inc=cT)
            td = cT.n
            p.op("vector", lambda e: e.tensor_scalar(
                m11b, ps["psH"].ap()[0:1, 0:1], 1.0 / (8192.0 * 2048.0), None,
                ALU.mult), wait=[(cT, td)], inc=cV)
            gset("psH", cV)
            p.op("vector", lambda e: e.tensor_scalar_max(m11b, m11b, 1e-5), inc=cV)
            gd = cV.n
            p.op("tensor", lambda e: e.matmul(
                ps["psH"].ap()[:, 0:1], ones_r.ap()[:, :], m11b,
                start=True, stop=True), wait=[(cV, gd)] + gwait("psH"), inc=cT)
            td = cT.n
            p.op("vector", lambda e, key=key: e.tensor_copy(
                gamcol[key], ps["psH"].ap()[:, 0:1]), wait=[(cT, td)], inc=cV)
            gset("psH", cV)
            p.op("vector", lambda e, key=key: e.tensor_scalar(
                g05col[key], gamcol[key], 0.5, None, ALU.mult), inc=cV)
            gammas[key] = (cV, cV.n)

        for tt in range(TT):
            p.op("vector", lambda e, tt=tt: e.reduce_max(
                gxc[:, tt:tt + 1], xtile(tt), AX.X, apply_absolute_value=True),
                wait=list(ready), inc=cV)
            p.op("vector", lambda e, tt=tt: e.tensor_scalar_max(
                gxc[:, tt:tt + 1], gxc[:, tt:tt + 1], 1e-5), inc=cV)
            p.op("vector", lambda e, tt=tt: e.reciprocal(
                sxc[:, tt:tt + 1], gxc[:, tt:tt + 1]), inc=cV)
            p.op("vector", lambda e, tt=tt: e.tensor_scalar(
                sxc[:, tt:tt + 1], sxc[:, tt:tt + 1], 127.0, None, ALU.mult), inc=cV)
            p.wait("vector", gwait("l2f_kxtm2"))
            quant_round(kxtm, xtile(tt), sxc[:, tt:tt + 1], qtmp, D)
            kd = cV.n
            for dc in range(DC):
                bank = TPB[dc % 4]
                p.op("tensor", lambda e, dc=dc, bank=bank: e.transpose(
                    ps[bank].ap()[:, 0:128], kxtm[:, dc * 128:(dc + 1) * 128],
                    sident.ap()[:, :]), wait=[(cV, kd)] + gwait(bank), inc=cT)
                td2 = cT.n
                p.op("vector", lambda e, dc=dc, tt=tt, bank=bank: e.tensor_copy(
                    kxf[:, (dc * TT + tt) * 128:(dc * TT + tt + 1) * 128],
                    ps[bank].ap()[:, 0:128]), wait=[(cT, td2)], inc=cV)
                gset(bank, cV)
            guard["l2f_kxtm2"] = (cT, cT.n)
        kx_done = cV.n

        def kxt(dc, tt):
            return kxf[:, (dc * TT + tt) * 128:(dc * TT + tt + 1) * 128]

        for key, dst in (("g", sgg), ("v", sgv)):
            p.op("vector", lambda e, key=key, dst=dst: e.tensor_tensor(
                dst, gxc, gamcol[key].broadcast_to([128, 8]), ALU.mult),
                wait=[gammas[key], (cV, kx_done)], inc=cV)
            p.op("vector", lambda e, dst=dst: e.tensor_scalar(
                dst, dst, 1.0 / 127.0, None, ALU.mult), inc=cV)
        sg_done = cV.n

        TDN = {"l2f_gw": "l2f_tg", "l2f_vw": "l2f_tv", "l2f_ow": "l2f_to"}

        def ternize(wname, sl, cols, dst_bf, key, extra=()):
            if PRE:
                tk = prep_state["ttoks"][(TDN[wname], sl, cols)]
                return p.dma("sync", lambda e, wname=wname, sl=sl, cols=cols,
                             dst_bf=dst_bf: e.dma_start(
                    out=dst_bf,
                    in_=dram[TDN[wname]].ap()[sl, :, cols * 512:(cols + 1) * 512]),
                    wait=list(extra) + [tk])
            tl = p.dma("sync", lambda e, wname=wname, sl=sl, cols=cols: e.dma_start(
                out=wfp, in_=W[wname][sl, :, cols * 512:(cols + 1) * 512]),
                wait=gwait("l2f_wfp") + list(extra))
            p.op("scalar", lambda e: e.activation(wabs, wfp, AF.Abs),
                 wait=[tl] + gwait("l2f_wsg"), inc=cSc)
            p.op("scalar", lambda e: e.sign(wsg, wfp), inc=cSc)
            sd = cSc.n
            guard["l2f_wfp"] = (cSc, sd)
            p.op("vector", lambda e, key=key: e.tensor_scalar(
                wabs, wabs, g05col[key], None, ALU.is_ge),
                wait=[(cSc, sd), gammas[key]], inc=cV)
            p.op("vector", lambda e, dst_bf=dst_bf: e.tensor_tensor(
                dst_bf, wabs, wsg, ALU.mult), inc=cV)
            guard["l2f_wsg"] = (cV, cV.n)
            return (cV, cV.n)

        h_toks = {}
        for ip in range(IP):
            tgb = p.dma("sync", lambda e, ip=ip: e.dma_start(
                out=gbrow, in_=W["l2f_gb"][0:1, ip * 512:(ip + 1) * 512]),
                wait=gwait("l2f_gbrow"))
            gb_done = bcast_row(gbrow, gbb, 512, wait=[tgb])
            guard["l2f_gbrow"] = (cV, gb_done)
            tvb = p.dma("sync", lambda e, ip=ip: e.dma_start(
                out=vbrow, in_=W["l2f_vb"][0:1, ip * 512:(ip + 1) * 512]),
                wait=gwait("l2f_vbrow"))
            vb_done = bcast_row(vbrow, vbb, 512, wait=[tvb])
            guard["l2f_vbrow"] = (cV, vb_done)
            tern_toks = []
            for dc in range(DC):
                tern_toks.append(ternize(
                    "l2f_gw", dc, ip, terng[:, dc * 512:(dc + 1) * 512], "g",
                    extra=gwait(("l2f_tern", 0))))
                tern_toks.append(ternize(
                    "l2f_vw", dc, ip, ternv[:, dc * 512:(dc + 1) * 512], "v",
                    extra=gwait(("l2f_tern", 0))))
            mmg = mmv = 0
            for tt in range(TT):
                for bank, tern in (("psA", terng), ("psB", ternv)):
                    for dc in range(DC):
                        p.op("tensor", lambda e, dc=dc, tt=tt, bank=bank, tern=tern: e.matmul(
                            ps[bank].ap()[:, :], kxt(dc, tt),
                            tern[:, dc * 512:(dc + 1) * 512],
                            start=(dc == 0), stop=(dc == DC - 1)),
                            wait=(tern_toks + [(cV, kx_done)] + gwait(bank))
                            if dc == 0 else (),
                            inc=(cT if dc == DC - 1 else None))
                mmg, mmv = cT.n - 1, cT.n
                p.op("vector", lambda e, tt=tt: e.tensor_scalar(
                    gpiece, ps["psA"].ap()[:, :], sgg[:, tt:tt + 1], None, ALU.mult),
                    wait=[(cT, mmg), (cV, sg_done)] + gwait("l2f_gp"), inc=cV)
                gset("psA", cV)
                p.op("vector", lambda e: e.tensor_add(gpiece, gpiece, gbb),
                     wait=[(cV, gb_done)], inc=cV)
                vg = cV.n
                p.op("scalar", lambda e: e.activation(gpiece, gpiece, AF.Silu),
                     wait=[(cV, vg)], inc=cSc)
                sg_ = cSc.n
                p.op("vector", lambda e, tt=tt: e.tensor_scalar(
                    hpiece, ps["psB"].ap()[:, :], sgv[:, tt:tt + 1], None, ALU.mult),
                    wait=[(cT, mmv)] + gwait("l2f_hp"), inc=cV)
                gset("psB", cV)
                p.op("vector", lambda e: e.tensor_add(hpiece, hpiece, vbb),
                     wait=[(cV, vb_done)], inc=cV)
                p.op("vector", lambda e: e.tensor_mul(hpiece, hpiece, gpiece),
                     wait=[(cSc, sg_)], inc=cV)
                guard["l2f_gp"] = (cV, cV.n)
                p.op("vector", lambda e: e.reduce_max(
                    tmp1, hpiece, AX.X, apply_absolute_value=True), inc=cV)
                p.op("vector", lambda e, tt=tt: e.tensor_max(
                    hmx[:, tt:tt + 1], hmx[:, tt:tt + 1], tmp1), inc=cV)
                hd = cV.n
                h_toks[(tt, ip)] = p.dma("gpsimd", lambda e, tt=tt, ip=ip: e.dma_start(
                    out=dram["l2f_h"].ap()[tt * 128:(tt + 1) * 128,
                                           ip * 512:(ip + 1) * 512],
                    in_=hpiece), wait=[(cV, hd)])
                guard["l2f_hp"] = h_toks[(tt, ip)]
            guard[("l2f_tern", 0)] = (cT, mmv)

        p.op("vector", lambda e: e.tensor_scalar_max(hmx, hmx, 1e-5), inc=cV)
        p.op("vector", lambda e: e.reciprocal(shc, hmx), inc=cV)
        p.op("vector", lambda e: e.tensor_scalar(shc, shc, 127.0, None, ALU.mult),
             inc=cV)
        p.op("vector", lambda e: e.tensor_tensor(
            sgo, hmx, gamcol["o"].broadcast_to([128, 8]), ALU.mult),
            wait=[gammas["o"]], inc=cV)
        p.op("vector", lambda e: e.tensor_scalar(
            sgo, sgo, 1.0 / 127.0, None, ALU.mult), inc=cV)
        sh_done = cV.n
        kh_toks = {}
        for tt in range(TT):
            for pcq in range(4):
                tl = p.dma("sync", lambda e, tt=tt, pcq=pcq: e.dma_start(
                    out=kxtm,
                    in_=dram["l2f_h"].ap()[tt * 128:(tt + 1) * 128,
                                           pcq * 2048:(pcq + 1) * 2048]),
                    wait=gwait("l2f_kxtm2") + [h_toks[(tt, pcq * 4 + i)]
                                               for i in range(4)])
                p.op("vector", lambda e: e.engine_nop(),
                     wait=[tl, (cV, sh_done)], inc=cV)
                quant_round(kxtm, kxtm, shc[:, tt:tt + 1], qtmp, D)
                qd = cV.n
                for dc in range(DC):
                    bank = TPB[dc % 4]
                    p.op("tensor", lambda e, dc=dc, bank=bank: e.transpose(
                        ps[bank].ap()[:, 0:128], kxtm[:, dc * 128:(dc + 1) * 128],
                        sident.ap()[:, :]), wait=[(cV, qd)] + gwait(bank), inc=cT)
                    td2 = cT.n
                    p.op("vector", lambda e, dc=dc, bank=bank: e.tensor_copy(
                        khstage[:, dc * 128:(dc + 1) * 128],
                        ps[bank].ap()[:, 0:128]),
                        wait=[(cT, td2)] + gwait("l2f_khs"), inc=cV)
                    gset(bank, cV)
                guard["l2f_kxtm2"] = (cT, cT.n)
                kd = cV.n
                kh_toks[(tt, pcq)] = p.dma("gpsimd", lambda e, tt=tt, pcq=pcq: e.dma_start(
                    out=dram["l2f_kh"].ap()[pcq * 16:(pcq + 1) * 16, :,
                                            tt * 128:(tt + 1) * 128]
                    .transpose([1, 0, 2]),
                    in_=khstage.rearrange("p (c k) -> p c k", k=128)),
                    wait=[(cV, kd)] + gwait("l2f_khs"))
                guard["l2f_khs"] = kh_toks[(tt, pcq)]

        pnames8 = ("psA", "psB", "psC", "psD", "psE", "psF", "psG", "psH")
        res_done = {}
        for oc in range(4):
            for ic in range(IC):
                tko = p.dma("sync", lambda e, ic=ic: e.dma_start(
                    out=khic, in_=dram["l2f_kh"].ap()[ic // 4 if False else ic]
                    if False else dram["l2f_kh"].ap()[ic]),
                    wait=gwait("l2f_khic") + [kh_toks[(tt2, ic // 16)]
                                              for tt2 in range(TT)])
                od = ternize("l2f_ow", ic, oc, owt, "o",
                             extra=gwait("l2f_wsg"))
                for tt in range(TT):
                    p.op("tensor", lambda e, ic=ic, tt=tt: e.matmul(
                        ps[pnames8[tt]].ap()[:, :],
                        khic[:, tt * 128:(tt + 1) * 128], owt,
                        start=(ic == 0), stop=(ic == IC - 1)),
                        wait=[tko, od] + (gwait(pnames8[tt]) if ic == 0 else []),
                        inc=(cT if tt == TT - 1 else None))
                guard["l2f_khic"] = (cT, cT.n)
                guard["l2f_wsg"] = (cT, cT.n)
            mmo = cT.n
            for tt in range(TT):
                p.op("vector", lambda e, tt=tt, oc=oc: e.scalar_tensor_tensor(
                    gpiece, ps[pnames8[tt]].ap()[:, :], sgo[:, tt:tt + 1],
                    obb[:, oc * 512:(oc + 1) * 512], ALU.mult, ALU.add),
                    wait=[(cT, mmo), (cV, obb_done)] + gwait("l2f_gp"), inc=cV)
                p.op("vector", lambda e, tt=tt, oc=oc: e.tensor_add(
                    xtile(tt)[:, oc * 512:(oc + 1) * 512],
                    xtile(tt)[:, oc * 512:(oc + 1) * 512], gpiece), inc=cV)
                guard["l2f_gp"] = (cV, cV.n)
                gset(pnames8[tt], cV)
                res_done[(tt, oc)] = (cV, cV.n)
        ln_waits = []
        for tt in range(TT):
            ln_waits.append(ln_tile_inplace(
                xtile(tt), sq, wait=[res_done[(tt, oc)] for oc in range(4)]))
        arena.release()
        return [(cSc, max(ln_waits))]

    prep_state = {"steps": [], "done": {}}

    def build_l2f_prepass(buf):
        qa = buf[:, 0:512]
        qb_ = buf[:, 512:1024]
        qs = buf[:, 1024:1536]
        acc1 = buf[:, 1536:1537]
        tmp1 = buf[:, 1537:1538]
        m11 = buf[0:1, 1538:1539]
        tern_bf = buf[:, 1539:1539 + 256].bitcast(BF16)
        steps = prep_state["steps"]
        wspec = (("l2f_gw", "l2f_tg", 16, "g", 0), ("l2f_vw", "l2f_tv", 16, "v", 1),
                 ("l2f_ow", "l2f_to", 64, "o", 2))

        def emit_gamma_piece(wname, sl, pc, first, last, key, ci):
            def go():
                tl = p.dma("scalar", lambda e: e.dma_start(
                    out=qa, in_=W[wname][sl, :, pc * 512:(pc + 1) * 512]),
                    wait=gwait("pp_qa"))
                if first:
                    p.op("vector", lambda e: e.memset(acc1, 0.0),
                         wait=gwait("pp_acc"), inc=cV)
                p.op("scalar", lambda e: e.activation(
                    qb_, qa, AF.Abs, accum_out=tmp1),
                    wait=[tl] + gwait("pp_tmp1"), inc=cSc)
                sd = cSc.n
                guard["pp_qa"] = (cSc, sd)
                p.op("vector", lambda e: e.tensor_add(acc1, acc1, tmp1),
                     wait=[(cSc, sd)], inc=cV)
                gset("pp_tmp1", cV)
                if last:
                    ad = cV.n
                    p.op("tensor", lambda e: e.matmul(
                        ps["psH"].ap()[0:1, 0:1], ones_c.ap()[:, :], acc1,
                        start=True, stop=True),
                        wait=[(cV, ad), (cV, ones_ready)] + gwait("psH"), inc=cT)
                    td = cT.n
                    p.op("vector", lambda e: e.tensor_scalar(
                        m11, ps["psH"].ap()[0:1, 0:1], 1.0 / (8192.0 * 2048.0),
                        None, ALU.mult), wait=[(cT, td)], inc=cV)
                    gset("psH", cV)
                    p.op("vector", lambda e: e.tensor_scalar_max(m11, m11, 1e-5),
                         inc=cV)
                    gd = cV.n
                    p.op("tensor", lambda e: e.matmul(
                        ps["psH"].ap()[:, 0:1], ones_r.ap()[:, :], m11,
                        start=True, stop=True),
                        wait=[(cV, gd)] + gwait("psH"), inc=cT)
                    td2 = cT.n
                    p.op("vector", lambda e, ci=ci: e.tensor_copy(
                        gpre[:, ci:ci + 1], ps["psH"].ap()[:, 0:1]),
                        wait=[(cT, td2)], inc=cV)
                    gset("psH", cV)
                    p.op("vector", lambda e, ci=ci: e.tensor_scalar(
                        gpre[:, 3 + ci:4 + ci], gpre[:, ci:ci + 1], 0.5, None,
                        ALU.mult), inc=cV)
                    prep_state["done"][key] = (cV, cV.n)
                    gset("pp_acc", cV)
            return go

        def emit_tern_piece(wname, dname, sl, pc, key, ci):
            def go():
                tl = p.dma("scalar", lambda e: e.dma_start(
                    out=qa, in_=W[wname][sl, :, pc * 512:(pc + 1) * 512]),
                    wait=gwait("pp_qa"))
                p.op("scalar", lambda e: e.activation(qb_, qa, AF.Abs),
                     wait=[tl] + gwait("pp_qb"), inc=cSc)
                p.op("scalar", lambda e: e.sign(qs, qa), inc=cSc)
                sd = cSc.n
                guard["pp_qa"] = (cSc, sd)
                p.op("vector", lambda e, ci=ci: e.tensor_scalar(
                    qb_, qb_, gpre[:, 3 + ci:4 + ci], None, ALU.is_ge),
                    wait=[(cSc, sd), prep_state["done"][key]], inc=cV)
                p.op("vector", lambda e: e.tensor_tensor(
                    tern_bf, qb_, qs, ALU.mult),
                    wait=gwait("pp_tern"), inc=cV)
                gset("pp_qb", cV)
                vd = cV.n
                tk = p.dma("gpsimd", lambda e: e.dma_start(
                    out=dram[dname].ap()[sl, :, pc * 512:(pc + 1) * 512],
                    in_=tern_bf), wait=[(cV, vd)])
                guard["pp_tern"] = tk
                prep_state.setdefault("ttoks", {})[(dname, sl, pc)] = tk
            return go

        for wname, dname, nslab, key, ci in wspec:
            npcs = (8192 if nslab == 16 else 2048) // 512
            tot = nslab * npcs
            i = 0
            for sl in range(nslab):
                for pc in range(npcs):
                    steps.append(emit_gamma_piece(wname, sl, pc, i == 0,
                                                  i == tot - 1, key, ci))
                    i += 1
        for wname, dname, nslab, key, ci in wspec:
            npcs = (8192 if nslab == 16 else 2048) // 512
            for sl in range(nslab):
                for pc in range(npcs):
                    steps.append(emit_tern_piece(wname, dname, sl, pc, key, ci))

    def emit_prepass_chunk(k):
        if gpre is None:
            return
        steps = prep_state["steps"]
        for _ in range(min(k, len(steps))):
            steps.pop(0)()

    def stage_l2a(ready):
        NS = 16
        TCN = 4
        TC = T // TCN
        RG = [[0, 1], [2, 3], [4, 5], [6, 7]]

        def fence():
            w = [(cT, cT.n), (cV, cV.n), (cSc, cSc.n)]
            for key, ctrs in p.pools.items():
                for c in ctrs:
                    w.append((c, c.n))
            p.wait("sync", w)
            p.wait("gpsimd", w)

        emit_prepass_chunk(10 ** 9)
        arena.mark()
        halob = arena.alloc(48)
        h0 = arena.alloc(48)
        h1 = arena.alloc(48)
        haloe = arena.alloc(48)

        arena.mark()
        xT = arena.alloc(DC * T)
        wsl = arena.alloc(2 * D)
        evb = arena.alloc(T)

        pairs = []
        for tt in range(TT):
            for dc in range(DC):
                pairs.append((xtile(tt)[:, dc * 128:(dc + 1) * 128],
                              xT[:, dc * T + tt * 128:dc * T + (tt + 1) * 128]))
        xt_done = transpose_tiles(pairs, wait=list(ready))

        xc_toks, r_toks = {}, {}
        for ec in range(32):
            slot = ec % 2
            wsl_s = wsl[:, slot * D:(slot + 1) * D]
            twl = p.dma("sync", lambda e, ec=ec, wsl_s=wsl_s: e.dma_start(
                out=wsl_s, in_=W["l2a_win"][ec]), wait=gwait(("l2a_w", slot)))
            for tc2 in range(2):
                bank = "psE" if tc2 == 0 else "psF"
                for dc in range(DC):
                    p.op("tensor", lambda e, dc=dc, tc2=tc2, wsl_s=wsl_s, bank=bank: e.matmul(
                        ps[bank].ap()[:, :], wsl_s[:, dc * 128:(dc + 1) * 128],
                        xT[:, dc * T + tc2 * 512:dc * T + (tc2 + 1) * 512],
                        start=(dc == 0), stop=(dc == DC - 1)),
                        wait=[twl, (cV, xt_done)] + (gwait(bank) if dc == 0 else []),
                        inc=(cT if dc == DC - 1 else None))
                mm = cT.n
                p.op("vector", lambda e, tc2=tc2, bank=bank: e.tensor_copy(
                    evb[:, tc2 * 512:(tc2 + 1) * 512], ps[bank].ap()[:, :]),
                    wait=[(cT, mm)] + gwait("l2a_evb"), inc=cV)
                gset(bank, cV)
            gset(("l2a_w", slot), cT)
            ed = cV.n
            if ec < 16:
                p.op("vector", lambda e, ec=ec: e.tensor_copy(
                    halob[:, ec * 3:(ec + 1) * 3], evb[:, T - 3:T]), inc=cV)
                tsp = p.dma("gpsimd", lambda e, ec=ec: e.dma_start(
                    out=dram["l2a_xc"].ap()[ec * 128:(ec + 1) * 128, :], in_=evb),
                    wait=[(cV, cV.n)])
                xc_toks[ec] = tsp
            else:
                tsp = p.dma("gpsimd", lambda e, ec=ec: e.dma_start(
                    out=dram["l2a_r"].ap()[(ec - 16) * 128:(ec - 15) * 128, :],
                    in_=evb), wait=[(cV, ed)])
                r_toks[ec - 16] = tsp
            guard["l2a_evb"] = tsp
        hde = cV.n
        thi = p.dma("gpsimd", lambda e: e.dma_start(
            out=dram["l2a_hloin"].ap(), in_=halob), wait=[(cV, hde)])
        p.op("gpsimd", lambda e: e.collective_compute(
            "AllGather", ALU.bypass, replica_groups=RG,
            ins=[dram["l2a_hloin"].ap()], outs=[dram["l2a_hloout"].ap()]),
            wait=[thi], inc=cG)
        ccd = cG.n
        t0 = p.dma("gpsimd", lambda e: e.dma_start(
            out=h0, in_=dram["l2a_hloout"].ap()[0:128, :]), wait=[(cG, ccd)])
        t1 = p.dma("gpsimd", lambda e: e.dma_start(
            out=h1, in_=dram["l2a_hloout"].ap()[128:256, :]), wait=[(cG, ccd)])
        p.op("vector", lambda e: e.tensor_scalar(
            haloe, h0, gatesb.ap()[:, 0:1], None, ALU.mult), wait=[t0, t1], inc=cV)
        p.op("vector", lambda e: e.scalar_tensor_tensor(
            haloe, h1, gatesb.ap()[:, 1:2], haloe, ALU.mult, ALU.add), inc=cV)
        halo_done = cV.n
        arena.release()
        fence()

        arena.mark()
        XST = 1027
        xcbuf = arena.alloc(DC * XST)
        cwsl = arena.alloc(8192)
        wxsl = arena.alloc(DC * 160)
        cbf = arena.alloc(16)
        piece = arena.alloc(512)
        upiece = arena.alloc(512)

        twx = p.dma("sync", lambda e: e.dma_start(
            out=wxsl.rearrange("p (c k) -> p c k", k=160), in_=W["l2a_wx"]))
        tcb = p.dma("sync", lambda e: e.dma_start(out=cbf, in_=W["l2a_convb"]))
        for ic in range(DC):
            tld = p.dma("sync", lambda e, ic=ic: e.dma_start(
                out=xcbuf[:, ic * XST + 3:(ic + 1) * XST],
                in_=dram["l2a_xc"].ap()[ic * 128:(ic + 1) * 128, :]),
                wait=[xc_toks[ic]])
            p.op("vector", lambda e, ic=ic: e.tensor_copy(
                xcbuf[:, ic * XST:ic * XST + 3], haloe[:, ic * 3:(ic + 1) * 3]),
                wait=[tld, (cV, halo_done)], inc=cV)
        xcb_done = cV.n
        u_toks, xs_toks = {}, {}
        for oc in range(DC):
            tcw = p.dma("sync", lambda e, oc=oc: e.dma_start(
                out=cwsl, in_=W["l2a_cw"][oc]), wait=gwait("l2a_cwsl"))
            for tc2 in range(2):
                bank = "psE" if tc2 == 0 else "psF"
                first = True
                for tap in range(4):
                    for ic in range(DC):
                        p.op("tensor", lambda e, tap=tap, ic=ic, tc2=tc2, bank=bank,
                             first=first: e.matmul(
                            ps[bank].ap()[:, :],
                            cwsl[:, tap * 2048 + ic * 128:tap * 2048 + (ic + 1) * 128],
                            xcbuf[:, ic * XST + tap + tc2 * 512:
                                  ic * XST + tap + tc2 * 512 + 512],
                            start=first, stop=(tap == 3 and ic == DC - 1)),
                            wait=[tcw, (cV, xcb_done)] + (gwait(bank) if first else []),
                            inc=(cT if (tap == 3 and ic == DC - 1) else None))
                        first = False
                mm = cT.n
                p.op("vector", lambda e, oc=oc, bank=bank: e.tensor_scalar(
                    piece, ps[bank].ap()[:, :], cbf[:, oc:oc + 1], None, ALU.add),
                    wait=[(cT, mm), tcb] + gwait("l2a_piece"), inc=cV)
                gset(bank, cV)
                pd = cV.n
                xs_toks[(oc, tc2)] = p.dma("gpsimd", lambda e, oc=oc, tc2=tc2: e.dma_start(
                    out=dram["l2a_xs"].ap()[oc * 128:(oc + 1) * 128,
                                            tc2 * 512:(tc2 + 1) * 512],
                    in_=piece), wait=[(cV, pd)])
                guard["l2a_piece"] = xs_toks[(oc, tc2)]
                p.op("scalar", lambda e: e.activation(upiece, piece, AF.Silu),
                     wait=[(cV, pd)] + gwait("l2a_upiece"), inc=cSc)
                ud = cSc.n
                u_toks[(oc, tc2)] = p.dma("gpsimd", lambda e, oc=oc, tc2=tc2: e.dma_start(
                    out=dram["l2a_u"].ap()[oc * 128:(oc + 1) * 128,
                                           tc2 * 512:(tc2 + 1) * 512],
                    in_=upiece), wait=[(cSc, ud)])
                guard["l2a_upiece"] = u_toks[(oc, tc2)]
                dbank = "psA" if tc2 == 0 else "psB"
                dbank2 = "psC" if tc2 == 0 else "psD"
                p.op("tensor", lambda e, oc=oc, dbank=dbank: e.matmul(
                    ps[dbank].ap()[:, :], wxsl[:, oc * 160:oc * 160 + 128], upiece,
                    start=(oc == 0), stop=(oc == DC - 1)),
                    wait=[twx, (cSc, ud)] + (gwait(dbank) if oc == 0 else []),
                    inc=None)
                p.op("tensor", lambda e, oc=oc, dbank2=dbank2: e.matmul(
                    ps[dbank2].ap()[0:32, :], wxsl[:, oc * 160 + 128:(oc + 1) * 160],
                    upiece, start=(oc == 0), stop=(oc == DC - 1)),
                    wait=(gwait(dbank2) if oc == 0 else []),
                    inc=(cT if oc == DC - 1 else None))
            gset("l2a_cwsl", cT)
        mmdbc = cT.n
        arena.release()
        fence()

        arena.mark()
        dpre = arena.alloc(T)
        bcb = arena.alloc(T)
        wdt = arena.alloc(D)
        bdtf = arena.alloc(16)
        dtile = arena.alloc(T)
        spz = arena.alloc(2 * T)
        for tc2 in range(2):
            p.op("vector", lambda e, tc2=tc2: e.tensor_copy(
                dpre[:, tc2 * 512:(tc2 + 1) * 512],
                ps["psA" if tc2 == 0 else "psB"].ap()[:, :]),
                wait=[(cT, mmdbc)], inc=cV)
            gset("psA" if tc2 == 0 else "psB", cV)
            p.op("vector", lambda e, tc2=tc2: e.tensor_copy(
                bcb[0:32, tc2 * 512:(tc2 + 1) * 512],
                ps["psC" if tc2 == 0 else "psD"].ap()[0:32, :]), inc=cV)
            gset("psC" if tc2 == 0 else "psD", cV)
        dpre_done = cV.n
        twd = p.dma("sync", lambda e: e.dma_start(out=wdt, in_=W["l2a_wdt"]))
        tbd = p.dma("sync", lambda e: e.dma_start(out=bdtf, in_=W["l2a_bdt"]))
        d_toks = {}
        for dc in range(DC):
            for tc2 in range(2):
                bank = "psE" if tc2 == 0 else "psF"
                p.op("tensor", lambda e, dc=dc, tc2=tc2, bank=bank: e.matmul(
                    ps[bank].ap()[:, :], wdt[:, dc * 128:(dc + 1) * 128],
                    dpre[:, tc2 * 512:(tc2 + 1) * 512], start=True, stop=True),
                    wait=[twd, (cV, dpre_done)] + gwait(bank), inc=cT)
                mm = cT.n
                zsl = spz[:, tc2 * 512:(tc2 + 1) * 512]
                asl = spz[:, 1024 + tc2 * 512:1024 + (tc2 + 1) * 512]
                p.op("vector", lambda e, dc=dc, bank=bank, zsl=zsl: e.tensor_scalar(
                    zsl, ps[bank].ap()[:, :], bdtf[:, dc:dc + 1], None, ALU.add),
                    wait=[(cT, mm), tbd] + gwait(("l2a_z", tc2)), inc=cV)
                gset(bank, cV)
                zd = cV.n
                p.op("scalar", lambda e, zsl=zsl, asl=asl: e.activation(
                    asl, zsl, AF.Abs), wait=[(cV, zd)], inc=cSc)
                p.op("scalar", lambda e, asl=asl: e.activation(
                    asl, asl, AF.Exp, scale=-1.0), inc=cSc)
                p.op("scalar", lambda e, asl=asl: e.activation(
                    asl, asl, AF.Ln, bias=1.0), inc=cSc)
                lnd = cSc.n
                p.op("vector", lambda e, dc=dc, tc2=tc2, zsl=zsl, asl=asl: e.scalar_tensor_tensor(
                    dtile[:, tc2 * 512:(tc2 + 1) * 512], zsl, 0.0, asl,
                    ALU.max, ALU.add),
                    wait=[(cSc, lnd)] + gwait("l2a_dtile"), inc=cV)
                guard[("l2a_z", tc2)] = (cV, cV.n)
            dd = cV.n
            d_toks[dc] = p.dma("gpsimd", lambda e, dc=dc: e.dma_start(
                out=dram["l2a_delta"].ap()[dc * 128:(dc + 1) * 128, :], in_=dtile),
                wait=[(cV, dd)] + gwait("l2a_dtile"))
            guard["l2a_dtile"] = d_toks[dc]
        bc_tok = p.dma("gpsimd", lambda e: e.dma_start(
            out=dram["l2a_bc"].ap(), in_=bcb[0:32, :]), wait=[(cV, dpre_done)])
        arena.release()
        fence()

        arena.mark()
        Bb = arena.alloc(NS * TC)
        Cb = arena.alloc(NS * TC)
        dAb2 = [arena.alloc(NS * TC), arena.alloc(NS * TC)]
        dBu2 = [arena.alloc(NS * TC), arena.alloc(NS * TC)]
        hb = arena.alloc(NS * TC)
        dsl2 = [arena.alloc(TC), arena.alloc(TC)]
        usl2 = [arena.alloc(TC), arena.alloc(TC)]
        ysl2 = [arena.alloc(TC), arena.alloc(TC)]
        flat = hb
        hprev = arena.alloc(256)
        hin = arena.alloc(256)
        hcA = arena.alloc(256)
        hcB = arena.alloc(256)
        Abuf = arena.alloc(256)
        zeroc = arena.alloc(1)

        p.op("vector", lambda e: e.memset(hprev, 0.0), inc=cV)
        p.op("vector", lambda e: e.memset(zeroc, 0.0), inc=cV)
        tal = p.dma("sync", lambda e: e.dma_start(
            out=Abuf.rearrange("p (c k) -> p c k", k=16), in_=W["l2a_alog"]))
        p.op("scalar", lambda e: e.activation(Abuf, Abuf, AF.Exp),
             wait=[tal], inc=cSc)
        sa = cSc.n
        p.op("vector", lambda e: e.tensor_scalar(Abuf, Abuf, -1.0, None, ALU.mult),
             wait=[(cSc, sa)], inc=cV)
        a_done = cV.n

        y_toks = {}
        for tcn in range(TCN):
            bcd = {}
            for which, dst, rowoff in (("B", Bb, 0), ("C", Cb, 16)):
                tfl = p.dma("gpsimd", lambda e, rowoff=rowoff, tcn=tcn: e.dma_start(
                    out=flat[0:1, 0:NS * TC],
                    in_=dram["l2a_bc"].ap()[rowoff:rowoff + 16,
                                            tcn * TC:(tcn + 1) * TC]),
                    wait=gwait("l2a_flat") + [bc_tok, (cV, cV.n)])
                dn = 0
                for j in range(0, NS * TC, 512):
                    p.op("tensor", lambda e, j=j: e.matmul(
                        ps["psG"].ap()[:, :], ones_r.ap()[:, :],
                        flat[0:1, j:j + 512], start=True, stop=True),
                        wait=[tfl, (cV, ones_ready)] + gwait("psG"), inc=cT)
                    td = cT.n
                    p.op("vector", lambda e, j=j, dst=dst: e.tensor_copy(
                        dst[:, j:j + 512], ps["psG"].ap()[:, :]),
                        wait=[(cT, td)], inc=cV)
                    gset("psG", cV)
                bcd[which] = cV.n
                guard["l2a_flat"] = (cV, cV.n)
            for dc in range(DC):
                sl = dc % 2
                dAb, dBu = dAb2[sl], dBu2[sl]
                dsl, usl, ysl = dsl2[sl], usl2[sl], ysl2[sl]
                tdl = p.dma("sync", lambda e, dc=dc, tcn=tcn, dsl=dsl: e.dma_start(
                    out=dsl, in_=dram["l2a_delta"].ap()[dc * 128:(dc + 1) * 128,
                                                        tcn * TC:(tcn + 1) * TC]),
                    wait=gwait(("l2a_dsl", sl)) + [d_toks[dc]])
                tul = p.dma("sync", lambda e, dc=dc, tcn=tcn, usl=usl: e.dma_start(
                    out=usl, in_=dram["l2a_u"].ap()[dc * 128:(dc + 1) * 128,
                                                    tcn * TC:(tcn + 1) * TC]),
                    wait=gwait(("l2a_usl", sl)) + [u_toks[(dc, 0)], u_toks[(dc, 1)]])
                for n in range(NS):
                    p.op("scalar", lambda e, dc=dc, n=n, dAb=dAb, dsl=dsl: e.activation(
                        dAb[:, n * TC:(n + 1) * TC], dsl, AF.Exp,
                        scale=Abuf[:, dc * 16 + n:dc * 16 + n + 1]),
                        wait=([tdl, (cV, a_done)] + gwait(("l2a_dAb", sl)))
                        if n == 0 else (), inc=cSc)
                da_done = cSc.n
                p.op("vector", lambda e, ysl=ysl, dsl=dsl, usl=usl: e.tensor_mul(
                    ysl, dsl, usl),
                    wait=[tul, tdl] + gwait(("l2a_ysl", sl)), inc=cV)
                p.op("vector", lambda e, dBu=dBu, ysl=ysl: e.tensor_tensor(
                    dBu.rearrange("p (n t) -> p n t", t=TC),
                    ysl.unsqueeze(1).broadcast_to([128, NS, TC]),
                    Bb.rearrange("p (n t) -> p n t", t=TC), ALU.mult),
                    wait=[(cV, bcd["B"])], inc=cV)
                dbu_done = cV.n
                for n in range(NS):
                    p.op("vector", lambda e, dc=dc, n=n, dAb=dAb, dBu=dBu: e.tensor_tensor_scan(
                        hb[:, n * TC:(n + 1) * TC], dAb[:, n * TC:(n + 1) * TC],
                        dBu[:, n * TC:(n + 1) * TC],
                        hprev[:, dc * 16 + n:dc * 16 + n + 1],
                        ALU.mult, ALU.add),
                        wait=[(cSc, da_done), (cV, dbu_done)], inc=cV)
                p.op("vector", lambda e, dc=dc: e.tensor_copy(
                    hprev[:, dc * 16:(dc + 1) * 16],
                    hb[:, TC - 1:NS * TC:TC]), inc=cV)
                guard[("l2a_usl", sl)] = (cV, cV.n)
                p.op("vector", lambda e, dAb=dAb: e.tensor_tensor(
                    dAb.rearrange("p (n t) -> p n t", t=TC),
                    hb.rearrange("p (n t) -> p n t", t=TC),
                    Cb.rearrange("p (n t) -> p n t", t=TC), ALU.mult),
                    wait=[(cV, bcd["C"])], inc=cV)
                guard[("l2a_dAb", sl)] = (cV, cV.n)
                p.op("vector", lambda e, ysl=ysl, dAb=dAb: e.reduce_sum(
                    ysl, dAb.rearrange("p (n t) -> p t n", t=TC), AX.X), inc=cV)
                guard[("l2a_dsl", sl)] = (cV, cV.n)
                yd = cV.n
                y_toks[(dc, tcn)] = p.dma("gpsimd", lambda e, dc=dc, tcn=tcn, ysl=ysl: e.dma_start(
                    out=dram["l2a_y"].ap()[dc * 128:(dc + 1) * 128,
                                           tcn * TC:(tcn + 1) * TC], in_=ysl),
                    wait=[(cV, yd)])
                guard[("l2a_ysl", sl)] = y_toks[(dc, tcn)]

        hd = cV.n
        thx = p.dma("gpsimd", lambda e: e.dma_start(
            out=dram["l2a_hin"].ap(), in_=hprev), wait=[(cV, hd)])
        p.op("gpsimd", lambda e: e.collective_compute(
            "AllGather", ALU.bypass, replica_groups=RG,
            ins=[dram["l2a_hin"].ap()], outs=[dram["l2a_hout"].ap()]),
            wait=[thx], inc=cG)
        ccd = cG.n
        t0 = p.dma("gpsimd", lambda e: e.dma_start(
            out=hcA, in_=dram["l2a_hout"].ap()[0:128, :]), wait=[(cG, ccd)])
        t1 = p.dma("gpsimd", lambda e: e.dma_start(
            out=hcB, in_=dram["l2a_hout"].ap()[128:256, :]), wait=[(cG, ccd)])
        p.op("vector", lambda e: e.tensor_scalar(
            hin, hcA, gatesb.ap()[:, 0:1], None, ALU.mult), wait=[t0, t1], inc=cV)
        p.op("vector", lambda e: e.scalar_tensor_tensor(
            hin, hcB, gatesb.ap()[:, 1:2], hin, ALU.mult, ALU.add), inc=cV)
        hin_done = cV.n

        tdd = p.dma("sync", lambda e: e.dma_start(out=hcA[:, 0:16],
                                                  in_=W["l2a_dD"]),
                    wait=[(cV, hin_done)])
        z_toks = {}
        hp2 = hcB
        p.op("vector", lambda e: e.tensor_copy(hp2, hin), wait=[(cV, hin_done)],
             inc=cV)
        for tcn in range(TCN):
            tfl = p.dma("gpsimd", lambda e, tcn=tcn: e.dma_start(
                out=flat[0:1, 0:NS * TC],
                in_=dram["l2a_bc"].ap()[16:32, tcn * TC:(tcn + 1) * TC]),
                wait=gwait("l2a_flat") + [bc_tok, (cV, cV.n)])
            for j in range(0, NS * TC, 512):
                p.op("tensor", lambda e, j=j: e.matmul(
                    ps["psG"].ap()[:, :], ones_r.ap()[:, :],
                    flat[0:1, j:j + 512], start=True, stop=True),
                    wait=[tfl, (cV, ones_ready)] + gwait("psG"), inc=cT)
                td = cT.n
                p.op("vector", lambda e, j=j: e.tensor_copy(
                    Cb[:, j:j + 512], ps["psG"].ap()[:, :]),
                    wait=[(cT, td)], inc=cV)
                gset("psG", cV)
            cb_done = cV.n
            guard["l2a_flat"] = (cV, cV.n)
            for dc in range(DC):
                sl = dc % 2
                dAb, dsl, usl, ysl = dAb2[sl], dsl2[sl], usl2[sl], ysl2[sl]
                rsl = dBu2[sl][:, 0:TC]
                tdl = p.dma("sync", lambda e, dc=dc, tcn=tcn, dsl=dsl: e.dma_start(
                    out=dsl, in_=dram["l2a_delta"].ap()[dc * 128:(dc + 1) * 128,
                                                        tcn * TC:(tcn + 1) * TC]),
                    wait=gwait(("l2a_dsl", sl)))
                for n in range(NS):
                    p.op("scalar", lambda e, dc=dc, n=n, dAb=dAb, dsl=dsl: e.activation(
                        dAb[:, n * TC:(n + 1) * TC], dsl, AF.Exp,
                        scale=Abuf[:, dc * 16 + n:dc * 16 + n + 1]),
                        wait=([tdl] + gwait(("l2a_dAb", sl))) if n == 0 else (),
                        inc=cSc)
                da_done = cSc.n
                guard[("l2a_dsl", sl)] = (cSc, da_done)
                for n in range(NS):
                    p.op("vector", lambda e, dc=dc, n=n, dAb=dAb: e.tensor_tensor_scan(
                        hb[:, n * TC:(n + 1) * TC], dAb[:, n * TC:(n + 1) * TC],
                        zeroc.broadcast_to([128, TC]),
                        hp2[:, dc * 16 + n:dc * 16 + n + 1],
                        ALU.mult, ALU.add),
                        wait=[(cSc, da_done)], inc=cV)
                p.op("vector", lambda e, dc=dc: e.tensor_copy(
                    hp2[:, dc * 16:(dc + 1) * 16],
                    hb[:, TC - 1:NS * TC:TC]), inc=cV)
                p.op("vector", lambda e, dAb=dAb: e.tensor_tensor(
                    dAb.rearrange("p (n t) -> p n t", t=TC),
                    hb.rearrange("p (n t) -> p n t", t=TC),
                    Cb.rearrange("p (n t) -> p n t", t=TC), ALU.mult),
                    wait=[(cV, cb_done)], inc=cV)
                guard[("l2a_dAb", sl)] = (cV, cV.n)
                p.op("vector", lambda e, usl=usl, dAb=dAb: e.reduce_sum(
                    usl, dAb.rearrange("p (n t) -> p t n", t=TC), AX.X),
                    wait=gwait(("l2a_usl", sl)), inc=cV)
                ycd = cV.n
                tyl = p.dma("sync", lambda e, dc=dc, tcn=tcn, ysl=ysl: e.dma_start(
                    out=ysl, in_=dram["l2a_y"].ap()[dc * 128:(dc + 1) * 128,
                                                    tcn * TC:(tcn + 1) * TC]),
                    wait=gwait(("l2a_ysl", sl)) + [y_toks[(dc, tcn)]])
                p.op("vector", lambda e, ysl=ysl, usl=usl: e.tensor_add(
                    ysl, ysl, usl), wait=[tyl], inc=cV)
                ad_ = cV.n
                tu2 = p.dma("sync", lambda e, dc=dc, tcn=tcn, usl=usl: e.dma_start(
                    out=usl, in_=dram["l2a_u"].ap()[dc * 128:(dc + 1) * 128,
                                                    tcn * TC:(tcn + 1) * TC]),
                    wait=[(cV, ad_)])
                tr2 = p.dma("sync", lambda e, dc=dc, tcn=tcn, rsl=rsl: e.dma_start(
                    out=rsl, in_=dram["l2a_r"].ap()[dc * 128:(dc + 1) * 128,
                                                    tcn * TC:(tcn + 1) * TC]),
                    wait=gwait(("l2a_rsl", sl)))
                p.op("scalar", lambda e, rsl=rsl: e.activation(rsl, rsl, AF.Silu),
                     wait=[tr2], inc=cSc)
                sr = cSc.n
                p.op("vector", lambda e, dc=dc, ysl=ysl, usl=usl: e.scalar_tensor_tensor(
                    ysl, usl, hcA[:, dc:dc + 1], ysl, ALU.mult, ALU.add),
                    wait=[tu2, tdd], inc=cV)
                guard[("l2a_usl", sl)] = (cV, cV.n)
                p.op("vector", lambda e, ysl=ysl, rsl=rsl: e.tensor_mul(
                    ysl, ysl, rsl), wait=[(cSc, sr)], inc=cV)
                zd = cV.n
                guard[("l2a_rsl", sl)] = (cV, zd)
                z_toks[(dc, tcn)] = p.dma("gpsimd", lambda e, dc=dc, tcn=tcn, ysl=ysl: e.dma_start(
                    out=dram["l2a_z"].ap()[dc * 128:(dc + 1) * 128,
                                           tcn * TC:(tcn + 1) * TC], in_=ysl),
                    wait=[(cV, zd)])
                guard[("l2a_ysl", sl)] = z_toks[(dc, tcn)]
        arena.release()
        fence()

        arena.mark()
        zbuf = arena.alloc(DC * T)
        wosl = arena.alloc(2 * D)
        xspc = arena.alloc(512)
        ofm_t = arena.alloc(512)
        sq = zbuf[:, 0:D]
        tz = {}
        for dc in range(DC):
            tz[dc] = p.dma("sync", lambda e, dc=dc: e.dma_start(
                out=zbuf[:, dc * T:(dc + 1) * T],
                in_=dram["l2a_z"].ap()[dc * 128:(dc + 1) * 128, :]),
                wait=[z_toks[(dc, tcn2)] for tcn2 in range(TCN)])
        res_done = {}
        for oc in range(DC):
            slot = oc % 2
            wo_s = wosl[:, slot * D:(slot + 1) * D]
            two = p.dma("sync", lambda e, oc=oc, wo_s=wo_s: e.dma_start(
                out=wo_s, in_=W["l2a_wout"][oc]), wait=gwait(("l2a_wo", slot)))
            for tc2 in range(2):
                bank = "psE" if tc2 == 0 else "psF"
                for dc in range(DC):
                    p.op("tensor", lambda e, dc=dc, tc2=tc2, wo_s=wo_s, bank=bank: e.matmul(
                        ps[bank].ap()[:, :], wo_s[:, dc * 128:(dc + 1) * 128],
                        zbuf[:, dc * T + tc2 * 512:dc * T + (tc2 + 1) * 512],
                        start=(dc == 0), stop=(dc == DC - 1)),
                        wait=[two, tz[dc]] + (gwait(bank) if dc == 0 else []),
                        inc=(cT if dc == DC - 1 else None))
                mm = cT.n
                txs = p.dma("sync", lambda e, oc=oc, tc2=tc2: e.dma_start(
                    out=xspc, in_=dram["l2a_xs"].ap()[oc * 128:(oc + 1) * 128,
                                                      tc2 * 512:(tc2 + 1) * 512]),
                    wait=gwait("l2a_xspc") + [xs_toks[(oc, tc2)]])
                p.op("vector", lambda e, bank=bank: e.tensor_add(
                    ofm_t, ps[bank].ap()[:, :], xspc),
                    wait=[(cT, mm), txs] + gwait("l2a_ofmt"), inc=cV)
                gset(bank, cV)
                guard["l2a_xspc"] = (cV, cV.n)
                od = cV.n
                for j in range(4):
                    tbank = TPB[j]
                    p.op("tensor", lambda e, j=j, tbank=tbank: e.transpose(
                        ps[tbank].ap()[:, 0:128], ofm_t[:, j * 128:(j + 1) * 128],
                        sident.ap()[:, :]), wait=[(cV, od)] + gwait(tbank), inc=cT)
                    td = cT.n
                    tt = tc2 * 4 + j
                    p.op("vector", lambda e, tt=tt, oc=oc, tbank=tbank: e.tensor_add(
                        xtile(tt)[:, oc * 128:(oc + 1) * 128],
                        xtile(tt)[:, oc * 128:(oc + 1) * 128],
                        ps[tbank].ap()[:, 0:128]), wait=[(cT, td)], inc=cV)
                    gset(tbank, cV)
                    res_done[(tt, oc)] = (cV, cV.n)
                guard["l2a_ofmt"] = (cV, cV.n)
            gset(("l2a_wo", slot), cT)
        all_mm = (cT, cT.n)
        ln_waits = []
        for tt in range(TT):
            ln_waits.append(ln_tile_inplace(
                xtile(tt), sq,
                wait=[res_done[(tt, oc)] for oc in range(DC)] + [all_mm]))
        arena.release()
        arena.release()
        return [(cSc, max(ln_waits))]

    def stage_l3a(ready):
        return stage_attn(ready, "l3a", 16, False)

    def stage_l0a(ready):
        return stage_attn(ready, "l0a", 4, True)

    stage_fns = {"l1f": stage_l1f, "l0f": stage_l0f, "l3f": stage_l3f,
                 "l1a": stage_l1a, "l3a": stage_l3a, "l0a": stage_l0a,
                 "l2f": stage_l2f, "l2a": stage_l2a}
    if gpre is not None:
        build_l2f_prepass(ppbuf)
    ready = stream_ready
    for s in stages:
        if s not in stage_fns:
            raise NotImplementedError(s)
        p.wait("sync", ready)
        p.wait("gpsimd", ready)
        ready = stage_fns[s](ready)

    toks = []
    for tt in range(TT):
        toks.append(p.dma("gpsimd", lambda e, tt=tt: e.dma_start(
            out=out[tt * 128:(tt + 1) * 128, :], in_=xtile(tt)),
            wait=list(ready)))
    p.wait("gpsimd", toks)

    p.finalize()
    return nc



def _ecslabs(wt):
    nec = wt.shape[1] // 128
    return np.ascontiguousarray(
        wt.reshape(16, 128, nec, 128).transpose(2, 1, 0, 3).reshape(nec, 128, 2048))


def _attn_prep(sh, pre, WqT, WkT, WvT, WoT, bq, bk, bv, bo, kvc):
    sh[pre + "_wq"] = _ecslabs(WqT)
    sh[pre + "_wk"] = _ecslabs(WkT)
    sh[pre + "_wv"] = np.ascontiguousarray(WvT.reshape(16, 128, kvc * 128))
    sh[pre + "_woT"] = np.ascontiguousarray(WoT.reshape(16, 128, 2048))
    sh[pre + "_bqf"] = np.ascontiguousarray(bq.reshape(16, 128).T)
    sh[pre + "_bkf"] = np.ascontiguousarray(bk.reshape(kvc, 128).T)
    sh[pre + "_bv"] = np.ascontiguousarray(bv[None, :])
    sh[pre + "_bo"] = np.ascontiguousarray(bo[None, :])


def _tri_patterns():
    pats = []
    for i in range(4):
        m = np.zeros((128, 512), np.float32)
        for qt in range(4):
            if qt < i:
                m[:, qt * 128:(qt + 1) * 128] = NEG
            elif qt == i:
                m[:, qt * 128:(qt + 1) * 128] = np.where(
                    np.arange(128)[:, None] <= np.arange(128)[None, :], 0.0, NEG
                ).astype(np.float32)
        pats.append(m)
    return np.concatenate(pats, axis=1)


def _rope_tables(t0):
    d = 128
    invf = 1.0 / (10000.0 ** (np.arange(0, d, 2) / d))
    tpos = np.arange(t0, t0 + T, dtype=np.float64)
    emb = np.concatenate([np.outer(invf, tpos)] * 2, axis=0)
    cos = np.cos(emb).astype(np.float32)
    sin = np.sin(emb).astype(np.float32)
    return np.ascontiguousarray(cos), np.ascontiguousarray(sin)


def _prep_weights(inputs, stages=ALL_STAGES):
    sh = {"ident": np.eye(128, dtype=np.float32)}
    f32 = lambda a: np.asarray(a, dtype=np.float32)
    if "l3a" in stages:
        WqkvT = f32(inputs["L3_attn_W_qkv"]).T
        bqkv = f32(inputs["L3_attn_b_qkv"])
        _attn_prep(sh, "l3a", WqkvT[:, 0:2048], WqkvT[:, 2048:4096],
                   WqkvT[:, 4096:6144], f32(inputs["L3_attn_W_out"]).T,
                   bqkv[0:2048], bqkv[2048:4096], bqkv[4096:6144],
                   f32(inputs["L3_attn_b_out"]), 16)
    if "l0a" in stages:
        z = np.zeros(2048, np.float32)
        zk = np.zeros(512, np.float32)
        _attn_prep(sh, "l0a", f32(inputs["L0_attn_Wq"]).T,
                   f32(inputs["L0_attn_Wk"]).T, f32(inputs["L0_attn_Wv"]).T,
                   f32(inputs["L0_attn_Wo"]).T, z, zk, zk, z, 4)
    if "l0f" in stages:
        Wb = f32(inputs["L0_ffn_W"])
        sh["l0f_wT"] = np.ascontiguousarray(Wb.T.reshape(16, 128, 2048))
        acat = f32(inputs["L0_ffn_lora_A"]).transpose(2, 0, 1).reshape(2048, 128)
        sh["l0f_acat"] = np.ascontiguousarray(
            acat.reshape(16, 128, 128).transpose(1, 0, 2).reshape(128, 2048))
        sh["l0f_bcat"] = np.ascontiguousarray(
            f32(inputs["L0_ffn_lora_B"]).transpose(0, 2, 1).reshape(128, 2048))
        gwt = f32(inputs["L0_ffn_gate_w"]).T
        sh["l0f_gwt"] = np.ascontiguousarray(
            gwt.reshape(16, 128, 8).transpose(1, 0, 2).reshape(128, 128))
        sh["l0f_brow"] = np.ascontiguousarray(f32(inputs["L0_ffn_b"])[None, :])
        sh["l0f_gbrow"] = np.ascontiguousarray(f32(inputs["L0_ffn_gate_b"])[None, :])
    if "l3f" in stages:
        W1 = f32(inputs["L3_ffn_W1"])
        sh["l3f_w1"] = np.ascontiguousarray(
            W1.reshape(16, 128, 16, 128).transpose(2, 1, 0, 3).reshape(16, 128, 2048))
        vcat = f32(inputs["L3_ffn_V"]).transpose(2, 0, 1).reshape(2048, 512)
        sh["l3f_vcat"] = np.ascontiguousarray(vcat.reshape(16, 128, 512))
        ucat = f32(inputs["L3_ffn_U"]).transpose(0, 2, 1).reshape(512, 2048)
        sh["l3f_ucat"] = np.ascontiguousarray(ucat.reshape(4, 128, 2048))
        w2 = f32(inputs["L3_ffn_W2"])
        sh["l3f_w2"] = np.ascontiguousarray(
            w2.reshape(16, 128, 8).transpose(1, 0, 2).reshape(128, 128))
        sh["l3f_b1f"] = np.ascontiguousarray(f32(inputs["L3_ffn_b1"]).reshape(16, 128).T)
        sh["l3f_b2row"] = np.ascontiguousarray(f32(inputs["L3_ffn_b2"])[None, :])
    if "l1a" in stages:
        for nm, key in (("l1a_wq", "L1_attn_Wq"), ("l1a_wk", "L1_attn_Wk"),
                        ("l1a_wv", "L1_attn_Wv")):
            wt_ = f32(inputs[key]).T
            sh[nm] = np.ascontiguousarray(
                wt_.reshape(16, 128, 16, 128).transpose(2, 1, 0, 3).reshape(16, 128, 2048))
        sh["l1a_woT"] = np.ascontiguousarray(f32(inputs["L1_attn_Wo"]).T.reshape(16, 128, 2048))
        for nm, key in (("l1a_bq", "L1_attn_bq"), ("l1a_bk", "L1_attn_bk"),
                        ("l1a_bv", "L1_attn_bv")):
            sh[nm] = np.ascontiguousarray(f32(inputs[key]).reshape(16, 128).T)
        sh["l1a_bo"] = np.ascontiguousarray(f32(inputs["L1_attn_bo"])[None, :])
    if "l2a" in stages:
        Win = f32(inputs["L2_attn_W_in"])
        sh["l2a_win"] = np.ascontiguousarray(
            Win.reshape(16, 128, 32, 128).transpose(2, 1, 0, 3).reshape(32, 128, 2048))
        cwT = f32(inputs["L2_attn_conv_w"]).transpose(2, 1, 0)
        sh["l2a_cw"] = np.ascontiguousarray(
            cwT.reshape(4, 16, 128, 16, 128).transpose(3, 2, 0, 1, 4)
            .reshape(16, 128, 8192))
        sh["l2a_convb"] = np.ascontiguousarray(
            f32(inputs["L2_attn_conv_b"]).reshape(16, 128).T)
        wx = f32(inputs["L2_attn_W_x"])
        sh["l2a_wx"] = np.ascontiguousarray(
            wx.reshape(16, 128, 160).transpose(1, 0, 2).reshape(128, 16 * 160))
        sh["l2a_wdt"] = np.ascontiguousarray(f32(inputs["L2_attn_W_dt"]))
        sh["l2a_bdt"] = np.ascontiguousarray(
            f32(inputs["L2_attn_b_dt"]).reshape(16, 128).T)
        sh["l2a_alog"] = np.ascontiguousarray(
            f32(inputs["L2_attn_A_log"]).reshape(16, 128, 16))
        sh["l2a_dD"] = np.ascontiguousarray(
            f32(inputs["L2_attn_D"]).reshape(16, 128).T)
        sh["l2a_wout"] = np.ascontiguousarray(
            f32(inputs["L2_attn_W_out"]).reshape(16, 128, 16, 128)
            .transpose(2, 1, 0, 3).reshape(16, 128, 2048))
    if "l2f" in stages:
        sh["l2f_gw"] = np.ascontiguousarray(
            f32(inputs["L2_ffn_gate_w"]).T.reshape(16, 128, 8192))
        sh["l2f_vw"] = np.ascontiguousarray(
            f32(inputs["L2_ffn_val_w"]).T.reshape(16, 128, 8192))
        sh["l2f_ow"] = np.ascontiguousarray(
            f32(inputs["L2_ffn_out_w"]).T.reshape(64, 128, 2048))
        sh["l2f_gb"] = np.ascontiguousarray(f32(inputs["L2_ffn_gate_b"])[None, :])
        sh["l2f_vb"] = np.ascontiguousarray(f32(inputs["L2_ffn_val_b"])[None, :])
        sh["l2f_ob"] = np.ascontiguousarray(f32(inputs["L2_ffn_out_b"])[None, :])
    if "l1f" in stages:
        gwT = f32(inputs["L1_ffn_gate_w"]).T
        uwT = f32(inputs["L1_ffn_up_w"]).T
        dwT = f32(inputs["L1_ffn_down_w"]).T
        sh["l1f_gw"] = np.ascontiguousarray(
            gwT.reshape(16, 128, 44, 128).transpose(2, 1, 0, 3).reshape(44, 128, 2048))
        sh["l1f_uw"] = np.ascontiguousarray(
            uwT.reshape(16, 128, 44, 128).transpose(2, 1, 0, 3).reshape(44, 128, 2048))
        sh["l1f_dw"] = np.ascontiguousarray(dwT.reshape(44, 128, 2048))
        sh["l1f_gs"] = np.ascontiguousarray(
            f32(inputs["L1_ffn_gate_s"]).reshape(44, 128).T)
        sh["l1f_us"] = np.ascontiguousarray(
            f32(inputs["L1_ffn_up_s"]).reshape(44, 128).T)
        sh["l1f_ds"] = np.ascontiguousarray(f32(inputs["L1_ffn_down_s"])[None, :])
    return sh


def _percore_maps(inputs, stages=ALL_STAGES):
    hs = np.asarray(inputs["L0_attn_hidden_states"], dtype=np.float32)
    sh = _prep_weights(inputs, stages)
    maps = []
    pats = _tri_patterns()
    zer = np.zeros_like(pats)
    triA = np.ascontiguousarray(np.concatenate([pats, zer], axis=1))
    triB = np.ascontiguousarray(np.concatenate([zer, pats], axis=1))
    kmA = np.zeros((128, 32), np.float32)
    for qb in range(2):
        for gg in range(8, 16):
            kmA[:, qb * 16 + gg] = NEG
        for gg in range(4 * qb + 4, 8):
            kmA[:, qb * 16 + gg] = NEG
    kmB = np.zeros((128, 32), np.float32)
    for core in range(8):
        seq, t0 = core // 2, (core % 2) * T
        m = dict(sh)
        m["x0"] = np.ascontiguousarray(hs[seq, t0:t0 + T, :])
        g = np.zeros((128, 8), dtype=np.float32)
        if core % 2 == 1:
            g[:, 0] = 1.0
            g[:, 2] = 1.0
        else:
            g[:, 3] = 1.0
        m["gates"] = g
        for pre in ("l3a", "l0a"):
            if pre + "_trisel" not in sh and pre + "_wq" not in sh:
                continue
            m[pre + "_trisel"] = triB if core % 2 else triA
            m[pre + "_kmaskq"] = kmB if core % 2 else kmA
        if "l0a_wq" in sh:
            cos, sinm = _rope_tables(t0)
            m["l0a_cos"] = cos
            m["l0a_sinm"] = sinm
            R = np.zeros((128, 128), np.float32)
            for dd in range(64):
                R[dd, dd + 64] = -1.0
                R[dd + 64, dd] = 1.0
            m["rotm"] = np.ascontiguousarray(R.T)
        maps.append(m)
    return maps


def run(inputs, stages=ALL_STAGES):
    maps = _percore_maps(inputs, stages)
    nc = build_program(stages)
    res = run_bass_kernel_spmd(nc, maps, core_ids=list(range(8)))
    outa = np.empty((4, 2048, 2048), dtype=np.float32)
    for core in range(8):
        seq, t0 = core // 2, (core % 2) * T
        outa[seq, t0:t0 + T, :] = res.results[core]["out"]
    return outa


def kernel(**inputs):
    return run(inputs, ALL_STAGES)


# version 10
# speedup vs baseline: 1.0342x; 1.0127x over previous
import numpy as np
import concourse.bass as bass
import concourse.mybir as mybir
from concourse.bass_utils import run_bass_kernel_spmd

F32 = mybir.dt.float32
BF16 = mybir.dt.bfloat16
I32 = mybir.dt.int32
AF = mybir.ActivationFunctionType
ALU = mybir.AluOpType
AX = mybir.AxisListType

T = 1024
D = 2048
TT = T // 128
DC = D // 128
NEG = -1.0e4
EPS = 1e-5

ENGINES = ("sync", "tensor", "vector", "scalar", "gpsimd")
L1F_CUT = 4
ALL_STAGES = ("l0a", "l0f", "l1a", "l1f", "l2a", "l2f", "l3a", "l3f")


class Ctr:
    def __init__(self, nc, name):
        self.h = nc.alloc_semaphore(name)
        self.n = 0


class Prog:

    def __init__(self, nc):
        self.nc = nc
        self.q = {e: [] for e in ENGINES}
        self.ctrs = {}
        self.pools = {}
        self.pool_i = {}
        self.waited = {e: {} for e in ENGINES}
        self.selfc = {}

    def ctr(self, name):
        if name not in self.ctrs:
            self.ctrs[name] = Ctr(self.nc, name)
        return self.ctrs[name]

    def _dedup(self, e, wait):
        w = []
        for c, v in wait:
            if v <= 0:
                continue
            key = id(c)
            if v > self.waited[e].get(key, 0):
                self.waited[e][key] = v
                w.append((c.h, v))
        return w

    def op(self, e, fn, wait=(), inc=None, k=1):
        selfc = self.selfc.get(e)
        wait = list(wait)
        if selfc is not None:
            wait.append((selfc, selfc.n))
            if inc is None:
                inc = selfc
        w = self._dedup(e, wait)
        if inc is not None:
            inc.n += k

        def g(eng, w=w, fn=fn, inc=inc, k=k):
            for h, v in w:
                eng.wait_ge(h, v)
            bi = fn(eng)
            if inc is not None:
                bi.then_inc(inc.h, k)

        self.q[e].append(g)
        return inc.n if inc is not None else None

    def dma(self, e, fn, wait=(), pool=None, nctr=8):
        pool = pool or e
        key = (e, pool)
        if key not in self.pools:
            self.pools[key] = [self.ctr(f"dma_{e}_{pool}_{i}") for i in range(nctr)]
            self.pool_i[key] = -1
        self.pool_i[key] = (self.pool_i[key] + 1) % len(self.pools[key])
        c = self.pools[key][self.pool_i[key]]
        self.op(e, fn, wait=list(wait) + [(c, c.n)], inc=c, k=16)
        return (c, c.n)

    def wait(self, e, waits):
        w = self._dedup(e, waits)
        if not w:
            return

        def g(eng, w=w):
            for h, v in w:
                eng.wait_ge(h, v)

        self.q[e].append(g)

    def finalize(self):
        with self.nc.Block() as block:
            def runner(stream):
                def body(eng):
                    for g in stream:
                        g(eng)
                return body
            block.sync(runner(self.q["sync"]))
            block.tensor(runner(self.q["tensor"]))
            block.vector(runner(self.q["vector"]))
            block.scalar(runner(self.q["scalar"]))
            block.gpsimd(runner(self.q["gpsimd"]))


class Arena:
    def __init__(self, nc, name, cols):
        self.t = nc.alloc_sbuf_tensor(name, [128, cols], F32)
        self.cols = cols
        self.off = 0
        self.marks = []
        self.hi = 0

    def alloc(self, cols):
        assert self.off + cols <= self.cols, \
            f"arena overflow {self.off}+{cols}>{self.cols}"
        ap = self.t.ap()[:, self.off:self.off + cols]
        self.off += cols
        self.hi = max(self.hi, self.off)
        return ap

    def mark(self):
        self.marks.append(self.off)

    def release(self):
        self.off = self.marks.pop()



def build_program(stages=ALL_STAGES):
    nc = bass.Bass("TRN2", target_bir_lowering=False, debug=False)
    p = Prog(nc)

    def din(name, shape, dtype=F32):
        return nc.dram_tensor(name, list(shape), dtype, kind="ExternalInput").ap()

    x0 = din("x0", (T, D))
    ident_in = din("ident", (128, 128))
    gates_in = din("gates", (128, 8))
    rotm_in = din("rotm", (128, 128)) if "l0a" in stages else None

    W = {}

    def wdecl(name, shape):
        W[name] = din(name, shape)

    if "l0f" in stages:
        wdecl("l0f_wT", (DC, 128, D))
        wdecl("l0f_acat", (128, 2048))
        wdecl("l0f_bcat", (128, 2048))
        wdecl("l0f_gwt", (128, 128))
        wdecl("l0f_brow", (1, D))
        wdecl("l0f_gbrow", (1, 8))
    if "l3f" in stages:
        wdecl("l3f_w1", (DC, 128, D))
        wdecl("l3f_vcat", (DC, 128, 512))
        wdecl("l3f_ucat", (4, 128, D))
        wdecl("l3f_w2", (128, 128))
        wdecl("l3f_b1f", (128, 16))
        wdecl("l3f_b2row", (1, 8))
    if "l1a" in stages:
        wdecl("l1a_wq", (DC, 128, D))
        wdecl("l1a_wk", (DC, 128, D))
        wdecl("l1a_wv", (DC, 128, D))
        wdecl("l1a_woT", (DC, 128, D))
        wdecl("l1a_bq", (128, 16))
        wdecl("l1a_bk", (128, 16))
        wdecl("l1a_bv", (128, 16))
        wdecl("l1a_bo", (1, D))
    if "l2a" in stages:
        wdecl("l2a_win", (32, 128, D))
        wdecl("l2a_cw", (DC, 128, 8192))
        wdecl("l2a_convb", (128, 16))
        wdecl("l2a_wx", (128, DC * 160))
        wdecl("l2a_wdt", (128, D))
        wdecl("l2a_bdt", (128, 16))
        wdecl("l2a_alog", (DC, 128, 16))
        wdecl("l2a_dD", (128, 16))
        wdecl("l2a_wout", (DC, 128, D))
    if "l2f" in stages:
        wdecl("l2f_gw", (DC, 128, 8192))
        wdecl("l2f_vw", (DC, 128, 8192))
        wdecl("l2f_ow", (64, 128, D))
        wdecl("l2f_gb", (1, 8192))
        wdecl("l2f_vb", (1, 8192))
        wdecl("l2f_ob", (1, D))
    if "l1f" in stages:
        wdecl("l1f_gw", (44, 128, D))
        wdecl("l1f_uw", (44, 128, D))
        wdecl("l1f_dw", (44, 128, D))
        wdecl("l1f_gs", (128, 44))
        wdecl("l1f_us", (128, 44))
        wdecl("l1f_ds", (1, D))

    out = nc.dram_tensor("out", [T, D], F32, kind="ExternalOutput").ap()

    dram = {}
    for pre, kvc in (("l3a", 16), ("l0a", 4)):
        if pre not in stages:
            continue
        KVp = kvc * 128
        wdecl(pre + "_wq", (DC, 128, D))
        wdecl(pre + "_wk", (kvc, 128, D))
        wdecl(pre + "_wv", (DC, 128, KVp))
        wdecl(pre + "_woT", (DC, 128, D))
        wdecl(pre + "_bqf", (128, 16))
        wdecl(pre + "_bkf", (128, max(kvc, 1)))
        wdecl(pre + "_bv", (1, KVp))
        wdecl(pre + "_bo", (1, D))
        wdecl(pre + "_trisel", (128, 4096))
        wdecl(pre + "_kmaskq", (128, 32))
        if pre == "l0a":
            wdecl(pre + "_cos", (128, T))
            wdecl(pre + "_sinm", (128, T))
        dram[pre + "_qf"] = nc.dram_tensor(pre + "_qf", [D, T], F32)
        dram[pre + "_kin"] = nc.dram_tensor(pre + "_kin", [KVp, T], F32)
        dram[pre + "_kout"] = nc.dram_tensor(pre + "_kout", [2 * KVp, T], F32)
        dram[pre + "_vin"] = nc.dram_tensor(pre + "_vin", [T, KVp], F32)
        dram[pre + "_vout"] = nc.dram_tensor(pre + "_vout", [2 * T, KVp], F32)
        dram[pre + "_mkin"] = nc.dram_tensor(pre + "_mkin", [1, 2], F32)
        dram[pre + "_mkout"] = nc.dram_tensor(pre + "_mkout", [2, 2], F32)
    if "l2a" in stages:
        for nm in ("l2a_xc", "l2a_r", "l2a_xs", "l2a_u", "l2a_delta",
                   "l2a_y", "l2a_z"):
            dram[nm] = nc.dram_tensor(nm, [D, T], F32)
        dram["l2a_bc"] = nc.dram_tensor("l2a_bc", [32, T], F32)
        dram["l2a_hloin"] = nc.dram_tensor("l2a_hloin", [128, 48], F32)
        dram["l2a_hloout"] = nc.dram_tensor("l2a_hloout", [256, 48], F32)
        dram["l2a_hin"] = nc.dram_tensor("l2a_hin", [128, 256], F32)
        dram["l2a_hout"] = nc.dram_tensor("l2a_hout", [256, 256], F32)
    if "l2f" in stages:
        dram["l2f_h"] = nc.dram_tensor("l2f_h", [T, 8192], F32)
        if "l2a" in stages:
            dram["l2f_tg"] = nc.dram_tensor("l2f_tg", [DC, 16, 128, 512], BF16)
            dram["l2f_tv"] = nc.dram_tensor("l2f_tv", [DC, 16, 128, 512], BF16)
            dram["l2f_to"] = nc.dram_tensor("l2f_to", [64, 4, 128, 512], BF16)
        dram["l2f_kh"] = nc.dram_tensor("l2f_kh", [64, 128, T], BF16)
    if "l1a" in stages:
        dram["l1a_q"] = nc.dram_tensor("l1a_q", [D, T], F32)
        dram["l1a_kv"] = nc.dram_tensor("l1a_kv", [D, T], F32)
        dram["l1a_cin"] = nc.dram_tensor("l1a_cin", [128, 16], F32)
        dram["l1a_cout"] = nc.dram_tensor("l1a_cout", [256, 16], F32)

    xs_t = nc.alloc_sbuf_tensor("x_stream", [128, TT * D], F32)

    def xtile(tt):
        return xs_t.ap()[:, tt * D:(tt + 1) * D]

    sident = nc.alloc_sbuf_tensor("sident", [128, 128], F32)
    rotm_s = nc.alloc_sbuf_tensor("rotm_sb", [128, 128], F32)
    ones_r = nc.alloc_sbuf_tensor("ones_r", [1, 128], F32)
    ones_c = nc.alloc_sbuf_tensor("ones_c", [128, 1], F32)
    lnbuf = nc.alloc_sbuf_tensor("lnbuf", [128, 16], F32)
    gatesb = nc.alloc_sbuf_tensor("gatesb", [128, 8], F32)
    arena = Arena(nc, "arena", 36350)
    gpre = None
    ppbuf = None
    if "l2f" in stages and "l2a" in stages:
        gpre = arena.alloc(8)
        ppbuf = arena.alloc(1800)

    ps = {n: nc.alloc_psum_tensor(n, [128, 512], F32)
          for n in ("psA", "psB", "psC", "psD", "psE", "psF", "psG", "psH")}

    guard = {}

    def gget(key):
        return guard.get(key, None)

    def gset(key, ctr):
        guard[key] = (ctr, ctr.n)

    def gwait(key):
        g = guard.get(key)
        return [g] if g else []

    cL = p.ctr("c_load")
    cS = p.ctr("c_store")
    cT = p.ctr("c_tensor")
    cV = p.ctr("c_vector")
    cSc = p.ctr("c_scalar")
    cG = p.ctr("c_gpsimd")
    p.selfc = {"vector": cV, "scalar": cSc, "gpsimd": cG}

    stream_ready = []
    for tt in range(TT):
        stream_ready.append(p.dma("sync", lambda e, tt=tt: e.dma_start(
            out=xtile(tt), in_=x0[tt * 128:(tt + 1) * 128, :])))
    stream_ready.append(p.dma("sync", lambda e: e.dma_start(
        out=sident.ap()[:, :], in_=ident_in)))
    stream_ready.append(p.dma("sync", lambda e: e.dma_start(
        out=gatesb.ap()[:, :], in_=gates_in)))
    if "l0a" in stages:
        rot_ready = p.dma("sync", lambda e: e.dma_start(
            out=rotm_s.ap()[:, :], in_=rotm_in))
    else:
        rot_ready = stream_ready[0]
    p.op("vector", lambda e: e.memset(ones_r.ap()[:, :], 1.0), inc=cV)
    p.op("vector", lambda e: e.memset(ones_c.ap()[:, :], 1.0), inc=cV)
    p.op("vector", lambda e: e.memset(lnbuf.ap()[:, 8:9], float(EPS)), inc=cV)
    ones_ready = cV.n


    TPB = ("psA", "psB", "psC", "psD")

    def transpose_tiles(pairs, wait=()):
        first = list(wait)
        for j, (src, dst) in enumerate(pairs):
            bank = TPB[j % 4]
            w = gwait(bank) + (first if j < 4 else [])
            p.op("tensor", lambda e, src=src, bank=bank: e.transpose(
                ps[bank].ap()[:, 0:128], src,
                sident.ap()[:, :]), wait=w, inc=cT)
            td = cT.n
            p.op("vector", lambda e, dst=dst, bank=bank: e.tensor_copy(
                dst, ps[bank].ap()[:, 0:128]),
                wait=[(cT, td)], inc=cV)
            gset(bank, cV)
        return cV.n

    def ln_tile_inplace(tile_ap, sq_ap, wait=()):
        b = lnbuf.ap()
        s_sum, s_sq, s_mean, s_var = b[:, 0:1], b[:, 1:2], b[:, 2:3], b[:, 3:4]
        s_std, s_rstd, s_bias, s_msq = b[:, 4:5], b[:, 5:6], b[:, 6:7], b[:, 7:8]
        p.op("vector", lambda e: e.reduce_sum(s_sum, tile_ap, AX.X),
             wait=list(wait) + gwait("lnbuf"), inc=cV)
        vsum = cV.n
        p.op("scalar", lambda e: e.activation(sq_ap, tile_ap, AF.Square,
                                              accum_out=s_sq),
             wait=list(wait) + gwait("lnbuf"), inc=cSc)
        ssq = cSc.n
        p.op("vector", lambda e: e.tensor_scalar(s_mean, s_sum, 1.0 / D, None,
                                                 ALU.mult),
             wait=[(cV, vsum)], inc=cV)
        p.op("vector", lambda e: e.tensor_mul(s_msq, s_mean, s_mean), inc=cV)
        p.op("vector", lambda e: e.tensor_scalar(s_var, s_sq, 1.0 / D, None,
                                                 ALU.mult),
             wait=[(cSc, ssq)], inc=cV)
        p.op("vector", lambda e: e.tensor_sub(s_var, s_var, s_msq), inc=cV)
        vvar = cV.n
        p.op("scalar", lambda e: e.activation(s_std, s_var, AF.Sqrt,
                                              bias=lnbuf.ap()[:, 8:9]),
             wait=[(cV, vvar), (cV, ones_ready)], inc=cSc)
        sstd = cSc.n
        p.op("vector", lambda e: e.reciprocal(s_rstd, s_std),
             wait=[(cSc, sstd)], inc=cV)
        p.op("vector", lambda e: e.tensor_mul(s_bias, s_mean, s_rstd), inc=cV)
        p.op("vector", lambda e: e.tensor_scalar(s_bias, s_bias, -1.0, None,
                                                 ALU.mult), inc=cV)
        vaff = cV.n
        p.op("scalar", lambda e: e.activation(tile_ap, tile_ap, AF.Identity,
                                              bias=s_bias, scale=s_rstd),
             wait=[(cV, vaff)], inc=cSc)
        gset("lnbuf", cSc)
        return cSc.n

    def bcast_row(row_ap, dst_ap, ncols, wait=()):
        done = 0
        for j in range(0, ncols, 512):
            w = min(512, ncols - j)
            p.op("tensor", lambda e, j=j, w=w: e.matmul(
                ps["psF"].ap()[:, 0:w], ones_r.ap()[:, :], row_ap[:, j:j + w],
                start=True, stop=True),
                wait=list(wait) + [(cV, ones_ready)] + gwait("psF"), inc=cT)
            td = cT.n
            p.op("vector", lambda e, j=j, w=w: e.tensor_copy(
                dst_ap[:, j:j + w], ps["psF"].ap()[:, 0:w]),
                wait=[(cT, td)], inc=cV)
            gset("psF", cV)
            done = cV.n
        return done

    def stage_l1f(ready):
        IC = 44
        arena.mark()
        xT_h = arena.alloc(DC * 512)
        hbuf = arena.alloc(9 * 512)
        wbuf = arena.alloc(2 * 2 * D)
        dwb = arena.alloc(2 * 512)
        dsb = arena.alloc(D)
        dsrow_a = arena.alloc(D)
        acc = arena.alloc(4 * D)
        scl = arena.alloc(96)
        sq = xT_h[:, 0:D]

        gs_s, us_s = scl[:, 0:44], scl[:, 44:88]
        t1 = p.dma("sync", lambda e: e.dma_start(out=gs_s, in_=W["l1f_gs"]))
        t2 = p.dma("sync", lambda e: e.dma_start(out=us_s, in_=W["l1f_us"]))
        scl_ready = [t1, t2]
        dsrow = dsrow_a[0:1, :]
        t3 = p.dma("sync", lambda e: e.dma_start(out=dsrow, in_=W["l1f_ds"]))
        dsb_done = bcast_row(dsrow, dsb, D, wait=[t3])

        ln_waits = []
        for th in range(2):
            tok0 = th * 4
            pairs = []
            for tt4 in range(4):
                for dc in range(DC):
                    pairs.append((xtile(tok0 + tt4)[:, dc * 128:(dc + 1) * 128],
                                  xT_h[:, dc * 512 + tt4 * 128:dc * 512 + tt4 * 128 + 128]))
            xt_done = transpose_tiles(pairs, wait=list(ready))

            def xT_col(dc):
                return xT_h[:, dc * 512:(dc + 1) * 512]

            if L1F_CUT < 2:
                continue

            GRPS = (9, 9, 9, 9, 8)
            for grp in range(5):
                gbase = sum(GRPS[:grp])
                gsz = GRPS[grp]
                for icg in range(gsz):
                    ic = gbase + icg
                    slot = ic % 2
                    gw_s = wbuf[:, slot * 2 * D:slot * 2 * D + D]
                    uw_s = wbuf[:, slot * 2 * D + D:slot * 2 * D + 2 * D]
                    tg = p.dma("sync", lambda e, ic=ic, gw_s=gw_s: e.dma_start(
                        out=gw_s, in_=W["l1f_gw"][ic]),
                        wait=gwait(("l1f_w", slot)))
                    tu = p.dma("sync", lambda e, ic=ic, uw_s=uw_s: e.dma_start(
                        out=uw_s, in_=W["l1f_uw"][ic]))
                    for dc in range(DC):
                        p.op("tensor", lambda e, dc=dc, gw_s=gw_s: e.matmul(
                            ps["psA"].ap()[:, :], gw_s[:, dc * 128:(dc + 1) * 128],
                            xT_col(dc), start=(dc == 0), stop=(dc == DC - 1)),
                            wait=([tg, tu, (cV, xt_done)] + gwait("psA"))
                            if dc == 0 else (), inc=None)
                    for dc in range(DC):
                        p.op("tensor", lambda e, dc=dc, uw_s=uw_s: e.matmul(
                            ps["psB"].ap()[:, :], uw_s[:, dc * 128:(dc + 1) * 128],
                            xT_col(dc), start=(dc == 0), stop=(dc == DC - 1)),
                            wait=gwait("psB") if dc == 0 else (),
                            inc=(cT if dc == DC - 1 else None))
                    gset(("l1f_w", slot), cT)
                    mm = cT.n
                    hsl = hbuf[:, icg * 512:(icg + 1) * 512]
                    p.op("scalar", lambda e, ic=ic, hsl=hsl: e.activation(
                        hsl, ps["psA"].ap()[:, :], AF.Silu,
                        scale=gs_s[:, ic:ic + 1]),
                        wait=[(cT, mm)] + scl_ready + gwait("l1f_hfree"),
                        inc=cSc)
                    sg = cSc.n
                    gset("psA", cSc)
                    p.op("vector", lambda e, ic=ic, hsl=hsl: e.scalar_tensor_tensor(
                        hsl, ps["psB"].ap()[:, :], us_s[:, ic:ic + 1], hsl,
                        ALU.mult, ALU.mult),
                        wait=[(cSc, sg)], inc=cV)
                    gset("psB", cV)
                    guard[("hready", icg)] = (cV, cV.n)

                if L1F_CUT < 3:
                    continue
                pnames = ("psC", "psD", "psE", "psF")
                for og in range(4):
                    for icg in range(gsz):
                        ic = gbase + icg
                        dslot = icg % 2
                        dw_s = dwb[:, dslot * 512:(dslot + 1) * 512]
                        tdw = p.dma("sync", lambda e, ic=ic, og=og, dw_s=dw_s: e.dma_start(
                            out=dw_s, in_=W["l1f_dw"][ic, :, og * 512:(og + 1) * 512]),
                            wait=gwait(("l1f_dw", dslot)))
                        for t4 in range(4):
                            p.op("tensor", lambda e, icg=icg, t4=t4, dw_s=dw_s,
                                 gsz=gsz: e.matmul(
                                ps[pnames[t4]].ap()[:, :],
                                hbuf[:, icg * 512 + t4 * 128:icg * 512 + (t4 + 1) * 128],
                                dw_s, start=(icg == 0), stop=(icg == gsz - 1)),
                                wait=[tdw, guard[("hready", icg)]]
                                + (gwait(pnames[t4]) if icg == 0 else []),
                                inc=(cT if (t4 == 3) else None))
                        gset(("l1f_dw", dslot), cT)
                    mmd = cT.n
                    for t4 in range(4):
                        dst = acc[:, t4 * D + og * 512:t4 * D + (og + 1) * 512]
                        if grp == 0:
                            p.op("vector", lambda e, t4=t4, dst=dst: e.tensor_copy(
                                dst, ps[pnames[t4]].ap()[:, :]),
                                wait=[(cT, mmd)], inc=cV)
                        else:
                            p.op("vector", lambda e, t4=t4, dst=dst: e.tensor_add(
                                dst, dst, ps[pnames[t4]].ap()[:, :]),
                                wait=[(cT, mmd)], inc=cV)
                        gset(pnames[t4], cV)
                gset("l1f_hfree", cT)
                emit_prepass_chunk(42)
            if L1F_CUT < 4:
                continue
            for tt4 in range(4):
                tt = tok0 + tt4
                for og in range(4):
                    a_sl = acc[:, tt4 * D + og * 512:tt4 * D + (og + 1) * 512]
                    x_sl = xtile(tt)[:, og * 512:(og + 1) * 512]
                    p.op("vector", lambda e, a_sl=a_sl, og=og: e.tensor_mul(
                        a_sl, a_sl, dsb[:, og * 512:(og + 1) * 512]),
                        wait=[(cV, dsb_done)], inc=cV)
                    vm = cV.n
                    p.op("vector", lambda e, x_sl=x_sl, a_sl=a_sl: e.tensor_add(
                        x_sl, x_sl, a_sl), wait=[(cV, vm)], inc=cV)
                vres = cV.n
                ln_waits.append(ln_tile_inplace(xtile(tt), sq, wait=[(cV, vres)]))
        arena.release()
        if L1F_CUT < 4:
            return [(cT, cT.n), (cV, cV.n), (cSc, cSc.n)]
        return [(cSc, max(ln_waits))]

    def stage_l0f(ready):
        arena.mark()
        xT = arena.alloc(DC * T)
        acat = arena.alloc(DC * 128)
        bcat = arena.alloc(128 * 16)
        gwt = arena.alloc(DC * 8)
        wrow_a = arena.alloc(D)
        gbrow_a = arena.alloc(8)
        wtile = arena.alloc(2 * 512)
        lgb = arena.alloc(TT * 8)
        wgb = arena.alloc(TT * 8)
        sgb = arena.alloc(40)
        xaw_t = arena.alloc(128)
        xaw_f = arena.alloc(TT * 128)
        sq = acat[:, 0:D]

        tw = [p.dma("sync", lambda e: e.dma_start(out=acat, in_=W["l0f_acat"])),
              p.dma("sync", lambda e: e.dma_start(out=bcat, in_=W["l0f_bcat"])),
              p.dma("sync", lambda e: e.dma_start(out=gwt, in_=W["l0f_gwt"])),
              p.dma("sync", lambda e: e.dma_start(out=wrow_a[0:1, :], in_=W["l0f_brow"])),
              p.dma("sync", lambda e: e.dma_start(out=gbrow_a[0:1, :], in_=W["l0f_gbrow"]))]

        pairs = []
        for tt in range(TT):
            for dc in range(DC):
                pairs.append((xtile(tt)[:, dc * 128:(dc + 1) * 128],
                              xT[:, dc * T + tt * 128:dc * T + (tt + 1) * 128]))
        xt_done = transpose_tiles(pairs, wait=list(ready))

        def xTt(dc, tt):
            return xT[:, dc * T + tt * 128:dc * T + (tt + 1) * 128]

        for tt in range(TT):
            for dc in range(DC):
                p.op("tensor", lambda e, dc=dc, tt=tt: e.matmul(
                    ps["psE"].ap()[:, 0:8], xTt(dc, tt), gwt[:, dc * 8:(dc + 1) * 8],
                    start=(dc == 0), stop=False),
                    wait=([(cV, xt_done)] + tw + gwait("psE")) if dc == 0 else (),
                    inc=None)
            p.op("tensor", lambda e: e.matmul(
                ps["psE"].ap()[:, 0:8], ones_r.ap()[:, :], gbrow_a[0:1, 0:8],
                start=False, stop=True), wait=[(cV, ones_ready)], inc=cT)
            mm = cT.n
            lg = lgb[:, tt * 8:(tt + 1) * 8]
            wg = wgb[:, tt * 8:(tt + 1) * 8]
            m1, m2 = sgb[:, 32:33], sgb[:, 33:34]
            den, rcp = sgb[:, 34:35], sgb[:, 35:36]
            tb, msk, et = sgb[:, 0:8], sgb[:, 8:16], sgb[:, 16:24]
            p.op("vector", lambda e, lg=lg: e.tensor_copy(lg, ps["psE"].ap()[:, 0:8]),
                 wait=[(cT, mm)], inc=cV)
            gset("psE", cV)
            p.op("vector", lambda e, lg=lg: e.reduce_max(m1, lg, AX.X))
            p.op("vector", lambda e, lg=lg: e.tensor_scalar(tb, lg, m1, None, ALU.subtract))
            p.op("vector", lambda e: e.tensor_scalar(msk, tb, 0.0, None, ALU.is_ge))
            p.op("vector", lambda e: e.scalar_tensor_tensor(msk, msk, -1e9, tb,
                                                            ALU.mult, ALU.add))
            p.op("vector", lambda e: e.reduce_max(m2, msk, AX.X))
            p.op("vector", lambda e: e.tensor_scalar(msk, tb, m2, None, ALU.is_ge),
                 inc=cV)
            vsel = cV.n
            p.op("scalar", lambda e: e.activation(et, tb, AF.Exp),
                 wait=[(cV, vsel)], inc=cSc)
            se = cSc.n
            p.op("vector", lambda e: e.tensor_mul(et, et, msk), wait=[(cSc, se)])
            p.op("vector", lambda e: e.reduce_sum(den, et, AX.X))
            p.op("vector", lambda e: e.reciprocal(rcp, den))
            p.op("vector", lambda e, wg=wg: e.tensor_scalar(wg, et, rcp, None,
                                                            ALU.mult), inc=cV)
            guard[("l0f_wg", tt)] = (cV, cV.n)
            emit_prepass_chunk(48)

        for tt in range(TT):
            for dc in range(DC):
                p.op("tensor", lambda e, dc=dc, tt=tt: e.matmul(
                    ps["psF"].ap()[:, 0:128], xTt(dc, tt), acat[:, dc * 128:(dc + 1) * 128],
                    start=(dc == 0), stop=(dc == DC - 1)),
                    wait=(gwait("psF")) if dc == 0 else (),
                    inc=(cT if dc == DC - 1 else None))
            mm = cT.n
            wg = wgb[:, tt * 8:(tt + 1) * 8]
            p.op("vector", lambda e, tt=tt, wg=wg: e.tensor_tensor(
                xaw_t.rearrange("p (e r) -> p e r", r=16),
                ps["psF"].ap()[:, 0:128].rearrange("p (e r) -> p e r", r=16),
                wg.broadcast_to([128, 8, 16]), ALU.mult),
                wait=[(cT, mm), guard[("l0f_wg", tt)]], inc=cV)
            gset("psF", cV)
            vx = cV.n
            transpose_tiles([(xaw_t, xaw_f[:, tt * 128:(tt + 1) * 128])],
                            wait=[(cV, vx)])
        xawf_done = cV.n

        pnames8 = ("psA", "psB", "psC", "psD", "psE", "psF", "psG", "psH")
        res_done = {}
        for oc in range(4):
            for dc in range(DC):
                slot = dc % 2
                wt = wtile[:, slot * 512:(slot + 1) * 512]
                twt = p.dma("sync", lambda e, dc=dc, oc=oc, wt=wt: e.dma_start(
                    out=wt, in_=W["l0f_wT"][dc, :, oc * 512:(oc + 1) * 512]),
                    wait=gwait(("l0f_wt", slot)))
                for tt in range(TT):
                    p.op("tensor", lambda e, dc=dc, tt=tt, wt=wt: e.matmul(
                        ps[pnames8[tt]].ap()[:, :], xTt(dc, tt), wt,
                        start=(dc == 0), stop=False),
                        wait=[twt] + (gwait(pnames8[tt]) + [(cV, xawf_done)]
                                      if dc == 0 else []),
                        inc=(cT if tt == TT - 1 else None))
                gset(("l0f_wt", slot), cT)
            for tt in range(TT):
                p.op("tensor", lambda e, tt=tt, oc=oc: e.matmul(
                    ps[pnames8[tt]].ap()[:, :], xaw_f[:, tt * 128:(tt + 1) * 128],
                    bcat[:, oc * 512:(oc + 1) * 512], start=False, stop=False),
                    inc=None)
                p.op("tensor", lambda e, tt=tt, oc=oc: e.matmul(
                    ps[pnames8[tt]].ap()[:, :], ones_r.ap()[:, :],
                    wrow_a[0:1, oc * 512:(oc + 1) * 512], start=False, stop=True),
                    inc=cT)
                mm = cT.n
                p.op("vector", lambda e, tt=tt, oc=oc: e.tensor_add(
                    xtile(tt)[:, oc * 512:(oc + 1) * 512],
                    xtile(tt)[:, oc * 512:(oc + 1) * 512],
                    ps[pnames8[tt]].ap()[:, :]), wait=[(cT, mm)], inc=cV)
                gset(pnames8[tt], cV)
                res_done[(tt, oc)] = (cV, cV.n)
        ln_waits = []
        for tt in range(TT):
            ln_waits.append(ln_tile_inplace(
                xtile(tt), sq, wait=[res_done[(tt, oc)] for oc in range(4)]))
        arena.release()
        return [(cSc, max(ln_waits))]

    def stage_l3f(ready):
        arena.mark()
        xT = arena.alloc(DC * T)
        vxb = arena.alloc(TT * 512)
        vxsf = arena.alloc(TT * 512)
        w1b = arena.alloc(2 * D)
        vct = arena.alloc(2 * 512)
        uct = arena.alloc(2 * 512)
        hec = arena.alloc(T)
        w2s = arena.alloc(DC * 8)
        alps = arena.alloc(TT * 8)
        alph = arena.alloc(TT * 8)
        b1f = arena.alloc(16)
        onesrow = arena.alloc(512)
        b2r = arena.alloc(8)
        alphaT = arena.alloc(T)
        sgb3 = arena.alloc(16)
        sq = xT[:, 0:D]

        tw = [p.dma("sync", lambda e: e.dma_start(out=w2s, in_=W["l3f_w2"])),
              p.dma("sync", lambda e: e.dma_start(out=b1f, in_=W["l3f_b1f"])),
              p.dma("sync", lambda e: e.dma_start(out=b2r[0:1, :], in_=W["l3f_b2row"]))]
        p.op("vector", lambda e: e.memset(onesrow[0:1, :], 1.0), inc=cV)
        ones512 = cV.n

        pairs = []
        for tt in range(TT):
            for dc in range(DC):
                pairs.append((xtile(tt)[:, dc * 128:(dc + 1) * 128],
                              xT[:, dc * T + tt * 128:dc * T + (tt + 1) * 128]))
        xt_done = transpose_tiles(pairs, wait=list(ready))

        def xTt(dc, tt):
            return xT[:, dc * T + tt * 128:dc * T + (tt + 1) * 128]

        pnames8 = ("psA", "psB", "psC", "psD", "psE", "psF", "psG", "psH")

        for dc in range(DC):
            slot = dc % 2
            vt = vct[:, slot * 512:(slot + 1) * 512]
            tv = p.dma("sync", lambda e, dc=dc, vt=vt: e.dma_start(
                out=vt, in_=W["l3f_vcat"][dc]), wait=gwait(("l3f_vct", slot)))
            for tt in range(TT):
                p.op("tensor", lambda e, dc=dc, tt=tt, vt=vt: e.matmul(
                    ps[pnames8[tt]].ap()[:, :], xTt(dc, tt), vt,
                    start=(dc == 0), stop=(dc == DC - 1)),
                    wait=[tv] + ([(cV, xt_done)] + gwait(pnames8[tt])
                                 if dc == 0 else []),
                    inc=(cT if tt == TT - 1 else None))
            gset(("l3f_vct", slot), cT)
        mmv = cT.n
        for tt in range(TT):
            p.op("vector", lambda e, tt=tt: e.tensor_copy(
                vxb[:, tt * 512:(tt + 1) * 512], ps[pnames8[tt]].ap()[:, :]),
                wait=[(cT, mmv)], inc=cV)
            gset(pnames8[tt], cV)

        for ec in range(DC):
            slot = ec % 2
            w1s = w1b[:, slot * D:(slot + 1) * D]
            tw1 = p.dma("sync", lambda e, ec=ec, w1s=w1s: e.dma_start(
                out=w1s, in_=W["l3f_w1"][ec]), wait=gwait(("l3f_w1", slot)))
            for tc in range(2):
                bank = "psA" if tc == 0 else "psB"
                for dc in range(DC):
                    p.op("tensor", lambda e, dc=dc, tc=tc, w1s=w1s, bank=bank: e.matmul(
                        ps[bank].ap()[:, :], w1s[:, dc * 128:(dc + 1) * 128],
                        xT[:, dc * T + tc * 512:dc * T + (tc + 1) * 512],
                        start=(dc == 0), stop=(dc == DC - 1)),
                        wait=[tw1] + (gwait(bank) if dc == 0 else []),
                        inc=(cT if dc == DC - 1 else None))
                mmh = cT.n
                p.op("scalar", lambda e, ec=ec, tc=tc, bank=bank: e.activation(
                    hec[:, tc * 512:(tc + 1) * 512], ps[bank].ap()[:, :], AF.Relu,
                    bias=b1f[:, ec:ec + 1]),
                    wait=[(cT, mmh)] + tw + gwait(("l3f_hec", tc)), inc=cSc)
                gset(bank, cSc)
                sh_done = cSc.n
                abank = "psC" if tc == 0 else "psD"
                p.op("tensor", lambda e, ec=ec, tc=tc, abank=abank: e.matmul(
                    ps[abank].ap()[0:8, :], w2s[:, ec * 8:(ec + 1) * 8],
                    hec[:, tc * 512:(tc + 1) * 512],
                    start=(ec == 0), stop=False),
                    wait=[(cSc, sh_done)] + (gwait(abank) if ec == 0 else []),
                    inc=cT)
                guard[("l3f_hec", tc)] = (cT, cT.n)
            gset(("l3f_w1", slot), cT)
        for tc in range(2):
            abank = "psC" if tc == 0 else "psD"
            p.op("tensor", lambda e, abank=abank: e.matmul(
                ps[abank].ap()[0:8, :], b2r[0:1, 0:8], onesrow[0:1, :],
                start=False, stop=True), wait=[(cV, ones512)], inc=cT)
            mmb = cT.n
            p.op("vector", lambda e, tc=tc, abank=abank: e.tensor_copy(
                alphaT[0:8, tc * 512:(tc + 1) * 512], ps[abank].ap()[0:8, :]),
                wait=[(cT, mmb)], inc=cV)
            gset(abank, cV)
        at_done = cV.n

        for tt in range(TT):
            bank = TPB[tt % 4]
            p.op("tensor", lambda e, tt=tt, bank=bank: e.transpose(
                ps[bank].ap()[:, 0:8], alphaT[0:8, tt * 128:(tt + 1) * 128],
                sident.ap()[0:8, 0:8]),
                wait=[(cV, at_done)] + gwait(bank), inc=cT)
            td = cT.n
            p.op("vector", lambda e, tt=tt, bank=bank: e.tensor_copy(
                alps[:, tt * 8:(tt + 1) * 8], ps[bank].ap()[:, 0:8]),
                wait=[(cT, td)], inc=cV)
            gset(bank, cV)
            ap_ = alps[:, tt * 8:(tt + 1) * 8]
            aw = alph[:, tt * 8:(tt + 1) * 8]
            m1, den, rcp = sgb3[:, 0:1], sgb3[:, 1:2], sgb3[:, 2:3]
            tb = sgb3[:, 8:16]
            p.op("vector", lambda e, ap_=ap_: e.reduce_max(m1, ap_, AX.X))
            p.op("vector", lambda e, ap_=ap_: e.tensor_scalar(tb, ap_, m1, None,
                                                              ALU.subtract), inc=cV)
            vt_ = cV.n
            p.op("scalar", lambda e: e.activation(tb, tb, AF.Exp),
                 wait=[(cV, vt_)], inc=cSc)
            se = cSc.n
            p.op("vector", lambda e: e.reduce_sum(den, tb, AX.X), wait=[(cSc, se)])
            p.op("vector", lambda e: e.reciprocal(rcp, den))
            p.op("vector", lambda e, aw=aw: e.tensor_scalar(aw, tb, rcp, None,
                                                            ALU.mult), inc=cV)
            guard[("l3f_aw", tt)] = (cV, cV.n)

        for tt in range(TT):
            p.op("vector", lambda e, tt=tt: e.tensor_tensor(
                hec[:, 0:512].rearrange("p (k r) -> p k r", r=64),
                vxb[:, tt * 512:(tt + 1) * 512].rearrange("p (k r) -> p k r", r=64),
                alph[:, tt * 8:(tt + 1) * 8].broadcast_to([128, 8, 64]), ALU.mult),
                wait=[guard[("l3f_aw", tt)]] + gwait(("l3f_vxs",))
                + gwait(("l3f_hec", 0)) + gwait(("l3f_hec", 1)),
                inc=cV)
            vsc = cV.n
            prs = [(hec[:, j * 128:(j + 1) * 128],
                    vxsf[:, tt * 512 + j * 128:tt * 512 + (j + 1) * 128])
                   for j in range(4)]
            transpose_tiles(prs, wait=[(cV, vsc)])
            gset(("l3f_vxs",), cV)
        vxsf_done = cV.n

        res_done = {}
        for oc in range(4):
            for krc in range(4):
                slot = krc % 2
                ut = uct[:, slot * 512:(slot + 1) * 512]
                tu = p.dma("sync", lambda e, krc=krc, oc=oc, ut=ut: e.dma_start(
                    out=ut, in_=W["l3f_ucat"][krc, :, oc * 512:(oc + 1) * 512]),
                    wait=gwait(("l3f_uct", slot)))
                for tt in range(TT):
                    p.op("tensor", lambda e, krc=krc, tt=tt, ut=ut: e.matmul(
                        ps[pnames8[tt]].ap()[:, :],
                        vxsf[:, tt * 512 + krc * 128:tt * 512 + (krc + 1) * 128], ut,
                        start=(krc == 0), stop=(krc == 3)),
                        wait=[tu, (cV, vxsf_done)] + (gwait(pnames8[tt])
                                                      if krc == 0 else []),
                        inc=(cT if tt == TT - 1 else None))
                gset(("l3f_uct", slot), cT)
            mmo = cT.n
            for tt in range(TT):
                p.op("vector", lambda e, tt=tt, oc=oc: e.tensor_add(
                    xtile(tt)[:, oc * 512:(oc + 1) * 512],
                    xtile(tt)[:, oc * 512:(oc + 1) * 512],
                    ps[pnames8[tt]].ap()[:, :]), wait=[(cT, mmo)], inc=cV)
                gset(pnames8[tt], cV)
                res_done[(tt, oc)] = (cV, cV.n)
        ln_waits = []
        for tt in range(TT):
            ln_waits.append(ln_tile_inplace(
                xtile(tt), sq, wait=[res_done[(tt, oc)] for oc in range(4)]))
        arena.release()
        return [(cSc, max(ln_waits))]

    def stage_l1a(ready):
        SCALE = float(128.0 ** -0.5)
        arena.mark()
        xT = arena.alloc(DC * T)
        wsl = arena.alloc(2 * D)
        kbuf = arena.alloc(T)
        ubuf = arena.alloc(T)
        phib = arena.alloc(T)
        bqf = arena.alloc(16)
        bkf = arena.alloc(16)
        bvf = arena.alloc(16)
        borow = arena.alloc(D)
        bob = arena.alloc(D)
        carryb = arena.alloc(16)
        c0b = arena.alloc(16)
        c1b = arena.alloc(16)
        ceff = arena.alloc(16)
        wot = arena.alloc(2 * 512)
        sq = arena.alloc(D)
        ctx = xT

        tw = [p.dma("sync", lambda e: e.dma_start(out=bqf, in_=W["l1a_bq"])),
              p.dma("sync", lambda e: e.dma_start(out=bkf, in_=W["l1a_bk"])),
              p.dma("sync", lambda e: e.dma_start(out=bvf, in_=W["l1a_bv"])),
              p.dma("sync", lambda e: e.dma_start(out=borow[0:1, :], in_=W["l1a_bo"]))]
        bob_done = bcast_row(borow[0:1, :], bob, D, wait=[tw[3]])

        pairs = []
        for tt in range(TT):
            for dc in range(DC):
                pairs.append((xtile(tt)[:, dc * 128:(dc + 1) * 128],
                              xT[:, dc * T + tt * 128:dc * T + (tt + 1) * 128]))
        xt_done = transpose_tiles(pairs, wait=list(ready))

        def proj_ec(wname, ec, slot, bias_col, extra_wait=()):
            wsl_s = wsl[:, slot * D:(slot + 1) * D]
            twl = p.dma("sync", lambda e, ec=ec, wsl_s=wsl_s, wname=wname: e.dma_start(
                out=wsl_s, in_=W[wname][ec]), wait=gwait(("l1a_w", slot)))
            mms = []
            for tc in range(2):
                bank = "psE" if tc == 0 else "psF"
                for dc in range(DC):
                    p.op("tensor", lambda e, dc=dc, tc=tc, wsl_s=wsl_s, bank=bank: e.matmul(
                        ps[bank].ap()[:, :], wsl_s[:, dc * 128:(dc + 1) * 128],
                        xT[:, dc * T + tc * 512:dc * T + (tc + 1) * 512],
                        start=(dc == 0), stop=(dc == DC - 1)),
                        wait=[twl, (cV, xt_done)] + (gwait(bank) + list(extra_wait)
                                                     if dc == 0 else []),
                        inc=(cT if dc == DC - 1 else None))
                mms.append(cT.n)
            gset(("l1a_w", slot), cT)
            return mms

        q_tok, kv_tok = {}, {}
        for ec in range(DC):
            mms = proj_ec("l1a_wq", ec, 0, None)
            for tc in range(2):
                bank = "psE" if tc == 0 else "psF"
                p.op("scalar", lambda e, tc=tc, ec=ec, bank=bank: e.activation(
                    kbuf[:, tc * 512:(tc + 1) * 512], ps[bank].ap()[:, :],
                    AF.Identity, bias=bqf[:, ec:ec + 1]),
                    wait=[(cT, mms[tc])] + tw + gwait("l1a_kbuf"), inc=cSc)
                gset(bank, cSc)
            qd = cSc.n
            tq = p.dma("gpsimd", lambda e, ec=ec: e.dma_start(
                out=dram["l1a_q"].ap()[ec * 128:(ec + 1) * 128, :], in_=kbuf),
                wait=[(cSc, qd)])
            q_tok[ec] = tq
            guard["l1a_kbuf"] = tq
            mms = proj_ec("l1a_wk", ec, 1, None)
            for tc in range(2):
                bank = "psE" if tc == 0 else "psF"
                p.op("scalar", lambda e, tc=tc, ec=ec, bank=bank: e.activation(
                    kbuf[:, tc * 512:(tc + 1) * 512], ps[bank].ap()[:, :],
                    AF.Identity, bias=bkf[:, ec:ec + 1]),
                    wait=[(cT, mms[tc]), tq], inc=cSc)
                gset(bank, cSc)
            kd = cSc.n
            p.op("vector", lambda e: e.tensor_scalar_min(phib, kbuf, 0.0),
                 wait=[(cSc, kd)], inc=cV)
            vmin = cV.n
            p.op("scalar", lambda e: e.activation(phib, phib, AF.Exp),
                 wait=[(cV, vmin)], inc=cSc)
            se = cSc.n
            p.op("vector", lambda e: e.scalar_tensor_tensor(
                phib, kbuf, 0.0, phib, ALU.max, ALU.add),
                wait=[(cSc, se)], inc=cV)
            phid = cV.n
            mms = proj_ec("l1a_wv", ec, 0, ())
            for tc in range(2):
                bank = "psE" if tc == 0 else "psF"
                p.op("scalar", lambda e, tc=tc, ec=ec, bank=bank: e.activation(
                    ubuf[:, tc * 512:(tc + 1) * 512], ps[bank].ap()[:, :],
                    AF.Identity, bias=bvf[:, ec:ec + 1]),
                    wait=[(cT, mms[tc])] + gwait("l1a_ubuf"), inc=cSc)
                gset(bank, cSc)
            vd = cSc.n
            p.op("vector", lambda e: e.tensor_mul(ubuf, ubuf, phib),
                 wait=[(cSc, vd), (cV, phid)], inc=cV)
            p.op("vector", lambda e: e.tensor_tensor_scan(
                kbuf, ones_c.ap()[:, :].broadcast_to([128, T]), ubuf, 0.0,
                ALU.mult, ALU.add), inc=cV)
            p.op("vector", lambda e, ec=ec: e.tensor_copy(
                carryb[:, ec:ec + 1], kbuf[:, T - 1:T]), inc=cV)
            kvd = cV.n
            tkv = p.dma("gpsimd", lambda e, ec=ec: e.dma_start(
                out=dram["l1a_kv"].ap()[ec * 128:(ec + 1) * 128, :], in_=kbuf),
                wait=[(cV, kvd)])
            kv_tok[ec] = tkv
            guard["l1a_kbuf"] = tkv
            guard["l1a_ubuf"] = (cV, kvd)
            guard["l1a_ubuf2"] = (cV, kvd)
            guard["l1a_kbuf2"] = tkv
            emit_prepass_chunk(36)

        tci = p.dma("gpsimd", lambda e: e.dma_start(
            out=dram["l1a_cin"].ap(), in_=carryb), wait=[(cV, cV.n)])
        p.op("gpsimd", lambda e: e.collective_compute(
            "AllGather", ALU.bypass,
            replica_groups=[[0, 1], [2, 3], [4, 5], [6, 7]],
            ins=[dram["l1a_cin"].ap()], outs=[dram["l1a_cout"].ap()]),
            wait=[tci], inc=cG)
        ccd = cG.n
        t0 = p.dma("gpsimd", lambda e: e.dma_start(
            out=c0b, in_=dram["l1a_cout"].ap()[0:128, :]), wait=[(cG, ccd)])
        t1 = p.dma("gpsimd", lambda e: e.dma_start(
            out=c1b, in_=dram["l1a_cout"].ap()[128:256, :]), wait=[(cG, ccd)])
        p.op("vector", lambda e: e.tensor_scalar(
            ceff, c0b, gatesb.ap()[:, 0:1], None, ALU.mult), wait=[t0, t1], inc=cV)
        p.op("vector", lambda e: e.scalar_tensor_tensor(
            ceff, c1b, gatesb.ap()[:, 1:2], ceff, ALU.mult, ALU.add), inc=cV)
        ceff_done = cV.n

        for ec in range(DC):
            tq = p.dma("sync", lambda e, ec=ec: e.dma_start(
                out=ubuf, in_=dram["l1a_q"].ap()[ec * 128:(ec + 1) * 128, :]),
                wait=gwait("l1a_ubuf2") + [q_tok[ec]])
            tk = p.dma("sync", lambda e, ec=ec: e.dma_start(
                out=kbuf, in_=dram["l1a_kv"].ap()[ec * 128:(ec + 1) * 128, :]),
                wait=gwait("l1a_kbuf2") + [kv_tok[ec]])
            p.op("vector", lambda e: e.tensor_scalar_min(phib, ubuf, 0.0),
                 wait=[tq], inc=cV)
            vmin = cV.n
            p.op("scalar", lambda e: e.activation(phib, phib, AF.Exp),
                 wait=[(cV, vmin)], inc=cSc)
            se = cSc.n
            p.op("vector", lambda e: e.scalar_tensor_tensor(
                phib, ubuf, 0.0, phib, ALU.max, ALU.add),
                wait=[(cSc, se)], inc=cV)
            guard["l1a_ubuf2"] = (cV, cV.n)
            p.op("vector", lambda e, ec=ec: e.scalar_tensor_tensor(
                ctx[:, ec * T:(ec + 1) * T], kbuf, ceff[:, ec:ec + 1], phib,
                ALU.add, ALU.mult),
                wait=[tk, (cV, ceff_done)], inc=cV)
            guard["l1a_kbuf2"] = (cV, cV.n)
        ctx_done = cV.n

        pnames8 = ("psA", "psB", "psC", "psD", "psE", "psF", "psG", "psH")
        res_done = {}
        for oc in range(4):
            for ec in range(DC):
                slot = ec % 2
                wt = wot[:, slot * 512:(slot + 1) * 512]
                twt = p.dma("sync", lambda e, ec=ec, oc=oc, wt=wt: e.dma_start(
                    out=wt, in_=W["l1a_woT"][ec, :, oc * 512:(oc + 1) * 512]),
                    wait=gwait(("l1a_wo", slot)))
                for tt in range(TT):
                    p.op("tensor", lambda e, ec=ec, tt=tt, wt=wt: e.matmul(
                        ps[pnames8[tt]].ap()[:, :],
                        ctx[:, ec * T + tt * 128:ec * T + (tt + 1) * 128], wt,
                        start=(ec == 0), stop=(ec == DC - 1)),
                        wait=[twt, (cV, ctx_done)] + (gwait(pnames8[tt])
                                                      if ec == 0 else []),
                        inc=(cT if tt == TT - 1 else None))
                gset(("l1a_wo", slot), cT)
            mmo = cT.n
            for tt in range(TT):
                p.op("vector", lambda e, tt=tt, oc=oc: e.scalar_tensor_tensor(
                    phib[:, 0:512], ps[pnames8[tt]].ap()[:, :], SCALE,
                    bob[:, oc * 512:(oc + 1) * 512], ALU.mult, ALU.add),
                    wait=[(cT, mmo), (cV, bob_done)], inc=cV)
                p.op("vector", lambda e, tt=tt, oc=oc: e.tensor_add(
                    xtile(tt)[:, oc * 512:(oc + 1) * 512],
                    xtile(tt)[:, oc * 512:(oc + 1) * 512], phib[:, 0:512]), inc=cV)
                gset(pnames8[tt], cV)
                res_done[(tt, oc)] = (cV, cV.n)
        ln_waits = []
        for tt in range(TT):
            ln_waits.append(ln_tile_inplace(
                xtile(tt), sq, wait=[res_done[(tt, oc)] for oc in range(4)]))
        arena.release()
        return [(cSc, max(ln_waits))]

    def stage_attn(ready, pre, nheads_kv, rope):
        ISC = float(128.0 ** -0.5)
        KV = nheads_kv * 128
        KVC = nheads_kv
        arena.mark()
        ofm = arena.alloc(DC * T)
        wsl = arena.alloc(2 * D)
        khb = arena.alloc(2048)
        vhb = arena.alloc(2048)
        qbuf = arena.alloc(512)
        pb = arena.alloc(512)
        trisel_s = arena.alloc(4096)
        kmq_s = arena.alloc(32)
        smb = arena.alloc(32)
        kfb = arena.alloc(T)
        sqs = arena.alloc(T if rope else 512)
        bqf_s = arena.alloc(16)
        bkf_s = arena.alloc(16)
        bvrow = vhb[0:1, 0:KV]
        borow = khb[0:1, 0:D]

        tw = [p.dma("sync", lambda e: e.dma_start(out=trisel_s, in_=W[pre + "_trisel"])),
              p.dma("sync", lambda e: e.dma_start(out=kmq_s, in_=W[pre + "_kmaskq"])),
              p.dma("sync", lambda e: e.dma_start(out=bvrow, in_=W[pre + "_bv"])),
              p.dma("sync", lambda e: e.dma_start(out=bqf_s, in_=W[pre + "_bqf"])),
              p.dma("sync", lambda e: e.dma_start(out=bkf_s[:, 0:KVC],
                                                   in_=W[pre + "_bkf"]))]
        if rope:
            rcos = khb[:, 0:T]
            rsin = khb[:, T:2 * T]
            tw.append(p.dma("sync", lambda e: e.dma_start(out=rcos, in_=W[pre + "_cos"])))
            tw.append(p.dma("sync", lambda e: e.dma_start(out=rsin, in_=W[pre + "_sinm"])))

        pairs = []
        xT = ofm
        for tt in range(TT):
            for dc in range(DC):
                pairs.append((xtile(tt)[:, dc * 128:(dc + 1) * 128],
                              xT[:, dc * T + tt * 128:dc * T + (tt + 1) * 128]))
        xt_done = transpose_tiles(pairs, wait=list(ready))

        m11 = smb[0:1, 0:1]
        rmcol = smb[:, 1:2]
        mxcol = smb[:, 2:3]
        mcol = smb[:, 3:4]
        biasq = smb[:, 16:32]

        def rope_apply(buf, done_in):
            p.op("vector", lambda e: e.tensor_mul(sqs, buf, rcos),
                 wait=[done_in] + tw, inc=cV)
            t1d = cV.n
            for tc in range(2):
                bank = "psG" if tc == 0 else "psH"
                p.op("tensor", lambda e, tc=tc, bank=bank: e.matmul(
                    ps[bank].ap()[:, :], rotm_s.ap()[:, :],
                    buf[:, tc * 512:(tc + 1) * 512], start=True, stop=True),
                    wait=[done_in] + gwait(bank) + [rot_ready], inc=cT)
                rqd = cT.n
                p.op("vector", lambda e, tc=tc, bank=bank: e.tensor_tensor(
                    qbuf, ps[bank].ap()[:, :], rsin[:, tc * 512:(tc + 1) * 512],
                    ALU.mult), wait=[(cT, rqd), (cV, t1d)], inc=cV)
                gset(bank, cV)
                p.op("vector", lambda e, tc=tc: e.tensor_add(
                    buf[:, tc * 512:(tc + 1) * 512], qbuf,
                    sqs[:, tc * 512:(tc + 1) * 512]), inc=cV)
            return (cV, cV.n)

        def proj_fm(wname, ec, slot, extra=()):
            wsl_s = wsl[:, slot * D:(slot + 1) * D]
            twl = p.dma("sync", lambda e, ec=ec, wsl_s=wsl_s, wname=wname: e.dma_start(
                out=wsl_s, in_=W[wname][ec]), wait=gwait(("at_w", slot)))
            mms = []
            for tc in range(2):
                bank = "psE" if tc == 0 else "psF"
                for dc in range(DC):
                    p.op("tensor", lambda e, dc=dc, tc=tc, wsl_s=wsl_s, bank=bank: e.matmul(
                        ps[bank].ap()[:, :], wsl_s[:, dc * 128:(dc + 1) * 128],
                        xT[:, dc * T + tc * 512:dc * T + (tc + 1) * 512],
                        start=(dc == 0), stop=(dc == DC - 1)),
                        wait=[twl, (cV, xt_done)] + (gwait(bank) + list(extra)
                                                     if dc == 0 else []),
                        inc=(cT if dc == DC - 1 else None))
                mms.append(cT.n)
            gset(("at_w", slot), cT)
            return mms

        qtoks, ktoks, vtoks = {}, [], []
        for ec in range(DC):
            mms = proj_fm(pre + "_wq", ec, ec % 2)
            for tc in range(2):
                bank = "psE" if tc == 0 else "psF"
                p.op("scalar", lambda e, tc=tc, ec=ec, bank=bank: e.activation(
                    kfb[:, tc * 512:(tc + 1) * 512], ps[bank].ap()[:, :],
                    AF.Identity, bias=bqf_s[:, ec:ec + 1]),
                    wait=[(cT, mms[tc])] + tw + gwait("at_kfb"), inc=cSc)
                gset(bank, cSc)
            qd = (cSc, cSc.n)
            if rope:
                qd = rope_apply(kfb, qd)
            tqs = p.dma("gpsimd", lambda e, ec=ec: e.dma_start(
                out=dram[pre + "_qf"].ap()[ec * 128:(ec + 1) * 128, :], in_=kfb),
                wait=[qd])
            qtoks[ec] = tqs
            guard["at_kfb"] = tqs
        for ec in range(KVC):
            mms = proj_fm(pre + "_wk", ec, ec % 2)
            for tc in range(2):
                bank = "psE" if tc == 0 else "psF"
                p.op("scalar", lambda e, tc=tc, ec=ec, bank=bank: e.activation(
                    kfb[:, tc * 512:(tc + 1) * 512], ps[bank].ap()[:, :],
                    AF.Identity, bias=bkf_s[:, ec:ec + 1]),
                    wait=[(cT, mms[tc])] + gwait("at_kfb"), inc=cSc)
                gset(bank, cSc)
            kd = (cSc, cSc.n)
            if rope:
                kd = rope_apply(kfb, kd)
            tks = p.dma("gpsimd", lambda e, ec=ec: e.dma_start(
                out=dram[pre + "_kin"].ap()[ec * 128:(ec + 1) * 128, :], in_=kfb),
                wait=[kd])
            guard["at_kfb"] = tks
            ktoks.append(tks)
        pnames8 = ("psA", "psB", "psC", "psD", "psE", "psF", "psG", "psH")
        v_tok = None
        for oc in range(KV // 512):
            for dc in range(DC):
                slot = dc % 2
                wt = wsl[:, slot * D:slot * D + 512]
                twt = p.dma("sync", lambda e, dc=dc, oc=oc, wt=wt: e.dma_start(
                    out=wt, in_=W[pre + "_wv"][dc, :, oc * 512:(oc + 1) * 512]),
                    wait=gwait(("at_w", slot)))
                for tt in range(TT):
                    p.op("tensor", lambda e, dc=dc, tt=tt, wt=wt: e.matmul(
                        ps[pnames8[tt]].ap()[:, :],
                        xT[:, dc * T + tt * 128:dc * T + (tt + 1) * 128], wt,
                        start=(dc == 0), stop=False),
                        wait=[twt] + (gwait(pnames8[tt]) if dc == 0 else []),
                        inc=(cT if tt == TT - 1 else None))
                gset(("at_w", slot), cT)
            for tt in range(TT):
                p.op("tensor", lambda e, tt=tt, oc=oc: e.matmul(
                    ps[pnames8[tt]].ap()[:, :], ones_r.ap()[:, :],
                    bvrow[:, oc * 512:(oc + 1) * 512], start=False, stop=True),
                    wait=[(cV, ones_ready)] + tw, inc=cT)
                mmv = cT.n
                p.op("vector", lambda e, tt=tt: e.tensor_copy(
                    qbuf, ps[pnames8[tt]].ap()[:, :]),
                    wait=[(cT, mmv)] + gwait("at_qbuf"), inc=cV)
                gset(pnames8[tt], cV)
                vd = cV.n
                v_tok = p.dma("gpsimd", lambda e, tt=tt, oc=oc: e.dma_start(
                    out=dram[pre + "_vin"].ap()[tt * 128:(tt + 1) * 128,
                                                oc * 512:(oc + 1) * 512],
                    in_=qbuf), wait=[(cV, vd)])
                vtoks.append(v_tok)
                guard["at_qbuf"] = v_tok

        RG = [[0, 1], [2, 3], [4, 5], [6, 7]]
        p.op("gpsimd", lambda e: e.collective_compute(
            "AllGather", ALU.bypass, replica_groups=RG,
            ins=[dram[pre + "_kin"].ap()], outs=[dram[pre + "_kout"].ap()]),
            wait=ktoks, inc=cG)
        p.op("gpsimd", lambda e: e.collective_compute(
            "AllGather", ALU.bypass, replica_groups=RG,
            ins=[dram[pre + "_vin"].ap()], outs=[dram[pre + "_vout"].ap()]),
            wait=vtoks, inc=cG)
        cc_done = cG.n
        bias_done = (cV, 0)

        for h in range(DC):
            hkv = h // (DC // KVC)
            tkh = [p.dma("sync", lambda e, hkv=hkv: e.dma_start(
                out=khb[:, 0:T], in_=dram[pre + "_kout"].ap()[
                    hkv * 128:(hkv + 1) * 128, :]),
                wait=gwait("at_khb") + [(cG, cc_done)]),
                p.dma("sync", lambda e, hkv=hkv: e.dma_start(
                    out=khb[:, T:2 * T], in_=dram[pre + "_kout"].ap()[
                        KV + hkv * 128:KV + (hkv + 1) * 128, :]))]
            tvh = []
            for gq in range(4):
                tvh.append(p.dma("sync", lambda e, hkv=hkv, gq=gq: e.dma_start(
                    out=vhb[:, gq * 512:(gq + 1) * 512].rearrange(
                        "p (g c) -> p g c", c=128),
                    in_=dram[pre + "_vout"].ap()[gq * 512:(gq + 1) * 512,
                                                 hkv * 128:(hkv + 1) * 128]
                    .rearrange("(g p) c -> p g c", p=128)),
                    wait=(gwait("at_vhb") + [(cG, cc_done)]) if gq == 0 else ()))
            for qb in range(2):
                tqh = p.dma("sync", lambda e, h=h, qb=qb: e.dma_start(
                    out=qbuf, in_=dram[pre + "_qf"].ap()[
                        h * 128:(h + 1) * 128, qb * 512:(qb + 1) * 512]),
                    wait=gwait("at_qbuf") + [qtoks[h]])
                glist = list(range(0, 8)) + list(range(8, 12 + 4 * qb))
                nl = len(glist)

                def s_tile(g, sb, extra_w=()):
                    slot = -1
                    if 4 * qb <= g < 4 * qb + 4:
                        slot = g - 4 * qb
                    elif 8 + 4 * qb <= g < 12 + 4 * qb:
                        slot = 4 + (g - 8 - 4 * qb)
                    w0 = gwait(sb) + list(extra_w)
                    if slot >= 0:
                        p.op("tensor", lambda e, slot=slot, sb=sb: e.matmul(
                            ps[sb].ap()[:, :], sident.ap()[:, :],
                            trisel_s[:, slot * 512:(slot + 1) * 512],
                            start=True, stop=False), wait=w0, inc=None)
                    p.op("tensor", lambda e, g=g, sb=sb, slot=slot: e.matmul(
                        ps[sb].ap()[:, :], khb[:, g * 128:(g + 1) * 128], qbuf,
                        start=(slot < 0), stop=True),
                        wait=w0 if slot < 0 else (), inc=cT)
                    return cT.n

                p.op("vector", lambda e: e.memset(mxcol, -1e30), inc=cV)
                for idx, g in enumerate(glist):
                    sb = "psE" if idx % 2 == 0 else "psF"
                    smm = s_tile(g, sb, ([tqh] + tkh + tvh) if idx < 2 else ())
                    p.op("vector", lambda e, sb=sb: e.reduce_max(
                        rmcol, ps[sb].ap()[:, :], AX.X), wait=[(cT, smm)], inc=cV)
                    p.op("vector", lambda e, qb=qb, g=g: e.scalar_tensor_tensor(
                        mxcol, rmcol, kmq_s[:, qb * 16 + g:qb * 16 + g + 1],
                        mxcol, ALU.add, ALU.max), inc=cV)
                    gset(sb, cV)
                mx_done = cV.n
                p.op("tensor", lambda e: e.transpose(
                    ps["psG"].ap()[0:1, 0:128], mxcol, sident.ap()[:, :]),
                    wait=[(cV, mx_done)] + gwait("psG"), inc=cT)
                tpd = cT.n
                p.op("vector", lambda e: e.reduce_max(
                    m11, ps["psG"].ap()[0:1, 0:128], AX.X),
                    wait=[(cT, tpd)], inc=cV)
                gset("psG", cV)
                p.op("vector", lambda e: e.tensor_scalar(
                    m11, m11, -ISC, None, ALU.mult), inc=cV)
                nmd = cV.n
                p.op("tensor", lambda e: e.matmul(
                    ps["psG"].ap()[:, 0:1], ones_r.ap()[:, :], m11,
                    start=True, stop=True),
                    wait=[(cV, nmd), (cV, ones_ready)] + gwait("psG"), inc=cT)
                bmd = cT.n
                p.op("vector", lambda e: e.tensor_copy(mcol, ps["psG"].ap()[:, 0:1]),
                     wait=[(cT, bmd)], inc=cV)
                gset("psG", cV)
                p.op("vector", lambda e, qb=qb: e.tensor_tensor(
                    biasq, kmq_s[:, qb * 16:(qb + 1) * 16],
                    mcol.broadcast_to([128, 16]), ALU.add), inc=cV)
                bq_done = cV.n

                for idx, g in enumerate(glist):
                    sb = "psE" if idx % 2 == 0 else "psF"
                    smm = s_tile(g, sb)
                    p.op("scalar", lambda e, g=g, sb=sb: e.activation(
                        pb, ps[sb].ap()[:, :], AF.Exp, scale=ISC,
                        bias=biasq[:, g:g + 1]),
                        wait=[(cT, smm), (cV, bq_done)], inc=cSc)
                    gset(sb, cSc)
                    pd = cSc.n
                    p.op("tensor", lambda e, idx=idx, nl=nl: e.matmul(
                        ps["psH"].ap()[0:1, :], ones_c.ap()[:, :], pb,
                        start=(idx == 0), stop=(idx == nl - 1)),
                        wait=[(cSc, pd)] + (gwait("psH") if idx == 0 else []),
                        inc=None)
                    p.op("tensor", lambda e, g=g, idx=idx, nl=nl: e.matmul(
                        ps["psG"].ap()[:, :], vhb[:, g * 128:(g + 1) * 128], pb,
                        start=(idx == 0), stop=(idx == nl - 1)),
                        wait=(gwait("psG") if idx == 0 else []),
                        inc=(cT if idx == nl - 1 else None))
                mmo = cT.n
                p.op("vector", lambda e: e.tensor_copy(
                    pb[0:1, :], ps["psH"].ap()[0:1, :]), wait=[(cT, mmo)], inc=cV)
                p.op("vector", lambda e: e.reciprocal(pb[0:1, :], pb[0:1, :]),
                     inc=cV)
                rd = cV.n
                p.op("tensor", lambda e: e.matmul(
                    ps["psH"].ap()[:, :], ones_r.ap()[:, :], pb[0:1, :],
                    start=True, stop=True), wait=[(cV, rd)], inc=cT)
                rbd = cT.n
                gset("psH", cT)
                p.op("vector", lambda e: e.tensor_copy(
                    sqs[:, 0:512], ps["psH"].ap()[:, :]), wait=[(cT, rbd)], inc=cV)
                gset("psH", cV)
                p.op("vector", lambda e, h=h, qb=qb: e.tensor_tensor(
                    ofm[:, h * T + qb * 512:h * T + (qb + 1) * 512],
                    ps["psG"].ap()[:, :], sqs[:, 0:512], ALU.mult), inc=cV)
                gset("psG", cV)
                guard["at_qbuf"] = (cV, cV.n)
            gset("at_khb", cV)
            gset("at_vhb", cV)

        tbo = p.dma("sync", lambda e: e.dma_start(out=borow, in_=W[pre + "_bo"]),
                    wait=gwait("at_khb") + [(cT, cT.n)])
        res_done = {}
        for oc in range(4):
            for ec in range(DC):
                slot = ec % 2
                wt = wsl[:, slot * D:slot * D + 512]
                twt = p.dma("sync", lambda e, ec=ec, oc=oc, wt=wt: e.dma_start(
                    out=wt, in_=W[pre + "_woT"][ec, :, oc * 512:(oc + 1) * 512]),
                    wait=gwait(("at_w", slot)))
                for tt in range(TT):
                    qb, j = tt // 4, tt % 4
                    p.op("tensor", lambda e, ec=ec, tt=tt, qb=qb, j=j, wt=wt: e.matmul(
                        ps[pnames8[tt]].ap()[:, :],
                        ofm[:, ec * T + qb * 512 + j * 128:
                            ec * T + qb * 512 + (j + 1) * 128], wt,
                        start=(ec == 0), stop=False),
                        wait=[twt] + (gwait(pnames8[tt]) if ec == 0 else []),
                        inc=(cT if tt == TT - 1 else None))
                gset(("at_w", slot), cT)
            for tt in range(TT):
                p.op("tensor", lambda e, tt=tt, oc=oc: e.matmul(
                    ps[pnames8[tt]].ap()[:, :], ones_r.ap()[:, :],
                    borow[:, oc * 512:(oc + 1) * 512], start=False, stop=True),
                    wait=[(cV, ones_ready), tbo], inc=cT)
                mmw = cT.n
                p.op("vector", lambda e, tt=tt, oc=oc: e.tensor_add(
                    xtile(tt)[:, oc * 512:(oc + 1) * 512],
                    xtile(tt)[:, oc * 512:(oc + 1) * 512],
                    ps[pnames8[tt]].ap()[:, :]), wait=[(cT, mmw)], inc=cV)
                gset(pnames8[tt], cV)
                res_done[(tt, oc)] = (cV, cV.n)
        ln_waits = []
        for tt in range(TT):
            ln_waits.append(ln_tile_inplace(
                xtile(tt), ofm[:, 0:D], wait=[res_done[(tt, oc)] for oc in range(4)]))
        arena.release()
        return [(cSc, max(ln_waits))]

    def quant_round(dst_ap, src_ap, scol, qtmp, width):
        xs = qtmp[:, 0:width]
        tf = qtmp[:, width:2 * width]
        ti = qtmp[:, 2 * width:3 * width].bitcast(I32)
        p.op("vector", lambda e: e.tensor_scalar(xs, src_ap, scol, None, ALU.mult),
             inc=cV)
        p.op("vector", lambda e: e.tensor_copy(ti, xs), inc=cV)
        p.op("vector", lambda e: e.tensor_copy(tf, ti), inc=cV)
        p.op("vector", lambda e: e.tensor_sub(xs, xs, tf), inc=cV)
        p.op("vector", lambda e: e.scalar_tensor_tensor(
            tf, xs, 0.5, tf, ALU.is_ge, ALU.add), inc=cV)
        p.op("vector", lambda e: e.scalar_tensor_tensor(
            tf, xs, -0.5, tf, ALU.is_ge, ALU.add), inc=cV)
        p.op("vector", lambda e: e.tensor_scalar(dst_ap, tf, -1.0, None, ALU.add),
             inc=cV)
        return cV.n

    def stage_l2f(ready):
        IC = 64
        IP = 16
        arena.mark()
        kxf = arena.alloc(DC * T // 2).bitcast(BF16)
        terng = arena.alloc(DC * 256).bitcast(BF16)
        ternv = arena.alloc(DC * 256).bitcast(BF16)
        qtmp = arena.alloc(3 * D)
        if gpre is None:
            wfp = arena.alloc(512)
            wabs = arena.alloc(512)
            wsg = arena.alloc(512)
        else:
            wfp = wabs = wsg = None
        kxtm = arena.alloc(D)
        hpiece = arena.alloc(512)
        gpiece = arena.alloc(512)
        gbb = arena.alloc(512)
        vbb = arena.alloc(512)
        obb = arena.alloc(D)
        rows = arena.alloc(512 + 512 + D)
        khstage = arena.alloc(T).bitcast(BF16)
        khic = arena.alloc(T // 2).bitcast(BF16)
        owt = arena.alloc(256).bitcast(BF16)
        smq = arena.alloc(80)
        sq = qtmp[:, 0:D]
        gbrow = rows[0:1, 0:512]
        vbrow = rows[0:1, 512:1024]
        obrow = rows[0:1, 1024:1024 + D]

        gxc = smq[:, 0:8]
        sxc = smq[:, 8:16]
        hmx = smq[:, 16:24]
        shc = smq[:, 24:32]
        sgg = smq[:, 32:40]
        sgv = smq[:, 40:48]
        sgo = smq[:, 48:56]
        acc1 = smq[:, 58:59]
        tmp1 = smq[:, 59:60]
        m11b = smq[0:1, 60:61]
        gamcol = {"g": smq[:, 61:62], "v": smq[:, 62:63], "o": smq[:, 63:64]}
        g05col = {"g": smq[:, 64:65], "v": smq[:, 65:66], "o": smq[:, 66:67]}

        tob = p.dma("sync", lambda e: e.dma_start(out=obrow, in_=W["l2f_ob"]))
        obb_done = bcast_row(obrow, obb, D, wait=[tob])
        p.op("vector", lambda e: e.memset(hmx, 0.0), inc=cV)

        PRE = gpre is not None
        if PRE:
            gammas = {k: prep_state["done"][k] for k in "gvo"}
            for ci, key in enumerate("gvo"):
                p.op("vector", lambda e, ci=ci, key=key: e.tensor_copy(
                    gamcol[key], gpre[:, ci:ci + 1]),
                    wait=[prep_state["done"][key]], inc=cV)
                gammas[key] = (cV, cV.n)
        if not PRE:
            gammas = {}
        for wname, nslab, key in (() if PRE else
                                  (("l2f_gw", 16, "g"), ("l2f_vw", 16, "v"),
                                   ("l2f_ow", 64, "o"))):
            p.op("vector", lambda e: e.memset(acc1, 0.0), inc=cV)
            npc = 4 if nslab == 16 else 1
            for sl in range(nslab):
                for pc in range(npc):
                    tl = p.dma("sync", lambda e, wname=wname, sl=sl, pc=pc: e.dma_start(
                        out=qtmp[:, 0:2048],
                        in_=W[wname][sl, :, pc * 2048:(pc + 1) * 2048]),
                        wait=gwait("l2f_qtmp"))
                    p.op("scalar", lambda e: e.activation(
                        qtmp[:, 2048:4096], qtmp[:, 0:2048], AF.Abs,
                        accum_out=tmp1), wait=[tl] + gwait("l2f_tmp1"), inc=cSc)
                    sd = cSc.n
                    guard["l2f_qtmp"] = (cSc, sd)
                    p.op("vector", lambda e: e.tensor_add(acc1, acc1, tmp1),
                         wait=[(cSc, sd)], inc=cV)
                    gset("l2f_tmp1", cV)
            ad = cV.n
            p.op("tensor", lambda e: e.matmul(
                ps["psH"].ap()[0:1, 0:1], ones_c.ap()[:, :], acc1,
                start=True, stop=True),
                wait=[(cV, ad), (cV, ones_ready)] + gwait("psH"), inc=cT)
            td = cT.n
            p.op("vector", lambda e: e.tensor_scalar(
                m11b, ps["psH"].ap()[0:1, 0:1], 1.0 / (8192.0 * 2048.0), None,
                ALU.mult), wait=[(cT, td)], inc=cV)
            gset("psH", cV)
            p.op("vector", lambda e: e.tensor_scalar_max(m11b, m11b, 1e-5), inc=cV)
            gd = cV.n
            p.op("tensor", lambda e: e.matmul(
                ps["psH"].ap()[:, 0:1], ones_r.ap()[:, :], m11b,
                start=True, stop=True), wait=[(cV, gd)] + gwait("psH"), inc=cT)
            td = cT.n
            p.op("vector", lambda e, key=key: e.tensor_copy(
                gamcol[key], ps["psH"].ap()[:, 0:1]), wait=[(cT, td)], inc=cV)
            gset("psH", cV)
            p.op("vector", lambda e, key=key: e.tensor_scalar(
                g05col[key], gamcol[key], 0.5, None, ALU.mult), inc=cV)
            gammas[key] = (cV, cV.n)

        for tt in range(TT):
            p.op("vector", lambda e, tt=tt: e.reduce_max(
                gxc[:, tt:tt + 1], xtile(tt), AX.X, apply_absolute_value=True),
                wait=list(ready), inc=cV)
            p.op("vector", lambda e, tt=tt: e.tensor_scalar_max(
                gxc[:, tt:tt + 1], gxc[:, tt:tt + 1], 1e-5), inc=cV)
            p.op("vector", lambda e, tt=tt: e.reciprocal(
                sxc[:, tt:tt + 1], gxc[:, tt:tt + 1]), inc=cV)
            p.op("vector", lambda e, tt=tt: e.tensor_scalar(
                sxc[:, tt:tt + 1], sxc[:, tt:tt + 1], 127.0, None, ALU.mult), inc=cV)
            p.wait("vector", gwait("l2f_kxtm2"))
            quant_round(kxtm, xtile(tt), sxc[:, tt:tt + 1], qtmp, D)
            kd = cV.n
            for dc in range(DC):
                bank = TPB[dc % 4]
                p.op("tensor", lambda e, dc=dc, bank=bank: e.transpose(
                    ps[bank].ap()[:, 0:128], kxtm[:, dc * 128:(dc + 1) * 128],
                    sident.ap()[:, :]), wait=[(cV, kd)] + gwait(bank), inc=cT)
                td2 = cT.n
                p.op("vector", lambda e, dc=dc, tt=tt, bank=bank: e.tensor_copy(
                    kxf[:, (dc * TT + tt) * 128:(dc * TT + tt + 1) * 128],
                    ps[bank].ap()[:, 0:128]), wait=[(cT, td2)], inc=cV)
                gset(bank, cV)
            guard["l2f_kxtm2"] = (cT, cT.n)
        kx_done = cV.n

        def kxt(dc, tt):
            return kxf[:, (dc * TT + tt) * 128:(dc * TT + tt + 1) * 128]

        for key, dst in (("g", sgg), ("v", sgv)):
            p.op("vector", lambda e, key=key, dst=dst: e.tensor_tensor(
                dst, gxc, gamcol[key].broadcast_to([128, 8]), ALU.mult),
                wait=[gammas[key], (cV, kx_done)], inc=cV)
            p.op("vector", lambda e, dst=dst: e.tensor_scalar(
                dst, dst, 1.0 / 127.0, None, ALU.mult), inc=cV)
        sg_done = cV.n

        TDN = {"l2f_gw": "l2f_tg", "l2f_vw": "l2f_tv", "l2f_ow": "l2f_to"}

        def ternize(wname, sl, cols, dst_bf, key, extra=()):
            if PRE:
                tk = prep_state["ttoks"][(TDN[wname], sl, cols)]
                return p.dma("sync", lambda e, wname=wname, sl=sl, cols=cols,
                             dst_bf=dst_bf: e.dma_start(
                    out=dst_bf,
                    in_=dram[TDN[wname]].ap()[sl, cols]),
                    wait=list(extra) + [tk])
            tl = p.dma("sync", lambda e, wname=wname, sl=sl, cols=cols: e.dma_start(
                out=wfp, in_=W[wname][sl, :, cols * 512:(cols + 1) * 512]),
                wait=gwait("l2f_wfp") + list(extra))
            p.op("scalar", lambda e: e.activation(wabs, wfp, AF.Abs),
                 wait=[tl] + gwait("l2f_wsg"), inc=cSc)
            p.op("scalar", lambda e: e.sign(wsg, wfp), inc=cSc)
            sd = cSc.n
            guard["l2f_wfp"] = (cSc, sd)
            p.op("vector", lambda e, key=key: e.tensor_scalar(
                wabs, wabs, g05col[key], None, ALU.is_ge),
                wait=[(cSc, sd), gammas[key]], inc=cV)
            p.op("vector", lambda e, dst_bf=dst_bf: e.tensor_tensor(
                dst_bf, wabs, wsg, ALU.mult), inc=cV)
            guard["l2f_wsg"] = (cV, cV.n)
            return (cV, cV.n)

        h_toks = {}
        for ip in range(IP):
            tgb = p.dma("sync", lambda e, ip=ip: e.dma_start(
                out=gbrow, in_=W["l2f_gb"][0:1, ip * 512:(ip + 1) * 512]),
                wait=gwait("l2f_gbrow"))
            gb_done = bcast_row(gbrow, gbb, 512, wait=[tgb])
            guard["l2f_gbrow"] = (cV, gb_done)
            tvb = p.dma("sync", lambda e, ip=ip: e.dma_start(
                out=vbrow, in_=W["l2f_vb"][0:1, ip * 512:(ip + 1) * 512]),
                wait=gwait("l2f_vbrow"))
            vb_done = bcast_row(vbrow, vbb, 512, wait=[tvb])
            guard["l2f_vbrow"] = (cV, vb_done)
            tern_toks = []
            for dc in range(DC):
                tern_toks.append(ternize(
                    "l2f_gw", dc, ip, terng[:, dc * 512:(dc + 1) * 512], "g",
                    extra=gwait(("l2f_tern", 0))))
                tern_toks.append(ternize(
                    "l2f_vw", dc, ip, ternv[:, dc * 512:(dc + 1) * 512], "v",
                    extra=gwait(("l2f_tern", 0))))
            mmg = mmv = 0
            for tt in range(TT):
                for bank, tern in (("psA", terng), ("psB", ternv)):
                    for dc in range(DC):
                        p.op("tensor", lambda e, dc=dc, tt=tt, bank=bank, tern=tern: e.matmul(
                            ps[bank].ap()[:, :], kxt(dc, tt),
                            tern[:, dc * 512:(dc + 1) * 512],
                            start=(dc == 0), stop=(dc == DC - 1)),
                            wait=(tern_toks + [(cV, kx_done)] + gwait(bank))
                            if dc == 0 else (),
                            inc=(cT if dc == DC - 1 else None))
                mmg, mmv = cT.n - 1, cT.n
                p.op("vector", lambda e, tt=tt: e.tensor_scalar(
                    gpiece, ps["psA"].ap()[:, :], sgg[:, tt:tt + 1], None, ALU.mult),
                    wait=[(cT, mmg), (cV, sg_done)] + gwait("l2f_gp"), inc=cV)
                gset("psA", cV)
                p.op("vector", lambda e: e.tensor_add(gpiece, gpiece, gbb),
                     wait=[(cV, gb_done)], inc=cV)
                vg = cV.n
                p.op("scalar", lambda e: e.activation(gpiece, gpiece, AF.Silu),
                     wait=[(cV, vg)], inc=cSc)
                sg_ = cSc.n
                p.op("vector", lambda e, tt=tt: e.tensor_scalar(
                    hpiece, ps["psB"].ap()[:, :], sgv[:, tt:tt + 1], None, ALU.mult),
                    wait=[(cT, mmv)] + gwait("l2f_hp"), inc=cV)
                gset("psB", cV)
                p.op("vector", lambda e: e.tensor_add(hpiece, hpiece, vbb),
                     wait=[(cV, vb_done)], inc=cV)
                p.op("vector", lambda e: e.tensor_mul(hpiece, hpiece, gpiece),
                     wait=[(cSc, sg_)], inc=cV)
                guard["l2f_gp"] = (cV, cV.n)
                p.op("vector", lambda e: e.reduce_max(
                    tmp1, hpiece, AX.X, apply_absolute_value=True), inc=cV)
                p.op("vector", lambda e, tt=tt: e.tensor_max(
                    hmx[:, tt:tt + 1], hmx[:, tt:tt + 1], tmp1), inc=cV)
                hd = cV.n
                h_toks[(tt, ip)] = p.dma("gpsimd", lambda e, tt=tt, ip=ip: e.dma_start(
                    out=dram["l2f_h"].ap()[tt * 128:(tt + 1) * 128,
                                           ip * 512:(ip + 1) * 512],
                    in_=hpiece), wait=[(cV, hd)])
                guard["l2f_hp"] = h_toks[(tt, ip)]
            guard[("l2f_tern", 0)] = (cT, mmv)

        p.op("vector", lambda e: e.tensor_scalar_max(hmx, hmx, 1e-5), inc=cV)
        p.op("vector", lambda e: e.reciprocal(shc, hmx), inc=cV)
        p.op("vector", lambda e: e.tensor_scalar(shc, shc, 127.0, None, ALU.mult),
             inc=cV)
        p.op("vector", lambda e: e.tensor_tensor(
            sgo, hmx, gamcol["o"].broadcast_to([128, 8]), ALU.mult),
            wait=[gammas["o"]], inc=cV)
        p.op("vector", lambda e: e.tensor_scalar(
            sgo, sgo, 1.0 / 127.0, None, ALU.mult), inc=cV)
        sh_done = cV.n
        kh_toks = {}
        for tt in range(TT):
            for pcq in range(4):
                tl = p.dma("sync", lambda e, tt=tt, pcq=pcq: e.dma_start(
                    out=kxtm,
                    in_=dram["l2f_h"].ap()[tt * 128:(tt + 1) * 128,
                                           pcq * 2048:(pcq + 1) * 2048]),
                    wait=gwait("l2f_kxtm2") + [h_toks[(tt, pcq * 4 + i)]
                                               for i in range(4)])
                p.op("vector", lambda e: e.engine_nop(),
                     wait=[tl, (cV, sh_done)], inc=cV)
                quant_round(kxtm, kxtm, shc[:, tt:tt + 1], qtmp, D)
                qd = cV.n
                for dc in range(DC):
                    bank = TPB[dc % 4]
                    p.op("tensor", lambda e, dc=dc, bank=bank: e.transpose(
                        ps[bank].ap()[:, 0:128], kxtm[:, dc * 128:(dc + 1) * 128],
                        sident.ap()[:, :]), wait=[(cV, qd)] + gwait(bank), inc=cT)
                    td2 = cT.n
                    p.op("vector", lambda e, dc=dc, bank=bank: e.tensor_copy(
                        khstage[:, dc * 128:(dc + 1) * 128],
                        ps[bank].ap()[:, 0:128]),
                        wait=[(cT, td2)] + gwait("l2f_khs"), inc=cV)
                    gset(bank, cV)
                guard["l2f_kxtm2"] = (cT, cT.n)
                kd = cV.n
                kh_toks[(tt, pcq)] = p.dma("gpsimd", lambda e, tt=tt, pcq=pcq: e.dma_start(
                    out=dram["l2f_kh"].ap()[pcq * 16:(pcq + 1) * 16, :,
                                            tt * 128:(tt + 1) * 128]
                    .transpose([1, 0, 2]),
                    in_=khstage.rearrange("p (c k) -> p c k", k=128)),
                    wait=[(cV, kd)] + gwait("l2f_khs"))
                guard["l2f_khs"] = kh_toks[(tt, pcq)]

        pnames8 = ("psA", "psB", "psC", "psD", "psE", "psF", "psG", "psH")
        res_done = {}
        for oc in range(4):
            for ic in range(IC):
                tko = p.dma("sync", lambda e, ic=ic: e.dma_start(
                    out=khic, in_=dram["l2f_kh"].ap()[ic // 4 if False else ic]
                    if False else dram["l2f_kh"].ap()[ic]),
                    wait=gwait("l2f_khic") + [kh_toks[(tt2, ic // 16)]
                                              for tt2 in range(TT)])
                od = ternize("l2f_ow", ic, oc, owt, "o",
                             extra=gwait("l2f_wsg"))
                for tt in range(TT):
                    p.op("tensor", lambda e, ic=ic, tt=tt: e.matmul(
                        ps[pnames8[tt]].ap()[:, :],
                        khic[:, tt * 128:(tt + 1) * 128], owt,
                        start=(ic == 0), stop=(ic == IC - 1)),
                        wait=[tko, od] + (gwait(pnames8[tt]) if ic == 0 else []),
                        inc=(cT if tt == TT - 1 else None))
                guard["l2f_khic"] = (cT, cT.n)
                guard["l2f_wsg"] = (cT, cT.n)
            mmo = cT.n
            for tt in range(TT):
                p.op("vector", lambda e, tt=tt, oc=oc: e.scalar_tensor_tensor(
                    gpiece, ps[pnames8[tt]].ap()[:, :], sgo[:, tt:tt + 1],
                    obb[:, oc * 512:(oc + 1) * 512], ALU.mult, ALU.add),
                    wait=[(cT, mmo), (cV, obb_done)] + gwait("l2f_gp"), inc=cV)
                p.op("vector", lambda e, tt=tt, oc=oc: e.tensor_add(
                    xtile(tt)[:, oc * 512:(oc + 1) * 512],
                    xtile(tt)[:, oc * 512:(oc + 1) * 512], gpiece), inc=cV)
                guard["l2f_gp"] = (cV, cV.n)
                gset(pnames8[tt], cV)
                res_done[(tt, oc)] = (cV, cV.n)
        ln_waits = []
        for tt in range(TT):
            ln_waits.append(ln_tile_inplace(
                xtile(tt), sq, wait=[res_done[(tt, oc)] for oc in range(4)]))
        arena.release()
        return [(cSc, max(ln_waits))]

    prep_state = {"steps": [], "done": {}}

    def build_l2f_prepass(buf):
        qa = buf[:, 0:512]
        qb_ = buf[:, 512:1024]
        qs = buf[:, 1024:1536]
        acc1 = buf[:, 1536:1537]
        tmp1 = buf[:, 1537:1538]
        m11 = buf[0:1, 1538:1539]
        tern_bf = buf[:, 1539:1539 + 256].bitcast(BF16)
        steps = prep_state["steps"]
        wspec = (("l2f_gw", "l2f_tg", 16, "g", 0), ("l2f_vw", "l2f_tv", 16, "v", 1),
                 ("l2f_ow", "l2f_to", 64, "o", 2))

        def emit_gamma_piece(wname, sl, pc, first, last, key, ci):
            def go():
                tl = p.dma("scalar", lambda e: e.dma_start(
                    out=qa, in_=W[wname][sl, :, pc * 512:(pc + 1) * 512]),
                    wait=gwait("pp_qa"), pool="pp")
                if first:
                    p.op("vector", lambda e: e.memset(acc1, 0.0),
                         wait=gwait("pp_acc"), inc=cV)
                p.op("scalar", lambda e: e.activation(
                    qb_, qa, AF.Abs, accum_out=tmp1),
                    wait=[tl] + gwait("pp_tmp1"), inc=cSc)
                sd = cSc.n
                guard["pp_qa"] = (cSc, sd)
                p.op("vector", lambda e: e.tensor_add(acc1, acc1, tmp1),
                     wait=[(cSc, sd)], inc=cV)
                gset("pp_tmp1", cV)
                if last:
                    ad = cV.n
                    p.op("tensor", lambda e: e.matmul(
                        ps["psH"].ap()[0:1, 0:1], ones_c.ap()[:, :], acc1,
                        start=True, stop=True),
                        wait=[(cV, ad), (cV, ones_ready)] + gwait("psH"), inc=cT)
                    td = cT.n
                    p.op("vector", lambda e: e.tensor_scalar(
                        m11, ps["psH"].ap()[0:1, 0:1], 1.0 / (8192.0 * 2048.0),
                        None, ALU.mult), wait=[(cT, td)], inc=cV)
                    gset("psH", cV)
                    p.op("vector", lambda e: e.tensor_scalar_max(m11, m11, 1e-5),
                         inc=cV)
                    gd = cV.n
                    p.op("tensor", lambda e: e.matmul(
                        ps["psH"].ap()[:, 0:1], ones_r.ap()[:, :], m11,
                        start=True, stop=True),
                        wait=[(cV, gd)] + gwait("psH"), inc=cT)
                    td2 = cT.n
                    p.op("vector", lambda e, ci=ci: e.tensor_copy(
                        gpre[:, ci:ci + 1], ps["psH"].ap()[:, 0:1]),
                        wait=[(cT, td2)], inc=cV)
                    gset("psH", cV)
                    p.op("vector", lambda e, ci=ci: e.tensor_scalar(
                        gpre[:, 3 + ci:4 + ci], gpre[:, ci:ci + 1], 0.5, None,
                        ALU.mult), inc=cV)
                    prep_state["done"][key] = (cV, cV.n)
                    gset("pp_acc", cV)
            return go

        def emit_tern_piece(wname, dname, sl, pc, key, ci):
            def go():
                tl = p.dma("scalar", lambda e: e.dma_start(
                    out=qa, in_=W[wname][sl, :, pc * 512:(pc + 1) * 512]),
                    wait=gwait("pp_qa"), pool="pp")
                p.op("scalar", lambda e: e.activation(qb_, qa, AF.Abs),
                     wait=[tl] + gwait("pp_qb"), inc=cSc)
                p.op("scalar", lambda e: e.sign(qs, qa), inc=cSc)
                sd = cSc.n
                guard["pp_qa"] = (cSc, sd)
                p.op("vector", lambda e, ci=ci: e.tensor_scalar(
                    qb_, qb_, gpre[:, 3 + ci:4 + ci], None, ALU.is_ge),
                    wait=[(cSc, sd), prep_state["done"][key]], inc=cV)
                p.op("vector", lambda e: e.tensor_tensor(
                    tern_bf, qb_, qs, ALU.mult),
                    wait=gwait("pp_tern"), inc=cV)
                gset("pp_qb", cV)
                vd = cV.n
                tk = p.dma("gpsimd", lambda e: e.dma_start(
                    out=dram[dname].ap()[sl, pc],
                    in_=tern_bf), wait=[(cV, vd)], pool="pp")
                guard["pp_tern"] = tk
                prep_state.setdefault("ttoks", {})[(dname, sl, pc)] = tk
            return go

        for wname, dname, nslab, key, ci in wspec:
            npcs = (8192 if nslab == 16 else 2048) // 512
            tot = nslab * npcs
            i = 0
            for sl in range(nslab):
                for pc in range(npcs):
                    steps.append(emit_gamma_piece(wname, sl, pc, i == 0,
                                                  i == tot - 1, key, ci))
                    i += 1
        for wname, dname, nslab, key, ci in wspec:
            npcs = (8192 if nslab == 16 else 2048) // 512
            for sl in range(nslab):
                for pc in range(npcs):
                    steps.append(emit_tern_piece(wname, dname, sl, pc, key, ci))

    def emit_prepass_chunk(k):
        if gpre is None:
            return
        steps = prep_state["steps"]
        for _ in range(min(k, len(steps))):
            steps.pop(0)()

    def stage_l2a(ready):
        NS = 16
        TCN = 4
        TC = T // TCN
        RG = [[0, 1], [2, 3], [4, 5], [6, 7]]

        def fence():
            w = [(cT, cT.n), (cV, cV.n), (cSc, cSc.n)]
            for key, ctrs in p.pools.items():
                if key[1] == "pp":
                    continue
                for c in ctrs:
                    w.append((c, c.n))
            p.wait("sync", w)
            p.wait("gpsimd", w)

        emit_prepass_chunk(10 ** 9)
        arena.mark()
        halob = arena.alloc(48)
        h0 = arena.alloc(48)
        h1 = arena.alloc(48)
        haloe = arena.alloc(48)

        arena.mark()
        xT = arena.alloc(DC * T)
        wsl = arena.alloc(2 * D)
        evb = arena.alloc(T)

        pairs = []
        for tt in range(TT):
            for dc in range(DC):
                pairs.append((xtile(tt)[:, dc * 128:(dc + 1) * 128],
                              xT[:, dc * T + tt * 128:dc * T + (tt + 1) * 128]))
        xt_done = transpose_tiles(pairs, wait=list(ready))

        xc_toks, r_toks = {}, {}
        for ec in range(32):
            slot = ec % 2
            wsl_s = wsl[:, slot * D:(slot + 1) * D]
            twl = p.dma("sync", lambda e, ec=ec, wsl_s=wsl_s: e.dma_start(
                out=wsl_s, in_=W["l2a_win"][ec]), wait=gwait(("l2a_w", slot)))
            for tc2 in range(2):
                bank = "psE" if tc2 == 0 else "psF"
                for dc in range(DC):
                    p.op("tensor", lambda e, dc=dc, tc2=tc2, wsl_s=wsl_s, bank=bank: e.matmul(
                        ps[bank].ap()[:, :], wsl_s[:, dc * 128:(dc + 1) * 128],
                        xT[:, dc * T + tc2 * 512:dc * T + (tc2 + 1) * 512],
                        start=(dc == 0), stop=(dc == DC - 1)),
                        wait=[twl, (cV, xt_done)] + (gwait(bank) if dc == 0 else []),
                        inc=(cT if dc == DC - 1 else None))
                mm = cT.n
                p.op("vector", lambda e, tc2=tc2, bank=bank: e.tensor_copy(
                    evb[:, tc2 * 512:(tc2 + 1) * 512], ps[bank].ap()[:, :]),
                    wait=[(cT, mm)] + gwait("l2a_evb"), inc=cV)
                gset(bank, cV)
            gset(("l2a_w", slot), cT)
            ed = cV.n
            if ec < 16:
                p.op("vector", lambda e, ec=ec: e.tensor_copy(
                    halob[:, ec * 3:(ec + 1) * 3], evb[:, T - 3:T]), inc=cV)
                tsp = p.dma("gpsimd", lambda e, ec=ec: e.dma_start(
                    out=dram["l2a_xc"].ap()[ec * 128:(ec + 1) * 128, :], in_=evb),
                    wait=[(cV, cV.n)])
                xc_toks[ec] = tsp
            else:
                tsp = p.dma("gpsimd", lambda e, ec=ec: e.dma_start(
                    out=dram["l2a_r"].ap()[(ec - 16) * 128:(ec - 15) * 128, :],
                    in_=evb), wait=[(cV, ed)])
                r_toks[ec - 16] = tsp
            guard["l2a_evb"] = tsp
        hde = cV.n
        thi = p.dma("gpsimd", lambda e: e.dma_start(
            out=dram["l2a_hloin"].ap(), in_=halob), wait=[(cV, hde)])
        p.op("gpsimd", lambda e: e.collective_compute(
            "AllGather", ALU.bypass, replica_groups=RG,
            ins=[dram["l2a_hloin"].ap()], outs=[dram["l2a_hloout"].ap()]),
            wait=[thi], inc=cG)
        ccd = cG.n
        t0 = p.dma("gpsimd", lambda e: e.dma_start(
            out=h0, in_=dram["l2a_hloout"].ap()[0:128, :]), wait=[(cG, ccd)])
        t1 = p.dma("gpsimd", lambda e: e.dma_start(
            out=h1, in_=dram["l2a_hloout"].ap()[128:256, :]), wait=[(cG, ccd)])
        p.op("vector", lambda e: e.tensor_scalar(
            haloe, h0, gatesb.ap()[:, 0:1], None, ALU.mult), wait=[t0, t1], inc=cV)
        p.op("vector", lambda e: e.scalar_tensor_tensor(
            haloe, h1, gatesb.ap()[:, 1:2], haloe, ALU.mult, ALU.add), inc=cV)
        halo_done = cV.n
        arena.release()
        fence()

        arena.mark()
        XST = 1027
        xcbuf = arena.alloc(DC * XST)
        cwsl = arena.alloc(8192)
        wxsl = arena.alloc(DC * 160)
        cbf = arena.alloc(16)
        piece = arena.alloc(512)
        upiece = arena.alloc(512)

        twx = p.dma("sync", lambda e: e.dma_start(
            out=wxsl.rearrange("p (c k) -> p c k", k=160), in_=W["l2a_wx"]))
        tcb = p.dma("sync", lambda e: e.dma_start(out=cbf, in_=W["l2a_convb"]))
        for ic in range(DC):
            tld = p.dma("sync", lambda e, ic=ic: e.dma_start(
                out=xcbuf[:, ic * XST + 3:(ic + 1) * XST],
                in_=dram["l2a_xc"].ap()[ic * 128:(ic + 1) * 128, :]),
                wait=[xc_toks[ic]])
            p.op("vector", lambda e, ic=ic: e.tensor_copy(
                xcbuf[:, ic * XST:ic * XST + 3], haloe[:, ic * 3:(ic + 1) * 3]),
                wait=[tld, (cV, halo_done)], inc=cV)
        xcb_done = cV.n
        u_toks, xs_toks = {}, {}
        for oc in range(DC):
            tcw = p.dma("sync", lambda e, oc=oc: e.dma_start(
                out=cwsl, in_=W["l2a_cw"][oc]), wait=gwait("l2a_cwsl"))
            for tc2 in range(2):
                bank = "psE" if tc2 == 0 else "psF"
                first = True
                for tap in range(4):
                    for ic in range(DC):
                        p.op("tensor", lambda e, tap=tap, ic=ic, tc2=tc2, bank=bank,
                             first=first: e.matmul(
                            ps[bank].ap()[:, :],
                            cwsl[:, tap * 2048 + ic * 128:tap * 2048 + (ic + 1) * 128],
                            xcbuf[:, ic * XST + tap + tc2 * 512:
                                  ic * XST + tap + tc2 * 512 + 512],
                            start=first, stop=(tap == 3 and ic == DC - 1)),
                            wait=[tcw, (cV, xcb_done)] + (gwait(bank) if first else []),
                            inc=(cT if (tap == 3 and ic == DC - 1) else None))
                        first = False
                mm = cT.n
                p.op("vector", lambda e, oc=oc, bank=bank: e.tensor_scalar(
                    piece, ps[bank].ap()[:, :], cbf[:, oc:oc + 1], None, ALU.add),
                    wait=[(cT, mm), tcb] + gwait("l2a_piece"), inc=cV)
                gset(bank, cV)
                pd = cV.n
                xs_toks[(oc, tc2)] = p.dma("gpsimd", lambda e, oc=oc, tc2=tc2: e.dma_start(
                    out=dram["l2a_xs"].ap()[oc * 128:(oc + 1) * 128,
                                            tc2 * 512:(tc2 + 1) * 512],
                    in_=piece), wait=[(cV, pd)])
                guard["l2a_piece"] = xs_toks[(oc, tc2)]
                p.op("scalar", lambda e: e.activation(upiece, piece, AF.Silu),
                     wait=[(cV, pd)] + gwait("l2a_upiece"), inc=cSc)
                ud = cSc.n
                u_toks[(oc, tc2)] = p.dma("gpsimd", lambda e, oc=oc, tc2=tc2: e.dma_start(
                    out=dram["l2a_u"].ap()[oc * 128:(oc + 1) * 128,
                                           tc2 * 512:(tc2 + 1) * 512],
                    in_=upiece), wait=[(cSc, ud)])
                guard["l2a_upiece"] = u_toks[(oc, tc2)]
                dbank = "psA" if tc2 == 0 else "psB"
                dbank2 = "psC" if tc2 == 0 else "psD"
                p.op("tensor", lambda e, oc=oc, dbank=dbank: e.matmul(
                    ps[dbank].ap()[:, :], wxsl[:, oc * 160:oc * 160 + 128], upiece,
                    start=(oc == 0), stop=(oc == DC - 1)),
                    wait=[twx, (cSc, ud)] + (gwait(dbank) if oc == 0 else []),
                    inc=None)
                p.op("tensor", lambda e, oc=oc, dbank2=dbank2: e.matmul(
                    ps[dbank2].ap()[0:32, :], wxsl[:, oc * 160 + 128:(oc + 1) * 160],
                    upiece, start=(oc == 0), stop=(oc == DC - 1)),
                    wait=(gwait(dbank2) if oc == 0 else []),
                    inc=(cT if oc == DC - 1 else None))
            gset("l2a_cwsl", cT)
        mmdbc = cT.n
        arena.release()
        fence()

        arena.mark()
        dpre = arena.alloc(T)
        bcb = arena.alloc(T)
        wdt = arena.alloc(D)
        bdtf = arena.alloc(16)
        dtile = arena.alloc(T)
        spz = arena.alloc(2 * T)
        for tc2 in range(2):
            p.op("vector", lambda e, tc2=tc2: e.tensor_copy(
                dpre[:, tc2 * 512:(tc2 + 1) * 512],
                ps["psA" if tc2 == 0 else "psB"].ap()[:, :]),
                wait=[(cT, mmdbc)], inc=cV)
            gset("psA" if tc2 == 0 else "psB", cV)
            p.op("vector", lambda e, tc2=tc2: e.tensor_copy(
                bcb[0:32, tc2 * 512:(tc2 + 1) * 512],
                ps["psC" if tc2 == 0 else "psD"].ap()[0:32, :]), inc=cV)
            gset("psC" if tc2 == 0 else "psD", cV)
        dpre_done = cV.n
        twd = p.dma("sync", lambda e: e.dma_start(out=wdt, in_=W["l2a_wdt"]))
        tbd = p.dma("sync", lambda e: e.dma_start(out=bdtf, in_=W["l2a_bdt"]))
        d_toks = {}
        for dc in range(DC):
            for tc2 in range(2):
                bank = "psE" if tc2 == 0 else "psF"
                p.op("tensor", lambda e, dc=dc, tc2=tc2, bank=bank: e.matmul(
                    ps[bank].ap()[:, :], wdt[:, dc * 128:(dc + 1) * 128],
                    dpre[:, tc2 * 512:(tc2 + 1) * 512], start=True, stop=True),
                    wait=[twd, (cV, dpre_done)] + gwait(bank), inc=cT)
                mm = cT.n
                zsl = spz[:, tc2 * 512:(tc2 + 1) * 512]
                asl = spz[:, 1024 + tc2 * 512:1024 + (tc2 + 1) * 512]
                p.op("vector", lambda e, dc=dc, bank=bank, zsl=zsl: e.tensor_scalar(
                    zsl, ps[bank].ap()[:, :], bdtf[:, dc:dc + 1], None, ALU.add),
                    wait=[(cT, mm), tbd] + gwait(("l2a_z", tc2)), inc=cV)
                gset(bank, cV)
                zd = cV.n
                p.op("scalar", lambda e, zsl=zsl, asl=asl: e.activation(
                    asl, zsl, AF.Abs), wait=[(cV, zd)], inc=cSc)
                p.op("scalar", lambda e, asl=asl: e.activation(
                    asl, asl, AF.Exp, scale=-1.0), inc=cSc)
                p.op("scalar", lambda e, asl=asl: e.activation(
                    asl, asl, AF.Ln, bias=1.0), inc=cSc)
                lnd = cSc.n
                p.op("vector", lambda e, dc=dc, tc2=tc2, zsl=zsl, asl=asl: e.scalar_tensor_tensor(
                    dtile[:, tc2 * 512:(tc2 + 1) * 512], zsl, 0.0, asl,
                    ALU.max, ALU.add),
                    wait=[(cSc, lnd)] + gwait("l2a_dtile"), inc=cV)
                guard[("l2a_z", tc2)] = (cV, cV.n)
            dd = cV.n
            d_toks[dc] = p.dma("gpsimd", lambda e, dc=dc: e.dma_start(
                out=dram["l2a_delta"].ap()[dc * 128:(dc + 1) * 128, :], in_=dtile),
                wait=[(cV, dd)] + gwait("l2a_dtile"))
            guard["l2a_dtile"] = d_toks[dc]
        bc_tok = p.dma("gpsimd", lambda e: e.dma_start(
            out=dram["l2a_bc"].ap(), in_=bcb[0:32, :]), wait=[(cV, dpre_done)])
        arena.release()
        fence()

        arena.mark()
        Bb = arena.alloc(NS * TC)
        Cb = arena.alloc(NS * TC)
        dAb2 = [arena.alloc(NS * TC), arena.alloc(NS * TC)]
        dBu2 = [arena.alloc(NS * TC), arena.alloc(NS * TC)]
        hb = arena.alloc(NS * TC)
        dsl2 = [arena.alloc(TC), arena.alloc(TC)]
        usl2 = [arena.alloc(TC), arena.alloc(TC)]
        ysl2 = [arena.alloc(TC), arena.alloc(TC)]
        flat = hb
        hprev = arena.alloc(256)
        hin = arena.alloc(256)
        hcA = arena.alloc(256)
        hcB = arena.alloc(256)
        Abuf = arena.alloc(256)
        zeroc = arena.alloc(1)

        p.op("vector", lambda e: e.memset(hprev, 0.0), inc=cV)
        p.op("vector", lambda e: e.memset(zeroc, 0.0), inc=cV)
        tal = p.dma("sync", lambda e: e.dma_start(
            out=Abuf.rearrange("p (c k) -> p c k", k=16), in_=W["l2a_alog"]))
        p.op("scalar", lambda e: e.activation(Abuf, Abuf, AF.Exp),
             wait=[tal], inc=cSc)
        sa = cSc.n
        p.op("vector", lambda e: e.tensor_scalar(Abuf, Abuf, -1.0, None, ALU.mult),
             wait=[(cSc, sa)], inc=cV)
        a_done = cV.n

        y_toks = {}
        for tcn in range(TCN):
            bcd = {}
            for which, dst, rowoff in (("B", Bb, 0), ("C", Cb, 16)):
                tfl = p.dma("gpsimd", lambda e, rowoff=rowoff, tcn=tcn: e.dma_start(
                    out=flat[0:1, 0:NS * TC],
                    in_=dram["l2a_bc"].ap()[rowoff:rowoff + 16,
                                            tcn * TC:(tcn + 1) * TC]),
                    wait=gwait("l2a_flat") + [bc_tok, (cV, cV.n)])
                dn = 0
                for j in range(0, NS * TC, 512):
                    p.op("tensor", lambda e, j=j: e.matmul(
                        ps["psG"].ap()[:, :], ones_r.ap()[:, :],
                        flat[0:1, j:j + 512], start=True, stop=True),
                        wait=[tfl, (cV, ones_ready)] + gwait("psG"), inc=cT)
                    td = cT.n
                    p.op("vector", lambda e, j=j, dst=dst: e.tensor_copy(
                        dst[:, j:j + 512], ps["psG"].ap()[:, :]),
                        wait=[(cT, td)], inc=cV)
                    gset("psG", cV)
                bcd[which] = cV.n
                guard["l2a_flat"] = (cV, cV.n)
            for dc in range(DC):
                sl = dc % 2
                dAb, dBu = dAb2[sl], dBu2[sl]
                dsl, usl, ysl = dsl2[sl], usl2[sl], ysl2[sl]
                tdl = p.dma("sync", lambda e, dc=dc, tcn=tcn, dsl=dsl: e.dma_start(
                    out=dsl, in_=dram["l2a_delta"].ap()[dc * 128:(dc + 1) * 128,
                                                        tcn * TC:(tcn + 1) * TC]),
                    wait=gwait(("l2a_dsl", sl)) + [d_toks[dc]])
                tul = p.dma("sync", lambda e, dc=dc, tcn=tcn, usl=usl: e.dma_start(
                    out=usl, in_=dram["l2a_u"].ap()[dc * 128:(dc + 1) * 128,
                                                    tcn * TC:(tcn + 1) * TC]),
                    wait=gwait(("l2a_usl", sl)) + [u_toks[(dc, 0)], u_toks[(dc, 1)]])
                for n in range(NS):
                    p.op("scalar", lambda e, dc=dc, n=n, dAb=dAb, dsl=dsl: e.activation(
                        dAb[:, n * TC:(n + 1) * TC], dsl, AF.Exp,
                        scale=Abuf[:, dc * 16 + n:dc * 16 + n + 1]),
                        wait=([tdl, (cV, a_done)] + gwait(("l2a_dAb", sl)))
                        if n == 0 else (), inc=cSc)
                da_done = cSc.n
                p.op("vector", lambda e, ysl=ysl, dsl=dsl, usl=usl: e.tensor_mul(
                    ysl, dsl, usl),
                    wait=[tul, tdl] + gwait(("l2a_ysl", sl)), inc=cV)
                p.op("vector", lambda e, dBu=dBu, ysl=ysl: e.tensor_tensor(
                    dBu.rearrange("p (n t) -> p n t", t=TC),
                    ysl.unsqueeze(1).broadcast_to([128, NS, TC]),
                    Bb.rearrange("p (n t) -> p n t", t=TC), ALU.mult),
                    wait=[(cV, bcd["B"])], inc=cV)
                dbu_done = cV.n
                for n in range(NS):
                    p.op("vector", lambda e, dc=dc, n=n, dAb=dAb, dBu=dBu: e.tensor_tensor_scan(
                        hb[:, n * TC:(n + 1) * TC], dAb[:, n * TC:(n + 1) * TC],
                        dBu[:, n * TC:(n + 1) * TC],
                        hprev[:, dc * 16 + n:dc * 16 + n + 1],
                        ALU.mult, ALU.add),
                        wait=[(cSc, da_done), (cV, dbu_done)], inc=cV)
                p.op("vector", lambda e, dc=dc: e.tensor_copy(
                    hprev[:, dc * 16:(dc + 1) * 16],
                    hb[:, TC - 1:NS * TC:TC]), inc=cV)
                guard[("l2a_usl", sl)] = (cV, cV.n)
                p.op("vector", lambda e, dAb=dAb: e.tensor_tensor(
                    dAb.rearrange("p (n t) -> p n t", t=TC),
                    hb.rearrange("p (n t) -> p n t", t=TC),
                    Cb.rearrange("p (n t) -> p n t", t=TC), ALU.mult),
                    wait=[(cV, bcd["C"])], inc=cV)
                guard[("l2a_dAb", sl)] = (cV, cV.n)
                p.op("vector", lambda e, ysl=ysl, dAb=dAb: e.reduce_sum(
                    ysl, dAb.rearrange("p (n t) -> p t n", t=TC), AX.X), inc=cV)
                guard[("l2a_dsl", sl)] = (cV, cV.n)
                yd = cV.n
                y_toks[(dc, tcn)] = p.dma("gpsimd", lambda e, dc=dc, tcn=tcn, ysl=ysl: e.dma_start(
                    out=dram["l2a_y"].ap()[dc * 128:(dc + 1) * 128,
                                           tcn * TC:(tcn + 1) * TC], in_=ysl),
                    wait=[(cV, yd)])
                guard[("l2a_ysl", sl)] = y_toks[(dc, tcn)]

        hd = cV.n
        thx = p.dma("gpsimd", lambda e: e.dma_start(
            out=dram["l2a_hin"].ap(), in_=hprev), wait=[(cV, hd)])
        p.op("gpsimd", lambda e: e.collective_compute(
            "AllGather", ALU.bypass, replica_groups=RG,
            ins=[dram["l2a_hin"].ap()], outs=[dram["l2a_hout"].ap()]),
            wait=[thx], inc=cG)
        ccd = cG.n
        t0 = p.dma("gpsimd", lambda e: e.dma_start(
            out=hcA, in_=dram["l2a_hout"].ap()[0:128, :]), wait=[(cG, ccd)])
        t1 = p.dma("gpsimd", lambda e: e.dma_start(
            out=hcB, in_=dram["l2a_hout"].ap()[128:256, :]), wait=[(cG, ccd)])
        p.op("vector", lambda e: e.tensor_scalar(
            hin, hcA, gatesb.ap()[:, 0:1], None, ALU.mult), wait=[t0, t1], inc=cV)
        p.op("vector", lambda e: e.scalar_tensor_tensor(
            hin, hcB, gatesb.ap()[:, 1:2], hin, ALU.mult, ALU.add), inc=cV)
        hin_done = cV.n

        tdd = p.dma("sync", lambda e: e.dma_start(out=hcA[:, 0:16],
                                                  in_=W["l2a_dD"]),
                    wait=[(cV, hin_done)])
        z_toks = {}
        hp2 = hcB
        p.op("vector", lambda e: e.tensor_copy(hp2, hin), wait=[(cV, hin_done)],
             inc=cV)
        for tcn in range(TCN):
            tfl = p.dma("gpsimd", lambda e, tcn=tcn: e.dma_start(
                out=flat[0:1, 0:NS * TC],
                in_=dram["l2a_bc"].ap()[16:32, tcn * TC:(tcn + 1) * TC]),
                wait=gwait("l2a_flat") + [bc_tok, (cV, cV.n)])
            for j in range(0, NS * TC, 512):
                p.op("tensor", lambda e, j=j: e.matmul(
                    ps["psG"].ap()[:, :], ones_r.ap()[:, :],
                    flat[0:1, j:j + 512], start=True, stop=True),
                    wait=[tfl, (cV, ones_ready)] + gwait("psG"), inc=cT)
                td = cT.n
                p.op("vector", lambda e, j=j: e.tensor_copy(
                    Cb[:, j:j + 512], ps["psG"].ap()[:, :]),
                    wait=[(cT, td)], inc=cV)
                gset("psG", cV)
            cb_done = cV.n
            guard["l2a_flat"] = (cV, cV.n)
            for dc in range(DC):
                sl = dc % 2
                dAb, dsl, usl, ysl = dAb2[sl], dsl2[sl], usl2[sl], ysl2[sl]
                rsl = dBu2[sl][:, 0:TC]
                tdl = p.dma("sync", lambda e, dc=dc, tcn=tcn, dsl=dsl: e.dma_start(
                    out=dsl, in_=dram["l2a_delta"].ap()[dc * 128:(dc + 1) * 128,
                                                        tcn * TC:(tcn + 1) * TC]),
                    wait=gwait(("l2a_dsl", sl)))
                for n in range(NS):
                    p.op("scalar", lambda e, dc=dc, n=n, dAb=dAb, dsl=dsl: e.activation(
                        dAb[:, n * TC:(n + 1) * TC], dsl, AF.Exp,
                        scale=Abuf[:, dc * 16 + n:dc * 16 + n + 1]),
                        wait=([tdl] + gwait(("l2a_dAb", sl))) if n == 0 else (),
                        inc=cSc)
                da_done = cSc.n
                guard[("l2a_dsl", sl)] = (cSc, da_done)
                for n in range(NS):
                    p.op("vector", lambda e, dc=dc, n=n, dAb=dAb: e.tensor_tensor_scan(
                        hb[:, n * TC:(n + 1) * TC], dAb[:, n * TC:(n + 1) * TC],
                        zeroc.broadcast_to([128, TC]),
                        hp2[:, dc * 16 + n:dc * 16 + n + 1],
                        ALU.mult, ALU.add),
                        wait=[(cSc, da_done)], inc=cV)
                p.op("vector", lambda e, dc=dc: e.tensor_copy(
                    hp2[:, dc * 16:(dc + 1) * 16],
                    hb[:, TC - 1:NS * TC:TC]), inc=cV)
                p.op("vector", lambda e, dAb=dAb: e.tensor_tensor(
                    dAb.rearrange("p (n t) -> p n t", t=TC),
                    hb.rearrange("p (n t) -> p n t", t=TC),
                    Cb.rearrange("p (n t) -> p n t", t=TC), ALU.mult),
                    wait=[(cV, cb_done)], inc=cV)
                guard[("l2a_dAb", sl)] = (cV, cV.n)
                p.op("vector", lambda e, usl=usl, dAb=dAb: e.reduce_sum(
                    usl, dAb.rearrange("p (n t) -> p t n", t=TC), AX.X),
                    wait=gwait(("l2a_usl", sl)), inc=cV)
                ycd = cV.n
                tyl = p.dma("sync", lambda e, dc=dc, tcn=tcn, ysl=ysl: e.dma_start(
                    out=ysl, in_=dram["l2a_y"].ap()[dc * 128:(dc + 1) * 128,
                                                    tcn * TC:(tcn + 1) * TC]),
                    wait=gwait(("l2a_ysl", sl)) + [y_toks[(dc, tcn)]])
                p.op("vector", lambda e, ysl=ysl, usl=usl: e.tensor_add(
                    ysl, ysl, usl), wait=[tyl], inc=cV)
                ad_ = cV.n
                tu2 = p.dma("sync", lambda e, dc=dc, tcn=tcn, usl=usl: e.dma_start(
                    out=usl, in_=dram["l2a_u"].ap()[dc * 128:(dc + 1) * 128,
                                                    tcn * TC:(tcn + 1) * TC]),
                    wait=[(cV, ad_)])
                tr2 = p.dma("sync", lambda e, dc=dc, tcn=tcn, rsl=rsl: e.dma_start(
                    out=rsl, in_=dram["l2a_r"].ap()[dc * 128:(dc + 1) * 128,
                                                    tcn * TC:(tcn + 1) * TC]),
                    wait=gwait(("l2a_rsl", sl)))
                p.op("scalar", lambda e, rsl=rsl: e.activation(rsl, rsl, AF.Silu),
                     wait=[tr2], inc=cSc)
                sr = cSc.n
                p.op("vector", lambda e, dc=dc, ysl=ysl, usl=usl: e.scalar_tensor_tensor(
                    ysl, usl, hcA[:, dc:dc + 1], ysl, ALU.mult, ALU.add),
                    wait=[tu2, tdd], inc=cV)
                guard[("l2a_usl", sl)] = (cV, cV.n)
                p.op("vector", lambda e, ysl=ysl, rsl=rsl: e.tensor_mul(
                    ysl, ysl, rsl), wait=[(cSc, sr)], inc=cV)
                zd = cV.n
                guard[("l2a_rsl", sl)] = (cV, zd)
                z_toks[(dc, tcn)] = p.dma("gpsimd", lambda e, dc=dc, tcn=tcn, ysl=ysl: e.dma_start(
                    out=dram["l2a_z"].ap()[dc * 128:(dc + 1) * 128,
                                           tcn * TC:(tcn + 1) * TC], in_=ysl),
                    wait=[(cV, zd)])
                guard[("l2a_ysl", sl)] = z_toks[(dc, tcn)]
        arena.release()
        fence()

        arena.mark()
        zbuf = arena.alloc(DC * T)
        wosl = arena.alloc(2 * D)
        xspc = arena.alloc(512)
        ofm_t = arena.alloc(512)
        sq = zbuf[:, 0:D]
        tz = {}
        for dc in range(DC):
            tz[dc] = p.dma("sync", lambda e, dc=dc: e.dma_start(
                out=zbuf[:, dc * T:(dc + 1) * T],
                in_=dram["l2a_z"].ap()[dc * 128:(dc + 1) * 128, :]),
                wait=[z_toks[(dc, tcn2)] for tcn2 in range(TCN)])
        res_done = {}
        for oc in range(DC):
            slot = oc % 2
            wo_s = wosl[:, slot * D:(slot + 1) * D]
            two = p.dma("sync", lambda e, oc=oc, wo_s=wo_s: e.dma_start(
                out=wo_s, in_=W["l2a_wout"][oc]), wait=gwait(("l2a_wo", slot)))
            for tc2 in range(2):
                bank = "psE" if tc2 == 0 else "psF"
                for dc in range(DC):
                    p.op("tensor", lambda e, dc=dc, tc2=tc2, wo_s=wo_s, bank=bank: e.matmul(
                        ps[bank].ap()[:, :], wo_s[:, dc * 128:(dc + 1) * 128],
                        zbuf[:, dc * T + tc2 * 512:dc * T + (tc2 + 1) * 512],
                        start=(dc == 0), stop=(dc == DC - 1)),
                        wait=[two, tz[dc]] + (gwait(bank) if dc == 0 else []),
                        inc=(cT if dc == DC - 1 else None))
                mm = cT.n
                txs = p.dma("sync", lambda e, oc=oc, tc2=tc2: e.dma_start(
                    out=xspc, in_=dram["l2a_xs"].ap()[oc * 128:(oc + 1) * 128,
                                                      tc2 * 512:(tc2 + 1) * 512]),
                    wait=gwait("l2a_xspc") + [xs_toks[(oc, tc2)]])
                p.op("vector", lambda e, bank=bank: e.tensor_add(
                    ofm_t, ps[bank].ap()[:, :], xspc),
                    wait=[(cT, mm), txs] + gwait("l2a_ofmt"), inc=cV)
                gset(bank, cV)
                guard["l2a_xspc"] = (cV, cV.n)
                od = cV.n
                for j in range(4):
                    tbank = TPB[j]
                    p.op("tensor", lambda e, j=j, tbank=tbank: e.transpose(
                        ps[tbank].ap()[:, 0:128], ofm_t[:, j * 128:(j + 1) * 128],
                        sident.ap()[:, :]), wait=[(cV, od)] + gwait(tbank), inc=cT)
                    td = cT.n
                    tt = tc2 * 4 + j
                    p.op("vector", lambda e, tt=tt, oc=oc, tbank=tbank: e.tensor_add(
                        xtile(tt)[:, oc * 128:(oc + 1) * 128],
                        xtile(tt)[:, oc * 128:(oc + 1) * 128],
                        ps[tbank].ap()[:, 0:128]), wait=[(cT, td)], inc=cV)
                    gset(tbank, cV)
                    res_done[(tt, oc)] = (cV, cV.n)
                guard["l2a_ofmt"] = (cV, cV.n)
            gset(("l2a_wo", slot), cT)
        all_mm = (cT, cT.n)
        ln_waits = []
        for tt in range(TT):
            ln_waits.append(ln_tile_inplace(
                xtile(tt), sq,
                wait=[res_done[(tt, oc)] for oc in range(DC)] + [all_mm]))
        arena.release()
        arena.release()
        return [(cSc, max(ln_waits))]

    def stage_l3a(ready):
        return stage_attn(ready, "l3a", 16, False)

    def stage_l0a(ready):
        return stage_attn(ready, "l0a", 4, True)

    stage_fns = {"l1f": stage_l1f, "l0f": stage_l0f, "l3f": stage_l3f,
                 "l1a": stage_l1a, "l3a": stage_l3a, "l0a": stage_l0a,
                 "l2f": stage_l2f, "l2a": stage_l2a}
    if gpre is not None:
        build_l2f_prepass(ppbuf)
    ready = stream_ready
    for s in stages:
        if s not in stage_fns:
            raise NotImplementedError(s)
        p.wait("sync", ready)
        p.wait("gpsimd", ready)
        ready = stage_fns[s](ready)

    toks = []
    for tt in range(TT):
        toks.append(p.dma("gpsimd", lambda e, tt=tt: e.dma_start(
            out=out[tt * 128:(tt + 1) * 128, :], in_=xtile(tt)),
            wait=list(ready)))
    p.wait("gpsimd", toks)

    p.finalize()
    return nc



def _ecslabs(wt):
    nec = wt.shape[1] // 128
    return np.ascontiguousarray(
        wt.reshape(16, 128, nec, 128).transpose(2, 1, 0, 3).reshape(nec, 128, 2048))


def _attn_prep(sh, pre, WqT, WkT, WvT, WoT, bq, bk, bv, bo, kvc):
    sh[pre + "_wq"] = _ecslabs(WqT)
    sh[pre + "_wk"] = _ecslabs(WkT)
    sh[pre + "_wv"] = np.ascontiguousarray(WvT.reshape(16, 128, kvc * 128))
    sh[pre + "_woT"] = np.ascontiguousarray(WoT.reshape(16, 128, 2048))
    sh[pre + "_bqf"] = np.ascontiguousarray(bq.reshape(16, 128).T)
    sh[pre + "_bkf"] = np.ascontiguousarray(bk.reshape(kvc, 128).T)
    sh[pre + "_bv"] = np.ascontiguousarray(bv[None, :])
    sh[pre + "_bo"] = np.ascontiguousarray(bo[None, :])


def _tri_patterns():
    pats = []
    for i in range(4):
        m = np.zeros((128, 512), np.float32)
        for qt in range(4):
            if qt < i:
                m[:, qt * 128:(qt + 1) * 128] = NEG
            elif qt == i:
                m[:, qt * 128:(qt + 1) * 128] = np.where(
                    np.arange(128)[:, None] <= np.arange(128)[None, :], 0.0, NEG
                ).astype(np.float32)
        pats.append(m)
    return np.concatenate(pats, axis=1)


def _rope_tables(t0):
    d = 128
    invf = 1.0 / (10000.0 ** (np.arange(0, d, 2) / d))
    tpos = np.arange(t0, t0 + T, dtype=np.float64)
    emb = np.concatenate([np.outer(invf, tpos)] * 2, axis=0)
    cos = np.cos(emb).astype(np.float32)
    sin = np.sin(emb).astype(np.float32)
    return np.ascontiguousarray(cos), np.ascontiguousarray(sin)


def _prep_weights(inputs, stages=ALL_STAGES):
    sh = {"ident": np.eye(128, dtype=np.float32)}
    f32 = lambda a: np.asarray(a, dtype=np.float32)
    if "l3a" in stages:
        WqkvT = f32(inputs["L3_attn_W_qkv"]).T
        bqkv = f32(inputs["L3_attn_b_qkv"])
        _attn_prep(sh, "l3a", WqkvT[:, 0:2048], WqkvT[:, 2048:4096],
                   WqkvT[:, 4096:6144], f32(inputs["L3_attn_W_out"]).T,
                   bqkv[0:2048], bqkv[2048:4096], bqkv[4096:6144],
                   f32(inputs["L3_attn_b_out"]), 16)
    if "l0a" in stages:
        z = np.zeros(2048, np.float32)
        zk = np.zeros(512, np.float32)
        _attn_prep(sh, "l0a", f32(inputs["L0_attn_Wq"]).T,
                   f32(inputs["L0_attn_Wk"]).T, f32(inputs["L0_attn_Wv"]).T,
                   f32(inputs["L0_attn_Wo"]).T, z, zk, zk, z, 4)
    if "l0f" in stages:
        Wb = f32(inputs["L0_ffn_W"])
        sh["l0f_wT"] = np.ascontiguousarray(Wb.T.reshape(16, 128, 2048))
        acat = f32(inputs["L0_ffn_lora_A"]).transpose(2, 0, 1).reshape(2048, 128)
        sh["l0f_acat"] = np.ascontiguousarray(
            acat.reshape(16, 128, 128).transpose(1, 0, 2).reshape(128, 2048))
        sh["l0f_bcat"] = np.ascontiguousarray(
            f32(inputs["L0_ffn_lora_B"]).transpose(0, 2, 1).reshape(128, 2048))
        gwt = f32(inputs["L0_ffn_gate_w"]).T
        sh["l0f_gwt"] = np.ascontiguousarray(
            gwt.reshape(16, 128, 8).transpose(1, 0, 2).reshape(128, 128))
        sh["l0f_brow"] = np.ascontiguousarray(f32(inputs["L0_ffn_b"])[None, :])
        sh["l0f_gbrow"] = np.ascontiguousarray(f32(inputs["L0_ffn_gate_b"])[None, :])
    if "l3f" in stages:
        W1 = f32(inputs["L3_ffn_W1"])
        sh["l3f_w1"] = np.ascontiguousarray(
            W1.reshape(16, 128, 16, 128).transpose(2, 1, 0, 3).reshape(16, 128, 2048))
        vcat = f32(inputs["L3_ffn_V"]).transpose(2, 0, 1).reshape(2048, 512)
        sh["l3f_vcat"] = np.ascontiguousarray(vcat.reshape(16, 128, 512))
        ucat = f32(inputs["L3_ffn_U"]).transpose(0, 2, 1).reshape(512, 2048)
        sh["l3f_ucat"] = np.ascontiguousarray(ucat.reshape(4, 128, 2048))
        w2 = f32(inputs["L3_ffn_W2"])
        sh["l3f_w2"] = np.ascontiguousarray(
            w2.reshape(16, 128, 8).transpose(1, 0, 2).reshape(128, 128))
        sh["l3f_b1f"] = np.ascontiguousarray(f32(inputs["L3_ffn_b1"]).reshape(16, 128).T)
        sh["l3f_b2row"] = np.ascontiguousarray(f32(inputs["L3_ffn_b2"])[None, :])
    if "l1a" in stages:
        for nm, key in (("l1a_wq", "L1_attn_Wq"), ("l1a_wk", "L1_attn_Wk"),
                        ("l1a_wv", "L1_attn_Wv")):
            wt_ = f32(inputs[key]).T
            sh[nm] = np.ascontiguousarray(
                wt_.reshape(16, 128, 16, 128).transpose(2, 1, 0, 3).reshape(16, 128, 2048))
        sh["l1a_woT"] = np.ascontiguousarray(f32(inputs["L1_attn_Wo"]).T.reshape(16, 128, 2048))
        for nm, key in (("l1a_bq", "L1_attn_bq"), ("l1a_bk", "L1_attn_bk"),
                        ("l1a_bv", "L1_attn_bv")):
            sh[nm] = np.ascontiguousarray(f32(inputs[key]).reshape(16, 128).T)
        sh["l1a_bo"] = np.ascontiguousarray(f32(inputs["L1_attn_bo"])[None, :])
    if "l2a" in stages:
        Win = f32(inputs["L2_attn_W_in"])
        sh["l2a_win"] = np.ascontiguousarray(
            Win.reshape(16, 128, 32, 128).transpose(2, 1, 0, 3).reshape(32, 128, 2048))
        cwT = f32(inputs["L2_attn_conv_w"]).transpose(2, 1, 0)
        sh["l2a_cw"] = np.ascontiguousarray(
            cwT.reshape(4, 16, 128, 16, 128).transpose(3, 2, 0, 1, 4)
            .reshape(16, 128, 8192))
        sh["l2a_convb"] = np.ascontiguousarray(
            f32(inputs["L2_attn_conv_b"]).reshape(16, 128).T)
        wx = f32(inputs["L2_attn_W_x"])
        sh["l2a_wx"] = np.ascontiguousarray(
            wx.reshape(16, 128, 160).transpose(1, 0, 2).reshape(128, 16 * 160))
        sh["l2a_wdt"] = np.ascontiguousarray(f32(inputs["L2_attn_W_dt"]))
        sh["l2a_bdt"] = np.ascontiguousarray(
            f32(inputs["L2_attn_b_dt"]).reshape(16, 128).T)
        sh["l2a_alog"] = np.ascontiguousarray(
            f32(inputs["L2_attn_A_log"]).reshape(16, 128, 16))
        sh["l2a_dD"] = np.ascontiguousarray(
            f32(inputs["L2_attn_D"]).reshape(16, 128).T)
        sh["l2a_wout"] = np.ascontiguousarray(
            f32(inputs["L2_attn_W_out"]).reshape(16, 128, 16, 128)
            .transpose(2, 1, 0, 3).reshape(16, 128, 2048))
    if "l2f" in stages:
        sh["l2f_gw"] = np.ascontiguousarray(
            f32(inputs["L2_ffn_gate_w"]).T.reshape(16, 128, 8192))
        sh["l2f_vw"] = np.ascontiguousarray(
            f32(inputs["L2_ffn_val_w"]).T.reshape(16, 128, 8192))
        sh["l2f_ow"] = np.ascontiguousarray(
            f32(inputs["L2_ffn_out_w"]).T.reshape(64, 128, 2048))
        sh["l2f_gb"] = np.ascontiguousarray(f32(inputs["L2_ffn_gate_b"])[None, :])
        sh["l2f_vb"] = np.ascontiguousarray(f32(inputs["L2_ffn_val_b"])[None, :])
        sh["l2f_ob"] = np.ascontiguousarray(f32(inputs["L2_ffn_out_b"])[None, :])
    if "l1f" in stages:
        gwT = f32(inputs["L1_ffn_gate_w"]).T
        uwT = f32(inputs["L1_ffn_up_w"]).T
        dwT = f32(inputs["L1_ffn_down_w"]).T
        sh["l1f_gw"] = np.ascontiguousarray(
            gwT.reshape(16, 128, 44, 128).transpose(2, 1, 0, 3).reshape(44, 128, 2048))
        sh["l1f_uw"] = np.ascontiguousarray(
            uwT.reshape(16, 128, 44, 128).transpose(2, 1, 0, 3).reshape(44, 128, 2048))
        sh["l1f_dw"] = np.ascontiguousarray(dwT.reshape(44, 128, 2048))
        sh["l1f_gs"] = np.ascontiguousarray(
            f32(inputs["L1_ffn_gate_s"]).reshape(44, 128).T)
        sh["l1f_us"] = np.ascontiguousarray(
            f32(inputs["L1_ffn_up_s"]).reshape(44, 128).T)
        sh["l1f_ds"] = np.ascontiguousarray(f32(inputs["L1_ffn_down_s"])[None, :])
    return sh


def _percore_maps(inputs, stages=ALL_STAGES):
    hs = np.asarray(inputs["L0_attn_hidden_states"], dtype=np.float32)
    sh = _prep_weights(inputs, stages)
    maps = []
    pats = _tri_patterns()
    zer = np.zeros_like(pats)
    triA = np.ascontiguousarray(np.concatenate([pats, zer], axis=1))
    triB = np.ascontiguousarray(np.concatenate([zer, pats], axis=1))
    kmA = np.zeros((128, 32), np.float32)
    for qb in range(2):
        for gg in range(8, 16):
            kmA[:, qb * 16 + gg] = NEG
        for gg in range(4 * qb + 4, 8):
            kmA[:, qb * 16 + gg] = NEG
    kmB = np.zeros((128, 32), np.float32)
    for core in range(8):
        seq, t0 = core // 2, (core % 2) * T
        m = dict(sh)
        m["x0"] = np.ascontiguousarray(hs[seq, t0:t0 + T, :])
        g = np.zeros((128, 8), dtype=np.float32)
        if core % 2 == 1:
            g[:, 0] = 1.0
            g[:, 2] = 1.0
        else:
            g[:, 3] = 1.0
        m["gates"] = g
        for pre in ("l3a", "l0a"):
            if pre + "_trisel" not in sh and pre + "_wq" not in sh:
                continue
            m[pre + "_trisel"] = triB if core % 2 else triA
            m[pre + "_kmaskq"] = kmB if core % 2 else kmA
        if "l0a_wq" in sh:
            cos, sinm = _rope_tables(t0)
            m["l0a_cos"] = cos
            m["l0a_sinm"] = sinm
            R = np.zeros((128, 128), np.float32)
            for dd in range(64):
                R[dd, dd + 64] = -1.0
                R[dd + 64, dd] = 1.0
            m["rotm"] = np.ascontiguousarray(R.T)
        maps.append(m)
    return maps


def run(inputs, stages=ALL_STAGES):
    maps = _percore_maps(inputs, stages)
    nc = build_program(stages)
    res = run_bass_kernel_spmd(nc, maps, core_ids=list(range(8)))
    outa = np.empty((4, 2048, 2048), dtype=np.float32)
    for core in range(8):
        seq, t0 = core // 2, (core % 2) * T
        outa[seq, t0:t0 + T, :] = res.results[core]["out"]
    return outa


def kernel(**inputs):
    return run(inputs, ALL_STAGES)
